# version 07
# baseline (speedup 1.0000x reference)
import contextlib
import numpy as np

import concourse.bass as bass
import concourse.mybir as mybir
from concourse.bass_utils import run_bass_kernel_spmd

F32 = mybir.dt.float32
BF16 = mybir.dt.bfloat16
I32 = mybir.dt.int32
AF = mybir.ActivationFunctionType
ALU = mybir.AluOpType

L = 8192
DM = 1024
DI = 2048
NDB = 16
KC = 8
N = 64
G = 4
NGRP = N // G
T = 256
TT = T // 128
W = G * (T + 1)
HF = 3072
NHC = 24
EPS = 1e-5
FS = float(1.0 / 127.0)

SEMS = ("LD", "PE", "ACT", "DVE", "POOL", "OUT")
ENGS = ("sync", "vector", "scalar", "tensor", "gpsimd")


DMA_SEMS = ("LD", "OUT")


class Em:
    def __init__(self):
        self.q = {e: [] for e in ENGS}
        self.cnt = {s: 0 for s in SEMS}
        self.hw = {e: {s: 0 for s in SEMS} for e in ENGS}
        self.pend = {s: 0 for s in DMA_SEMS}

    def wait(self, eng, sem, val):
        if val is None or val <= 0:
            return
        if sem in DMA_SEMS:
            val = self.cnt[sem]
            self.pend[sem] = max(self.pend[sem], val)
        assert val <= self.cnt[sem], f"wait-on-future {sem} {val} > {self.cnt[sem]}"
        if self.hw[eng][sem] >= val:
            return
        self.hw[eng][sem] = val
        self.q[eng].append(("W", sem, val))

    ENG_SEM = {"vector": "DVE", "scalar": "ACT", "gpsimd": "POOL", "tensor": "PE"}

    def op(self, eng, sem, amt, fn):
        own = self.ENG_SEM.get(eng)
        if own is not None and eng != "tensor":
            v = self.cnt[own]
            if v > self.hw[eng][own]:
                self.hw[eng][own] = v
                self.q[eng].append(("W", own, v))
        if sem in DMA_SEMS and self.pend[sem] > self.hw[eng][sem]:
            v = self.pend[sem]
            self.hw[eng][sem] = v
            self.q[eng].append(("W", sem, v))
        self.cnt[sem] += amt
        self.q[eng].append(("I", sem, amt, fn))
        return self.cnt[sem]

    def nosem(self, eng, fn):
        self.q[eng].append(("N", fn))

    def barrier(self, *sems_now):
        snap = {s: self.cnt[s] for s in (sems_now or SEMS)}
        for e in ENGS:
            for s, v in snap.items():
                self.wait(e, s, v)

    def replay(self, eng, E, sems):
        for item in self.q[eng]:
            if item[0] == "W":
                E.wait_ge(sems[item[1]], item[2])
            elif item[0] == "I":
                item[3](E).then_inc(sems[item[1]], item[2])
            else:
                item[1](E)


def build_nc(nch=L // T, dbg=False):
    nc = bass.Bass("TRN2", target_bir_lowering=False)
    NCH = nch

    x_d = nc.dram_tensor("x", [L, DM], F32, kind="ExternalInput").ap()
    w_in_d = nc.dram_tensor("w_in", [DM, 2 * DI], F32, kind="ExternalInput").ap()
    wxp_d = nc.dram_tensor("wxp", [DI, 192], F32, kind="ExternalInput").ap()
    wdt_d = nc.dram_tensor("wdt", [64, DI], F32, kind="ExternalInput").ap()
    wout_d = nc.dram_tensor("wout", [DI, DM], F32, kind="ExternalInput").ap()
    alog_d = nc.dram_tensor("alog", [128, NDB * N], F32, kind="ExternalInput").ap()
    dvec_d = nc.dram_tensor("dvec", [128, NDB], F32, kind="ExternalInput").ap()
    convw_d = nc.dram_tensor("convw", [128, NDB * 4], F32, kind="ExternalInput").ap()
    convb_d = nc.dram_tensor("convb", [128, NDB], F32, kind="ExternalInput").ap()
    bdt_d = nc.dram_tensor("bdt", [128, NDB], F32, kind="ExternalInput").ap()
    lnw_d = nc.dram_tensor("lnw", [1, DM], F32, kind="ExternalInput").ap()
    lnb_d = nc.dram_tensor("lnb", [1, DM], F32, kind="ExternalInput").ap()
    wgT_d = nc.dram_tensor("wgT", [DM, HF], I32, kind="ExternalInput").ap()
    wuT_d = nc.dram_tensor("wuT", [DM, HF], I32, kind="ExternalInput").ap()
    wdT_d = nc.dram_tensor("wdT", [HF, DM], I32, kind="ExternalInput").ap()
    idf_d = nc.dram_tensor("idf", [128, 128], F32, kind="ExternalInput").ap()
    idb_d = nc.dram_tensor("idb", [128, 128], BF16, kind="ExternalInput").ap()
    out_d = nc.dram_tensor("out", [L, DM], F32, kind="ExternalOutput").ap()

    wxi_s = nc.dram_tensor("wxi_s", [128, NDB * KC * 128], BF16, kind="Internal").ap()
    wz_s = nc.dram_tensor("wz_s", [128, NDB * KC * 128], BF16, kind="Internal").ap()
    wout_s = nc.dram_tensor("wout_s", [128, NDB * DM], BF16, kind="Internal").ap()
    wg_s = nc.dram_tensor("wg_s", [128, NHC * KC * 128], BF16, kind="Internal").ap()
    wu_s = nc.dram_tensor("wu_s", [128, NHC * KC * 128], BF16, kind="Internal").ap()
    wd_s = nc.dram_tensor("wd_s", [128, NHC * DM], BF16, kind="Internal").ap()
    bflat_d = nc.dram_tensor("bflat_d", [N, T], F32, kind="Internal").ap()

    dbg_outs = {}
    if dbg:
        for nm, shape in (("d_xn2", [128, KC * T]), ("d_u", [128, NDB * T]),
                          ("d_delta", [128, NDB * T]), ("d_B", [64, T]),
                          ("d_C", [64, T]), ("d_yacc", [128, TT * DI]),
                          ("d_y1", [128, NDB * T])):
            dt = BF16 if nm == "d_xn2" else F32
            dbg_outs[nm] = nc.dram_tensor(nm, shape, dt, kind="ExternalOutput").ap()

    em = Em()
    es = contextlib.ExitStack()
    with es:
        sb = lambda name, shape, dt=F32: es.enter_context(nc.sbuf_tensor(name, shape, dt))
        wxp_sb = sb("wxp_sb", [128, NDB * 256], BF16)
        wdt_sb = sb("wdt_sb", [128, DI], BF16)
        a_sb = sb("a_sb", [128, NDB * N])
        dvec_sb = sb("dvec_sb", [128, NDB])
        convw_sb = sb("convw_sb", [128, NDB * 4])
        convb_sb = sb("convb_sb", [128, NDB])
        bdt_sb = sb("bdt_sb", [128, NDB])
        lnwb_sb = sb("lnwb_sb", [128, 2 * DM])
        row_sb = sb("row_sb", [1, G * T])
        ones1_sb = sb("ones1_sb", [1, 128])
        idf_sb = sb("idf_sb", [128, 128])
        idb_sb = sb("idb_sb", [128, 128], BF16)
        wstream = sb("wstream", [128, 2048], BF16)
        wzbuf = sb("wzbuf", [128, 2 * 1024], BF16)
        woutb = sb("woutb", [128, 2 * 1024], BF16)
        wgb = sb("wgb", [128, 2 * 1024], BF16)
        wub = sb("wub", [128, 2 * 1024], BF16)
        wdb = sb("wdb", [128, 2 * 1024], BF16)
        xt_sb = sb("xt_sb", [128, DM])
        bn_sb = sb("bn_sb", [128, 16])
        ko_sb = sb("ko_sb", [128, 8])
        sc_d = sb("sc_d", [128, TT * 8])
        sc_a = sb("sc_a", [128, TT * 8])
        big1 = sb("big1", [128, DM])
        xn2tm = sb("xn2tm", [128, DM], BF16)
        xn2fm = sb("xn2fm", [128, KC * T], BF16)
        xib = sb("xib", [128, 3 + T])
        cvb = sb("cvb", [128, T])
        halo_sb = sb("halo_sb", [128, NDB * 3])
        u_sb = sb("u_sb", [128, NDB * T])
        ubf_sb = sb("ubf_sb", [128, 2 * T], BF16)
        delta_sb = sb("delta_sb", [128, NDB * T])
        du_sb = sb("du_sb", [128, NDB * T])
        dtrb_sb = sb("dtrb_sb", [128, T], BF16)
        Bsb = sb("Bsb", [128, T])
        Csb = sb("Csb", [128, T])
        ctm_sb = sb("ctm_sb", [128, TT * N])
        dec_sb = [sb(f"dec{i}", [128, W]) for i in range(2)]
        dbu_sb = [sb(f"dbu{i}", [128, W]) for i in range(2)]
        NH = 6
        h_sb = [sb(f"h{i}", [128, W]) for i in range(NH)]
        bbc_sb = sb("bbc_sb", [128, 2 * G * T])
        hstate_sb = sb("hstate_sb", [128, NDB * N])
        yacc_sb = sb("yacc_sb", [128, TT * DI])
        y1f_sb = sb("y1f_sb", [128, T])
        sz_sb = sb("sz_sb", [128, T])
        y1bf_sb = sb("y1bf_sb", [128, 2 * T], BF16)
        x2_sb = sb("x2_sb", [128, TT * DM])
        xn3f_sb = sb("xn3f_sb", [128, TT * KC * 128], BF16)
        hh_sb = sb("hh_sb", [128, 512], BF16)

        ssp_sb = cvb
        sfn_sb = big1
        xn3t_sb = xn2tm
        x3_sb = big1
        psA = es.enter_context(nc.psum_tensor("psA", [128, 512], F32))
        psB = es.enter_context(nc.psum_tensor("psB", [128, 512], F32))
        psC = es.enter_context(nc.psum_tensor("psC", [128, 512], F32))
        psE = es.enter_context(nc.psum_tensor("psE", [128, 1024], F32))
        psF = es.enter_context(nc.psum_tensor("psF", [128, 1024], F32))
        psT = es.enter_context(nc.psum_tensor("psT", [128, 1024], BF16))

        sems = {s: es.enter_context(nc.semaphore(s)) for s in SEMS}
        block = es.enter_context(nc.Block())

        LD, PE, ACT, DVE, POOL, OUT = SEMS

        def dma(src, dst):
            return em.op("sync", LD, 16, lambda E, s=src, d=dst: E.dma_start(out=d, in_=s))

        ld0 = dma(alog_d, a_sb[:])
        dma(dvec_d, dvec_sb[:])
        dma(convw_d, convw_sb[:])
        dma(convb_d, convb_sb[:])
        dma(bdt_d, bdt_sb[:])
        dma(idf_d, idf_sb[:])
        ld_idb = dma(idb_d, idb_sb[:])

        em.wait("scalar", LD, ld0)
        a_exp = em.op("scalar", ACT, 1, lambda E: E.activation(
            out=big1[:], in_=a_sb[:], func=AF.Exp))
        em.wait("vector", ACT, a_exp)
        a_done = em.op("vector", DVE, 1, lambda E: E.tensor_scalar_mul(
            out=a_sb[:], in0=big1[:], scalar1=-1.0))
        ones_done = em.op("vector", DVE, 1, lambda E: E.memset(ones1_sb[:], 1.0))
        for i, v in enumerate((1.0 + EPS, EPS * EPS, EPS, 1.0, -1.0, -0.5, FS)):
            kdone = em.op("vector", DVE, 1, lambda E, ii=i, vv=v: E.memset(
                ko_sb[:, ii:ii + 1], vv))
        em.wait("scalar", DVE, kdone)
        z_dve = em.op("vector", DVE, 1, lambda E: E.memset(halo_sb[:], 0.0))
        z_dve = em.op("vector", DVE, 1, lambda E: E.memset(hstate_sb[:], 0.0))
        for d in dec_sb:
            z_dve = em.op("vector", DVE, 1, lambda E, dd=d: E.memset(dd[:], 0.0))

        em.wait("tensor", DVE, ones_done)
        em.wait("tensor", LD, ld_idb)
        lnbc_done = 0
        for which, srcd in ((0, lnw_d), (1, lnb_d)):
            em.wait("sync", ACT, lnbc_done)
            lrow = dma(srcd, row_sb[:, 0:DM])
            em.wait("tensor", LD, lrow)
            pe_c = 0
            for half in range(2):
                pe_c = em.op("tensor", PE, 1, lambda E, hf=half: E.matmul(
                    out=psE[:, hf * 512:(hf + 1) * 512], lhsT=ones1_sb[:],
                    rhs=row_sb[:, hf * 512:hf * 512 + 512], start=True, stop=True))
            em.wait("scalar", PE, pe_c)
            for half in range(2):
                lnbc_done = em.op("scalar", ACT, 1, lambda E, w=which, hf=half:
                                  E.activation(
                    out=lnwb_sb[:, w * DM + hf * 512: w * DM + hf * 512 + 512],
                    in_=psE[:, hf * 512:(hf + 1) * 512], func=AF.Copy))

        wi32 = xt_sb[:].bitcast(I32)
        cvt = a_done
        for dst_s, c0 in ((wxi_s, 0), (wz_s, DI)):
            for kc in range(KC):
                for hh in range(2):
                    em.wait("sync", DVE, cvt)
                    lw = dma(w_in_d[kc * 128:(kc + 1) * 128,
                                    c0 + hh * DM:c0 + (hh + 1) * DM], big1[:])
                    em.wait("vector", LD, lw)
                    cvt = em.op("vector", DVE, 1, lambda E, h2=hh: E.tensor_copy(
                        out=wstream[:, h2 * DM:(h2 + 1) * DM], in_=big1[:]))
                em.wait("sync", DVE, cvt)
                dv = dst_s.rearrange("p (db k j) -> p db k j", db=NDB, k=KC)[:, :, kc, :]
                sv = wstream[:].rearrange("p (db j) -> p db j", db=NDB)
                wb = em.op("sync", LD, 16, lambda E, d=dv, s=sv: E.dma_start(out=d, in_=s))
                em.wait("vector", LD, wb)
        for kc in range(NDB):
            em.wait("sync", DVE, cvt)
            lw = dma(wout_d[kc * 128:(kc + 1) * 128, :], big1[:])
            em.wait("vector", LD, lw)
            cvt = em.op("vector", DVE, 1, lambda E: E.tensor_copy(
                out=wstream[:, 0:DM], in_=big1[:]))
            em.wait("sync", DVE, cvt)
            wb = em.op("sync", LD, 16, lambda E, k=kc: E.dma_start(
                out=wout_s[:, k * DM:(k + 1) * DM], in_=wstream[:, 0:DM]))
            em.wait("vector", LD, wb)
        zw = em.op("vector", DVE, 1, lambda E: E.memset(wxp_sb[:], 0.0))
        for kc in range(NDB):
            em.wait("sync", DVE, max(cvt, zw))
            lw = dma(wxp_d[kc * 128:(kc + 1) * 128, :], big1[:, 0:192])
            em.wait("vector", LD, lw)
            cvt = em.op("vector", DVE, 1, lambda E, k=kc: E.tensor_copy(
                out=wxp_sb[:, k * 256:k * 256 + 192], in_=big1[:, 0:192]))
        for hh in range(2):
            em.wait("sync", DVE, cvt)
            lw = em.op("sync", LD, 16, lambda E, h2=hh: E.dma_start(
                out=big1[0:64, :], in_=wdt_d[:, h2 * DM:(h2 + 1) * DM]))
            em.wait("vector", LD, lw)
            cvt = em.op("vector", DVE, 1, lambda E, h2=hh: E.tensor_copy(
                out=wdt_sb[0:64, h2 * DM:(h2 + 1) * DM], in_=big1[0:64, :]))
            em.wait("sync", DVE, cvt)
        for src_d, dst_s in ((wgT_d, wg_s), (wuT_d, wu_s)):
            for kc in range(KC):
                for hh in range(3):
                    em.wait("sync", DVE, cvt)
                    lw = dma(src_d[kc * 128:(kc + 1) * 128, hh * 1024:(hh + 1) * 1024], wi32)
                    em.wait("vector", LD, lw)
                    cvt = em.op("vector", DVE, 1, lambda E: E.tensor_copy(
                        out=wstream[:, 0:DM], in_=wi32))
                    em.wait("sync", DVE, cvt)
                    dv = dst_s.rearrange("p (hc k j) -> p hc k j", hc=NHC, k=KC)[
                        :, hh * 8:(hh + 1) * 8, kc, :]
                    sv = wstream[:, 0:DM].rearrange("p (hc j) -> p hc j", hc=8)
                    wb = em.op("sync", LD, 16, lambda E, d=dv, s=sv: E.dma_start(out=d, in_=s))
                    em.wait("vector", LD, wb)
        for hc in range(NHC):
            em.wait("sync", DVE, cvt)
            lw = dma(wdT_d[hc * 128:(hc + 1) * 128, :], wi32)
            em.wait("vector", LD, lw)
            cvt = em.op("vector", DVE, 1, lambda E: E.tensor_copy(
                out=wstream[:, 0:DM], in_=wi32))
            em.wait("sync", DVE, cvt)
            wb = em.op("sync", LD, 16, lambda E, h=hc: E.dma_start(
                out=wd_s[:, h * DM:(h + 1) * DM], in_=wstream[:, 0:DM]))
            em.wait("vector", LD, wb)

        em.barrier()

        last = {"dec_rd": [0, 0], "dbu_rd": [0, 0],
                "h_rd_pe": [0] * 6, "h_rd_act": [0] * 6}
        for i in range(6):
            last[f"h_wr_{i}"] = 0

        def emit_A(c):
            for tt in range(TT):
                em.wait("sync", DVE, last.get("xt_rd_dve", 0))
                ldx = dma(x_d[c * T + tt * 128: c * T + (tt + 1) * 128, :], xt_sb[:])
                em.wait("vector", LD, ldx)
                em.op("vector", DVE, 1, lambda E: E.bn_stats(
                    out=bn_sb[:, 0:6], in_=xt_sb[:, 0:512]))
                em.op("vector", DVE, 1, lambda E: E.bn_stats(
                    out=bn_sb[:, 6:12], in_=xt_sb[:, 512:1024]))
                em.op("vector", DVE, 1, lambda E: E.engine_nop())
                bs = em.op("vector", DVE, 1, lambda E: E.bn_aggr(
                    out=bn_sb[:, 12:14], in_=bn_sb[:, 0:12]))
                o = tt * 8
                bs = em.op("vector", DVE, 1, lambda E, o=o: E.tensor_copy(
                    out=sc_d[:, o:o + 2], in_=bn_sb[:, 12:14]))
                em.wait("scalar", DVE, bs)
                em.op("scalar", ACT, 1, lambda E, o=o: E.activation(
                    out=sc_a[:, o + 3:o + 4], in_=sc_d[:, o + 1:o + 2], func=AF.Ln,
                    scale=ko_sb[:, 0:1], bias=ko_sb[:, 1:2]))
                em.op("scalar", ACT, 1, lambda E, o=o: E.activation(
                    out=sc_a[:, o + 4:o + 5], in_=sc_d[:, o + 1:o + 2], func=AF.Ln,
                    bias=ko_sb[:, 2:3]))
                em.op("scalar", ACT, 1, lambda E, o=o: E.activation(
                    out=sc_a[:, o:o + 1], in_=sc_a[:, o + 3:o + 4], func=AF.Exp,
                    scale=ko_sb[:, 5:6]))
                ac = em.op("scalar", ACT, 1, lambda E, o=o: E.activation(
                    out=sc_a[:, o + 1:o + 2], in_=sc_a[:, o + 4:o + 5], func=AF.Exp,
                    scale=ko_sb[:, 5:6]))
                em.wait("vector", ACT, ac)
                em.op("vector", DVE, 1, lambda E, o=o: E.scalar_tensor_tensor(
                    out=sc_d[:, o + 2:o + 3], in0=sc_d[:, o:o + 1], scalar=-1.0,
                    in1=sc_a[:, o:o + 1], op0=ALU.mult, op1=ALU.mult))
                em.op("vector", DVE, 1, lambda E, o=o: E.scalar_tensor_tensor(
                    out=sc_d[:, o + 3:o + 4], in0=sc_d[:, o:o + 1], scalar=-1.0,
                    in1=sc_a[:, o + 1:o + 2], op0=ALU.mult, op1=ALU.mult))
                xrd = em.op("vector", DVE, 1, lambda E, o=o: E.scalar_tensor_tensor(
                    out=xt_sb[:], in0=xt_sb[:], scalar=sc_a[:, o:o + 1],
                    in1=lnwb_sb[:, 0:DM], op0=ALU.mult, op1=ALU.mult))
                em.op("vector", DVE, 1, lambda E, o=o: E.scalar_tensor_tensor(
                    out=xt_sb[:], in0=lnwb_sb[:, 0:DM], scalar=sc_d[:, o + 2:o + 3],
                    in1=xt_sb[:], op0=ALU.mult, op1=ALU.add))
                dv = em.op("vector", DVE, 1, lambda E: E.tensor_add(
                    out=xn2tm[:], in0=xt_sb[:], in1=lnwb_sb[:, DM:2 * DM]))
                last["xt_rd_dve"] = dv
                em.wait("tensor", DVE, dv)
                em.wait("tensor", ACT, last.get("psT_rd", 0))
                pe_t = 0
                for kc in range(KC):
                    pe_t = em.op("tensor", PE, 1, lambda E, k=kc: E.matmul(
                        out=psT[:, k * 128:(k + 1) * 128],
                        lhsT=xn2tm[:, k * 128:(k + 1) * 128], rhs=idb_sb[:],
                        is_transpose=True, start=(k == 0), stop=(k == KC - 1)))
                em.wait("scalar", PE, pe_t)
                ev = 0
                for kc in range(KC):
                    ev = em.op("scalar", ACT, 1, lambda E, k=kc, t=tt: E.activation(
                        out=xn2fm[:, k * T + t * 128:k * T + t * 128 + 128],
                        in_=psT[:, k * 128:(k + 1) * 128], func=AF.Copy))
                last["xn2fm_wr"] = ev
                last["psT_rd"] = ev

            last["pending_xdbl"] = None

            def flush_xdbl():
                d, p, ub = last["pending_xdbl"]
                em.wait("tensor", DVE, ub)
                if d == 0:
                    em.wait("tensor", ACT, last.get("xdbl_ps_rd", 0))
                    em.wait("tensor", ACT, last.get("z_ps_rd", 0))
                em.op("tensor", PE, 1, lambda E, d=d, p=p: E.matmul(
                    out=psC[:, 0:T], lhsT=wxp_sb[:, d * 256:d * 256 + 128],
                    rhs=ubf_sb[:, p * T:(p + 1) * T], start=(d == 0), stop=False))
                pe_d = em.op("tensor", PE, 1, lambda E, d=d, p=p: E.matmul(
                    out=psC[:, T:2 * T], lhsT=wxp_sb[:, d * 256 + 128:d * 256 + 256],
                    rhs=ubf_sb[:, p * T:(p + 1) * T], start=False, stop=(d == NDB - 1)))
                last[f"ubf_rd_{p}"] = pe_d
                last["pending_xdbl"] = None

            for db in range(NDB):
                par = db % 2
                em.wait("sync", PE, last.get(f"wxi_rd_{par}", 0))
                lw = dma(wxi_s[:, db * 1024:(db + 1) * 1024],
                         wstream[:, par * 1024:(par + 1) * 1024])
                em.wait("tensor", LD, lw)
                em.wait("tensor", ACT, last["xn2fm_wr"])
                em.wait("tensor", ACT, last.get("xi_ps_rd", 0))
                em.wait("tensor", DVE, last.get("g_dve_1", 0))
                em.wait("tensor", DVE, last.get("yf_ps_rd", 0))
                if last.get("pending_xdbl") is not None:
                    flush_xdbl()
                pe_x = 0
                for kc in range(KC):
                    pe_x = em.op("tensor", PE, 1, lambda E, k=kc, p=par: E.matmul(
                        out=psA[:, 0:T],
                        lhsT=wstream[:, p * 1024 + k * 128:p * 1024 + (k + 1) * 128],
                        rhs=xn2fm[:, k * T:(k + 1) * T],
                        start=(k == 0), stop=(k == KC - 1)))
                last[f"wxi_rd_{par}"] = pe_x
                em.wait("scalar", PE, pe_x)
                em.wait("scalar", DVE, last.get("xib_rd", 0))
                em.op("scalar", ACT, 1, lambda E, d=db: E.activation(
                    out=xib[:, 0:3], in_=halo_sb[:, d * 3:(d + 1) * 3], func=AF.Copy))
                ev = em.op("scalar", ACT, 1, lambda E: E.activation(
                    out=xib[:, 3:3 + T], in_=psA[:, 0:T], func=AF.Copy))
                last["xi_ps_rd"] = ev
                em.wait("vector", ACT, ev)
                em.wait("vector", ACT, last.get("cvb_rd", 0))
                cv = em.op("vector", DVE, 1, lambda E, d=db: E.tensor_scalar(
                    out=cvb[:], in0=xib[:, 0:T], scalar1=convw_sb[:, d * 4:d * 4 + 1],
                    scalar2=None, op0=ALU.mult))
                for k in range(1, 4):
                    cv = em.op("vector", DVE, 1, lambda E, d=db, kk=k: E.scalar_tensor_tensor(
                        out=cvb[:], in0=xib[:, kk:kk + T],
                        scalar=convw_sb[:, d * 4 + kk:d * 4 + kk + 1],
                        in1=cvb[:], op0=ALU.mult, op1=ALU.add))
                last["xib_rd"] = cv
                em.wait("scalar", DVE, cv)
                uu = em.op("scalar", ACT, 1, lambda E, d=db: E.activation(
                    out=u_sb[:, d * T:(d + 1) * T], in_=cvb[:], func=AF.Silu,
                    bias=convb_sb[:, d:d + 1]))
                last["cvb_rd"] = uu
                em.op("scalar", ACT, 1, lambda E, d=db: E.activation(
                    out=halo_sb[:, d * 3:(d + 1) * 3], in_=xib[:, T:T + 3], func=AF.Copy))
                em.wait("vector", ACT, uu)
                em.wait("vector", PE, last.get(f"ubf_rd_{par}", 0))
                ub = em.op("vector", DVE, 1, lambda E, d=db, p=par: E.tensor_copy(
                    out=ubf_sb[:, p * T:(p + 1) * T], in_=u_sb[:, d * T:(d + 1) * T]))
                last["pending_xdbl"] = (db, par, ub)

            flush_xdbl()
            last["u_wr"] = uu
            xdbl_pe = em.cnt[PE]

            em.wait("scalar", PE, xdbl_pe)
            em.wait("scalar", LD, 1)
            ev_dtr = em.op("scalar", ACT, 1, lambda E: E.activation(
                out=dtrb_sb[0:64, :], in_=psC[0:64, 0:T], func=AF.Copy))
            ev_B = em.op("scalar", ACT, 1, lambda E: E.activation(
                out=Bsb[64:128, :], in_=psC[64:128, 0:T], func=AF.Copy))
            ev_C = em.op("scalar", ACT, 1, lambda E: E.activation(
                out=Csb[0:64, :], in_=psC[0:64, T:2 * T], func=AF.Copy))
            last["xdbl_ps_rd"] = ev_C
            em.wait("sync", ACT, ev_B)
            bf1 = em.op("sync", LD, 16, lambda E: E.dma_start(
                out=bflat_d, in_=Bsb[64:128, :]))
            last["bf1"] = bf1
            em.wait("tensor", ACT, ev_C)
            em.wait("tensor", ACT, last["xi_ps_rd"])
            em.wait("tensor", DVE, last.get("g_dve_1", 0))
            em.wait("tensor", DVE, last.get("yf_ps_rd", 0))
            pe_c2 = 0
            for tt in range(TT):
                pe_c2 = em.op("tensor", PE, 1, lambda E, t=tt: E.matmul(
                    out=psA[:, 256 + t * 64:256 + (t + 1) * 64],
                    lhsT=Csb[0:64, t * 128:(t + 1) * 128], rhs=idf_sb[0:64, 0:64],
                    is_transpose=True, start=(t == 0), stop=(t == TT - 1)))
            em.wait("scalar", PE, pe_c2)
            ev_ctm = em.op("scalar", ACT, 1, lambda E: E.activation(
                out=ctm_sb[:], in_=psA[:, 256:256 + TT * N], func=AF.Copy))
            last["ev_ctm"] = ev_ctm
            for db in range(NDB):
                em.wait("tensor", ACT, ev_dtr)
                em.wait("tensor", ACT, last.get("dlt_ps_rd", 0))
                em.wait("tensor", DVE, last.get("g_dve_0", 0))
                pe_dl = em.op("tensor", PE, 1, lambda E, d=db: E.matmul(
                    out=psB[:, 0:T], lhsT=wdt_sb[0:64, d * 128:(d + 1) * 128],
                    rhs=dtrb_sb[0:64, :], start=True, stop=True))
                em.wait("scalar", PE, pe_dl)
                em.wait("scalar", DVE, last.get("ssp_rd", 0))
                em.op("scalar", ACT, 1, lambda E, d=db: E.activation(
                    out=ssp_sb[:], in_=psB[:, 0:T], func=AF.Abs, bias=bdt_sb[:, d:d + 1]))
                em.op("scalar", ACT, 1, lambda E: E.activation(
                    out=ssp_sb[:], in_=ssp_sb[:], func=AF.Exp, scale=ko_sb[:, 4:5]))
                em.op("scalar", ACT, 1, lambda E: E.activation(
                    out=ssp_sb[:], in_=ssp_sb[:], func=AF.Ln, bias=ko_sb[:, 3:4]))
                em.wait("scalar", DVE, last.get("sz_rd", 0))
                em.wait("scalar", OUT, 1)
                a4 = em.op("scalar", ACT, 1, lambda E, d=db: E.activation(
                    out=y1f_sb[:], in_=psB[:, 0:T], func=AF.Relu,
                    bias=bdt_sb[:, d:d + 1]))
                last["dlt_ps_rd"] = a4
                em.wait("vector", ACT, a4)
                em.wait("vector", ACT, last.get(f"delta_rd_{db}", 0))
                ddl = em.op("vector", DVE, 1, lambda E, d=db: E.tensor_add(
                    out=delta_sb[:, d * T:(d + 1) * T], in0=y1f_sb[:], in1=ssp_sb[:]))
                last[f"delta_wr_{db}"] = ddl
                last["ssp_rd"] = ddl
                em.wait("vector", ACT, last["u_wr"])
                em.op("vector", DVE, 1, lambda E, d=db: E.tensor_mul(
                    out=du_sb[:, d * T:(d + 1) * T], in0=delta_sb[:, d * T:(d + 1) * T],
                    in1=u_sb[:, d * T:(d + 1) * T]))

            if dbg and c == 0:
                em.wait("sync", ACT, last["xn2fm_wr"])
                em.op("sync", OUT, 16, lambda E: E.dma_start(
                    out=dbg_outs["d_xn2"], in_=xn2fm[:]))
                em.wait("sync", ACT, last["u_wr"])
                em.op("sync", OUT, 16, lambda E: E.dma_start(out=dbg_outs["d_u"], in_=u_sb[:]))
                em.wait("sync", DVE, em.cnt[DVE])
                em.op("sync", OUT, 16, lambda E: E.dma_start(out=dbg_outs["d_delta"], in_=delta_sb[:]))
                em.wait("sync", ACT, ev_B)
                em.op("sync", OUT, 16, lambda E: E.dma_start(out=dbg_outs["d_B"], in_=Bsb[64:128, :]))
                em.wait("sync", ACT, ev_C)
                em.op("sync", OUT, 16, lambda E: E.dma_start(out=dbg_outs["d_C"], in_=Csb[0:64, :]))


        def emit_B(c):
            em.wait("vector", PE, last.get("yacc_rd_pe", 0))
            em.op("vector", DVE, 1, lambda E: E.memset(yacc_sb[:], 0.0))
            em.wait("tensor", ACT, last["ev_ctm"])
            em.wait("tensor", ACT, last["dlt_ps_rd"])
            em.wait("tensor", DVE, last.get("g_dve_0", 0))
            em.wait("tensor", DVE, last.get("ffn_ps_rd", 0))
            for grp in range(NGRP):
                bpar = grp % 2
                em.wait("sync", LD, last["bf1"])
                em.wait("sync", PE, last.get("row_rd_pe", 0))
                bfg = em.op("sync", LD, 16, lambda E, g=grp: E.dma_start(
                    out=row_sb[0:1, :],
                    in_=bflat_d[g * G:(g + 1) * G, :].rearrange("n t -> (n t)")[None, :]))
                em.wait("tensor", LD, bfg)
                em.wait("tensor", ACT, last.get("bbc_ps_rd", 0))
                pe_b = 0
                for half in range(2):
                    pe_b = em.op("tensor", PE, 1, lambda E, hf=half: E.matmul(
                        out=psE[:, hf * 512:(hf + 1) * 512], lhsT=ones1_sb[:],
                        rhs=row_sb[:, hf * 512:(hf + 1) * 512],
                        start=True, stop=True))
                last["row_rd_pe"] = pe_b
                em.wait("scalar", PE, pe_b)
                em.wait("scalar", DVE, last.get(f"bbc_rd_pool_{bpar}", 0))
                ev_b = 0
                for half in range(2):
                    ev_b = em.op("scalar", ACT, 1, lambda E, hf=half, bp=bpar: E.activation(
                        out=bbc_sb[:, bp * 1024 + hf * 512:bp * 1024 + (hf + 1) * 512],
                        in_=psE[:, hf * 512:(hf + 1) * 512], func=AF.Copy))
                last["bbc_ps_rd"] = ev_b

                for db in range(NDB):
                    par = db % 2
                    hq = (grp * NDB + db) % NH
                    dec, dbuf = dec_sb[par], dbu_sb[par]
                    em.wait("scalar", DVE, max(last["dec_rd"][par],
                                               last.get(f"delta_wr_{db}", 0)))
                    ac_d = 0
                    for g in range(G):
                        nn = grp * G + g
                        ac_d = em.op("scalar", ACT, 1, lambda E, d=db, n=nn, g=g, dd=dec:
                                     E.activation(
                            out=dd[:, g * (T + 1) + 1:(g + 1) * (T + 1)],
                            in_=delta_sb[:, d * T:(d + 1) * T], func=AF.Exp,
                            scale=a_sb[:, d * N + n:d * N + n + 1]))
                    last[f"delta_rd_{db}"] = ac_d
                    if last.get("pending_gather") is not None:
                        pd, pg0, phq, psc = last["pending_gather"]
                        em.wait("scalar", DVE, psc)
                        gat = em.op("scalar", ACT, 1, lambda E, d=pd, g0=pg0,
                                    hh=h_sb[phq]: E.activation(
                            out=hstate_sb[:, d * N + g0 * G:d * N + (g0 + 1) * G],
                            in_=hh[:].rearrange("p (g t) -> p g t", g=G)[:, :, T],
                            func=AF.Copy))
                        last["h_rd_act"][phq] = gat
                        last["pending_gather"] = None
                    dbu_pay = dbuf[:].rearrange("p (g t) -> p g t", g=G)[:, :, 1:]
                    em.wait("vector", ACT, ev_b)
                    em.wait("vector", ACT, ac_d)
                    em.wait("vector", ACT, last["h_rd_act"][hq])
                    po = em.op("vector", DVE, 1, lambda E, d=db, dp=dbu_pay, bp=bpar:
                               E.tensor_tensor(
                        out=dp,
                        in0=du_sb[:, d * T:(d + 1) * T].unsqueeze(1).broadcast_to(
                            [128, G, T]),
                        in1=bbc_sb[:, bp * 1024:(bp + 1) * 1024].rearrange(
                            "p (g t) -> p g t", g=G),
                        op=ALU.mult))
                    last[f"bbc_rd_pool_{bpar}"] = po
                    bnd = em.op("vector", DVE, 1, lambda E, d=db, g0=grp, dd=dbuf:
                                E.tensor_copy(
                        out=dd[:].rearrange("p (g t) -> p g t", g=G)[:, :, 0],
                        in_=hstate_sb[:, d * N + g0 * G:d * N + (g0 + 1) * G]))
                    em.wait("vector", PE, last["h_rd_pe"][hq])
                    em.wait("vector", ACT, last["h_rd_act"][hq])
                    sc_v = em.op("vector", DVE, 1, lambda E, dd=dec, bb=dbuf, hh=h_sb[hq]:
                                 E.tensor_tensor_scan(
                        out=hh[:], data0=dd[:], data1=bb[:], initial=0.0,
                        op0=ALU.mult, op1=ALU.add))
                    last["dec_rd"][par] = sc_v
                    last["dbu_rd"][par] = sc_v
                    last[f"h_wr_{hq}"] = sc_v
                    last["pending_gather"] = (db, grp, hq, sc_v)

                    if db % 4 == 3:
                        q0 = db - 3
                        hqs = [(grp * NDB + q0 + q) % NH for q in range(4)]
                        for g in range(G):
                            nn = grp * G + g
                            for tt in range(TT):
                                use_a = (g * TT + tt) % 2 == 0
                                rt = psA if use_a else psB
                                key = "rt_rd_A" if use_a else "rt_rd_B"
                                em.wait("tensor", DVE, last.get(key, 0))
                                pe_r = 0
                                for q in range(4):
                                    em.wait("tensor", DVE, last[f"h_wr_{hqs[q]}"])
                                    pe_r = em.op("tensor", PE, 1,
                                                 lambda E, qq=hqs[q], qi=q, gg=g, t=tt, r=rt: E.matmul(
                                        out=r[:, qi * 128:(qi + 1) * 128],
                                        lhsT=h_sb[qq][:, gg * (T + 1) + 1 + t * 128:
                                                      gg * (T + 1) + 1 + (t + 1) * 128],
                                        rhs=idf_sb[:], is_transpose=True,
                                        start=(qi == 0), stop=(qi == 3)))
                                for q in range(4):
                                    last["h_rd_pe"][hqs[q]] = pe_r
                                em.wait("vector", PE, pe_r)
                                st_v = em.op("vector", DVE, 1,
                                             lambda E, n=nn, t=tt, r=rt, qb=q0:
                                             E.scalar_tensor_tensor(
                                    out=yacc_sb[:, t * DI + qb * 128:t * DI + qb * 128 + 512],
                                    in0=r[:, 0:512],
                                    scalar=ctm_sb[:, t * N + n:t * N + n + 1],
                                    in1=yacc_sb[:, t * DI + qb * 128:t * DI + qb * 128 + 512],
                                    op0=ALU.mult, op1=ALU.add))
                                last[key] = st_v
            if last.get("pending_gather") is not None:
                pd, pg0, phq, psc = last["pending_gather"]
                em.wait("scalar", DVE, psc)
                gat = em.op("scalar", ACT, 1, lambda E, d=pd, g0=pg0,
                            hh=h_sb[phq]: E.activation(
                    out=hstate_sb[:, d * N + g0 * G:d * N + (g0 + 1) * G],
                    in_=hh[:].rearrange("p (g t) -> p g t", g=G)[:, :, T],
                    func=AF.Copy))
                last["h_rd_act"][phq] = gat
                last["pending_gather"] = None
            yacc_dve = em.cnt[DVE]
            last["yacc_dve"] = yacc_dve

            if dbg and c == 0:
                em.wait("sync", DVE, yacc_dve)
                em.op("sync", OUT, 16, lambda E: E.dma_start(
                    out=dbg_outs["d_yacc"], in_=yacc_sb[:]))


        def emit_C1(c):
            last["pending_wout"] = None

            def flush_wout():
                d, p, y1b, lwo = last["pending_wout"]
                em.wait("tensor", DVE, y1b)
                em.wait("tensor", LD, lwo)
                if d == 0:
                    em.wait("tensor", DVE, last.get("ffn_ps_rd", 0))
                    em.wait("tensor", ACT, last.get("bbc_ps_rd", 0))
                pe_o = 0
                for tt in range(TT):
                    dstp = psE if tt == 0 else psF
                    for half in range(2):
                        pe_o = em.op("tensor", PE, 1, lambda E, d=d, t=tt, hf=half,
                                     ds=dstp, p=p: E.matmul(
                            out=ds[:, hf * 512:(hf + 1) * 512],
                            lhsT=y1bf_sb[:, p * T + t * 128:p * T + (t + 1) * 128],
                            rhs=woutb[:, p * DM + hf * 512:p * DM + (hf + 1) * 512],
                            start=(d == 0), stop=(d == NDB - 1)))
                last[f"wout_rd_{p}"] = pe_o
                last[f"y1bf_rd_{p}"] = pe_o
                last["pending_wout"] = None

            for db in range(NDB):
                par = db % 2
                em.wait("sync", PE, last.get(f"wout_rd_{par}", 0))
                lwo = dma(wout_s[:, db * DM:(db + 1) * DM],
                          woutb[:, par * DM:(par + 1) * DM])
                em.wait("sync", PE, last.get(f"wz_rd_{par}", 0))
                lwz = dma(wz_s[:, db * 1024:(db + 1) * 1024],
                          wzbuf[:, par * 1024:(par + 1) * 1024])
                em.wait("tensor", DVE, last["yacc_dve"])
                em.wait("tensor", DVE, last.get("yf_ps_rd", 0))
                pe_y = 0
                for tt in range(TT):
                    pe_y = em.op("tensor", PE, 1, lambda E, d=db, t=tt: E.matmul(
                        out=psA[:, t * 128:(t + 1) * 128],
                        lhsT=yacc_sb[:, t * DI + d * 128:t * DI + (d + 1) * 128],
                        rhs=idf_sb[:], is_transpose=True,
                        start=(t == 0), stop=(t == TT - 1)))
                last["yacc_rd_pe"] = pe_y
                em.wait("vector", PE, pe_y)
                em.wait("vector", OUT, last.get("y1f_dump", 0))
                y1v = em.op("vector", DVE, 1, lambda E, d=db: E.scalar_tensor_tensor(
                    out=y1f_sb[:], in0=u_sb[:, d * T:(d + 1) * T],
                    scalar=dvec_sb[:, d:d + 1], in1=psA[:, 0:T],
                    op0=ALU.mult, op1=ALU.add))
                last["yf_ps_rd"] = y1v
                em.wait("tensor", LD, lwz)
                if db == 0:
                    em.wait("tensor", ACT, last["xdbl_ps_rd"])
                em.wait("tensor", ACT, last.get("z_ps_rd", 0))
                pe_z = 0
                for kc in range(KC):
                    pe_z = em.op("tensor", PE, 1, lambda E, k=kc, p=par: E.matmul(
                        out=psC[:, 0:T],
                        lhsT=wzbuf[:, p * 1024 + k * 128:p * 1024 + (k + 1) * 128],
                        rhs=xn2fm[:, k * T:(k + 1) * T],
                        start=(k == 0), stop=(k == KC - 1)))
                last[f"wz_rd_{par}"] = pe_z
                em.wait("scalar", PE, pe_z)
                em.wait("scalar", DVE, last.get("sz_rd", 0))
                szv = em.op("scalar", ACT, 1, lambda E: E.activation(
                    out=sz_sb[:], in_=psC[:, 0:T], func=AF.Silu))
                last["z_ps_rd"] = szv
                em.wait("vector", ACT, szv)
                em.wait("vector", PE, last.get(f"y1bf_rd_{par}", 0))
                y1b = em.op("vector", DVE, 1, lambda E, p=par: E.tensor_mul(
                    out=y1bf_sb[:, p * T:(p + 1) * T], in0=y1f_sb[:], in1=sz_sb[:]))
                last["sz_rd"] = y1b
                if last.get("pending_wout") is not None:
                    flush_wout()
                last["pending_wout"] = (db, par, y1b, lwo)
                if dbg and c == 0:
                    em.wait("sync", DVE, y1b)
                    last["y1f_dump"] = em.op("sync", OUT, 16, lambda E, d=db: E.dma_start(
                        out=dbg_outs["d_y1"][:, d * T:(d + 1) * T], in_=y1f_sb[:]))
            flush_wout()
            last["om_pe"] = em.cnt[PE]

        def emit_C2(c):

            for tt in range(TT):
                o = tt * 8
                em.wait("sync", DVE, last.get("xt_rd_dve", 0))
                em.wait("sync", ACT, last.get("xt_rd_act", 0))
                ldx = dma(x_d[c * T + tt * 128: c * T + (tt + 1) * 128, :], xt_sb[:])
                em.wait("vector", LD, ldx)
                em.wait("vector", PE, last["om_pe"])
                em.wait("vector", ACT, last.get("x2_rd_act", 0))
                omp = psE if tt == 0 else psF
                xv = em.op("vector", DVE, 1, lambda E, p=omp: E.tensor_add(
                    out=big1[:], in0=xt_sb[:], in1=p[:]))
                last["om_ps_rd"] = xv
                xv = em.op("vector", DVE, 1, lambda E, o=o: E.tensor_scalar(
                    out=xt_sb[:], in0=xt_sb[:], scalar1=sc_a[:, o + 1:o + 2],
                    scalar2=sc_d[:, o + 3:o + 4], op0=ALU.mult, op1=ALU.add))
                xv = em.op("vector", DVE, 1, lambda E, t=tt: E.tensor_add(
                    out=x2_sb[:, t * DM:(t + 1) * DM], in0=big1[:], in1=xt_sb[:]))
                last["xt_rd_dve"] = xv
                em.op("vector", DVE, 1, lambda E, t=tt: E.bn_stats(
                    out=bn_sb[:, 0:6], in_=x2_sb[:, t * DM:t * DM + 512]))
                em.op("vector", DVE, 1, lambda E, t=tt: E.bn_stats(
                    out=bn_sb[:, 6:12], in_=x2_sb[:, t * DM + 512:(t + 1) * DM]))
                em.op("vector", DVE, 1, lambda E: E.engine_nop())
                bs = em.op("vector", DVE, 1, lambda E: E.bn_aggr(
                    out=bn_sb[:, 12:14], in_=bn_sb[:, 0:12]))
                em.wait("scalar", DVE, bs)
                em.op("scalar", ACT, 1, lambda E, o=o: E.activation(
                    out=sc_a[:, o + 5:o + 6], in_=bn_sb[:, 13:14], func=AF.Ln,
                    bias=ko_sb[:, 2:3]))
                ac = em.op("scalar", ACT, 1, lambda E, o=o: E.activation(
                    out=sc_a[:, o + 2:o + 3], in_=sc_a[:, o + 5:o + 6], func=AF.Exp,
                    scale=ko_sb[:, 5:6]))
                em.wait("vector", ACT, ac)
                q3v = em.op("vector", DVE, 1, lambda E, o=o: E.scalar_tensor_tensor(
                    out=sc_d[:, o + 4:o + 5], in0=bn_sb[:, 12:13], scalar=-1.0,
                    in1=sc_a[:, o + 2:o + 3], op0=ALU.mult, op1=ALU.mult))
                xn3v = em.op("vector", DVE, 1, lambda E, o=o, t=tt: E.tensor_scalar(
                    out=xn3t_sb[:], in0=x2_sb[:, t * DM:(t + 1) * DM],
                    scalar1=sc_a[:, o + 2:o + 3], scalar2=sc_d[:, o + 4:o + 5],
                    op0=ALU.mult, op1=ALU.add))
                last["x2_rd_act"] = 0
                last["x2_rd_dve2"] = xn3v
                em.wait("tensor", DVE, xn3v)
                em.wait("tensor", ACT, last.get("xn3_ps_rd", 0))
                pe_t = 0
                for kc in range(KC):
                    pe_t = em.op("tensor", PE, 1, lambda E, k=kc: E.matmul(
                        out=psT[:, k * 128:(k + 1) * 128],
                        lhsT=xn3t_sb[:, k * 128:(k + 1) * 128], rhs=idb_sb[:],
                        is_transpose=True, start=(k == 0), stop=(k == KC - 1)))
                em.wait("scalar", PE, pe_t)
                em.wait("scalar", PE, last.get("xn3f_rd", 0))
                ev = 0
                for kc in range(KC):
                    ev = em.op("scalar", ACT, 1, lambda E, k=kc, t=tt: E.activation(
                        out=xn3f_sb[:, k * T + t * 128:k * T + t * 128 + 128],
                        in_=psT[:, k * 128:(k + 1) * 128], func=AF.Copy))
                last["psT_rd"] = ev
                last["xn3f_wr"] = ev


        def emit_C3(c):
            last["pending_down"] = None

            def flush_down():
                h, p, hhv = last["pending_down"]
                em.wait("tensor", DVE, hhv)
                if h == 0:
                    em.wait("tensor", DVE, last.get("x3_ffn_rd", 0))
                pe_dn = 0
                for tt in range(TT):
                    dstp = psE if tt == 0 else psF
                    for half in range(2):
                        pe_dn = em.op("tensor", PE, 1, lambda E, h=h, t=tt, hf=half,
                                      ds=dstp, p=p: E.matmul(
                            out=ds[:, hf * 512:(hf + 1) * 512],
                            lhsT=hh_sb[:, p * 256 + t * 128:p * 256 + (t + 1) * 128],
                            rhs=wdb[:, p * DM + hf * 512:p * DM + (hf + 1) * 512],
                            start=(h == 0), stop=(h == NHC - 1)))
                last[f"hh_rd_{p}"] = pe_dn
                last[f"wffn_rd_{p}"] = pe_dn
                last["pending_down"] = None

            for hc in range(NHC):
                par = hc % 2
                em.wait("sync", PE, last.get(f"wffn_rd_{par}", 0))
                dma(wg_s[:, hc * 1024:(hc + 1) * 1024], wgb[:, par * 1024:(par + 1) * 1024])
                dma(wu_s[:, hc * 1024:(hc + 1) * 1024], wub[:, par * 1024:(par + 1) * 1024])
                ldn = dma(wd_s[:, hc * DM:(hc + 1) * DM], wdb[:, par * DM:(par + 1) * DM])
                em.wait("tensor", LD, ldn)
                em.wait("tensor", ACT, last["xn3f_wr"])
                em.wait("tensor", DVE, last.get(f"g_dve_{par}", 0))
                em.wait("tensor", ACT, last.get(f"g_act_{par}", 0))
                gu_ps = psB if par == 0 else psA
                if hc == 0:
                    em.wait("tensor", ACT, last["dlt_ps_rd"])
                if hc == 1:
                    em.wait("tensor", DVE, last["yf_ps_rd"])
                    em.wait("tensor", ACT, last["xi_ps_rd"])
                    em.wait("tensor", ACT, last["ev_ctm"])
                pe_g = 0
                for kc in range(KC):
                    pe_g = em.op("tensor", PE, 1, lambda E, k=kc, p=par, gp=gu_ps: E.matmul(
                        out=gp[:, 0:256],
                        lhsT=wgb[:, p * 1024 + k * 128:p * 1024 + (k + 1) * 128],
                        rhs=xn3f_sb[:, k * T:(k + 1) * T],
                        start=(k == 0), stop=(k == KC - 1)))
                em.wait("scalar", PE, pe_g)
                em.wait("scalar", DVE, last.get(f"sfn_rd_{par}", 0))
                sg = em.op("scalar", ACT, 1, lambda E, p=par, gp=gu_ps: E.activation(
                    out=sfn_sb[:, p * 256:(p + 1) * 256],
                    in_=gp[:, 0:256], func=AF.Silu,
                    scale=ko_sb[:, 6:7]))
                last[f"g_act_{par}"] = sg
                em.wait("tensor", ACT, sg)
                pe_u = 0
                for kc in range(KC):
                    pe_u = em.op("tensor", PE, 1, lambda E, k=kc, p=par, gp=gu_ps: E.matmul(
                        out=gp[:, 0:256],
                        lhsT=wub[:, p * 1024 + k * 128:p * 1024 + (k + 1) * 128],
                        rhs=xn3f_sb[:, k * T:(k + 1) * T],
                        start=(k == 0), stop=(k == KC - 1)))
                if last.get("pending_down") is not None:
                    flush_down()
                em.wait("vector", PE, pe_u)
                em.wait("vector", ACT, sg)
                em.wait("vector", PE, last.get(f"hh_rd_{par}", 0))
                hhv = em.op("vector", DVE, 1, lambda E, p=par, gp=gu_ps: E.scalar_tensor_tensor(
                    out=hh_sb[:, p * 256:(p + 1) * 256],
                    in0=gp[:, 0:256],
                    scalar=FS, in1=sfn_sb[:, p * 256:(p + 1) * 256],
                    op0=ALU.mult, op1=ALU.mult))
                last[f"sfn_rd_{par}"] = hhv
                last[f"g_dve_{par}"] = hhv
                last["pending_down"] = (hc, par, hhv)
            flush_down()
            ffn_pe = em.cnt[PE]
            for tt in range(TT):
                em.wait("vector", PE, ffn_pe)
                em.wait("vector", OUT, last.get("x3_rd", 0))
                ffp = psE if tt == 0 else psF
                x3v = em.op("vector", DVE, 1, lambda E, p=ffp, t=tt: E.scalar_tensor_tensor(
                    out=x3_sb[:], in0=p[:], scalar=FS,
                    in1=x2_sb[:, t * DM:(t + 1) * DM], op0=ALU.mult, op1=ALU.add))
                last["x3_ffn_rd"] = x3v
                last["ffn_ps_rd"] = x3v
                em.wait("sync", DVE, x3v)
                ov = em.op("sync", OUT, 16, lambda E, t=tt, cc=c: E.dma_start(
                    out=out_d[cc * T + t * 128: cc * T + (t + 1) * 128, :], in_=x3_sb[:]))
                last["x3_rd"] = ov


        emit_A(0)
        for c in range(NCH):
            emit_B(c)
            emit_C1(c)
            emit_C2(c)
            if c + 1 < NCH:
                emit_A(c + 1)
            emit_C3(c)

        em.wait("sync", OUT, em.cnt[OUT])

        @block.sync
        def _(E):
            em.replay("sync", E, sems)

        @block.vector
        def _(E):
            em.replay("vector", E, sems)

        @block.scalar
        def _(E):
            em.replay("scalar", E, sems)

        @block.tensor
        def _(E):
            em.replay("tensor", E, sems)

        @block.gpsimd
        def _(E):
            em.replay("gpsimd", E, sems)

    return nc


def _bf16():
    import ml_dtypes
    return ml_dtypes.bfloat16


def host_inputs(inputs, b):
    return {
        "x": np.ascontiguousarray(inputs["L0_attn_x"][b]),
        "w_in": np.ascontiguousarray(inputs["L0_attn_W_in"]),
        "wxp": np.ascontiguousarray(inputs["L0_attn_W_xproj"]),
        "wdt": np.ascontiguousarray(inputs["L0_attn_W_dt"]),
        "wout": np.ascontiguousarray(inputs["L0_attn_W_out"]),
        "alog": np.ascontiguousarray(
            inputs["L0_attn_A_log"].reshape(NDB, 128, N).transpose(1, 0, 2).reshape(128, NDB * N)),
        "dvec": np.ascontiguousarray(inputs["L0_attn_D"].reshape(NDB, 128).T),
        "convw": np.ascontiguousarray(
            inputs["L0_attn_conv_w"].reshape(NDB, 128, 4).transpose(1, 0, 2).reshape(128, NDB * 4)),
        "convb": np.ascontiguousarray(inputs["L0_attn_conv_b"].reshape(NDB, 128).T),
        "bdt": np.ascontiguousarray(inputs["L0_attn_b_dt"].reshape(NDB, 128).T),
        "lnw": np.ascontiguousarray(inputs["L0_attn_ln_w"].reshape(1, DM)),
        "lnb": np.ascontiguousarray(inputs["L0_attn_ln_b"].reshape(1, DM)),
        "wgT": np.ascontiguousarray(inputs["L0_ffn_w_gate"].T),
        "wuT": np.ascontiguousarray(inputs["L0_ffn_w_up"].T),
        "wdT": np.ascontiguousarray(inputs["L0_ffn_w_down"].T),
        "idf": np.eye(128, dtype=np.float32),
        "idb": np.eye(128, dtype=np.float32).astype(_bf16()),
    }


def kernel(**inputs):
    nc = build_nc()
    in_maps = [host_inputs(inputs, core // 2) for core in range(8)]
    res = run_bass_kernel_spmd(nc, in_maps, core_ids=list(range(8)))
    out = np.empty((4, L, DM), dtype=np.float32)
    for b in range(4):
        out[b, :L // 2] = res.results[2 * b]["out"][:L // 2]
        out[b, L // 2:] = res.results[2 * b + 1]["out"][L // 2:]
    return out


# version 08
# speedup vs baseline: 1.0530x; 1.0530x over previous
import contextlib
import numpy as np

import concourse.bass as bass
import concourse.mybir as mybir
from concourse.bass_utils import run_bass_kernel_spmd

F32 = mybir.dt.float32
BF16 = mybir.dt.bfloat16
I32 = mybir.dt.int32
AF = mybir.ActivationFunctionType
ALU = mybir.AluOpType

L = 8192
DM = 1024
DI = 2048
NDB = 16
KC = 8
N = 64
G = 4
NGRP = N // G
T = 256
TT = T // 128
W = G * (T + 1)
HF = 3072
NHC = 24
EPS = 1e-5
FS = float(1.0 / 127.0)

SEMS = ("LD", "PE", "ACT", "DVE", "POOL", "OUT")
ENGS = ("sync", "vector", "scalar", "tensor", "gpsimd")


DMA_SEMS = ("LD", "OUT")


class Em:
    def __init__(self):
        self.q = {e: [] for e in ENGS}
        self.cnt = {s: 0 for s in SEMS}
        self.hw = {e: {s: 0 for s in SEMS} for e in ENGS}
        self.pend = {s: 0 for s in DMA_SEMS}

    def wait(self, eng, sem, val):
        if val is None or val <= 0:
            return
        if sem in DMA_SEMS:
            val = self.cnt[sem]
            self.pend[sem] = max(self.pend[sem], val)
        assert val <= self.cnt[sem], f"wait-on-future {sem} {val} > {self.cnt[sem]}"
        if self.hw[eng][sem] >= val:
            return
        self.hw[eng][sem] = val
        self.q[eng].append(("W", sem, val))

    ENG_SEM = {"vector": "DVE", "scalar": "ACT", "gpsimd": "POOL", "tensor": "PE"}

    def op(self, eng, sem, amt, fn):
        own = self.ENG_SEM.get(eng)
        if own is not None and eng != "tensor":
            v = self.cnt[own]
            if v > self.hw[eng][own]:
                self.hw[eng][own] = v
                self.q[eng].append(("W", own, v))
        if sem in DMA_SEMS and self.pend[sem] > self.hw[eng][sem]:
            v = self.pend[sem]
            self.hw[eng][sem] = v
            self.q[eng].append(("W", sem, v))
        self.cnt[sem] += amt
        self.q[eng].append(("I", sem, amt, fn))
        return self.cnt[sem]

    def nosem(self, eng, fn):
        self.q[eng].append(("N", fn))

    def barrier(self, *sems_now):
        snap = {s: self.cnt[s] for s in (sems_now or SEMS)}
        for e in ENGS:
            for s, v in snap.items():
                self.wait(e, s, v)

    def replay(self, eng, E, sems):
        for item in self.q[eng]:
            if item[0] == "W":
                E.wait_ge(sems[item[1]], item[2])
            elif item[0] == "I":
                item[3](E).then_inc(sems[item[1]], item[2])
            else:
                item[1](E)


def build_nc(nch=L // T, dbg=False):
    nc = bass.Bass("TRN2", target_bir_lowering=False)
    NCH = nch

    x_d = nc.dram_tensor("x", [L, DM], F32, kind="ExternalInput").ap()
    w_in_d = nc.dram_tensor("w_in", [DM, 2 * DI], F32, kind="ExternalInput").ap()
    wxp_d = nc.dram_tensor("wxp", [DI, 192], F32, kind="ExternalInput").ap()
    wdt_d = nc.dram_tensor("wdt", [64, DI], F32, kind="ExternalInput").ap()
    wout_d = nc.dram_tensor("wout", [DI, DM], F32, kind="ExternalInput").ap()
    alog_d = nc.dram_tensor("alog", [128, NDB * N], F32, kind="ExternalInput").ap()
    dvec_d = nc.dram_tensor("dvec", [128, NDB], F32, kind="ExternalInput").ap()
    convw_d = nc.dram_tensor("convw", [128, NDB * 4], F32, kind="ExternalInput").ap()
    convb_d = nc.dram_tensor("convb", [128, NDB], F32, kind="ExternalInput").ap()
    bdt_d = nc.dram_tensor("bdt", [128, NDB], F32, kind="ExternalInput").ap()
    lnw_d = nc.dram_tensor("lnw", [1, DM], F32, kind="ExternalInput").ap()
    lnb_d = nc.dram_tensor("lnb", [1, DM], F32, kind="ExternalInput").ap()
    wgT_d = nc.dram_tensor("wgT", [DM, HF], I32, kind="ExternalInput").ap()
    wuT_d = nc.dram_tensor("wuT", [DM, HF], I32, kind="ExternalInput").ap()
    wdT_d = nc.dram_tensor("wdT", [HF, DM], I32, kind="ExternalInput").ap()
    idf_d = nc.dram_tensor("idf", [128, 128], F32, kind="ExternalInput").ap()
    idb_d = nc.dram_tensor("idb", [128, 128], BF16, kind="ExternalInput").ap()
    out_d = nc.dram_tensor("out", [L, DM], F32, kind="ExternalOutput").ap()

    wxi_s = nc.dram_tensor("wxi_s", [128, NDB * KC * 128], BF16, kind="Internal").ap()
    wz_s = nc.dram_tensor("wz_s", [128, NDB * KC * 128], BF16, kind="Internal").ap()
    wout_s = nc.dram_tensor("wout_s", [128, NDB * DM], BF16, kind="Internal").ap()
    wg_s = nc.dram_tensor("wg_s", [128, NHC * KC * 128], BF16, kind="Internal").ap()
    wu_s = nc.dram_tensor("wu_s", [128, NHC * KC * 128], BF16, kind="Internal").ap()
    wd_s = nc.dram_tensor("wd_s", [128, NHC * DM], BF16, kind="Internal").ap()
    bflat_d = nc.dram_tensor("bflat_d", [N, T], F32, kind="Internal").ap()

    dbg_outs = {}
    if dbg:
        for nm, shape in (("d_xn2", [128, KC * T]), ("d_u", [128, NDB * T]),
                          ("d_delta", [128, NDB * T]), ("d_B", [64, T]),
                          ("d_C", [64, T]), ("d_yacc", [128, TT * DI]),
                          ("d_y1", [128, NDB * T])):
            dt = BF16 if nm == "d_xn2" else F32
            dbg_outs[nm] = nc.dram_tensor(nm, shape, dt, kind="ExternalOutput").ap()

    em = Em()
    es = contextlib.ExitStack()
    with es:
        sb = lambda name, shape, dt=F32: es.enter_context(nc.sbuf_tensor(name, shape, dt))
        wxp_sb = sb("wxp_sb", [128, NDB * 256], BF16)
        wdt_sb = sb("wdt_sb", [128, DI], BF16)
        a_sb = sb("a_sb", [128, NDB * N])
        dvec_sb = sb("dvec_sb", [128, NDB])
        convw_sb = sb("convw_sb", [128, NDB * 4])
        convb_sb = sb("convb_sb", [128, NDB])
        bdt_sb = sb("bdt_sb", [128, NDB])
        lnwb_sb = sb("lnwb_sb", [128, 2 * DM])
        row_sb = sb("row_sb", [1, G * T])
        ones1_sb = sb("ones1_sb", [1, 128])
        idf_sb = sb("idf_sb", [128, 128])
        idb_sb = sb("idb_sb", [128, 128], BF16)
        wstream = sb("wstream", [128, 2048], BF16)
        wzbuf = sb("wzbuf", [128, 2 * 1024], BF16)
        woutb = sb("woutb", [128, 2 * 1024], BF16)
        wgb = sb("wgb", [128, 2 * 1024], BF16)
        wub = sb("wub", [128, 2 * 1024], BF16)
        wdb = sb("wdb", [128, 2 * 1024], BF16)
        xt_sb = sb("xt_sb", [128, DM])
        bn_sb = sb("bn_sb", [128, 16])
        ko_sb = sb("ko_sb", [128, 8])
        sc_d = sb("sc_d", [128, TT * 8])
        sc_a = sb("sc_a", [128, TT * 8])
        big1 = sb("big1", [128, DM])
        xn2tm = sb("xn2tm", [128, DM], BF16)
        xn2fm = sb("xn2fm", [128, KC * T], BF16)
        xib = sb("xib", [128, 3 + T])
        cvb = sb("cvb", [128, T])
        halo_sb = sb("halo_sb", [128, NDB * 3])
        u_sb = sb("u_sb", [128, NDB * T])
        ubf_sb = sb("ubf_sb", [128, 2 * T], BF16)
        delta_sb = sb("delta_sb", [128, NDB * T])
        du_sb = sb("du_sb", [128, NDB * T])
        dtrb_sb = sb("dtrb_sb", [128, T], BF16)
        Bsb = sb("Bsb", [128, T])
        Csb = sb("Csb", [128, T])
        ctm_sb = sb("ctm_sb", [128, TT * N])
        dec_sb = [sb(f"dec{i}", [128, W]) for i in range(2)]
        dbu_sb = [sb(f"dbu{i}", [128, W]) for i in range(2)]
        NH = 6
        h_sb = [sb(f"h{i}", [128, W]) for i in range(NH)]
        bbc_sb = sb("bbc_sb", [128, 2 * G * T])
        hstate_sb = sb("hstate_sb", [128, NDB * N])
        yacc_sb = sb("yacc_sb", [128, TT * DI])
        y1f_sb = sb("y1f_sb", [128, T])
        sz_sb = sb("sz_sb", [128, T])
        y1bf_sb = sb("y1bf_sb", [128, 2 * T], BF16)
        x2_sb = sb("x2_sb", [128, TT * DM])
        xn3f_sb = sb("xn3f_sb", [128, TT * KC * 128], BF16)
        hh_sb = sb("hh_sb", [128, 512], BF16)

        ssp_sb = cvb
        sfn_sb = big1
        xn3t_sb = xn2tm
        x3_sb = big1
        psA = es.enter_context(nc.psum_tensor("psA", [128, 512], F32))
        psB = es.enter_context(nc.psum_tensor("psB", [128, 512], F32))
        psC = es.enter_context(nc.psum_tensor("psC", [128, 512], F32))
        psE = es.enter_context(nc.psum_tensor("psE", [128, 1024], F32))
        psF = es.enter_context(nc.psum_tensor("psF", [128, 1024], F32))
        psT = es.enter_context(nc.psum_tensor("psT", [128, 1024], BF16))

        sems = {s: es.enter_context(nc.semaphore(s)) for s in SEMS}
        block = es.enter_context(nc.Block())

        LD, PE, ACT, DVE, POOL, OUT = SEMS

        def dma(src, dst):
            return em.op("sync", LD, 16, lambda E, s=src, d=dst: E.dma_start(out=d, in_=s))

        ld0 = dma(alog_d, a_sb[:])
        dma(dvec_d, dvec_sb[:])
        dma(convw_d, convw_sb[:])
        dma(convb_d, convb_sb[:])
        dma(bdt_d, bdt_sb[:])
        dma(idf_d, idf_sb[:])
        ld_idb = dma(idb_d, idb_sb[:])

        em.wait("scalar", LD, ld0)
        a_exp = em.op("scalar", ACT, 1, lambda E: E.activation(
            out=big1[:], in_=a_sb[:], func=AF.Exp))
        em.wait("vector", ACT, a_exp)
        a_done = em.op("vector", DVE, 1, lambda E: E.tensor_scalar_mul(
            out=a_sb[:], in0=big1[:], scalar1=-1.0))
        ones_done = em.op("vector", DVE, 1, lambda E: E.memset(ones1_sb[:], 1.0))
        for i, v in enumerate((1.0 + EPS, EPS * EPS, EPS, 1.0, -1.0, -0.5, FS)):
            kdone = em.op("vector", DVE, 1, lambda E, ii=i, vv=v: E.memset(
                ko_sb[:, ii:ii + 1], vv))
        em.wait("scalar", DVE, kdone)
        z_dve = em.op("vector", DVE, 1, lambda E: E.memset(halo_sb[:], 0.0))
        z_dve = em.op("vector", DVE, 1, lambda E: E.memset(hstate_sb[:], 0.0))
        for d in dec_sb:
            z_dve = em.op("vector", DVE, 1, lambda E, dd=d: E.memset(dd[:], 0.0))

        em.wait("tensor", DVE, ones_done)
        em.wait("tensor", LD, ld_idb)
        lnbc_done = 0
        for which, srcd in ((0, lnw_d), (1, lnb_d)):
            em.wait("sync", ACT, lnbc_done)
            lrow = dma(srcd, row_sb[:, 0:DM])
            em.wait("tensor", LD, lrow)
            pe_c = 0
            for half in range(2):
                pe_c = em.op("tensor", PE, 1, lambda E, hf=half: E.matmul(
                    out=psE[:, hf * 512:(hf + 1) * 512], lhsT=ones1_sb[:],
                    rhs=row_sb[:, hf * 512:hf * 512 + 512], start=True, stop=True))
            em.wait("scalar", PE, pe_c)
            for half in range(2):
                lnbc_done = em.op("scalar", ACT, 1, lambda E, w=which, hf=half:
                                  E.activation(
                    out=lnwb_sb[:, w * DM + hf * 512: w * DM + hf * 512 + 512],
                    in_=psE[:, hf * 512:(hf + 1) * 512], func=AF.Copy))

        wi32 = xt_sb[:].bitcast(I32)
        cvt = a_done
        for dst_s, c0 in ((wxi_s, 0), (wz_s, DI)):
            for kc in range(KC):
                for hh in range(2):
                    em.wait("sync", DVE, cvt)
                    lw = dma(w_in_d[kc * 128:(kc + 1) * 128,
                                    c0 + hh * DM:c0 + (hh + 1) * DM], big1[:])
                    em.wait("vector", LD, lw)
                    cvt = em.op("vector", DVE, 1, lambda E, h2=hh: E.tensor_copy(
                        out=wstream[:, h2 * DM:(h2 + 1) * DM], in_=big1[:]))
                em.wait("sync", DVE, cvt)
                dv = dst_s.rearrange("p (db k j) -> p db k j", db=NDB, k=KC)[:, :, kc, :]
                sv = wstream[:].rearrange("p (db j) -> p db j", db=NDB)
                wb = em.op("sync", LD, 16, lambda E, d=dv, s=sv: E.dma_start(out=d, in_=s))
                em.wait("vector", LD, wb)
        for kc in range(NDB):
            em.wait("sync", DVE, cvt)
            lw = dma(wout_d[kc * 128:(kc + 1) * 128, :], big1[:])
            em.wait("vector", LD, lw)
            cvt = em.op("vector", DVE, 1, lambda E: E.tensor_copy(
                out=wstream[:, 0:DM], in_=big1[:]))
            em.wait("sync", DVE, cvt)
            wb = em.op("sync", LD, 16, lambda E, k=kc: E.dma_start(
                out=wout_s[:, k * DM:(k + 1) * DM], in_=wstream[:, 0:DM]))
            em.wait("vector", LD, wb)
        zw = em.op("vector", DVE, 1, lambda E: E.memset(wxp_sb[:], 0.0))
        for kc in range(NDB):
            em.wait("sync", DVE, max(cvt, zw))
            lw = dma(wxp_d[kc * 128:(kc + 1) * 128, :], big1[:, 0:192])
            em.wait("vector", LD, lw)
            cvt = em.op("vector", DVE, 1, lambda E, k=kc: E.tensor_copy(
                out=wxp_sb[:, k * 256:k * 256 + 192], in_=big1[:, 0:192]))
        for hh in range(2):
            em.wait("sync", DVE, cvt)
            lw = em.op("sync", LD, 16, lambda E, h2=hh: E.dma_start(
                out=big1[0:64, :], in_=wdt_d[:, h2 * DM:(h2 + 1) * DM]))
            em.wait("vector", LD, lw)
            cvt = em.op("vector", DVE, 1, lambda E, h2=hh: E.tensor_copy(
                out=wdt_sb[0:64, h2 * DM:(h2 + 1) * DM], in_=big1[0:64, :]))
            em.wait("sync", DVE, cvt)
        for src_d, dst_s in ((wgT_d, wg_s), (wuT_d, wu_s)):
            for kc in range(KC):
                for hh in range(3):
                    em.wait("sync", DVE, cvt)
                    lw = dma(src_d[kc * 128:(kc + 1) * 128, hh * 1024:(hh + 1) * 1024], wi32)
                    em.wait("vector", LD, lw)
                    cvt = em.op("vector", DVE, 1, lambda E: E.tensor_copy(
                        out=wstream[:, 0:DM], in_=wi32))
                    em.wait("sync", DVE, cvt)
                    dv = dst_s.rearrange("p (hc k j) -> p hc k j", hc=NHC, k=KC)[
                        :, hh * 8:(hh + 1) * 8, kc, :]
                    sv = wstream[:, 0:DM].rearrange("p (hc j) -> p hc j", hc=8)
                    wb = em.op("sync", LD, 16, lambda E, d=dv, s=sv: E.dma_start(out=d, in_=s))
                    em.wait("vector", LD, wb)
        for hc in range(NHC):
            em.wait("sync", DVE, cvt)
            lw = dma(wdT_d[hc * 128:(hc + 1) * 128, :], wi32)
            em.wait("vector", LD, lw)
            cvt = em.op("vector", DVE, 1, lambda E: E.tensor_copy(
                out=wstream[:, 0:DM], in_=wi32))
            em.wait("sync", DVE, cvt)
            wb = em.op("sync", LD, 16, lambda E, h=hc: E.dma_start(
                out=wd_s[:, h * DM:(h + 1) * DM], in_=wstream[:, 0:DM]))
            em.wait("vector", LD, wb)

        em.barrier()

        last = {"dec_rd": [0, 0], "dbu_rd": [0, 0],
                "h_rd_pe": [0] * 6, "h_rd_act": [0] * 6}
        for i in range(6):
            last[f"h_wr_{i}"] = 0

        def emit_A(c):
            for tt in range(TT):
                em.wait("sync", DVE, last.get("xt_rd_dve", 0))
                ldx = dma(x_d[c * T + tt * 128: c * T + (tt + 1) * 128, :], xt_sb[:])
                em.wait("vector", LD, ldx)
                em.op("vector", DVE, 1, lambda E: E.bn_stats(
                    out=bn_sb[:, 0:6], in_=xt_sb[:, 0:512]))
                em.op("vector", DVE, 1, lambda E: E.bn_stats(
                    out=bn_sb[:, 6:12], in_=xt_sb[:, 512:1024]))
                em.op("vector", DVE, 1, lambda E: E.engine_nop())
                bs = em.op("vector", DVE, 1, lambda E: E.bn_aggr(
                    out=bn_sb[:, 12:14], in_=bn_sb[:, 0:12]))
                o = tt * 8
                bs = em.op("vector", DVE, 1, lambda E, o=o: E.tensor_copy(
                    out=sc_d[:, o:o + 2], in_=bn_sb[:, 12:14]))
                em.wait("scalar", DVE, bs)
                em.op("scalar", ACT, 1, lambda E, o=o: E.activation(
                    out=sc_a[:, o + 3:o + 4], in_=sc_d[:, o + 1:o + 2], func=AF.Ln,
                    scale=ko_sb[:, 0:1], bias=ko_sb[:, 1:2]))
                em.op("scalar", ACT, 1, lambda E, o=o: E.activation(
                    out=sc_a[:, o + 4:o + 5], in_=sc_d[:, o + 1:o + 2], func=AF.Ln,
                    bias=ko_sb[:, 2:3]))
                em.op("scalar", ACT, 1, lambda E, o=o: E.activation(
                    out=sc_a[:, o:o + 1], in_=sc_a[:, o + 3:o + 4], func=AF.Exp,
                    scale=ko_sb[:, 5:6]))
                ac = em.op("scalar", ACT, 1, lambda E, o=o: E.activation(
                    out=sc_a[:, o + 1:o + 2], in_=sc_a[:, o + 4:o + 5], func=AF.Exp,
                    scale=ko_sb[:, 5:6]))
                em.wait("vector", ACT, ac)
                em.op("vector", DVE, 1, lambda E, o=o: E.scalar_tensor_tensor(
                    out=sc_d[:, o + 2:o + 3], in0=sc_d[:, o:o + 1], scalar=-1.0,
                    in1=sc_a[:, o:o + 1], op0=ALU.mult, op1=ALU.mult))
                em.op("vector", DVE, 1, lambda E, o=o: E.scalar_tensor_tensor(
                    out=sc_d[:, o + 3:o + 4], in0=sc_d[:, o:o + 1], scalar=-1.0,
                    in1=sc_a[:, o + 1:o + 2], op0=ALU.mult, op1=ALU.mult))
                xrd = em.op("vector", DVE, 1, lambda E, o=o: E.scalar_tensor_tensor(
                    out=xt_sb[:], in0=xt_sb[:], scalar=sc_a[:, o:o + 1],
                    in1=lnwb_sb[:, 0:DM], op0=ALU.mult, op1=ALU.mult))
                em.op("vector", DVE, 1, lambda E, o=o: E.scalar_tensor_tensor(
                    out=xt_sb[:], in0=lnwb_sb[:, 0:DM], scalar=sc_d[:, o + 2:o + 3],
                    in1=xt_sb[:], op0=ALU.mult, op1=ALU.add))
                dv = em.op("vector", DVE, 1, lambda E: E.tensor_add(
                    out=xn2tm[:], in0=xt_sb[:], in1=lnwb_sb[:, DM:2 * DM]))
                last["xt_rd_dve"] = dv
                em.wait("tensor", DVE, dv)
                em.wait("tensor", ACT, last.get("psT_rd", 0))
                pe_t = 0
                for kc in range(KC):
                    pe_t = em.op("tensor", PE, 1, lambda E, k=kc: E.matmul(
                        out=psT[:, k * 128:(k + 1) * 128],
                        lhsT=xn2tm[:, k * 128:(k + 1) * 128], rhs=idb_sb[:],
                        is_transpose=True, start=(k == 0), stop=(k == KC - 1)))
                em.wait("scalar", PE, pe_t)
                ev = 0
                for kc in range(KC):
                    ev = em.op("scalar", ACT, 1, lambda E, k=kc, t=tt: E.activation(
                        out=xn2fm[:, k * T + t * 128:k * T + t * 128 + 128],
                        in_=psT[:, k * 128:(k + 1) * 128], func=AF.Copy))
                last["xn2fm_wr"] = ev
                last["psT_rd"] = ev
                yield

            last["pending_xdbl"] = None

            def flush_xdbl():
                d, p, ub = last["pending_xdbl"]
                em.wait("tensor", DVE, ub)
                if d == 0:
                    em.wait("tensor", ACT, last.get("xdbl_ps_rd", 0))
                    em.wait("tensor", ACT, last.get("z_ps_rd", 0))
                em.op("tensor", PE, 1, lambda E, d=d, p=p: E.matmul(
                    out=psC[:, 0:T], lhsT=wxp_sb[:, d * 256:d * 256 + 128],
                    rhs=ubf_sb[:, p * T:(p + 1) * T], start=(d == 0), stop=False))
                pe_d = em.op("tensor", PE, 1, lambda E, d=d, p=p: E.matmul(
                    out=psC[:, T:2 * T], lhsT=wxp_sb[:, d * 256 + 128:d * 256 + 256],
                    rhs=ubf_sb[:, p * T:(p + 1) * T], start=False, stop=(d == NDB - 1)))
                last[f"ubf_rd_{p}"] = pe_d
                last["pending_xdbl"] = None

            for db in range(NDB):
                par = db % 2
                em.wait("sync", PE, last.get(f"wxi_rd_{par}", 0))
                lw = dma(wxi_s[:, db * 1024:(db + 1) * 1024],
                         wstream[:, par * 1024:(par + 1) * 1024])
                em.wait("tensor", LD, lw)
                em.wait("tensor", ACT, last["xn2fm_wr"])
                em.wait("tensor", ACT, last.get("xi_ps_rd", 0))
                em.wait("tensor", DVE, last.get("g_dve_1", 0))
                em.wait("tensor", DVE, last.get("yf_ps_rd", 0))
                if last.get("pending_xdbl") is not None:
                    flush_xdbl()
                pe_x = 0
                for kc in range(KC):
                    pe_x = em.op("tensor", PE, 1, lambda E, k=kc, p=par: E.matmul(
                        out=psA[:, 0:T],
                        lhsT=wstream[:, p * 1024 + k * 128:p * 1024 + (k + 1) * 128],
                        rhs=xn2fm[:, k * T:(k + 1) * T],
                        start=(k == 0), stop=(k == KC - 1)))
                last[f"wxi_rd_{par}"] = pe_x
                em.wait("scalar", PE, pe_x)
                em.wait("scalar", DVE, last.get("xib_rd", 0))
                em.op("scalar", ACT, 1, lambda E, d=db: E.activation(
                    out=xib[:, 0:3], in_=halo_sb[:, d * 3:(d + 1) * 3], func=AF.Copy))
                ev = em.op("scalar", ACT, 1, lambda E: E.activation(
                    out=xib[:, 3:3 + T], in_=psA[:, 0:T], func=AF.Copy))
                last["xi_ps_rd"] = ev
                em.wait("vector", ACT, ev)
                em.wait("vector", ACT, last.get("cvb_rd", 0))
                cv = em.op("vector", DVE, 1, lambda E, d=db: E.tensor_scalar(
                    out=cvb[:], in0=xib[:, 0:T], scalar1=convw_sb[:, d * 4:d * 4 + 1],
                    scalar2=None, op0=ALU.mult))
                for k in range(1, 4):
                    cv = em.op("vector", DVE, 1, lambda E, d=db, kk=k: E.scalar_tensor_tensor(
                        out=cvb[:], in0=xib[:, kk:kk + T],
                        scalar=convw_sb[:, d * 4 + kk:d * 4 + kk + 1],
                        in1=cvb[:], op0=ALU.mult, op1=ALU.add))
                last["xib_rd"] = cv
                em.wait("scalar", DVE, cv)
                uu = em.op("scalar", ACT, 1, lambda E, d=db: E.activation(
                    out=u_sb[:, d * T:(d + 1) * T], in_=cvb[:], func=AF.Silu,
                    bias=convb_sb[:, d:d + 1]))
                last["cvb_rd"] = uu
                em.op("scalar", ACT, 1, lambda E, d=db: E.activation(
                    out=halo_sb[:, d * 3:(d + 1) * 3], in_=xib[:, T:T + 3], func=AF.Copy))
                em.wait("vector", ACT, uu)
                em.wait("vector", PE, last.get(f"ubf_rd_{par}", 0))
                ub = em.op("vector", DVE, 1, lambda E, d=db, p=par: E.tensor_copy(
                    out=ubf_sb[:, p * T:(p + 1) * T], in_=u_sb[:, d * T:(d + 1) * T]))
                last["pending_xdbl"] = (db, par, ub)
                yield

            flush_xdbl()
            last["u_wr"] = uu
            xdbl_pe = em.cnt[PE]

            em.wait("scalar", PE, xdbl_pe)
            em.wait("scalar", LD, 1)
            ev_dtr = em.op("scalar", ACT, 1, lambda E: E.activation(
                out=dtrb_sb[0:64, :], in_=psC[0:64, 0:T], func=AF.Copy))
            last["ev_dtr"] = ev_dtr
            ev_B = em.op("scalar", ACT, 1, lambda E: E.activation(
                out=Bsb[64:128, :], in_=psC[64:128, 0:T], func=AF.Copy))
            ev_C = em.op("scalar", ACT, 1, lambda E: E.activation(
                out=Csb[0:64, :], in_=psC[0:64, T:2 * T], func=AF.Copy))
            last["xdbl_ps_rd"] = ev_C
            em.wait("sync", ACT, ev_B)
            bf1 = em.op("sync", LD, 16, lambda E: E.dma_start(
                out=bflat_d, in_=Bsb[64:128, :]))
            last["bf1"] = bf1
            em.wait("tensor", ACT, ev_C)
            em.wait("tensor", ACT, last["xi_ps_rd"])
            em.wait("tensor", DVE, last.get("g_dve_1", 0))
            em.wait("tensor", DVE, last.get("yf_ps_rd", 0))
            pe_c2 = 0
            for tt in range(TT):
                pe_c2 = em.op("tensor", PE, 1, lambda E, t=tt: E.matmul(
                    out=psA[:, 256 + t * 64:256 + (t + 1) * 64],
                    lhsT=Csb[0:64, t * 128:(t + 1) * 128], rhs=idf_sb[0:64, 0:64],
                    is_transpose=True, start=(t == 0), stop=(t == TT - 1)))
            em.wait("scalar", PE, pe_c2)
            ev_ctm = em.op("scalar", ACT, 1, lambda E: E.activation(
                out=ctm_sb[:], in_=psA[:, 256:256 + TT * N], func=AF.Copy))
            last["ev_ctm"] = ev_ctm

            if dbg and c == 0:
                em.wait("sync", ACT, last["xn2fm_wr"])
                em.op("sync", OUT, 16, lambda E: E.dma_start(
                    out=dbg_outs["d_xn2"], in_=xn2fm[:]))
                em.wait("sync", ACT, last["u_wr"])
                em.op("sync", OUT, 16, lambda E: E.dma_start(out=dbg_outs["d_u"], in_=u_sb[:]))
                em.wait("sync", ACT, ev_B)
                em.op("sync", OUT, 16, lambda E: E.dma_start(out=dbg_outs["d_B"], in_=Bsb[64:128, :]))
                em.wait("sync", ACT, ev_C)
                em.op("sync", OUT, 16, lambda E: E.dma_start(out=dbg_outs["d_C"], in_=Csb[0:64, :]))


        def emit_B(c):
            for db in range(NDB):
                em.wait("tensor", ACT, last["ev_dtr"])
                em.wait("tensor", ACT, last.get("dlt_ps_rd", 0))
                em.wait("tensor", DVE, last.get("g_dve_0", 0))
                pe_dl = em.op("tensor", PE, 1, lambda E, d=db: E.matmul(
                    out=psB[:, 0:T], lhsT=wdt_sb[0:64, d * 128:(d + 1) * 128],
                    rhs=dtrb_sb[0:64, :], start=True, stop=True))
                em.wait("scalar", PE, pe_dl)
                em.wait("scalar", DVE, last.get("ssp_rd", 0))
                em.op("scalar", ACT, 1, lambda E, d=db: E.activation(
                    out=ssp_sb[:], in_=psB[:, 0:T], func=AF.Abs, bias=bdt_sb[:, d:d + 1]))
                em.op("scalar", ACT, 1, lambda E: E.activation(
                    out=ssp_sb[:], in_=ssp_sb[:], func=AF.Exp, scale=ko_sb[:, 4:5]))
                em.op("scalar", ACT, 1, lambda E: E.activation(
                    out=ssp_sb[:], in_=ssp_sb[:], func=AF.Ln, bias=ko_sb[:, 3:4]))
                em.wait("scalar", DVE, last.get("sz_rd", 0))
                em.wait("scalar", OUT, 1)
                a4 = em.op("scalar", ACT, 1, lambda E, d=db: E.activation(
                    out=y1f_sb[:], in_=psB[:, 0:T], func=AF.Relu,
                    bias=bdt_sb[:, d:d + 1]))
                last["dlt_ps_rd"] = a4
                em.wait("vector", ACT, a4)
                em.wait("vector", ACT, last.get(f"delta_rd_{db}", 0))
                ddl = em.op("vector", DVE, 1, lambda E, d=db: E.tensor_add(
                    out=delta_sb[:, d * T:(d + 1) * T], in0=y1f_sb[:], in1=ssp_sb[:]))
                last[f"delta_wr_{db}"] = ddl
                last["ssp_rd"] = ddl
                em.wait("vector", ACT, last["u_wr"])
                em.op("vector", DVE, 1, lambda E, d=db: E.tensor_mul(
                    out=du_sb[:, d * T:(d + 1) * T], in0=delta_sb[:, d * T:(d + 1) * T],
                    in1=u_sb[:, d * T:(d + 1) * T]))
            if dbg and c == 0:
                em.wait("sync", DVE, em.cnt[DVE])
                em.op("sync", OUT, 16, lambda E: E.dma_start(
                    out=dbg_outs["d_delta"], in_=delta_sb[:]))
            em.wait("vector", PE, last.get("yacc_rd_pe", 0))
            em.op("vector", DVE, 1, lambda E: E.memset(yacc_sb[:], 0.0))
            em.wait("tensor", ACT, last["ev_ctm"])
            em.wait("tensor", ACT, last["dlt_ps_rd"])
            em.wait("tensor", DVE, last.get("g_dve_0", 0))
            em.wait("tensor", DVE, last.get("ffn_ps_rd", 0))
            for grp in range(NGRP):
                bpar = grp % 2
                em.wait("sync", LD, last["bf1"])
                em.wait("sync", PE, last.get("row_rd_pe", 0))
                bfg = em.op("sync", LD, 16, lambda E, g=grp: E.dma_start(
                    out=row_sb[0:1, :],
                    in_=bflat_d[g * G:(g + 1) * G, :].rearrange("n t -> (n t)")[None, :]))
                em.wait("tensor", LD, bfg)
                em.wait("tensor", ACT, last.get("bbc_ps_rd", 0))
                pe_b = 0
                for half in range(2):
                    pe_b = em.op("tensor", PE, 1, lambda E, hf=half: E.matmul(
                        out=psE[:, hf * 512:(hf + 1) * 512], lhsT=ones1_sb[:],
                        rhs=row_sb[:, hf * 512:(hf + 1) * 512],
                        start=True, stop=True))
                last["row_rd_pe"] = pe_b
                em.wait("scalar", PE, pe_b)
                em.wait("scalar", DVE, last.get(f"bbc_rd_pool_{bpar}", 0))
                ev_b = 0
                for half in range(2):
                    ev_b = em.op("scalar", ACT, 1, lambda E, hf=half, bp=bpar: E.activation(
                        out=bbc_sb[:, bp * 1024 + hf * 512:bp * 1024 + (hf + 1) * 512],
                        in_=psE[:, hf * 512:(hf + 1) * 512], func=AF.Copy))
                last["bbc_ps_rd"] = ev_b

                for db in range(NDB):
                    par = db % 2
                    hq = (grp * NDB + db) % NH
                    dec, dbuf = dec_sb[par], dbu_sb[par]
                    em.wait("scalar", DVE, max(last["dec_rd"][par],
                                               last.get(f"delta_wr_{db}", 0)))
                    ac_d = 0
                    for g in range(G):
                        nn = grp * G + g
                        ac_d = em.op("scalar", ACT, 1, lambda E, d=db, n=nn, g=g, dd=dec:
                                     E.activation(
                            out=dd[:, g * (T + 1) + 1:(g + 1) * (T + 1)],
                            in_=delta_sb[:, d * T:(d + 1) * T], func=AF.Exp,
                            scale=a_sb[:, d * N + n:d * N + n + 1]))
                    last[f"delta_rd_{db}"] = ac_d
                    if last.get("pending_gather") is not None:
                        pd, pg0, phq, psc = last["pending_gather"]
                        em.wait("scalar", DVE, psc)
                        gat = em.op("scalar", ACT, 1, lambda E, d=pd, g0=pg0,
                                    hh=h_sb[phq]: E.activation(
                            out=hstate_sb[:, d * N + g0 * G:d * N + (g0 + 1) * G],
                            in_=hh[:].rearrange("p (g t) -> p g t", g=G)[:, :, T],
                            func=AF.Copy))
                        last["h_rd_act"][phq] = gat
                        last["pending_gather"] = None
                    dbu_pay = dbuf[:].rearrange("p (g t) -> p g t", g=G)[:, :, 1:]
                    em.wait("vector", ACT, ev_b)
                    em.wait("vector", ACT, ac_d)
                    em.wait("vector", ACT, last["h_rd_act"][hq])
                    po = em.op("vector", DVE, 1, lambda E, d=db, dp=dbu_pay, bp=bpar:
                               E.tensor_tensor(
                        out=dp,
                        in0=du_sb[:, d * T:(d + 1) * T].unsqueeze(1).broadcast_to(
                            [128, G, T]),
                        in1=bbc_sb[:, bp * 1024:(bp + 1) * 1024].rearrange(
                            "p (g t) -> p g t", g=G),
                        op=ALU.mult))
                    last[f"bbc_rd_pool_{bpar}"] = po
                    bnd = em.op("vector", DVE, 1, lambda E, d=db, g0=grp, dd=dbuf:
                                E.tensor_copy(
                        out=dd[:].rearrange("p (g t) -> p g t", g=G)[:, :, 0],
                        in_=hstate_sb[:, d * N + g0 * G:d * N + (g0 + 1) * G]))
                    em.wait("vector", PE, last["h_rd_pe"][hq])
                    em.wait("vector", ACT, last["h_rd_act"][hq])
                    sc_v = em.op("vector", DVE, 1, lambda E, dd=dec, bb=dbuf, hh=h_sb[hq]:
                                 E.tensor_tensor_scan(
                        out=hh[:], data0=dd[:], data1=bb[:], initial=0.0,
                        op0=ALU.mult, op1=ALU.add))
                    last["dec_rd"][par] = sc_v
                    last["dbu_rd"][par] = sc_v
                    last[f"h_wr_{hq}"] = sc_v
                    last["pending_gather"] = (db, grp, hq, sc_v)

                    if db % 4 == 3:
                        q0 = db - 3
                        hqs = [(grp * NDB + q0 + q) % NH for q in range(4)]
                        for g in range(G):
                            nn = grp * G + g
                            for tt in range(TT):
                                use_a = (g * TT + tt) % 2 == 0
                                rt = psA if use_a else psB
                                key = "rt_rd_A" if use_a else "rt_rd_B"
                                em.wait("tensor", DVE, last.get(key, 0))
                                pe_r = 0
                                for q in range(4):
                                    em.wait("tensor", DVE, last[f"h_wr_{hqs[q]}"])
                                    pe_r = em.op("tensor", PE, 1,
                                                 lambda E, qq=hqs[q], qi=q, gg=g, t=tt, r=rt: E.matmul(
                                        out=r[:, qi * 128:(qi + 1) * 128],
                                        lhsT=h_sb[qq][:, gg * (T + 1) + 1 + t * 128:
                                                      gg * (T + 1) + 1 + (t + 1) * 128],
                                        rhs=idf_sb[:], is_transpose=True,
                                        start=(qi == 0), stop=(qi == 3)))
                                for q in range(4):
                                    last["h_rd_pe"][hqs[q]] = pe_r
                                em.wait("vector", PE, pe_r)
                                st_v = em.op("vector", DVE, 1,
                                             lambda E, n=nn, t=tt, r=rt, qb=q0:
                                             E.scalar_tensor_tensor(
                                    out=yacc_sb[:, t * DI + qb * 128:t * DI + qb * 128 + 512],
                                    in0=r[:, 0:512],
                                    scalar=ctm_sb[:, t * N + n:t * N + n + 1],
                                    in1=yacc_sb[:, t * DI + qb * 128:t * DI + qb * 128 + 512],
                                    op0=ALU.mult, op1=ALU.add))
                                last[key] = st_v
            if last.get("pending_gather") is not None:
                pd, pg0, phq, psc = last["pending_gather"]
                em.wait("scalar", DVE, psc)
                gat = em.op("scalar", ACT, 1, lambda E, d=pd, g0=pg0,
                            hh=h_sb[phq]: E.activation(
                    out=hstate_sb[:, d * N + g0 * G:d * N + (g0 + 1) * G],
                    in_=hh[:].rearrange("p (g t) -> p g t", g=G)[:, :, T],
                    func=AF.Copy))
                last["h_rd_act"][phq] = gat
                last["pending_gather"] = None
            yacc_dve = em.cnt[DVE]
            last["yacc_dve"] = yacc_dve

            if dbg and c == 0:
                em.wait("sync", DVE, yacc_dve)
                em.op("sync", OUT, 16, lambda E: E.dma_start(
                    out=dbg_outs["d_yacc"], in_=yacc_sb[:]))


        def emit_C1(c):
            last["pending_wout"] = None

            def flush_wout():
                d, p, y1b, lwo = last["pending_wout"]
                em.wait("tensor", DVE, y1b)
                em.wait("tensor", LD, lwo)
                if d == 0:
                    em.wait("tensor", DVE, last.get("ffn_ps_rd", 0))
                    em.wait("tensor", ACT, last.get("bbc_ps_rd", 0))
                pe_o = 0
                for tt in range(TT):
                    dstp = psE if tt == 0 else psF
                    for half in range(2):
                        pe_o = em.op("tensor", PE, 1, lambda E, d=d, t=tt, hf=half,
                                     ds=dstp, p=p: E.matmul(
                            out=ds[:, hf * 512:(hf + 1) * 512],
                            lhsT=y1bf_sb[:, p * T + t * 128:p * T + (t + 1) * 128],
                            rhs=woutb[:, p * DM + hf * 512:p * DM + (hf + 1) * 512],
                            start=(d == 0), stop=(d == NDB - 1)))
                last[f"wout_rd_{p}"] = pe_o
                last[f"y1bf_rd_{p}"] = pe_o
                last["pending_wout"] = None

            for db in range(NDB):
                par = db % 2
                em.wait("sync", PE, last.get(f"wout_rd_{par}", 0))
                lwo = dma(wout_s[:, db * DM:(db + 1) * DM],
                          woutb[:, par * DM:(par + 1) * DM])
                em.wait("sync", PE, last.get(f"wz_rd_{par}", 0))
                lwz = dma(wz_s[:, db * 1024:(db + 1) * 1024],
                          wzbuf[:, par * 1024:(par + 1) * 1024])
                em.wait("tensor", DVE, last["yacc_dve"])
                em.wait("tensor", DVE, last.get("yf_ps_rd", 0))
                pe_y = 0
                for tt in range(TT):
                    pe_y = em.op("tensor", PE, 1, lambda E, d=db, t=tt: E.matmul(
                        out=psA[:, t * 128:(t + 1) * 128],
                        lhsT=yacc_sb[:, t * DI + d * 128:t * DI + (d + 1) * 128],
                        rhs=idf_sb[:], is_transpose=True,
                        start=(t == 0), stop=(t == TT - 1)))
                last["yacc_rd_pe"] = pe_y
                em.wait("vector", PE, pe_y)
                em.wait("vector", OUT, last.get("y1f_dump", 0))
                y1v = em.op("vector", DVE, 1, lambda E, d=db: E.scalar_tensor_tensor(
                    out=y1f_sb[:], in0=u_sb[:, d * T:(d + 1) * T],
                    scalar=dvec_sb[:, d:d + 1], in1=psA[:, 0:T],
                    op0=ALU.mult, op1=ALU.add))
                last["yf_ps_rd"] = y1v
                em.wait("tensor", LD, lwz)
                if db == 0:
                    em.wait("tensor", ACT, last["xdbl_ps_rd"])
                em.wait("tensor", ACT, last.get("z_ps_rd", 0))
                pe_z = 0
                for kc in range(KC):
                    pe_z = em.op("tensor", PE, 1, lambda E, k=kc, p=par: E.matmul(
                        out=psC[:, 0:T],
                        lhsT=wzbuf[:, p * 1024 + k * 128:p * 1024 + (k + 1) * 128],
                        rhs=xn2fm[:, k * T:(k + 1) * T],
                        start=(k == 0), stop=(k == KC - 1)))
                last[f"wz_rd_{par}"] = pe_z
                em.wait("scalar", PE, pe_z)
                em.wait("scalar", DVE, last.get("sz_rd", 0))
                szv = em.op("scalar", ACT, 1, lambda E: E.activation(
                    out=sz_sb[:], in_=psC[:, 0:T], func=AF.Silu))
                last["z_ps_rd"] = szv
                em.wait("vector", ACT, szv)
                em.wait("vector", PE, last.get(f"y1bf_rd_{par}", 0))
                y1b = em.op("vector", DVE, 1, lambda E, p=par: E.tensor_mul(
                    out=y1bf_sb[:, p * T:(p + 1) * T], in0=y1f_sb[:], in1=sz_sb[:]))
                last["sz_rd"] = y1b
                if last.get("pending_wout") is not None:
                    flush_wout()
                last["pending_wout"] = (db, par, y1b, lwo)
                if dbg and c == 0:
                    em.wait("sync", DVE, y1b)
                    last["y1f_dump"] = em.op("sync", OUT, 16, lambda E, d=db: E.dma_start(
                        out=dbg_outs["d_y1"][:, d * T:(d + 1) * T], in_=y1f_sb[:]))
            flush_wout()
            last["om_pe"] = em.cnt[PE]

        def emit_C2(c):

            for tt in range(TT):
                o = tt * 8
                em.wait("sync", DVE, last.get("xt_rd_dve", 0))
                em.wait("sync", ACT, last.get("xt_rd_act", 0))
                ldx = dma(x_d[c * T + tt * 128: c * T + (tt + 1) * 128, :], xt_sb[:])
                em.wait("vector", LD, ldx)
                em.wait("vector", PE, last["om_pe"])
                em.wait("vector", ACT, last.get("x2_rd_act", 0))
                omp = psE if tt == 0 else psF
                xv = em.op("vector", DVE, 1, lambda E, p=omp: E.tensor_add(
                    out=big1[:], in0=xt_sb[:], in1=p[:]))
                last["om_ps_rd"] = xv
                xv = em.op("vector", DVE, 1, lambda E, o=o: E.tensor_scalar(
                    out=xt_sb[:], in0=xt_sb[:], scalar1=sc_a[:, o + 1:o + 2],
                    scalar2=sc_d[:, o + 3:o + 4], op0=ALU.mult, op1=ALU.add))
                xv = em.op("vector", DVE, 1, lambda E, t=tt: E.tensor_add(
                    out=x2_sb[:, t * DM:(t + 1) * DM], in0=big1[:], in1=xt_sb[:]))
                last["xt_rd_dve"] = xv
                em.op("vector", DVE, 1, lambda E, t=tt: E.bn_stats(
                    out=bn_sb[:, 0:6], in_=x2_sb[:, t * DM:t * DM + 512]))
                em.op("vector", DVE, 1, lambda E, t=tt: E.bn_stats(
                    out=bn_sb[:, 6:12], in_=x2_sb[:, t * DM + 512:(t + 1) * DM]))
                em.op("vector", DVE, 1, lambda E: E.engine_nop())
                bs = em.op("vector", DVE, 1, lambda E: E.bn_aggr(
                    out=bn_sb[:, 12:14], in_=bn_sb[:, 0:12]))
                em.wait("scalar", DVE, bs)
                em.op("scalar", ACT, 1, lambda E, o=o: E.activation(
                    out=sc_a[:, o + 5:o + 6], in_=bn_sb[:, 13:14], func=AF.Ln,
                    bias=ko_sb[:, 2:3]))
                ac = em.op("scalar", ACT, 1, lambda E, o=o: E.activation(
                    out=sc_a[:, o + 2:o + 3], in_=sc_a[:, o + 5:o + 6], func=AF.Exp,
                    scale=ko_sb[:, 5:6]))
                em.wait("vector", ACT, ac)
                q3v = em.op("vector", DVE, 1, lambda E, o=o: E.scalar_tensor_tensor(
                    out=sc_d[:, o + 4:o + 5], in0=bn_sb[:, 12:13], scalar=-1.0,
                    in1=sc_a[:, o + 2:o + 3], op0=ALU.mult, op1=ALU.mult))
                xn3v = em.op("vector", DVE, 1, lambda E, o=o, t=tt: E.tensor_scalar(
                    out=xn3t_sb[:], in0=x2_sb[:, t * DM:(t + 1) * DM],
                    scalar1=sc_a[:, o + 2:o + 3], scalar2=sc_d[:, o + 4:o + 5],
                    op0=ALU.mult, op1=ALU.add))
                last["x2_rd_act"] = 0
                last["x2_rd_dve2"] = xn3v
                em.wait("tensor", DVE, xn3v)
                em.wait("tensor", ACT, last.get("xn3_ps_rd", 0))
                pe_t = 0
                for kc in range(KC):
                    pe_t = em.op("tensor", PE, 1, lambda E, k=kc: E.matmul(
                        out=psT[:, k * 128:(k + 1) * 128],
                        lhsT=xn3t_sb[:, k * 128:(k + 1) * 128], rhs=idb_sb[:],
                        is_transpose=True, start=(k == 0), stop=(k == KC - 1)))
                em.wait("scalar", PE, pe_t)
                em.wait("scalar", PE, last.get("xn3f_rd", 0))
                ev = 0
                for kc in range(KC):
                    ev = em.op("scalar", ACT, 1, lambda E, k=kc, t=tt: E.activation(
                        out=xn3f_sb[:, k * T + t * 128:k * T + t * 128 + 128],
                        in_=psT[:, k * 128:(k + 1) * 128], func=AF.Copy))
                last["psT_rd"] = ev
                last["xn3f_wr"] = ev


        def emit_C3(c, ga=iter(())):
            last["pending_down"] = None

            def flush_down():
                h, p, hhv = last["pending_down"]
                em.wait("tensor", DVE, hhv)
                if h == 0:
                    em.wait("tensor", DVE, last.get("x3_ffn_rd", 0))
                pe_dn = 0
                for tt in range(TT):
                    dstp = psE if tt == 0 else psF
                    for half in range(2):
                        pe_dn = em.op("tensor", PE, 1, lambda E, h=h, t=tt, hf=half,
                                      ds=dstp, p=p: E.matmul(
                            out=ds[:, hf * 512:(hf + 1) * 512],
                            lhsT=hh_sb[:, p * 256 + t * 128:p * 256 + (t + 1) * 128],
                            rhs=wdb[:, p * DM + hf * 512:p * DM + (hf + 1) * 512],
                            start=(h == 0), stop=(h == NHC - 1)))
                last[f"hh_rd_{p}"] = pe_dn
                last[f"wffn_rd_{p}"] = pe_dn
                last["pending_down"] = None

            for hc in range(NHC):
                par = hc % 2
                next(ga, None)
                em.wait("sync", PE, last.get(f"wffn_rd_{par}", 0))
                dma(wg_s[:, hc * 1024:(hc + 1) * 1024], wgb[:, par * 1024:(par + 1) * 1024])
                dma(wu_s[:, hc * 1024:(hc + 1) * 1024], wub[:, par * 1024:(par + 1) * 1024])
                ldn = dma(wd_s[:, hc * DM:(hc + 1) * DM], wdb[:, par * DM:(par + 1) * DM])
                em.wait("tensor", LD, ldn)
                em.wait("tensor", ACT, last["xn3f_wr"])
                em.wait("tensor", DVE, last.get(f"g_dve_{par}", 0))
                em.wait("tensor", ACT, last.get(f"g_act_{par}", 0))
                gu_ps = psB[:] if par == 0 else psT[:].bitcast(F32)
                if hc == 0:
                    em.wait("tensor", ACT, last["dlt_ps_rd"])
                if hc == 1:
                    em.wait("tensor", ACT, last.get("psT_rd", 0))
                pe_g = 0
                for kc in range(KC):
                    pe_g = em.op("tensor", PE, 1, lambda E, k=kc, p=par, gp=gu_ps: E.matmul(
                        out=gp[:, 0:256],
                        lhsT=wgb[:, p * 1024 + k * 128:p * 1024 + (k + 1) * 128],
                        rhs=xn3f_sb[:, k * T:(k + 1) * T],
                        start=(k == 0), stop=(k == KC - 1)))
                em.wait("scalar", PE, pe_g)
                em.wait("scalar", DVE, last.get(f"sfn_rd_{par}", 0))
                sg = em.op("scalar", ACT, 1, lambda E, p=par, gp=gu_ps: E.activation(
                    out=sfn_sb[:, p * 256:(p + 1) * 256],
                    in_=gp[:, 0:256], func=AF.Silu,
                    scale=ko_sb[:, 6:7]))
                last[f"g_act_{par}"] = sg
                em.wait("tensor", ACT, sg)
                pe_u = 0
                for kc in range(KC):
                    pe_u = em.op("tensor", PE, 1, lambda E, k=kc, p=par, gp=gu_ps: E.matmul(
                        out=gp[:, 0:256],
                        lhsT=wub[:, p * 1024 + k * 128:p * 1024 + (k + 1) * 128],
                        rhs=xn3f_sb[:, k * T:(k + 1) * T],
                        start=(k == 0), stop=(k == KC - 1)))
                if last.get("pending_down") is not None:
                    flush_down()
                em.wait("vector", PE, pe_u)
                em.wait("vector", ACT, sg)
                em.wait("vector", PE, last.get(f"hh_rd_{par}", 0))
                hhv = em.op("vector", DVE, 1, lambda E, p=par, gp=gu_ps: E.scalar_tensor_tensor(
                    out=hh_sb[:, p * 256:(p + 1) * 256],
                    in0=gp[:, 0:256],
                    scalar=FS, in1=sfn_sb[:, p * 256:(p + 1) * 256],
                    op0=ALU.mult, op1=ALU.mult))
                last[f"sfn_rd_{par}"] = hhv
                last[f"g_dve_{par}"] = hhv
                last["pending_down"] = (hc, par, hhv)
            flush_down()
            ffn_pe = em.cnt[PE]
            for tt in range(TT):
                em.wait("vector", PE, ffn_pe)
                em.wait("vector", OUT, last.get("x3_rd", 0))
                ffp = psE if tt == 0 else psF
                x3v = em.op("vector", DVE, 1, lambda E, p=ffp, t=tt: E.scalar_tensor_tensor(
                    out=x3_sb[:], in0=p[:], scalar=FS,
                    in1=x2_sb[:, t * DM:(t + 1) * DM], op0=ALU.mult, op1=ALU.add))
                last["x3_ffn_rd"] = x3v
                last["ffn_ps_rd"] = x3v
                em.wait("sync", DVE, x3v)
                ov = em.op("sync", OUT, 16, lambda E, t=tt, cc=c: E.dma_start(
                    out=out_d[cc * T + t * 128: cc * T + (t + 1) * 128, :], in_=x3_sb[:]))
                last["x3_rd"] = ov


        for _ in emit_A(0):
            pass
        for c in range(NCH):
            emit_B(c)
            emit_C1(c)
            emit_C2(c)
            ga = emit_A(c + 1) if c + 1 < NCH else iter(())
            for _ in range(2):
                next(ga, None)
            emit_C3(c, ga)
            for _ in ga:
                pass

        em.wait("sync", OUT, em.cnt[OUT])

        @block.sync
        def _(E):
            em.replay("sync", E, sems)

        @block.vector
        def _(E):
            em.replay("vector", E, sems)

        @block.scalar
        def _(E):
            em.replay("scalar", E, sems)

        @block.tensor
        def _(E):
            em.replay("tensor", E, sems)

        @block.gpsimd
        def _(E):
            em.replay("gpsimd", E, sems)

    return nc


def _bf16():
    import ml_dtypes
    return ml_dtypes.bfloat16


def host_inputs(inputs, b):
    return {
        "x": np.ascontiguousarray(inputs["L0_attn_x"][b]),
        "w_in": np.ascontiguousarray(inputs["L0_attn_W_in"]),
        "wxp": np.ascontiguousarray(inputs["L0_attn_W_xproj"]),
        "wdt": np.ascontiguousarray(inputs["L0_attn_W_dt"]),
        "wout": np.ascontiguousarray(inputs["L0_attn_W_out"]),
        "alog": np.ascontiguousarray(
            inputs["L0_attn_A_log"].reshape(NDB, 128, N).transpose(1, 0, 2).reshape(128, NDB * N)),
        "dvec": np.ascontiguousarray(inputs["L0_attn_D"].reshape(NDB, 128).T),
        "convw": np.ascontiguousarray(
            inputs["L0_attn_conv_w"].reshape(NDB, 128, 4).transpose(1, 0, 2).reshape(128, NDB * 4)),
        "convb": np.ascontiguousarray(inputs["L0_attn_conv_b"].reshape(NDB, 128).T),
        "bdt": np.ascontiguousarray(inputs["L0_attn_b_dt"].reshape(NDB, 128).T),
        "lnw": np.ascontiguousarray(inputs["L0_attn_ln_w"].reshape(1, DM)),
        "lnb": np.ascontiguousarray(inputs["L0_attn_ln_b"].reshape(1, DM)),
        "wgT": np.ascontiguousarray(inputs["L0_ffn_w_gate"].T),
        "wuT": np.ascontiguousarray(inputs["L0_ffn_w_up"].T),
        "wdT": np.ascontiguousarray(inputs["L0_ffn_w_down"].T),
        "idf": np.eye(128, dtype=np.float32),
        "idb": np.eye(128, dtype=np.float32).astype(_bf16()),
    }


def kernel(**inputs):
    nc = build_nc()
    in_maps = [host_inputs(inputs, core // 2) for core in range(8)]
    res = run_bass_kernel_spmd(nc, in_maps, core_ids=list(range(8)))
    out = np.empty((4, L, DM), dtype=np.float32)
    for b in range(4):
        out[b, :L // 2] = res.results[2 * b]["out"][:L // 2]
        out[b, L // 2:] = res.results[2 * b + 1]["out"][L // 2:]
    return out


# version 10
# speedup vs baseline: 1.0823x; 1.0278x over previous
import contextlib
import numpy as np

import concourse.bass as bass
import concourse.mybir as mybir
from concourse.bass_utils import run_bass_kernel_spmd

ds = bass.ds

F32 = mybir.dt.float32
BF16 = mybir.dt.bfloat16
I32 = mybir.dt.int32
AF = mybir.ActivationFunctionType
ALU = mybir.AluOpType

L = 8192
DM = 1024
DI = 2048
NDB = 16
KC = 8
N = 64
G = 4
NGRP = N // G
T = 256
TT = T // 128
W = G * (T + 1)
HF = 3072
NHC = 24
EPS = 1e-5
FS = float(1.0 / 127.0)

SEMS = ("LD", "PE", "ACT", "DVE", "POOL", "OUT")
ENGS = ("sync", "vector", "scalar", "tensor", "gpsimd")


DMA_SEMS = ("LD", "OUT")


class Em:
    def __init__(self):
        self.q = {e: [] for e in ENGS}
        self.cnt = {s: 0 for s in SEMS}
        self.hw = {e: {s: 0 for s in SEMS} for e in ENGS}
        self.pend = {s: 0 for s in DMA_SEMS}

    def wait(self, eng, sem, val):
        if val is None or val <= 0:
            return
        if sem in DMA_SEMS:
            val = self.cnt[sem]
            self.pend[sem] = max(self.pend[sem], val)
        assert val <= self.cnt[sem], f"wait-on-future {sem} {val} > {self.cnt[sem]}"
        if self.hw[eng][sem] >= val:
            return
        self.hw[eng][sem] = val
        self.q[eng].append(("W", sem, val))

    ENG_SEM = {"vector": "DVE", "scalar": "ACT", "gpsimd": "POOL", "tensor": "PE"}

    def op(self, eng, sem, amt, fn):
        own = self.ENG_SEM.get(eng)
        if own is not None and eng != "tensor":
            v = self.cnt[own]
            if v > self.hw[eng][own]:
                self.hw[eng][own] = v
                self.q[eng].append(("W", own, v))
        if sem in DMA_SEMS and self.pend[sem] > self.hw[eng][sem]:
            v = self.pend[sem]
            self.hw[eng][sem] = v
            self.q[eng].append(("W", sem, v))
        self.cnt[sem] += amt
        self.q[eng].append(("I", sem, amt, fn))
        return self.cnt[sem]

    def nosem(self, eng, fn):
        self.q[eng].append(("N", fn))

    def barrier(self, *sems_now):
        snap = {s: self.cnt[s] for s in (sems_now or SEMS)}
        for e in ENGS:
            for s, v in snap.items():
                self.wait(e, s, v)

    def replay(self, eng, E, sems):
        for item in self.q[eng]:
            if item[0] == "W":
                E.wait_ge(sems[item[1]], item[2])
            elif item[0] == "I":
                item[3](E).then_inc(sems[item[1]], item[2])
            else:
                item[1](E)


def build_nc(nch=L // T, dbg=False):
    nc = bass.Bass("TRN2", target_bir_lowering=False)
    NCH = nch

    x_d = nc.dram_tensor("x", [L, DM], F32, kind="ExternalInput").ap()
    w_in_d = nc.dram_tensor("w_in", [DM, 2 * DI], F32, kind="ExternalInput").ap()
    wxp_d = nc.dram_tensor("wxp", [DI, 192], F32, kind="ExternalInput").ap()
    wdt_d = nc.dram_tensor("wdt", [64, DI], F32, kind="ExternalInput").ap()
    wout_d = nc.dram_tensor("wout", [DI, DM], F32, kind="ExternalInput").ap()
    alog_d = nc.dram_tensor("alog", [128, NDB * N], F32, kind="ExternalInput").ap()
    dvec_d = nc.dram_tensor("dvec", [128, NDB], F32, kind="ExternalInput").ap()
    convw_d = nc.dram_tensor("convw", [128, NDB * 4], F32, kind="ExternalInput").ap()
    convb_d = nc.dram_tensor("convb", [128, NDB], F32, kind="ExternalInput").ap()
    bdt_d = nc.dram_tensor("bdt", [128, NDB], F32, kind="ExternalInput").ap()
    lnw_d = nc.dram_tensor("lnw", [1, DM], F32, kind="ExternalInput").ap()
    lnb_d = nc.dram_tensor("lnb", [1, DM], F32, kind="ExternalInput").ap()
    wgT_d = nc.dram_tensor("wgT", [DM, HF], I32, kind="ExternalInput").ap()
    wuT_d = nc.dram_tensor("wuT", [DM, HF], I32, kind="ExternalInput").ap()
    wdT_d = nc.dram_tensor("wdT", [HF, DM], I32, kind="ExternalInput").ap()
    idf_d = nc.dram_tensor("idf", [128, 128], F32, kind="ExternalInput").ap()
    idb_d = nc.dram_tensor("idb", [128, 128], BF16, kind="ExternalInput").ap()
    idh_d = nc.dram_tensor("idh", [128, 256], F32, kind="ExternalInput").ap()
    msel_d = nc.dram_tensor("msel", [128, 2], F32, kind="ExternalInput").ap()
    hoff_d = nc.dram_tensor("hoff", [1, 1], I32, kind="ExternalInput").ap()
    xs_d = nc.dram_tensor("xs", [L, DM], F32, kind="ExternalInput").ap()
    out_d = nc.dram_tensor("out", [L // 2, DM], F32, kind="ExternalOutput").ap()

    wxi_s = nc.dram_tensor("wxi_s", [128, NDB * KC * 128], BF16, kind="Internal").ap()
    wz_s = nc.dram_tensor("wz_s", [128, NDB * KC * 128], BF16, kind="Internal").ap()
    wout_s = nc.dram_tensor("wout_s", [128, NDB * DM], BF16, kind="Internal").ap()
    wg_s = nc.dram_tensor("wg_s", [128, NHC * KC * 128], BF16, kind="Internal").ap()
    wu_s = nc.dram_tensor("wu_s", [128, NHC * KC * 128], BF16, kind="Internal").ap()
    wd_s = nc.dram_tensor("wd_s", [128, NHC * DM], BF16, kind="Internal").ap()
    bflat_d = nc.dram_tensor("bflat_d", [N, T], F32, kind="Internal").ap()
    u_d = nc.dram_tensor("u_d", [128, NDB * T], F32, kind="Internal").ap()
    xn2_d = nc.dram_tensor("xn2_d", [128, KC * T], BF16, kind="Internal").ap()

    dbg_outs = {}
    if dbg:
        for nm, shape in (("d_xn2", [128, KC * T]), ("d_u", [128, NDB * T]),
                          ("d_delta", [128, NDB * T]), ("d_B", [64, T]),
                          ("d_C", [64, T]), ("d_yacc", [128, DI]),
                          ("d_y1", [128, NDB * 128])):
            dt = BF16 if nm == "d_xn2" else F32
            dbg_outs[nm] = nc.dram_tensor(nm, shape, dt, kind="ExternalOutput").ap()

    em = Em()
    SN = {}
    sreg = nc.sync.alloc_register("hreg")
    nc.sync.enable_hardware_checks = False
    es = contextlib.ExitStack()
    with es:
        sb = lambda name, shape, dt=F32: es.enter_context(nc.sbuf_tensor(name, shape, dt))
        idh_sb = sb("idh_sb", [128, 256])
        msel_sb = sb("msel_sb", [128, 2])
        hoff_sb = es.enter_context(nc.sbuf_tensor("hoff_sb", [1, 1], I32))
        u_my = sb("u_my", [128, NDB * 128])
        xn2my = sb("xn2my", [128, KC * 128], BF16)
        scm_sb = sb("scm_sb", [128, 8])
        wxp_sb = sb("wxp_sb", [128, NDB * 256], BF16)
        wdt_sb = sb("wdt_sb", [128, DI], BF16)
        a_sb = sb("a_sb", [128, NDB * N])
        dvec_sb = sb("dvec_sb", [128, NDB])
        convw_sb = sb("convw_sb", [128, NDB * 4])
        convb_sb = sb("convb_sb", [128, NDB])
        bdt_sb = sb("bdt_sb", [128, NDB])
        lnwb_sb = sb("lnwb_sb", [128, 2 * DM])
        row_sb = sb("row_sb", [1, G * T])
        ones1_sb = sb("ones1_sb", [1, 128])
        idf_sb = sb("idf_sb", [128, 128])
        idb_sb = sb("idb_sb", [128, 128], BF16)
        wstream = sb("wstream", [128, 2048], BF16)
        wzbuf = sb("wzbuf", [128, 2 * 1024], BF16)
        woutb = sb("woutb", [128, 2 * 1024], BF16)
        wgb = sb("wgb", [128, 2 * 1024], BF16)
        wub = sb("wub", [128, 2 * 1024], BF16)
        wdb = sb("wdb", [128, 2 * 1024], BF16)
        xt_sb = sb("xt_sb", [128, DM])
        bn_sb = sb("bn_sb", [128, 16])
        ko_sb = sb("ko_sb", [128, 8])
        sc_d = sb("sc_d", [128, TT * 8])
        sc_a = sb("sc_a", [128, TT * 8])
        big1 = sb("big1", [128, DM])
        xn2tm = sb("xn2tm", [128, DM], BF16)
        xn2fm = sb("xn2fm", [128, KC * T], BF16)
        xib = sb("xib", [128, 3 + T])
        cvb = sb("cvb", [128, T])
        halo_sb = sb("halo_sb", [128, NDB * 3])
        u_sb = sb("u_sb", [128, NDB * T])
        ubf_sb = sb("ubf_sb", [128, 2 * T], BF16)
        delta_sb = sb("delta_sb", [128, NDB * T])
        du_sb = sb("du_sb", [128, NDB * T])
        dtrb_sb = sb("dtrb_sb", [128, T], BF16)
        Bsb = sb("Bsb", [128, T])
        Csb = sb("Csb", [128, T])
        ctm_sb = sb("ctm_sb", [128, N])
        dec_sb = [sb(f"dec{i}", [128, W]) for i in range(2)]
        dbu_sb = [sb(f"dbu{i}", [128, W]) for i in range(2)]
        NH = 6
        h_sb = [sb(f"h{i}", [128, W]) for i in range(NH)]
        bbc_sb = sb("bbc_sb", [128, 2 * G * T])
        hstate_sb = sb("hstate_sb", [128, NDB * N])
        yacc_sb = sb("yacc_sb", [128, DI])
        y1f_sb = sb("y1f_sb", [128, T])
        sz_sb = sb("sz_sb", [128, 128])
        y1bf_sb = sb("y1bf_sb", [128, 2 * 128], BF16)
        x2_sb = sb("x2_sb", [128, DM])
        xn3f_sb = sb("xn3f_sb", [128, KC * 128], BF16)
        hh_sb = sb("hh_sb", [128, 256], BF16)

        ssp_sb = cvb
        sfn_sb = big1
        xn3t_sb = xn2tm
        x3_sb = big1
        psA = es.enter_context(nc.psum_tensor("psA", [128, 512], F32))
        psB = es.enter_context(nc.psum_tensor("psB", [128, 512], F32))
        psC = es.enter_context(nc.psum_tensor("psC", [128, 512], F32))
        psE = es.enter_context(nc.psum_tensor("psE", [128, 1024], F32))
        psF = es.enter_context(nc.psum_tensor("psF", [128, 1024], F32))
        psT = es.enter_context(nc.psum_tensor("psT", [128, 1024], BF16))

        sems = {s: es.enter_context(nc.semaphore(s)) for s in SEMS}
        block = es.enter_context(nc.Block())

        LD, PE, ACT, DVE, POOL, OUT = SEMS

        def dma(src, dst):
            return em.op("sync", LD, 16, lambda E, s=src, d=dst: E.dma_start(out=d, in_=s))

        ldh = dma(hoff_d, hoff_sb[:])
        em.wait("sync", LD, ldh)
        em.nosem("sync", lambda E: E.reg_load(sreg, hoff_sb[0:1, 0:1]))
        em.nosem("sync", lambda E: SN.__setitem__("t", E.snap(sreg)))
        dma(idh_d, idh_sb[:])
        dma(msel_d, msel_sb[:])
        ld0 = dma(alog_d, a_sb[:])
        dma(dvec_d, dvec_sb[:])
        dma(convw_d, convw_sb[:])
        dma(convb_d, convb_sb[:])
        dma(bdt_d, bdt_sb[:])
        dma(idf_d, idf_sb[:])
        ld_idb = dma(idb_d, idb_sb[:])

        em.wait("scalar", LD, ld0)
        a_exp = em.op("scalar", ACT, 1, lambda E: E.activation(
            out=big1[:], in_=a_sb[:], func=AF.Exp))
        em.wait("vector", ACT, a_exp)
        a_done = em.op("vector", DVE, 1, lambda E: E.tensor_scalar_mul(
            out=a_sb[:], in0=big1[:], scalar1=-1.0))
        ones_done = em.op("vector", DVE, 1, lambda E: E.memset(ones1_sb[:], 1.0))
        for i, v in enumerate((1.0 + EPS, EPS * EPS, EPS, 1.0, -1.0, -0.5, FS)):
            kdone = em.op("vector", DVE, 1, lambda E, ii=i, vv=v: E.memset(
                ko_sb[:, ii:ii + 1], vv))
        em.wait("scalar", DVE, kdone)
        z_dve = em.op("vector", DVE, 1, lambda E: E.memset(halo_sb[:], 0.0))
        z_dve = em.op("vector", DVE, 1, lambda E: E.memset(hstate_sb[:], 0.0))
        for d in dec_sb:
            z_dve = em.op("vector", DVE, 1, lambda E, dd=d: E.memset(dd[:], 0.0))

        em.wait("tensor", DVE, ones_done)
        em.wait("tensor", LD, ld_idb)
        lnbc_done = 0
        for which, srcd in ((0, lnw_d), (1, lnb_d)):
            em.wait("sync", ACT, lnbc_done)
            lrow = dma(srcd, row_sb[:, 0:DM])
            em.wait("tensor", LD, lrow)
            pe_c = 0
            for half in range(2):
                pe_c = em.op("tensor", PE, 1, lambda E, hf=half: E.matmul(
                    out=psE[:, hf * 512:(hf + 1) * 512], lhsT=ones1_sb[:],
                    rhs=row_sb[:, hf * 512:hf * 512 + 512], start=True, stop=True))
            em.wait("scalar", PE, pe_c)
            for half in range(2):
                lnbc_done = em.op("scalar", ACT, 1, lambda E, w=which, hf=half:
                                  E.activation(
                    out=lnwb_sb[:, w * DM + hf * 512: w * DM + hf * 512 + 512],
                    in_=psE[:, hf * 512:(hf + 1) * 512], func=AF.Copy))

        wi32 = xt_sb[:].bitcast(I32)
        cvt = a_done
        for dst_s, c0 in ((wxi_s, 0), (wz_s, DI)):
            for kc in range(KC):
                for hh in range(2):
                    em.wait("sync", DVE, cvt)
                    lw = dma(w_in_d[kc * 128:(kc + 1) * 128,
                                    c0 + hh * DM:c0 + (hh + 1) * DM], big1[:])
                    em.wait("vector", LD, lw)
                    cvt = em.op("vector", DVE, 1, lambda E, h2=hh: E.tensor_copy(
                        out=wstream[:, h2 * DM:(h2 + 1) * DM], in_=big1[:]))
                em.wait("sync", DVE, cvt)
                dv = dst_s.rearrange("p (db k j) -> p db k j", db=NDB, k=KC)[:, :, kc, :]
                sv = wstream[:].rearrange("p (db j) -> p db j", db=NDB)
                wb = em.op("sync", LD, 16, lambda E, d=dv, s=sv: E.dma_start(out=d, in_=s))
                em.wait("vector", LD, wb)
        for kc in range(NDB):
            em.wait("sync", DVE, cvt)
            lw = dma(wout_d[kc * 128:(kc + 1) * 128, :], big1[:])
            em.wait("vector", LD, lw)
            cvt = em.op("vector", DVE, 1, lambda E: E.tensor_copy(
                out=wstream[:, 0:DM], in_=big1[:]))
            em.wait("sync", DVE, cvt)
            wb = em.op("sync", LD, 16, lambda E, k=kc: E.dma_start(
                out=wout_s[:, k * DM:(k + 1) * DM], in_=wstream[:, 0:DM]))
            em.wait("vector", LD, wb)
        zw = em.op("vector", DVE, 1, lambda E: E.memset(wxp_sb[:], 0.0))
        for kc in range(NDB):
            em.wait("sync", DVE, max(cvt, zw))
            lw = dma(wxp_d[kc * 128:(kc + 1) * 128, :], big1[:, 0:192])
            em.wait("vector", LD, lw)
            cvt = em.op("vector", DVE, 1, lambda E, k=kc: E.tensor_copy(
                out=wxp_sb[:, k * 256:k * 256 + 192], in_=big1[:, 0:192]))
        for hh in range(2):
            em.wait("sync", DVE, cvt)
            lw = em.op("sync", LD, 16, lambda E, h2=hh: E.dma_start(
                out=big1[0:64, :], in_=wdt_d[:, h2 * DM:(h2 + 1) * DM]))
            em.wait("vector", LD, lw)
            cvt = em.op("vector", DVE, 1, lambda E, h2=hh: E.tensor_copy(
                out=wdt_sb[0:64, h2 * DM:(h2 + 1) * DM], in_=big1[0:64, :]))
            em.wait("sync", DVE, cvt)
        for src_d, dst_s in ((wgT_d, wg_s), (wuT_d, wu_s)):
            for kc in range(KC):
                for hh in range(3):
                    em.wait("sync", DVE, cvt)
                    lw = dma(src_d[kc * 128:(kc + 1) * 128, hh * 1024:(hh + 1) * 1024], wi32)
                    em.wait("vector", LD, lw)
                    cvt = em.op("vector", DVE, 1, lambda E: E.tensor_copy(
                        out=wstream[:, 0:DM], in_=wi32))
                    em.wait("sync", DVE, cvt)
                    dv = dst_s.rearrange("p (hc k j) -> p hc k j", hc=NHC, k=KC)[
                        :, hh * 8:(hh + 1) * 8, kc, :]
                    sv = wstream[:, 0:DM].rearrange("p (hc j) -> p hc j", hc=8)
                    wb = em.op("sync", LD, 16, lambda E, d=dv, s=sv: E.dma_start(out=d, in_=s))
                    em.wait("vector", LD, wb)
        for hc in range(NHC):
            em.wait("sync", DVE, cvt)
            lw = dma(wdT_d[hc * 128:(hc + 1) * 128, :], wi32)
            em.wait("vector", LD, lw)
            cvt = em.op("vector", DVE, 1, lambda E: E.tensor_copy(
                out=wstream[:, 0:DM], in_=wi32))
            em.wait("sync", DVE, cvt)
            wb = em.op("sync", LD, 16, lambda E, h=hc: E.dma_start(
                out=wd_s[:, h * DM:(h + 1) * DM], in_=wstream[:, 0:DM]))
            em.wait("vector", LD, wb)

        em.barrier()

        last = {"dec_rd": [0, 0], "dbu_rd": [0, 0],
                "h_rd_pe": [0] * 6, "h_rd_act": [0] * 6}
        for i in range(6):
            last[f"h_wr_{i}"] = 0

        def emit_A(c):
            for tt in range(TT):
                em.wait("sync", DVE, last.get("xt_rd_dve", 0))
                ldx = dma(x_d[c * T + tt * 128: c * T + (tt + 1) * 128, :], xt_sb[:])
                em.wait("vector", LD, ldx)
                em.op("vector", DVE, 1, lambda E: E.bn_stats(
                    out=bn_sb[:, 0:6], in_=xt_sb[:, 0:512]))
                em.op("vector", DVE, 1, lambda E: E.bn_stats(
                    out=bn_sb[:, 6:12], in_=xt_sb[:, 512:1024]))
                em.op("vector", DVE, 1, lambda E: E.engine_nop())
                bs = em.op("vector", DVE, 1, lambda E: E.bn_aggr(
                    out=bn_sb[:, 12:14], in_=bn_sb[:, 0:12]))
                o = tt * 8
                bs = em.op("vector", DVE, 1, lambda E, o=o: E.tensor_copy(
                    out=sc_d[:, o:o + 2], in_=bn_sb[:, 12:14]))
                em.wait("scalar", DVE, bs)
                em.op("scalar", ACT, 1, lambda E, o=o: E.activation(
                    out=sc_a[:, o + 3:o + 4], in_=sc_d[:, o + 1:o + 2], func=AF.Ln,
                    scale=ko_sb[:, 0:1], bias=ko_sb[:, 1:2]))
                em.op("scalar", ACT, 1, lambda E, o=o: E.activation(
                    out=sc_a[:, o + 4:o + 5], in_=sc_d[:, o + 1:o + 2], func=AF.Ln,
                    bias=ko_sb[:, 2:3]))
                em.op("scalar", ACT, 1, lambda E, o=o: E.activation(
                    out=sc_a[:, o:o + 1], in_=sc_a[:, o + 3:o + 4], func=AF.Exp,
                    scale=ko_sb[:, 5:6]))
                ac = em.op("scalar", ACT, 1, lambda E, o=o: E.activation(
                    out=sc_a[:, o + 1:o + 2], in_=sc_a[:, o + 4:o + 5], func=AF.Exp,
                    scale=ko_sb[:, 5:6]))
                em.wait("vector", ACT, ac)
                em.op("vector", DVE, 1, lambda E, o=o: E.scalar_tensor_tensor(
                    out=sc_d[:, o + 2:o + 3], in0=sc_d[:, o:o + 1], scalar=-1.0,
                    in1=sc_a[:, o:o + 1], op0=ALU.mult, op1=ALU.mult))
                em.op("vector", DVE, 1, lambda E, o=o: E.scalar_tensor_tensor(
                    out=sc_d[:, o + 3:o + 4], in0=sc_d[:, o:o + 1], scalar=-1.0,
                    in1=sc_a[:, o + 1:o + 2], op0=ALU.mult, op1=ALU.mult))
                xrd = em.op("vector", DVE, 1, lambda E, o=o: E.scalar_tensor_tensor(
                    out=xt_sb[:], in0=xt_sb[:], scalar=sc_a[:, o:o + 1],
                    in1=lnwb_sb[:, 0:DM], op0=ALU.mult, op1=ALU.mult))
                em.op("vector", DVE, 1, lambda E, o=o: E.scalar_tensor_tensor(
                    out=xt_sb[:], in0=lnwb_sb[:, 0:DM], scalar=sc_d[:, o + 2:o + 3],
                    in1=xt_sb[:], op0=ALU.mult, op1=ALU.add))
                dv = em.op("vector", DVE, 1, lambda E: E.tensor_add(
                    out=xn2tm[:], in0=xt_sb[:], in1=lnwb_sb[:, DM:2 * DM]))
                last["xt_rd_dve"] = dv
                em.wait("tensor", DVE, dv)
                em.wait("tensor", ACT, last.get("psT_rd", 0))
                pe_t = 0
                for kc in range(KC):
                    pe_t = em.op("tensor", PE, 1, lambda E, k=kc: E.matmul(
                        out=psT[:, k * 128:(k + 1) * 128],
                        lhsT=xn2tm[:, k * 128:(k + 1) * 128], rhs=idb_sb[:],
                        is_transpose=True, start=(k == 0), stop=(k == KC - 1)))
                em.wait("scalar", PE, pe_t)
                ev = 0
                for kc in range(KC):
                    ev = em.op("scalar", ACT, 1, lambda E, k=kc, t=tt: E.activation(
                        out=xn2fm[:, k * T + t * 128:k * T + t * 128 + 128],
                        in_=psT[:, k * 128:(k + 1) * 128], func=AF.Copy))
                last["xn2fm_wr"] = ev
                last["psT_rd"] = ev
                yield

            last["pending_xdbl"] = None

            def flush_xdbl():
                d, p, ub = last["pending_xdbl"]
                em.wait("tensor", DVE, ub)
                if d == 0:
                    em.wait("tensor", ACT, last.get("xdbl_ps_rd", 0))
                    em.wait("tensor", ACT, last.get("z_ps_rd", 0))
                em.op("tensor", PE, 1, lambda E, d=d, p=p: E.matmul(
                    out=psC[:, 0:T], lhsT=wxp_sb[:, d * 256:d * 256 + 128],
                    rhs=ubf_sb[:, p * T:(p + 1) * T], start=(d == 0), stop=False))
                pe_d = em.op("tensor", PE, 1, lambda E, d=d, p=p: E.matmul(
                    out=psC[:, T:2 * T], lhsT=wxp_sb[:, d * 256 + 128:d * 256 + 256],
                    rhs=ubf_sb[:, p * T:(p + 1) * T], start=False, stop=(d == NDB - 1)))
                last[f"ubf_rd_{p}"] = pe_d
                last["pending_xdbl"] = None

            for db in range(NDB):
                par = db % 2
                em.wait("sync", PE, last.get(f"wxi_rd_{par}", 0))
                lw = dma(wxi_s[:, db * 1024:(db + 1) * 1024],
                         wstream[:, par * 1024:(par + 1) * 1024])
                em.wait("tensor", LD, lw)
                em.wait("tensor", ACT, last["xn2fm_wr"])
                em.wait("tensor", ACT, last.get("xi_ps_rd", 0))
                em.wait("tensor", DVE, last.get("g_dve_1", 0))
                em.wait("tensor", DVE, last.get("yf_ps_rd", 0))
                if last.get("pending_xdbl") is not None:
                    flush_xdbl()
                pe_x = 0
                for kc in range(KC):
                    pe_x = em.op("tensor", PE, 1, lambda E, k=kc, p=par: E.matmul(
                        out=psA[:, 0:T],
                        lhsT=wstream[:, p * 1024 + k * 128:p * 1024 + (k + 1) * 128],
                        rhs=xn2fm[:, k * T:(k + 1) * T],
                        start=(k == 0), stop=(k == KC - 1)))
                last[f"wxi_rd_{par}"] = pe_x
                em.wait("scalar", PE, pe_x)
                em.wait("scalar", DVE, last.get("xib_rd", 0))
                em.op("scalar", ACT, 1, lambda E, d=db: E.activation(
                    out=xib[:, 0:3], in_=halo_sb[:, d * 3:(d + 1) * 3], func=AF.Copy))
                ev = em.op("scalar", ACT, 1, lambda E: E.activation(
                    out=xib[:, 3:3 + T], in_=psA[:, 0:T], func=AF.Copy))
                last["xi_ps_rd"] = ev
                em.wait("vector", ACT, ev)
                em.wait("vector", ACT, last.get("cvb_rd", 0))
                cv = em.op("vector", DVE, 1, lambda E, d=db: E.tensor_scalar(
                    out=cvb[:], in0=xib[:, 0:T], scalar1=convw_sb[:, d * 4:d * 4 + 1],
                    scalar2=None, op0=ALU.mult))
                for k in range(1, 4):
                    cv = em.op("vector", DVE, 1, lambda E, d=db, kk=k: E.scalar_tensor_tensor(
                        out=cvb[:], in0=xib[:, kk:kk + T],
                        scalar=convw_sb[:, d * 4 + kk:d * 4 + kk + 1],
                        in1=cvb[:], op0=ALU.mult, op1=ALU.add))
                last["xib_rd"] = cv
                em.wait("scalar", DVE, cv)
                uu = em.op("scalar", ACT, 1, lambda E, d=db: E.activation(
                    out=u_sb[:, d * T:(d + 1) * T], in_=cvb[:], func=AF.Silu,
                    bias=convb_sb[:, d:d + 1]))
                last["cvb_rd"] = uu
                em.op("scalar", ACT, 1, lambda E, d=db: E.activation(
                    out=halo_sb[:, d * 3:(d + 1) * 3], in_=xib[:, T:T + 3], func=AF.Copy))
                em.wait("vector", ACT, uu)
                em.wait("vector", PE, last.get(f"ubf_rd_{par}", 0))
                ub = em.op("vector", DVE, 1, lambda E, d=db, p=par: E.tensor_copy(
                    out=ubf_sb[:, p * T:(p + 1) * T], in_=u_sb[:, d * T:(d + 1) * T]))
                last["pending_xdbl"] = (db, par, ub)
                yield

            flush_xdbl()
            last["u_wr"] = uu
            xdbl_pe = em.cnt[PE]

            em.wait("scalar", PE, xdbl_pe)
            em.wait("scalar", LD, 1)
            ev_dtr = em.op("scalar", ACT, 1, lambda E: E.activation(
                out=dtrb_sb[0:64, :], in_=psC[0:64, 0:T], func=AF.Copy))
            last["ev_dtr"] = ev_dtr
            ev_B = em.op("scalar", ACT, 1, lambda E: E.activation(
                out=Bsb[64:128, :], in_=psC[64:128, 0:T], func=AF.Copy))
            ev_C = em.op("scalar", ACT, 1, lambda E: E.activation(
                out=Csb[0:64, :], in_=psC[0:64, T:2 * T], func=AF.Copy))
            last["xdbl_ps_rd"] = ev_C
            em.wait("sync", ACT, ev_B)
            bf1 = em.op("sync", LD, 16, lambda E: E.dma_start(
                out=bflat_d, in_=Bsb[64:128, :]))
            last["bf1"] = bf1
            em.wait("tensor", ACT, ev_C)
            em.wait("tensor", ACT, last["xi_ps_rd"])
            em.wait("tensor", DVE, last.get("g_dve_1", 0))
            em.wait("tensor", DVE, last.get("yf_ps_rd", 0))
            pe_c2 = 0
            for tt in range(TT):
                pe_c2 = em.op("tensor", PE, 1, lambda E, t=tt: E.matmul(
                    out=psA[:, 256:256 + N],
                    lhsT=Csb[0:64, t * 128:(t + 1) * 128],
                    rhs=idh_sb[0:64, t * 128:t * 128 + N],
                    start=(t == 0), stop=(t == TT - 1)))
            em.wait("scalar", PE, pe_c2)
            ev_ctm = em.op("scalar", ACT, 1, lambda E: E.activation(
                out=ctm_sb[:], in_=psA[:, 256:256 + N], func=AF.Copy))
            last["ev_ctm"] = ev_ctm

            em.wait("sync", ACT, uu)
            em.wait("sync", DVE, last.get("yf_ps_rd", 0))
            bo2 = em.op("sync", LD, 16, lambda E: E.dma_start(out=u_d, in_=u_sb[:]))
            em.wait("sync", LD, bo2)
            em.op("sync", LD, 16, lambda E: E.dma_start(
                out=u_my[:].rearrange("p (db t) -> p db t", db=NDB),
                in_=u_d.rearrange("p (db t) -> p db t", db=NDB)[:, :, ds(SN["t"], 128)]))
            em.wait("sync", ACT, last["xn2fm_wr"])
            bo1 = em.op("sync", LD, 16, lambda E: E.dma_start(out=xn2_d, in_=xn2fm[:]))
            em.wait("sync", LD, bo1)
            em.wait("sync", PE, max(last.get("wz_rd_0", 0), last.get("wz_rd_1", 0)))
            em.op("sync", LD, 16, lambda E: E.dma_start(
                out=xn2my[:].rearrange("p (k t) -> p k t", k=KC),
                in_=xn2_d.rearrange("p (k t) -> p k t", k=KC)[:, :, ds(SN["t"], 128)]))

            if dbg and c == 0:
                em.wait("sync", ACT, last["xn2fm_wr"])
                em.op("sync", OUT, 16, lambda E: E.dma_start(
                    out=dbg_outs["d_xn2"], in_=xn2fm[:]))
                em.wait("sync", ACT, last["u_wr"])
                em.op("sync", OUT, 16, lambda E: E.dma_start(out=dbg_outs["d_u"], in_=u_sb[:]))
                em.wait("sync", ACT, ev_B)
                em.op("sync", OUT, 16, lambda E: E.dma_start(out=dbg_outs["d_B"], in_=Bsb[64:128, :]))
                em.wait("sync", ACT, ev_C)
                em.op("sync", OUT, 16, lambda E: E.dma_start(out=dbg_outs["d_C"], in_=Csb[0:64, :]))


        def emit_B(c):
            for db in range(NDB):
                em.wait("tensor", ACT, last["ev_dtr"])
                em.wait("tensor", ACT, last.get("dlt_ps_rd", 0))
                em.wait("tensor", DVE, last.get("g_dve_0", 0))
                pe_dl = em.op("tensor", PE, 1, lambda E, d=db: E.matmul(
                    out=psB[:, 0:T], lhsT=wdt_sb[0:64, d * 128:(d + 1) * 128],
                    rhs=dtrb_sb[0:64, :], start=True, stop=True))
                em.wait("scalar", PE, pe_dl)
                em.wait("scalar", DVE, last.get("ssp_rd", 0))
                em.op("scalar", ACT, 1, lambda E, d=db: E.activation(
                    out=ssp_sb[:], in_=psB[:, 0:T], func=AF.Abs, bias=bdt_sb[:, d:d + 1]))
                em.op("scalar", ACT, 1, lambda E: E.activation(
                    out=ssp_sb[:], in_=ssp_sb[:], func=AF.Exp, scale=ko_sb[:, 4:5]))
                em.op("scalar", ACT, 1, lambda E: E.activation(
                    out=ssp_sb[:], in_=ssp_sb[:], func=AF.Ln, bias=ko_sb[:, 3:4]))
                em.wait("scalar", DVE, last.get("sz_rd", 0))
                em.wait("scalar", OUT, 1)
                a4 = em.op("scalar", ACT, 1, lambda E, d=db: E.activation(
                    out=y1f_sb[:], in_=psB[:, 0:T], func=AF.Relu,
                    bias=bdt_sb[:, d:d + 1]))
                last["dlt_ps_rd"] = a4
                em.wait("vector", ACT, a4)
                em.wait("vector", ACT, last.get(f"delta_rd_{db}", 0))
                ddl = em.op("vector", DVE, 1, lambda E, d=db: E.tensor_add(
                    out=delta_sb[:, d * T:(d + 1) * T], in0=y1f_sb[:], in1=ssp_sb[:]))
                last[f"delta_wr_{db}"] = ddl
                last["ssp_rd"] = ddl
                em.wait("vector", ACT, last["u_wr"])
                em.op("vector", DVE, 1, lambda E, d=db: E.tensor_mul(
                    out=du_sb[:, d * T:(d + 1) * T], in0=delta_sb[:, d * T:(d + 1) * T],
                    in1=u_sb[:, d * T:(d + 1) * T]))
            if dbg and c == 0:
                em.wait("sync", DVE, em.cnt[DVE])
                em.op("sync", OUT, 16, lambda E: E.dma_start(
                    out=dbg_outs["d_delta"], in_=delta_sb[:]))
            em.wait("vector", PE, last.get("yacc_rd_pe", 0))
            em.op("vector", DVE, 1, lambda E: E.memset(yacc_sb[:], 0.0))
            em.wait("tensor", ACT, last["ev_ctm"])
            em.wait("tensor", ACT, last["dlt_ps_rd"])
            em.wait("tensor", DVE, last.get("g_dve_0", 0))
            em.wait("tensor", DVE, last.get("ffn_ps_rd", 0))
            for grp in range(NGRP):
                bpar = grp % 2
                em.wait("sync", LD, last["bf1"])
                em.wait("sync", PE, last.get("row_rd_pe", 0))
                bfg = em.op("sync", LD, 16, lambda E, g=grp: E.dma_start(
                    out=row_sb[0:1, :],
                    in_=bflat_d[g * G:(g + 1) * G, :].rearrange("n t -> (n t)")[None, :]))
                em.wait("tensor", LD, bfg)
                em.wait("tensor", ACT, last.get("bbc_ps_rd", 0))
                pe_b = 0
                for half in range(2):
                    pe_b = em.op("tensor", PE, 1, lambda E, hf=half: E.matmul(
                        out=psE[:, hf * 512:(hf + 1) * 512], lhsT=ones1_sb[:],
                        rhs=row_sb[:, hf * 512:(hf + 1) * 512],
                        start=True, stop=True))
                last["row_rd_pe"] = pe_b
                em.wait("scalar", PE, pe_b)
                em.wait("scalar", DVE, last.get(f"bbc_rd_pool_{bpar}", 0))
                ev_b = 0
                for half in range(2):
                    ev_b = em.op("scalar", ACT, 1, lambda E, hf=half, bp=bpar: E.activation(
                        out=bbc_sb[:, bp * 1024 + hf * 512:bp * 1024 + (hf + 1) * 512],
                        in_=psE[:, hf * 512:(hf + 1) * 512], func=AF.Copy))
                last["bbc_ps_rd"] = ev_b

                for db in range(NDB):
                    par = db % 2
                    hq = (grp * NDB + db) % NH
                    dec, dbuf = dec_sb[par], dbu_sb[par]
                    em.wait("scalar", DVE, max(last["dec_rd"][par],
                                               last.get(f"delta_wr_{db}", 0)))
                    ac_d = 0
                    for g in range(G):
                        nn = grp * G + g
                        ac_d = em.op("scalar", ACT, 1, lambda E, d=db, n=nn, g=g, dd=dec:
                                     E.activation(
                            out=dd[:, g * (T + 1) + 1:(g + 1) * (T + 1)],
                            in_=delta_sb[:, d * T:(d + 1) * T], func=AF.Exp,
                            scale=a_sb[:, d * N + n:d * N + n + 1]))
                    last[f"delta_rd_{db}"] = ac_d
                    if last.get("pending_gather") is not None:
                        pd, pg0, phq, psc = last["pending_gather"]
                        em.wait("scalar", DVE, psc)
                        gat = em.op("scalar", ACT, 1, lambda E, d=pd, g0=pg0,
                                    hh=h_sb[phq]: E.activation(
                            out=hstate_sb[:, d * N + g0 * G:d * N + (g0 + 1) * G],
                            in_=hh[:].rearrange("p (g t) -> p g t", g=G)[:, :, T],
                            func=AF.Copy))
                        last["h_rd_act"][phq] = gat
                        last["pending_gather"] = None
                    dbu_pay = dbuf[:].rearrange("p (g t) -> p g t", g=G)[:, :, 1:]
                    em.wait("vector", ACT, ev_b)
                    em.wait("vector", ACT, ac_d)
                    em.wait("vector", ACT, last["h_rd_act"][hq])
                    po = em.op("vector", DVE, 1, lambda E, d=db, dp=dbu_pay, bp=bpar:
                               E.tensor_tensor(
                        out=dp,
                        in0=du_sb[:, d * T:(d + 1) * T].unsqueeze(1).broadcast_to(
                            [128, G, T]),
                        in1=bbc_sb[:, bp * 1024:(bp + 1) * 1024].rearrange(
                            "p (g t) -> p g t", g=G),
                        op=ALU.mult))
                    last[f"bbc_rd_pool_{bpar}"] = po
                    bnd = em.op("vector", DVE, 1, lambda E, d=db, g0=grp, dd=dbuf:
                                E.tensor_copy(
                        out=dd[:].rearrange("p (g t) -> p g t", g=G)[:, :, 0],
                        in_=hstate_sb[:, d * N + g0 * G:d * N + (g0 + 1) * G]))
                    em.wait("vector", PE, last["h_rd_pe"][hq])
                    em.wait("vector", ACT, last["h_rd_act"][hq])
                    sc_v = em.op("vector", DVE, 1, lambda E, dd=dec, bb=dbuf, hh=h_sb[hq]:
                                 E.tensor_tensor_scan(
                        out=hh[:], data0=dd[:], data1=bb[:], initial=0.0,
                        op0=ALU.mult, op1=ALU.add))
                    last["dec_rd"][par] = sc_v
                    last["dbu_rd"][par] = sc_v
                    last[f"h_wr_{hq}"] = sc_v
                    last["pending_gather"] = (db, grp, hq, sc_v)

                    if db % 4 == 3:
                        q0 = db - 3
                        hqs = [(grp * NDB + q0 + q) % NH for q in range(4)]
                        for g in range(G):
                            nn = grp * G + g
                            use_a = g % 2 == 0
                            rt = psA if use_a else psB
                            key = "rt_rd_A" if use_a else "rt_rd_B"
                            em.wait("tensor", DVE, last.get(key, 0))
                            pe_r = 0
                            for q in range(4):
                                em.wait("tensor", DVE, last[f"h_wr_{hqs[q]}"])
                                for tt in range(TT):
                                    pe_r = em.op("tensor", PE, 1,
                                                 lambda E, qq=hqs[q], qi=q, gg=g, t=tt, r=rt: E.matmul(
                                        out=r[:, qi * 128:(qi + 1) * 128],
                                        lhsT=h_sb[qq][:, gg * (T + 1) + 1 + t * 128:
                                                      gg * (T + 1) + 1 + (t + 1) * 128],
                                        rhs=idh_sb[:, t * 128:(t + 1) * 128],
                                        start=(qi == 0 and t == 0),
                                        stop=(qi == 3 and t == TT - 1)))
                                last["h_rd_pe"][hqs[q]] = pe_r
                            em.wait("vector", PE, pe_r)
                            st_v = em.op("vector", DVE, 1,
                                         lambda E, n=nn, r=rt, qb=q0:
                                         E.scalar_tensor_tensor(
                                out=yacc_sb[:, qb * 128:qb * 128 + 512],
                                in0=r[:, 0:512],
                                scalar=ctm_sb[:, n:n + 1],
                                in1=yacc_sb[:, qb * 128:qb * 128 + 512],
                                op0=ALU.mult, op1=ALU.add))
                            last[key] = st_v
            if last.get("pending_gather") is not None:
                pd, pg0, phq, psc = last["pending_gather"]
                em.wait("scalar", DVE, psc)
                gat = em.op("scalar", ACT, 1, lambda E, d=pd, g0=pg0,
                            hh=h_sb[phq]: E.activation(
                    out=hstate_sb[:, d * N + g0 * G:d * N + (g0 + 1) * G],
                    in_=hh[:].rearrange("p (g t) -> p g t", g=G)[:, :, T],
                    func=AF.Copy))
                last["h_rd_act"][phq] = gat
                last["pending_gather"] = None
            yacc_dve = em.cnt[DVE]
            last["yacc_dve"] = yacc_dve

            if dbg and c == 0:
                em.wait("sync", DVE, yacc_dve)
                em.op("sync", OUT, 16, lambda E: E.dma_start(
                    out=dbg_outs["d_yacc"], in_=yacc_sb[:]))


        def emit_C1(c):
            last["pending_wout"] = None

            def flush_wout():
                d, p, y1b, lwo = last["pending_wout"]
                em.wait("tensor", DVE, y1b)
                em.wait("tensor", LD, lwo)
                if d == 0:
                    em.wait("tensor", DVE, last.get("ffn_ps_rd", 0))
                    em.wait("tensor", ACT, last.get("bbc_ps_rd", 0))
                pe_o = 0
                for half in range(2):
                    pe_o = em.op("tensor", PE, 1, lambda E, d=d, hf=half,
                                 p=p: E.matmul(
                        out=psE[:, hf * 512:(hf + 1) * 512],
                        lhsT=y1bf_sb[:, p * 128:(p + 1) * 128],
                        rhs=woutb[:, p * DM + hf * 512:p * DM + (hf + 1) * 512],
                        start=(d == 0), stop=(d == NDB - 1)))
                last[f"wout_rd_{p}"] = pe_o
                last[f"y1bf_rd_{p}"] = pe_o
                last["pending_wout"] = None

            for db in range(NDB):
                par = db % 2
                em.wait("sync", PE, last.get(f"wout_rd_{par}", 0))
                lwo = dma(wout_s[:, db * DM:(db + 1) * DM],
                          woutb[:, par * DM:(par + 1) * DM])
                em.wait("sync", PE, last.get(f"wz_rd_{par}", 0))
                lwz = dma(wz_s[:, db * 1024:(db + 1) * 1024],
                          wzbuf[:, par * 1024:(par + 1) * 1024])
                em.wait("tensor", DVE, last["yacc_dve"])
                em.wait("tensor", DVE, last.get("yf_ps_rd", 0))
                pe_y = em.op("tensor", PE, 1, lambda E, d=db: E.matmul(
                    out=psA[:, 0:128],
                    lhsT=yacc_sb[:, d * 128:(d + 1) * 128],
                    rhs=idf_sb[:], is_transpose=True, start=True, stop=True))
                last["yacc_rd_pe"] = pe_y
                em.wait("vector", PE, pe_y)
                em.wait("vector", OUT, last.get("y1f_dump", 0))
                em.wait("vector", LD, 1)
                y1v = em.op("vector", DVE, 1, lambda E, d=db: E.scalar_tensor_tensor(
                    out=y1f_sb[:, 0:128], in0=u_my[:, d * 128:(d + 1) * 128],
                    scalar=dvec_sb[:, d:d + 1], in1=psA[:, 0:128],
                    op0=ALU.mult, op1=ALU.add))
                last["yf_ps_rd"] = y1v
                em.wait("tensor", LD, lwz)
                if db == 0:
                    em.wait("tensor", ACT, last["xdbl_ps_rd"])
                em.wait("tensor", ACT, last.get("z_ps_rd", 0))
                pe_z = 0
                for kc in range(KC):
                    pe_z = em.op("tensor", PE, 1, lambda E, k=kc, p=par: E.matmul(
                        out=psC[:, 0:128],
                        lhsT=wzbuf[:, p * 1024 + k * 128:p * 1024 + (k + 1) * 128],
                        rhs=xn2my[:, k * 128:(k + 1) * 128],
                        start=(k == 0), stop=(k == KC - 1)))
                last[f"wz_rd_{par}"] = pe_z
                em.wait("scalar", PE, pe_z)
                em.wait("scalar", DVE, last.get("sz_rd", 0))
                szv = em.op("scalar", ACT, 1, lambda E: E.activation(
                    out=sz_sb[:], in_=psC[:, 0:128], func=AF.Silu))
                last["z_ps_rd"] = szv
                em.wait("vector", ACT, szv)
                em.wait("vector", PE, last.get(f"y1bf_rd_{par}", 0))
                y1b = em.op("vector", DVE, 1, lambda E, p=par: E.tensor_mul(
                    out=y1bf_sb[:, p * 128:(p + 1) * 128], in0=y1f_sb[:, 0:128],
                    in1=sz_sb[:]))
                last["sz_rd"] = y1b
                if last.get("pending_wout") is not None:
                    flush_wout()
                last["pending_wout"] = (db, par, y1b, lwo)
                if dbg and c == 0:
                    em.wait("sync", DVE, y1b)
                    last["y1f_dump"] = em.op("sync", OUT, 16, lambda E, d=db: E.dma_start(
                        out=dbg_outs["d_y1"][:, d * 128:(d + 1) * 128],
                        in_=y1f_sb[:, 0:128]))
            flush_wout()
            last["om_pe"] = em.cnt[PE]

        def emit_C2(c):
            sm = em.op("vector", DVE, 1, lambda E: E.tensor_scalar(
                out=scm_sb[:, 4:5], in0=sc_a[:, 9:10], scalar1=msel_sb[:, 1:2],
                scalar2=None, op0=ALU.mult))
            sm = em.op("vector", DVE, 1, lambda E: E.scalar_tensor_tensor(
                out=scm_sb[:, 0:1], in0=sc_a[:, 1:2], scalar=msel_sb[:, 0:1],
                in1=scm_sb[:, 4:5], op0=ALU.mult, op1=ALU.add))
            sm = em.op("vector", DVE, 1, lambda E: E.tensor_scalar(
                out=scm_sb[:, 5:6], in0=sc_d[:, 11:12], scalar1=msel_sb[:, 1:2],
                scalar2=None, op0=ALU.mult))
            sm = em.op("vector", DVE, 1, lambda E: E.scalar_tensor_tensor(
                out=scm_sb[:, 1:2], in0=sc_d[:, 3:4], scalar=msel_sb[:, 0:1],
                in1=scm_sb[:, 5:6], op0=ALU.mult, op1=ALU.add))

            em.wait("sync", DVE, last.get("xt_rd_dve", 0))
            em.wait("sync", ACT, last.get("xt_rd_act", 0))
            ldx = dma(xs_d[c * T:c * T + 128, :], xt_sb[:])
            em.wait("vector", LD, ldx)
            em.wait("vector", PE, last["om_pe"])
            xv = em.op("vector", DVE, 1, lambda E: E.tensor_add(
                out=big1[:], in0=xt_sb[:], in1=psE[:]))
            last["om_ps_rd"] = xv
            xv = em.op("vector", DVE, 1, lambda E: E.tensor_scalar(
                out=xt_sb[:], in0=xt_sb[:], scalar1=scm_sb[:, 0:1],
                scalar2=scm_sb[:, 1:2], op0=ALU.mult, op1=ALU.add))
            xv = em.op("vector", DVE, 1, lambda E: E.tensor_add(
                out=x2_sb[:], in0=big1[:], in1=xt_sb[:]))
            last["xt_rd_dve"] = xv
            em.op("vector", DVE, 1, lambda E: E.bn_stats(
                out=bn_sb[:, 0:6], in_=x2_sb[:, 0:512]))
            em.op("vector", DVE, 1, lambda E: E.bn_stats(
                out=bn_sb[:, 6:12], in_=x2_sb[:, 512:DM]))
            em.op("vector", DVE, 1, lambda E: E.engine_nop())
            bs = em.op("vector", DVE, 1, lambda E: E.bn_aggr(
                out=bn_sb[:, 12:14], in_=bn_sb[:, 0:12]))
            em.wait("scalar", DVE, bs)
            em.op("scalar", ACT, 1, lambda E: E.activation(
                out=sc_a[:, 13:14], in_=bn_sb[:, 13:14], func=AF.Ln,
                bias=ko_sb[:, 2:3]))
            ac = em.op("scalar", ACT, 1, lambda E: E.activation(
                out=sc_a[:, 14:15], in_=sc_a[:, 13:14], func=AF.Exp,
                scale=ko_sb[:, 5:6]))
            em.wait("vector", ACT, ac)
            q3v = em.op("vector", DVE, 1, lambda E: E.scalar_tensor_tensor(
                out=scm_sb[:, 3:4], in0=bn_sb[:, 12:13], scalar=-1.0,
                in1=sc_a[:, 14:15], op0=ALU.mult, op1=ALU.mult))
            xn3v = em.op("vector", DVE, 1, lambda E: E.tensor_scalar(
                out=xn3t_sb[:], in0=x2_sb[:],
                scalar1=sc_a[:, 14:15], scalar2=scm_sb[:, 3:4],
                op0=ALU.mult, op1=ALU.add))
            last["x2_rd_dve2"] = xn3v
            em.wait("tensor", DVE, xn3v)
            em.wait("tensor", ACT, last.get("psT_rd", 0))
            pe_t = 0
            for kc in range(KC):
                pe_t = em.op("tensor", PE, 1, lambda E, k=kc: E.matmul(
                    out=psT[:, k * 128:(k + 1) * 128],
                    lhsT=xn3t_sb[:, k * 128:(k + 1) * 128], rhs=idb_sb[:],
                    is_transpose=True, start=(k == 0), stop=(k == KC - 1)))
            em.wait("scalar", PE, pe_t)
            em.wait("scalar", PE, last.get("xn3f_rd", 0))
            ev = 0
            for kc in range(KC):
                ev = em.op("scalar", ACT, 1, lambda E, k=kc: E.activation(
                    out=xn3f_sb[:, k * 128:(k + 1) * 128],
                    in_=psT[:, k * 128:(k + 1) * 128], func=AF.Copy))
            last["psT_rd"] = ev
            last["xn3f_wr"] = ev


        def emit_C3(c, ga=iter(())):
            last["pending_down"] = None

            def flush_down():
                h, p, hhv = last["pending_down"]
                em.wait("tensor", DVE, hhv)
                if h == 0:
                    em.wait("tensor", DVE, last.get("x3_ffn_rd", 0))
                pe_dn = 0
                for half in range(2):
                    pe_dn = em.op("tensor", PE, 1, lambda E, h=h, hf=half,
                                  p=p: E.matmul(
                        out=psE[:, hf * 512:(hf + 1) * 512],
                        lhsT=hh_sb[:, p * 128:(p + 1) * 128],
                        rhs=wdb[:, p * DM + hf * 512:p * DM + (hf + 1) * 512],
                        start=(h == 0), stop=(h == NHC - 1)))
                last[f"hh_rd_{p}"] = pe_dn
                last[f"wffn_rd_{p}"] = pe_dn
                last["pending_down"] = None

            for hc in range(NHC):
                par = hc % 2
                next(ga, None)
                em.wait("sync", PE, last.get(f"wffn_rd_{par}", 0))
                dma(wg_s[:, hc * 1024:(hc + 1) * 1024], wgb[:, par * 1024:(par + 1) * 1024])
                dma(wu_s[:, hc * 1024:(hc + 1) * 1024], wub[:, par * 1024:(par + 1) * 1024])
                ldn = dma(wd_s[:, hc * DM:(hc + 1) * DM], wdb[:, par * DM:(par + 1) * DM])
                em.wait("tensor", LD, ldn)
                em.wait("tensor", ACT, last["xn3f_wr"])
                em.wait("tensor", DVE, last.get(f"g_dve_{par}", 0))
                em.wait("tensor", ACT, last.get(f"g_act_{par}", 0))
                gu_ps = psB if par == 0 else psF
                if hc == 0:
                    em.wait("tensor", ACT, last["dlt_ps_rd"])
                pe_g = 0
                for kc in range(KC):
                    pe_g = em.op("tensor", PE, 1, lambda E, k=kc, p=par, gp=gu_ps: E.matmul(
                        out=gp[:, 0:128],
                        lhsT=wgb[:, p * 1024 + k * 128:p * 1024 + (k + 1) * 128],
                        rhs=xn3f_sb[:, k * 128:(k + 1) * 128],
                        start=(k == 0), stop=(k == KC - 1)))
                em.wait("scalar", PE, pe_g)
                em.wait("scalar", DVE, last.get(f"sfn_rd_{par}", 0))
                sg = em.op("scalar", ACT, 1, lambda E, p=par, gp=gu_ps: E.activation(
                    out=sfn_sb[:, p * 128:(p + 1) * 128],
                    in_=gp[:, 0:128], func=AF.Silu,
                    scale=ko_sb[:, 6:7]))
                last[f"g_act_{par}"] = sg
                em.wait("tensor", ACT, sg)
                pe_u = 0
                for kc in range(KC):
                    pe_u = em.op("tensor", PE, 1, lambda E, k=kc, p=par, gp=gu_ps: E.matmul(
                        out=gp[:, 0:128],
                        lhsT=wub[:, p * 1024 + k * 128:p * 1024 + (k + 1) * 128],
                        rhs=xn3f_sb[:, k * 128:(k + 1) * 128],
                        start=(k == 0), stop=(k == KC - 1)))
                if last.get("pending_down") is not None:
                    flush_down()
                em.wait("vector", PE, pe_u)
                em.wait("vector", ACT, sg)
                em.wait("vector", PE, last.get(f"hh_rd_{par}", 0))
                hhv = em.op("vector", DVE, 1, lambda E, p=par, gp=gu_ps: E.scalar_tensor_tensor(
                    out=hh_sb[:, p * 128:(p + 1) * 128],
                    in0=gp[:, 0:128],
                    scalar=FS, in1=sfn_sb[:, p * 128:(p + 1) * 128],
                    op0=ALU.mult, op1=ALU.mult))
                last[f"sfn_rd_{par}"] = hhv
                last[f"g_dve_{par}"] = hhv
                last["pending_down"] = (hc, par, hhv)
            flush_down()
            ffn_pe = em.cnt[PE]
            last["xn3f_rd"] = ffn_pe
            em.wait("vector", PE, ffn_pe)
            em.wait("vector", OUT, last.get("x3_rd", 0))
            x3v = em.op("vector", DVE, 1, lambda E: E.scalar_tensor_tensor(
                out=x3_sb[:], in0=psE[:], scalar=FS,
                in1=x2_sb[:], op0=ALU.mult, op1=ALU.add))
            last["x3_ffn_rd"] = x3v
            last["ffn_ps_rd"] = x3v
            em.wait("sync", DVE, x3v)
            ov = em.op("sync", OUT, 16, lambda E, cc=c: E.dma_start(
                out=out_d[cc * 128:(cc + 1) * 128, :], in_=x3_sb[:]))
            last["x3_rd"] = ov


        for _ in emit_A(0):
            pass
        for c in range(NCH):
            emit_B(c)
            emit_C1(c)
            emit_C2(c)
            ga = emit_A(c + 1) if c + 1 < NCH else iter(())
            for _ in range(2):
                next(ga, None)
            emit_C3(c, ga)
            for _ in ga:
                pass

        em.wait("sync", OUT, em.cnt[OUT])

        @block.sync
        def _(E):
            em.replay("sync", E, sems)

        @block.vector
        def _(E):
            em.replay("vector", E, sems)

        @block.scalar
        def _(E):
            em.replay("scalar", E, sems)

        @block.tensor
        def _(E):
            em.replay("tensor", E, sems)

        @block.gpsimd
        def _(E):
            em.replay("gpsimd", E, sems)

    return nc


def _bf16():
    import ml_dtypes
    return ml_dtypes.bfloat16


def host_inputs(inputs, b, h=0):
    ident = np.eye(128, dtype=np.float32)
    zero = np.zeros((128, 128), dtype=np.float32)
    return {
        "idh": np.concatenate([ident if h == 0 else zero,
                               ident if h == 1 else zero], axis=1),
        "msel": np.tile(np.array([[1.0 - h, float(h)]], dtype=np.float32),
                        (128, 1)),
        "hoff": np.array([[h * 128]], dtype=np.int32),
        "xs": np.ascontiguousarray(np.concatenate(
            [inputs["L0_attn_x"][b][h * 128:],
             np.zeros((h * 128, DM), dtype=np.float32)])) if h else
        np.ascontiguousarray(inputs["L0_attn_x"][b]),
        "x": np.ascontiguousarray(inputs["L0_attn_x"][b]),
        "w_in": np.ascontiguousarray(inputs["L0_attn_W_in"]),
        "wxp": np.ascontiguousarray(inputs["L0_attn_W_xproj"]),
        "wdt": np.ascontiguousarray(inputs["L0_attn_W_dt"]),
        "wout": np.ascontiguousarray(inputs["L0_attn_W_out"]),
        "alog": np.ascontiguousarray(
            inputs["L0_attn_A_log"].reshape(NDB, 128, N).transpose(1, 0, 2).reshape(128, NDB * N)),
        "dvec": np.ascontiguousarray(inputs["L0_attn_D"].reshape(NDB, 128).T),
        "convw": np.ascontiguousarray(
            inputs["L0_attn_conv_w"].reshape(NDB, 128, 4).transpose(1, 0, 2).reshape(128, NDB * 4)),
        "convb": np.ascontiguousarray(inputs["L0_attn_conv_b"].reshape(NDB, 128).T),
        "bdt": np.ascontiguousarray(inputs["L0_attn_b_dt"].reshape(NDB, 128).T),
        "lnw": np.ascontiguousarray(inputs["L0_attn_ln_w"].reshape(1, DM)),
        "lnb": np.ascontiguousarray(inputs["L0_attn_ln_b"].reshape(1, DM)),
        "wgT": np.ascontiguousarray(inputs["L0_ffn_w_gate"].T),
        "wuT": np.ascontiguousarray(inputs["L0_ffn_w_up"].T),
        "wdT": np.ascontiguousarray(inputs["L0_ffn_w_down"].T),
        "idf": np.eye(128, dtype=np.float32),
        "idb": np.eye(128, dtype=np.float32).astype(_bf16()),
    }


def kernel(**inputs):
    nc = build_nc()
    in_maps = [host_inputs(inputs, core // 2, core % 2) for core in range(8)]
    res = run_bass_kernel_spmd(nc, in_maps, core_ids=list(range(8)))
    out = np.empty((4, L, DM), dtype=np.float32)
    nch = L // T
    for b in range(4):
        v = out[b].reshape(nch, 2, 128, DM)
        v[:, 0] = res.results[2 * b]["out"].reshape(nch, 128, DM)
        v[:, 1] = res.results[2 * b + 1]["out"].reshape(nch, 128, DM)
    return out


# version 11
# speedup vs baseline: 1.0861x; 1.0035x over previous
import contextlib
import numpy as np

import concourse.bass as bass
import concourse.mybir as mybir
from concourse.bass_utils import run_bass_kernel_spmd

ds = bass.ds

F32 = mybir.dt.float32
BF16 = mybir.dt.bfloat16
I32 = mybir.dt.int32
AF = mybir.ActivationFunctionType
ALU = mybir.AluOpType

L = 8192
DM = 1024
DI = 2048
NDB = 16
KC = 8
N = 64
G = 4
NGRP = N // G
T = 256
TT = T // 128
W = G * (T + 1)
HF = 3072
NHC = 24
EPS = 1e-5
FS = float(1.0 / 127.0)

SEMS = ("LD", "PE", "ACT", "DVE", "POOL", "OUT")
ENGS = ("sync", "vector", "scalar", "tensor", "gpsimd")


DMA_SEMS = ("LD", "OUT")


class Em:
    def __init__(self):
        self.q = {e: [] for e in ENGS}
        self.cnt = {s: 0 for s in SEMS}
        self.hw = {e: {s: 0 for s in SEMS} for e in ENGS}
        self.pend = {s: 0 for s in DMA_SEMS}

    def wait(self, eng, sem, val):
        if val is None or val <= 0:
            return
        if sem in DMA_SEMS:
            val = self.cnt[sem]
            self.pend[sem] = max(self.pend[sem], val)
        assert val <= self.cnt[sem], f"wait-on-future {sem} {val} > {self.cnt[sem]}"
        if self.hw[eng][sem] >= val:
            return
        self.hw[eng][sem] = val
        self.q[eng].append(("W", sem, val))

    ENG_SEM = {"vector": "DVE", "scalar": "ACT", "gpsimd": "POOL", "tensor": "PE"}

    def op(self, eng, sem, amt, fn):
        own = self.ENG_SEM.get(eng)
        if own is not None and eng != "tensor":
            v = self.cnt[own]
            if v > self.hw[eng][own]:
                self.hw[eng][own] = v
                self.q[eng].append(("W", own, v))
        if sem in DMA_SEMS and self.pend[sem] > self.hw[eng][sem]:
            v = self.pend[sem]
            self.hw[eng][sem] = v
            self.q[eng].append(("W", sem, v))
        self.cnt[sem] += amt
        self.q[eng].append(("I", sem, amt, fn))
        return self.cnt[sem]

    def nosem(self, eng, fn):
        self.q[eng].append(("N", fn))

    def barrier(self, *sems_now):
        snap = {s: self.cnt[s] for s in (sems_now or SEMS)}
        for e in ENGS:
            for s, v in snap.items():
                self.wait(e, s, v)

    def replay(self, eng, E, sems):
        for item in self.q[eng]:
            if item[0] == "W":
                E.wait_ge(sems[item[1]], item[2])
            elif item[0] == "I":
                item[3](E).then_inc(sems[item[1]], item[2])
            else:
                item[1](E)


def build_nc(nch=L // T, dbg=False):
    nc = bass.Bass("TRN2", target_bir_lowering=False)
    NCH = nch

    x_d = nc.dram_tensor("x", [L, DM], F32, kind="ExternalInput").ap()
    w_in_d = nc.dram_tensor("w_in", [DM, 2 * DI], F32, kind="ExternalInput").ap()
    wxp_d = nc.dram_tensor("wxp", [DI, 192], F32, kind="ExternalInput").ap()
    wdt_d = nc.dram_tensor("wdt", [64, DI], F32, kind="ExternalInput").ap()
    wout_d = nc.dram_tensor("wout", [DI, DM], F32, kind="ExternalInput").ap()
    alog_d = nc.dram_tensor("alog", [128, NDB * N], F32, kind="ExternalInput").ap()
    dvec_d = nc.dram_tensor("dvec", [128, NDB], F32, kind="ExternalInput").ap()
    convw_d = nc.dram_tensor("convw", [128, NDB * 4], F32, kind="ExternalInput").ap()
    convb_d = nc.dram_tensor("convb", [128, NDB], F32, kind="ExternalInput").ap()
    bdt_d = nc.dram_tensor("bdt", [128, NDB], F32, kind="ExternalInput").ap()
    lnw_d = nc.dram_tensor("lnw", [1, DM], F32, kind="ExternalInput").ap()
    lnb_d = nc.dram_tensor("lnb", [1, DM], F32, kind="ExternalInput").ap()
    wgT_d = nc.dram_tensor("wgT", [DM, HF], I32, kind="ExternalInput").ap()
    wuT_d = nc.dram_tensor("wuT", [DM, HF], I32, kind="ExternalInput").ap()
    wdT_d = nc.dram_tensor("wdT", [HF, DM], I32, kind="ExternalInput").ap()
    idf_d = nc.dram_tensor("idf", [128, 128], F32, kind="ExternalInput").ap()
    idb_d = nc.dram_tensor("idb", [128, 128], BF16, kind="ExternalInput").ap()
    idh_d = nc.dram_tensor("idh", [128, 256], F32, kind="ExternalInput").ap()
    msel_d = nc.dram_tensor("msel", [128, 2], F32, kind="ExternalInput").ap()
    hoff_d = nc.dram_tensor("hoff", [1, 1], I32, kind="ExternalInput").ap()
    xs_d = nc.dram_tensor("xs", [L, DM], F32, kind="ExternalInput").ap()
    out_d = nc.dram_tensor("out", [L // 2, DM], F32, kind="ExternalOutput").ap()

    wxi_s = nc.dram_tensor("wxi_s", [128, NDB * KC * 128], BF16, kind="Internal").ap()
    wz_s = nc.dram_tensor("wz_s", [128, NDB * KC * 128], BF16, kind="Internal").ap()
    wout_s = nc.dram_tensor("wout_s", [128, NDB * DM], BF16, kind="Internal").ap()
    wg_s = nc.dram_tensor("wg_s", [128, NHC * KC * 128], BF16, kind="Internal").ap()
    wu_s = nc.dram_tensor("wu_s", [128, NHC * KC * 128], BF16, kind="Internal").ap()
    wd_s = nc.dram_tensor("wd_s", [128, NHC * DM], BF16, kind="Internal").ap()
    bflat_d = nc.dram_tensor("bflat_d", [N, T], F32, kind="Internal").ap()
    u_d = nc.dram_tensor("u_d", [128, NDB * T], F32, kind="Internal").ap()
    xn2_d = nc.dram_tensor("xn2_d", [128, KC * T], BF16, kind="Internal").ap()

    dbg_outs = {}
    if dbg:
        for nm, shape in (("d_xn2", [128, KC * T]), ("d_u", [128, NDB * T]),
                          ("d_delta", [128, NDB * T]), ("d_B", [64, T]),
                          ("d_C", [64, T]), ("d_yacc", [128, TT * DI]),
                          ("d_y1", [128, NDB * 128])):
            dt = BF16 if nm == "d_xn2" else F32
            dbg_outs[nm] = nc.dram_tensor(nm, shape, dt, kind="ExternalOutput").ap()

    em = Em()
    SN = {}
    sreg = nc.sync.alloc_register("hreg")
    nc.sync.enable_hardware_checks = False
    es = contextlib.ExitStack()
    with es:
        sb = lambda name, shape, dt=F32: es.enter_context(nc.sbuf_tensor(name, shape, dt))
        idh_sb = sb("idh_sb", [128, 256])
        msel_sb = sb("msel_sb", [128, 2])
        hoff_sb = es.enter_context(nc.sbuf_tensor("hoff_sb", [1, 1], I32))
        xn2my = sb("xn2my", [128, KC * 128], BF16)
        scm_sb = sb("scm_sb", [128, 8])
        wxp_sb = sb("wxp_sb", [128, NDB * 256], BF16)
        wdt_sb = sb("wdt_sb", [128, DI], BF16)
        a_sb = sb("a_sb", [128, NDB * N])
        dvec_sb = sb("dvec_sb", [128, NDB])
        convw_sb = sb("convw_sb", [128, NDB * 4])
        convb_sb = sb("convb_sb", [128, NDB])
        bdt_sb = sb("bdt_sb", [128, NDB])
        lnwb_sb = sb("lnwb_sb", [128, 2 * DM])
        row_sb = sb("row_sb", [1, G * T])
        ones1_sb = sb("ones1_sb", [1, 128])
        idf_sb = sb("idf_sb", [128, 128])
        idb_sb = sb("idb_sb", [128, 128], BF16)
        wstream = sb("wstream", [128, 2048], BF16)
        wzbuf = sb("wzbuf", [128, 2 * 1024], BF16)
        woutb = sb("woutb", [128, 2 * 1024], BF16)
        wgb = sb("wgb", [128, 2 * 1024], BF16)
        wub = sb("wub", [128, 2 * 1024], BF16)
        wdb = sb("wdb", [128, 2 * 1024], BF16)
        xt_sb = sb("xt_sb", [128, DM])
        bn_sb = sb("bn_sb", [128, 16])
        ko_sb = sb("ko_sb", [128, 8])
        sc_d = sb("sc_d", [128, TT * 8])
        sc_a = sb("sc_a", [128, TT * 8])
        big1 = sb("big1", [128, DM])
        xn2tm = sb("xn2tm", [128, DM], BF16)
        xn2fm = sb("xn2fm", [128, KC * T], BF16)
        xib = sb("xib", [128, 3 + T])
        cvb = sb("cvb", [128, T])
        halo_sb = sb("halo_sb", [128, NDB * 3])
        u_sb = sb("u_sb", [128, NDB * T])
        ubf_sb = sb("ubf_sb", [128, 2 * T], BF16)
        delta_sb = sb("delta_sb", [128, NDB * T])
        du_sb = sb("du_sb", [128, NDB * T])
        dtrb_sb = sb("dtrb_sb", [128, T], BF16)
        Bsb = sb("Bsb", [128, T])
        Csb = sb("Csb", [128, T])
        ctm_sb = sb("ctm_sb", [128, TT * N])
        dec_sb = [sb(f"dec{i}", [128, W]) for i in range(2)]
        dbu_sb = [sb(f"dbu{i}", [128, W]) for i in range(2)]
        NH = 6
        h_sb = [sb(f"h{i}", [128, W]) for i in range(NH)]
        bbc_sb = sb("bbc_sb", [128, 2 * G * T])
        hstate_sb = sb("hstate_sb", [128, NDB * N])
        yacc_sb = sb("yacc_sb", [128, TT * DI])
        y1f_sb = sb("y1f_sb", [128, T])
        sz_sb = sb("sz_sb", [128, 128])
        y1bf_sb = sb("y1bf_sb", [128, 2 * 128], BF16)
        x2_sb = sb("x2_sb", [128, DM])
        xn3f_sb = sb("xn3f_sb", [128, KC * 128], BF16)
        hh_sb = sb("hh_sb", [128, 256], BF16)

        ssp_sb = cvb
        sfn_sb = big1
        xn3t_sb = xn2tm
        x3_sb = big1
        psA = es.enter_context(nc.psum_tensor("psA", [128, 512], F32))
        psB = es.enter_context(nc.psum_tensor("psB", [128, 512], F32))
        psC = es.enter_context(nc.psum_tensor("psC", [128, 512], F32))
        psE = es.enter_context(nc.psum_tensor("psE", [128, 1024], F32))
        psF = es.enter_context(nc.psum_tensor("psF", [128, 1024], F32))
        psT = es.enter_context(nc.psum_tensor("psT", [128, 1024], BF16))

        sems = {s: es.enter_context(nc.semaphore(s)) for s in SEMS}
        block = es.enter_context(nc.Block())

        LD, PE, ACT, DVE, POOL, OUT = SEMS

        def dma(src, dst):
            return em.op("sync", LD, 16, lambda E, s=src, d=dst: E.dma_start(out=d, in_=s))

        ldh = dma(hoff_d, hoff_sb[:])
        em.wait("sync", LD, ldh)
        em.nosem("sync", lambda E: E.reg_load(sreg, hoff_sb[0:1, 0:1]))
        em.nosem("sync", lambda E: SN.__setitem__("t", E.snap(sreg)))
        dma(idh_d, idh_sb[:])
        dma(msel_d, msel_sb[:])
        ld0 = dma(alog_d, a_sb[:])
        dma(dvec_d, dvec_sb[:])
        dma(convw_d, convw_sb[:])
        dma(convb_d, convb_sb[:])
        dma(bdt_d, bdt_sb[:])
        dma(idf_d, idf_sb[:])
        ld_idb = dma(idb_d, idb_sb[:])

        em.wait("scalar", LD, ld0)
        a_exp = em.op("scalar", ACT, 1, lambda E: E.activation(
            out=big1[:], in_=a_sb[:], func=AF.Exp))
        em.wait("vector", ACT, a_exp)
        a_done = em.op("vector", DVE, 1, lambda E: E.tensor_scalar_mul(
            out=a_sb[:], in0=big1[:], scalar1=-1.0))
        ones_done = em.op("vector", DVE, 1, lambda E: E.memset(ones1_sb[:], 1.0))
        for i, v in enumerate((1.0 + EPS, EPS * EPS, EPS, 1.0, -1.0, -0.5, FS)):
            kdone = em.op("vector", DVE, 1, lambda E, ii=i, vv=v: E.memset(
                ko_sb[:, ii:ii + 1], vv))
        em.wait("scalar", DVE, kdone)
        z_dve = em.op("vector", DVE, 1, lambda E: E.memset(halo_sb[:], 0.0))
        z_dve = em.op("vector", DVE, 1, lambda E: E.memset(hstate_sb[:], 0.0))
        for d in dec_sb:
            z_dve = em.op("vector", DVE, 1, lambda E, dd=d: E.memset(dd[:], 0.0))

        em.wait("tensor", DVE, ones_done)
        em.wait("tensor", LD, ld_idb)
        lnbc_done = 0
        for which, srcd in ((0, lnw_d), (1, lnb_d)):
            em.wait("sync", ACT, lnbc_done)
            lrow = dma(srcd, row_sb[:, 0:DM])
            em.wait("tensor", LD, lrow)
            pe_c = 0
            for half in range(2):
                pe_c = em.op("tensor", PE, 1, lambda E, hf=half: E.matmul(
                    out=psE[:, hf * 512:(hf + 1) * 512], lhsT=ones1_sb[:],
                    rhs=row_sb[:, hf * 512:hf * 512 + 512], start=True, stop=True))
            em.wait("scalar", PE, pe_c)
            for half in range(2):
                lnbc_done = em.op("scalar", ACT, 1, lambda E, w=which, hf=half:
                                  E.activation(
                    out=lnwb_sb[:, w * DM + hf * 512: w * DM + hf * 512 + 512],
                    in_=psE[:, hf * 512:(hf + 1) * 512], func=AF.Copy))

        wi32 = xt_sb[:].bitcast(I32)
        cvt = a_done
        for dst_s, c0 in ((wxi_s, 0), (wz_s, DI)):
            for kc in range(KC):
                for hh in range(2):
                    em.wait("sync", DVE, cvt)
                    lw = dma(w_in_d[kc * 128:(kc + 1) * 128,
                                    c0 + hh * DM:c0 + (hh + 1) * DM], big1[:])
                    em.wait("vector", LD, lw)
                    cvt = em.op("vector", DVE, 1, lambda E, h2=hh: E.tensor_copy(
                        out=wstream[:, h2 * DM:(h2 + 1) * DM], in_=big1[:]))
                em.wait("sync", DVE, cvt)
                dv = dst_s.rearrange("p (db k j) -> p db k j", db=NDB, k=KC)[:, :, kc, :]
                sv = wstream[:].rearrange("p (db j) -> p db j", db=NDB)
                wb = em.op("sync", LD, 16, lambda E, d=dv, s=sv: E.dma_start(out=d, in_=s))
                em.wait("vector", LD, wb)
        for kc in range(NDB):
            em.wait("sync", DVE, cvt)
            lw = dma(wout_d[kc * 128:(kc + 1) * 128, :], big1[:])
            em.wait("vector", LD, lw)
            cvt = em.op("vector", DVE, 1, lambda E: E.tensor_copy(
                out=wstream[:, 0:DM], in_=big1[:]))
            em.wait("sync", DVE, cvt)
            wb = em.op("sync", LD, 16, lambda E, k=kc: E.dma_start(
                out=wout_s[:, k * DM:(k + 1) * DM], in_=wstream[:, 0:DM]))
            em.wait("vector", LD, wb)
        zw = em.op("vector", DVE, 1, lambda E: E.memset(wxp_sb[:], 0.0))
        for kc in range(NDB):
            em.wait("sync", DVE, max(cvt, zw))
            lw = dma(wxp_d[kc * 128:(kc + 1) * 128, :], big1[:, 0:192])
            em.wait("vector", LD, lw)
            cvt = em.op("vector", DVE, 1, lambda E, k=kc: E.tensor_copy(
                out=wxp_sb[:, k * 256:k * 256 + 192], in_=big1[:, 0:192]))
        for hh in range(2):
            em.wait("sync", DVE, cvt)
            lw = em.op("sync", LD, 16, lambda E, h2=hh: E.dma_start(
                out=big1[0:64, :], in_=wdt_d[:, h2 * DM:(h2 + 1) * DM]))
            em.wait("vector", LD, lw)
            cvt = em.op("vector", DVE, 1, lambda E, h2=hh: E.tensor_copy(
                out=wdt_sb[0:64, h2 * DM:(h2 + 1) * DM], in_=big1[0:64, :]))
            em.wait("sync", DVE, cvt)
        for src_d, dst_s in ((wgT_d, wg_s), (wuT_d, wu_s)):
            for kc in range(KC):
                for hh in range(3):
                    em.wait("sync", DVE, cvt)
                    lw = dma(src_d[kc * 128:(kc + 1) * 128, hh * 1024:(hh + 1) * 1024], wi32)
                    em.wait("vector", LD, lw)
                    cvt = em.op("vector", DVE, 1, lambda E: E.tensor_copy(
                        out=wstream[:, 0:DM], in_=wi32))
                    em.wait("sync", DVE, cvt)
                    dv = dst_s.rearrange("p (hc k j) -> p hc k j", hc=NHC, k=KC)[
                        :, hh * 8:(hh + 1) * 8, kc, :]
                    sv = wstream[:, 0:DM].rearrange("p (hc j) -> p hc j", hc=8)
                    wb = em.op("sync", LD, 16, lambda E, d=dv, s=sv: E.dma_start(out=d, in_=s))
                    em.wait("vector", LD, wb)
        for hc in range(NHC):
            em.wait("sync", DVE, cvt)
            lw = dma(wdT_d[hc * 128:(hc + 1) * 128, :], wi32)
            em.wait("vector", LD, lw)
            cvt = em.op("vector", DVE, 1, lambda E: E.tensor_copy(
                out=wstream[:, 0:DM], in_=wi32))
            em.wait("sync", DVE, cvt)
            wb = em.op("sync", LD, 16, lambda E, h=hc: E.dma_start(
                out=wd_s[:, h * DM:(h + 1) * DM], in_=wstream[:, 0:DM]))
            em.wait("vector", LD, wb)

        em.barrier()

        last = {"dec_rd": [0, 0], "dbu_rd": [0, 0],
                "h_rd_pe": [0] * 6, "h_rd_act": [0] * 6}
        for i in range(6):
            last[f"h_wr_{i}"] = 0

        def emit_A(c):
            for tt in range(TT):
                em.wait("sync", DVE, last.get("xt_rd_dve", 0))
                ldx = dma(x_d[c * T + tt * 128: c * T + (tt + 1) * 128, :], xt_sb[:])
                em.wait("vector", LD, ldx)
                em.op("vector", DVE, 1, lambda E: E.bn_stats(
                    out=bn_sb[:, 0:6], in_=xt_sb[:, 0:512]))
                em.op("vector", DVE, 1, lambda E: E.bn_stats(
                    out=bn_sb[:, 6:12], in_=xt_sb[:, 512:1024]))
                em.op("vector", DVE, 1, lambda E: E.engine_nop())
                bs = em.op("vector", DVE, 1, lambda E: E.bn_aggr(
                    out=bn_sb[:, 12:14], in_=bn_sb[:, 0:12]))
                o = tt * 8
                bs = em.op("vector", DVE, 1, lambda E, o=o: E.tensor_copy(
                    out=sc_d[:, o:o + 2], in_=bn_sb[:, 12:14]))
                em.wait("scalar", DVE, bs)
                em.op("scalar", ACT, 1, lambda E, o=o: E.activation(
                    out=sc_a[:, o + 3:o + 4], in_=sc_d[:, o + 1:o + 2], func=AF.Ln,
                    scale=ko_sb[:, 0:1], bias=ko_sb[:, 1:2]))
                em.op("scalar", ACT, 1, lambda E, o=o: E.activation(
                    out=sc_a[:, o + 4:o + 5], in_=sc_d[:, o + 1:o + 2], func=AF.Ln,
                    bias=ko_sb[:, 2:3]))
                em.op("scalar", ACT, 1, lambda E, o=o: E.activation(
                    out=sc_a[:, o:o + 1], in_=sc_a[:, o + 3:o + 4], func=AF.Exp,
                    scale=ko_sb[:, 5:6]))
                ac = em.op("scalar", ACT, 1, lambda E, o=o: E.activation(
                    out=sc_a[:, o + 1:o + 2], in_=sc_a[:, o + 4:o + 5], func=AF.Exp,
                    scale=ko_sb[:, 5:6]))
                em.wait("vector", ACT, ac)
                em.op("vector", DVE, 1, lambda E, o=o: E.scalar_tensor_tensor(
                    out=sc_d[:, o + 2:o + 3], in0=sc_d[:, o:o + 1], scalar=-1.0,
                    in1=sc_a[:, o:o + 1], op0=ALU.mult, op1=ALU.mult))
                em.op("vector", DVE, 1, lambda E, o=o: E.scalar_tensor_tensor(
                    out=sc_d[:, o + 3:o + 4], in0=sc_d[:, o:o + 1], scalar=-1.0,
                    in1=sc_a[:, o + 1:o + 2], op0=ALU.mult, op1=ALU.mult))
                xrd = em.op("vector", DVE, 1, lambda E, o=o: E.scalar_tensor_tensor(
                    out=xt_sb[:], in0=xt_sb[:], scalar=sc_a[:, o:o + 1],
                    in1=lnwb_sb[:, 0:DM], op0=ALU.mult, op1=ALU.mult))
                em.op("vector", DVE, 1, lambda E, o=o: E.scalar_tensor_tensor(
                    out=xt_sb[:], in0=lnwb_sb[:, 0:DM], scalar=sc_d[:, o + 2:o + 3],
                    in1=xt_sb[:], op0=ALU.mult, op1=ALU.add))
                dv = em.op("vector", DVE, 1, lambda E: E.tensor_add(
                    out=xn2tm[:], in0=xt_sb[:], in1=lnwb_sb[:, DM:2 * DM]))
                last["xt_rd_dve"] = dv
                em.wait("tensor", DVE, dv)
                em.wait("tensor", ACT, last.get("psT_rd", 0))
                pe_t = 0
                for kc in range(KC):
                    pe_t = em.op("tensor", PE, 1, lambda E, k=kc: E.matmul(
                        out=psT[:, k * 128:(k + 1) * 128],
                        lhsT=xn2tm[:, k * 128:(k + 1) * 128], rhs=idb_sb[:],
                        is_transpose=True, start=(k == 0), stop=(k == KC - 1)))
                em.wait("scalar", PE, pe_t)
                ev = 0
                for kc in range(KC):
                    ev = em.op("scalar", ACT, 1, lambda E, k=kc, t=tt: E.activation(
                        out=xn2fm[:, k * T + t * 128:k * T + t * 128 + 128],
                        in_=psT[:, k * 128:(k + 1) * 128], func=AF.Copy))
                last["xn2fm_wr"] = ev
                last["psT_rd"] = ev
                yield

            last["pending_xdbl"] = None

            def flush_xdbl():
                d, p, ub = last["pending_xdbl"]
                em.wait("tensor", DVE, ub)
                if d == 0:
                    em.wait("tensor", ACT, last.get("xdbl_ps_rd", 0))
                    em.wait("tensor", ACT, last.get("z_ps_rd", 0))
                em.op("tensor", PE, 1, lambda E, d=d, p=p: E.matmul(
                    out=psC[:, 0:T], lhsT=wxp_sb[:, d * 256:d * 256 + 128],
                    rhs=ubf_sb[:, p * T:(p + 1) * T], start=(d == 0), stop=False))
                pe_d = em.op("tensor", PE, 1, lambda E, d=d, p=p: E.matmul(
                    out=psC[:, T:2 * T], lhsT=wxp_sb[:, d * 256 + 128:d * 256 + 256],
                    rhs=ubf_sb[:, p * T:(p + 1) * T], start=False, stop=(d == NDB - 1)))
                last[f"ubf_rd_{p}"] = pe_d
                last["pending_xdbl"] = None

            for db in range(NDB):
                par = db % 2
                em.wait("sync", PE, last.get(f"wxi_rd_{par}", 0))
                lw = dma(wxi_s[:, db * 1024:(db + 1) * 1024],
                         wstream[:, par * 1024:(par + 1) * 1024])
                em.wait("tensor", LD, lw)
                em.wait("tensor", ACT, last["xn2fm_wr"])
                em.wait("tensor", ACT, last.get("xi_ps_rd", 0))
                em.wait("tensor", DVE, last.get("g_dve_1", 0))
                em.wait("tensor", DVE, last.get("yf_ps_rd", 0))
                if last.get("pending_xdbl") is not None:
                    flush_xdbl()
                pe_x = 0
                for kc in range(KC):
                    pe_x = em.op("tensor", PE, 1, lambda E, k=kc, p=par: E.matmul(
                        out=psA[:, 0:T],
                        lhsT=wstream[:, p * 1024 + k * 128:p * 1024 + (k + 1) * 128],
                        rhs=xn2fm[:, k * T:(k + 1) * T],
                        start=(k == 0), stop=(k == KC - 1)))
                last[f"wxi_rd_{par}"] = pe_x
                em.wait("scalar", PE, pe_x)
                em.wait("scalar", DVE, last.get("xib_rd", 0))
                em.op("scalar", ACT, 1, lambda E, d=db: E.activation(
                    out=xib[:, 0:3], in_=halo_sb[:, d * 3:(d + 1) * 3], func=AF.Copy))
                ev = em.op("scalar", ACT, 1, lambda E: E.activation(
                    out=xib[:, 3:3 + T], in_=psA[:, 0:T], func=AF.Copy))
                last["xi_ps_rd"] = ev
                em.wait("vector", ACT, ev)
                em.wait("vector", ACT, last.get("cvb_rd", 0))
                cv = em.op("vector", DVE, 1, lambda E, d=db: E.tensor_scalar(
                    out=cvb[:], in0=xib[:, 0:T], scalar1=convw_sb[:, d * 4:d * 4 + 1],
                    scalar2=None, op0=ALU.mult))
                for k in range(1, 4):
                    cv = em.op("vector", DVE, 1, lambda E, d=db, kk=k: E.scalar_tensor_tensor(
                        out=cvb[:], in0=xib[:, kk:kk + T],
                        scalar=convw_sb[:, d * 4 + kk:d * 4 + kk + 1],
                        in1=cvb[:], op0=ALU.mult, op1=ALU.add))
                last["xib_rd"] = cv
                em.wait("scalar", DVE, cv)
                uu = em.op("scalar", ACT, 1, lambda E, d=db: E.activation(
                    out=u_sb[:, d * T:(d + 1) * T], in_=cvb[:], func=AF.Silu,
                    bias=convb_sb[:, d:d + 1]))
                last["cvb_rd"] = uu
                em.op("scalar", ACT, 1, lambda E, d=db: E.activation(
                    out=halo_sb[:, d * 3:(d + 1) * 3], in_=xib[:, T:T + 3], func=AF.Copy))
                em.wait("vector", ACT, uu)
                em.wait("vector", PE, last.get(f"ubf_rd_{par}", 0))
                ub = em.op("vector", DVE, 1, lambda E, d=db, p=par: E.tensor_copy(
                    out=ubf_sb[:, p * T:(p + 1) * T], in_=u_sb[:, d * T:(d + 1) * T]))
                last["pending_xdbl"] = (db, par, ub)
                yield

            flush_xdbl()
            last["u_wr"] = uu
            xdbl_pe = em.cnt[PE]

            em.wait("scalar", PE, xdbl_pe)
            em.wait("scalar", LD, 1)
            ev_dtr = em.op("scalar", ACT, 1, lambda E: E.activation(
                out=dtrb_sb[0:64, :], in_=psC[0:64, 0:T], func=AF.Copy))
            last["ev_dtr"] = ev_dtr
            ev_B = em.op("scalar", ACT, 1, lambda E: E.activation(
                out=Bsb[64:128, :], in_=psC[64:128, 0:T], func=AF.Copy))
            ev_C = em.op("scalar", ACT, 1, lambda E: E.activation(
                out=Csb[0:64, :], in_=psC[0:64, T:2 * T], func=AF.Copy))
            last["xdbl_ps_rd"] = ev_C
            em.wait("sync", ACT, ev_B)
            bf1 = em.op("sync", LD, 16, lambda E: E.dma_start(
                out=bflat_d, in_=Bsb[64:128, :]))
            last["bf1"] = bf1
            em.wait("tensor", ACT, ev_C)
            em.wait("tensor", ACT, last["xi_ps_rd"])
            em.wait("tensor", DVE, last.get("g_dve_1", 0))
            em.wait("tensor", DVE, last.get("yf_ps_rd", 0))
            pe_c2 = 0
            for tt in range(TT):
                pe_c2 = em.op("tensor", PE, 1, lambda E, t=tt: E.matmul(
                    out=psA[:, 256 + t * 64:256 + (t + 1) * 64],
                    lhsT=Csb[0:64, t * 128:(t + 1) * 128], rhs=idf_sb[0:64, 0:64],
                    is_transpose=True, start=(t == 0), stop=(t == TT - 1)))
            em.wait("scalar", PE, pe_c2)
            ev_ctm = em.op("scalar", ACT, 1, lambda E: E.activation(
                out=ctm_sb[:], in_=psA[:, 256:256 + TT * N], func=AF.Copy))
            last["ev_ctm"] = ev_ctm

            em.wait("sync", ACT, uu)
            em.wait("sync", DVE, last.get("yf_ps_rd", 0))
            bo2 = em.op("sync", LD, 16, lambda E: E.dma_start(out=u_d, in_=u_sb[:]))
            em.wait("sync", ACT, last["xn2fm_wr"])
            bo1 = em.op("sync", LD, 16, lambda E: E.dma_start(out=xn2_d, in_=xn2fm[:]))
            em.wait("sync", LD, bo1)
            em.wait("sync", PE, max(last.get("wz_rd_0", 0), last.get("wz_rd_1", 0)))
            em.op("sync", LD, 16, lambda E: E.dma_start(
                out=xn2my[:].rearrange("p (k t) -> p k t", k=KC),
                in_=xn2_d.rearrange("p (k t) -> p k t", k=KC)[:, :, ds(SN["t"], 128)]))

            if dbg and c == 0:
                em.wait("sync", ACT, last["xn2fm_wr"])
                em.op("sync", OUT, 16, lambda E: E.dma_start(
                    out=dbg_outs["d_xn2"], in_=xn2fm[:]))
                em.wait("sync", ACT, last["u_wr"])
                em.op("sync", OUT, 16, lambda E: E.dma_start(out=dbg_outs["d_u"], in_=u_sb[:]))
                em.wait("sync", ACT, ev_B)
                em.op("sync", OUT, 16, lambda E: E.dma_start(out=dbg_outs["d_B"], in_=Bsb[64:128, :]))
                em.wait("sync", ACT, ev_C)
                em.op("sync", OUT, 16, lambda E: E.dma_start(out=dbg_outs["d_C"], in_=Csb[0:64, :]))


        def emit_B(c):
            for db in range(NDB):
                em.wait("tensor", ACT, last["ev_dtr"])
                em.wait("tensor", ACT, last.get("dlt_ps_rd", 0))
                em.wait("tensor", DVE, last.get("g_dve_0", 0))
                pe_dl = em.op("tensor", PE, 1, lambda E, d=db: E.matmul(
                    out=psB[:, 0:T], lhsT=wdt_sb[0:64, d * 128:(d + 1) * 128],
                    rhs=dtrb_sb[0:64, :], start=True, stop=True))
                em.wait("scalar", PE, pe_dl)
                em.wait("scalar", DVE, last.get("ssp_rd", 0))
                em.op("scalar", ACT, 1, lambda E, d=db: E.activation(
                    out=ssp_sb[:], in_=psB[:, 0:T], func=AF.Abs, bias=bdt_sb[:, d:d + 1]))
                em.op("scalar", ACT, 1, lambda E: E.activation(
                    out=ssp_sb[:], in_=ssp_sb[:], func=AF.Exp, scale=ko_sb[:, 4:5]))
                em.op("scalar", ACT, 1, lambda E: E.activation(
                    out=ssp_sb[:], in_=ssp_sb[:], func=AF.Ln, bias=ko_sb[:, 3:4]))
                em.wait("scalar", DVE, last.get("sz_rd", 0))
                em.wait("scalar", OUT, 1)
                a4 = em.op("scalar", ACT, 1, lambda E, d=db: E.activation(
                    out=y1f_sb[:], in_=psB[:, 0:T], func=AF.Relu,
                    bias=bdt_sb[:, d:d + 1]))
                last["dlt_ps_rd"] = a4
                em.wait("vector", ACT, a4)
                em.wait("vector", ACT, last.get(f"delta_rd_{db}", 0))
                ddl = em.op("vector", DVE, 1, lambda E, d=db: E.tensor_add(
                    out=delta_sb[:, d * T:(d + 1) * T], in0=y1f_sb[:], in1=ssp_sb[:]))
                last[f"delta_wr_{db}"] = ddl
                last["ssp_rd"] = ddl
                em.wait("vector", ACT, last["u_wr"])
                em.op("vector", DVE, 1, lambda E, d=db: E.tensor_mul(
                    out=du_sb[:, d * T:(d + 1) * T], in0=delta_sb[:, d * T:(d + 1) * T],
                    in1=u_sb[:, d * T:(d + 1) * T]))
            if dbg and c == 0:
                em.wait("sync", DVE, em.cnt[DVE])
                em.op("sync", OUT, 16, lambda E: E.dma_start(
                    out=dbg_outs["d_delta"], in_=delta_sb[:]))
                em.wait("sync", OUT, 1)
            em.wait("sync", DVE, em.cnt[DVE])
            em.wait("sync", LD, 1)
            em.op("sync", LD, 16, lambda E: E.dma_start(
                out=u_sb[:, 0:NDB * 128].rearrange("p (db t) -> p db t", db=NDB),
                in_=u_d.rearrange("p (db t) -> p db t", db=NDB)[:, :, ds(SN["t"], 128)]))
            em.wait("vector", PE, last.get("yacc_rd_pe", 0))
            em.op("vector", DVE, 1, lambda E: E.memset(yacc_sb[:], 0.0))
            em.wait("tensor", ACT, last["ev_ctm"])
            em.wait("tensor", ACT, last["dlt_ps_rd"])
            em.wait("tensor", DVE, last.get("g_dve_0", 0))
            em.wait("tensor", DVE, last.get("ffn_ps_rd", 0))
            for grp in range(NGRP):
                bpar = grp % 2
                em.wait("sync", LD, last["bf1"])
                em.wait("sync", PE, last.get("row_rd_pe", 0))
                bfg = em.op("sync", LD, 16, lambda E, g=grp: E.dma_start(
                    out=row_sb[0:1, :],
                    in_=bflat_d[g * G:(g + 1) * G, :].rearrange("n t -> (n t)")[None, :]))
                em.wait("tensor", LD, bfg)
                em.wait("tensor", ACT, last.get("bbc_ps_rd", 0))
                pe_b = 0
                for half in range(2):
                    pe_b = em.op("tensor", PE, 1, lambda E, hf=half: E.matmul(
                        out=psE[:, hf * 512:(hf + 1) * 512], lhsT=ones1_sb[:],
                        rhs=row_sb[:, hf * 512:(hf + 1) * 512],
                        start=True, stop=True))
                last["row_rd_pe"] = pe_b
                em.wait("scalar", PE, pe_b)
                em.wait("scalar", DVE, last.get(f"bbc_rd_pool_{bpar}", 0))
                ev_b = 0
                for half in range(2):
                    ev_b = em.op("scalar", ACT, 1, lambda E, hf=half, bp=bpar: E.activation(
                        out=bbc_sb[:, bp * 1024 + hf * 512:bp * 1024 + (hf + 1) * 512],
                        in_=psE[:, hf * 512:(hf + 1) * 512], func=AF.Copy))
                last["bbc_ps_rd"] = ev_b

                for db in range(NDB):
                    par = db % 2
                    hq = (grp * NDB + db) % NH
                    dec, dbuf = dec_sb[par], dbu_sb[par]
                    em.wait("scalar", DVE, max(last["dec_rd"][par],
                                               last.get(f"delta_wr_{db}", 0)))
                    ac_d = 0
                    for g in range(G):
                        nn = grp * G + g
                        ac_d = em.op("scalar", ACT, 1, lambda E, d=db, n=nn, g=g, dd=dec:
                                     E.activation(
                            out=dd[:, g * (T + 1) + 1:(g + 1) * (T + 1)],
                            in_=delta_sb[:, d * T:(d + 1) * T], func=AF.Exp,
                            scale=a_sb[:, d * N + n:d * N + n + 1]))
                    last[f"delta_rd_{db}"] = ac_d
                    if last.get("pending_gather") is not None:
                        pd, pg0, phq, psc = last["pending_gather"]
                        em.wait("scalar", DVE, psc)
                        gat = em.op("scalar", ACT, 1, lambda E, d=pd, g0=pg0,
                                    hh=h_sb[phq]: E.activation(
                            out=hstate_sb[:, d * N + g0 * G:d * N + (g0 + 1) * G],
                            in_=hh[:].rearrange("p (g t) -> p g t", g=G)[:, :, T],
                            func=AF.Copy))
                        last["h_rd_act"][phq] = gat
                        last["pending_gather"] = None
                    dbu_pay = dbuf[:].rearrange("p (g t) -> p g t", g=G)[:, :, 1:]
                    em.wait("vector", ACT, ev_b)
                    em.wait("vector", ACT, ac_d)
                    em.wait("vector", ACT, last["h_rd_act"][hq])
                    po = em.op("vector", DVE, 1, lambda E, d=db, dp=dbu_pay, bp=bpar:
                               E.tensor_tensor(
                        out=dp,
                        in0=du_sb[:, d * T:(d + 1) * T].unsqueeze(1).broadcast_to(
                            [128, G, T]),
                        in1=bbc_sb[:, bp * 1024:(bp + 1) * 1024].rearrange(
                            "p (g t) -> p g t", g=G),
                        op=ALU.mult))
                    last[f"bbc_rd_pool_{bpar}"] = po
                    bnd = em.op("vector", DVE, 1, lambda E, d=db, g0=grp, dd=dbuf:
                                E.tensor_copy(
                        out=dd[:].rearrange("p (g t) -> p g t", g=G)[:, :, 0],
                        in_=hstate_sb[:, d * N + g0 * G:d * N + (g0 + 1) * G]))
                    em.wait("vector", PE, last["h_rd_pe"][hq])
                    em.wait("vector", ACT, last["h_rd_act"][hq])
                    sc_v = em.op("vector", DVE, 1, lambda E, dd=dec, bb=dbuf, hh=h_sb[hq]:
                                 E.tensor_tensor_scan(
                        out=hh[:], data0=dd[:], data1=bb[:], initial=0.0,
                        op0=ALU.mult, op1=ALU.add))
                    last["dec_rd"][par] = sc_v
                    last["dbu_rd"][par] = sc_v
                    last[f"h_wr_{hq}"] = sc_v
                    last["pending_gather"] = (db, grp, hq, sc_v)

                    if db % 4 == 3:
                        q0 = db - 3
                        hqs = [(grp * NDB + q0 + q) % NH for q in range(4)]
                        for g in range(G):
                            nn = grp * G + g
                            for tt in range(TT):
                                use_a = (g * TT + tt) % 2 == 0
                                rt = psA if use_a else psB
                                key = "rt_rd_A" if use_a else "rt_rd_B"
                                em.wait("tensor", DVE, last.get(key, 0))
                                pe_r = 0
                                for q in range(4):
                                    em.wait("tensor", DVE, last[f"h_wr_{hqs[q]}"])
                                    pe_r = em.op("tensor", PE, 1,
                                                 lambda E, qq=hqs[q], qi=q, gg=g, t=tt, r=rt: E.matmul(
                                        out=r[:, qi * 128:(qi + 1) * 128],
                                        lhsT=h_sb[qq][:, gg * (T + 1) + 1 + t * 128:
                                                      gg * (T + 1) + 1 + (t + 1) * 128],
                                        rhs=idf_sb[:], is_transpose=True,
                                        start=(qi == 0), stop=(qi == 3)))
                                for q in range(4):
                                    last["h_rd_pe"][hqs[q]] = pe_r
                                em.wait("vector", PE, pe_r)
                                st_v = em.op("vector", DVE, 1,
                                             lambda E, n=nn, t=tt, r=rt, qb=q0:
                                             E.scalar_tensor_tensor(
                                    out=yacc_sb[:, t * DI + qb * 128:t * DI + qb * 128 + 512],
                                    in0=r[:, 0:512],
                                    scalar=ctm_sb[:, t * N + n:t * N + n + 1],
                                    in1=yacc_sb[:, t * DI + qb * 128:t * DI + qb * 128 + 512],
                                    op0=ALU.mult, op1=ALU.add))
                                last[key] = st_v
            if last.get("pending_gather") is not None:
                pd, pg0, phq, psc = last["pending_gather"]
                em.wait("scalar", DVE, psc)
                gat = em.op("scalar", ACT, 1, lambda E, d=pd, g0=pg0,
                            hh=h_sb[phq]: E.activation(
                    out=hstate_sb[:, d * N + g0 * G:d * N + (g0 + 1) * G],
                    in_=hh[:].rearrange("p (g t) -> p g t", g=G)[:, :, T],
                    func=AF.Copy))
                last["h_rd_act"][phq] = gat
                last["pending_gather"] = None
            yacc_dve = em.cnt[DVE]
            last["yacc_dve"] = yacc_dve

            if dbg and c == 0:
                em.wait("sync", DVE, yacc_dve)
                em.op("sync", OUT, 16, lambda E: E.dma_start(
                    out=dbg_outs["d_yacc"], in_=yacc_sb[:]))


        def emit_C1(c):
            last["pending_wout"] = None

            def flush_wout():
                d, p, y1b, lwo = last["pending_wout"]
                em.wait("tensor", DVE, y1b)
                em.wait("tensor", LD, lwo)
                if d == 0:
                    em.wait("tensor", DVE, last.get("ffn_ps_rd", 0))
                    em.wait("tensor", ACT, last.get("bbc_ps_rd", 0))
                pe_o = 0
                for half in range(2):
                    pe_o = em.op("tensor", PE, 1, lambda E, d=d, hf=half,
                                 p=p: E.matmul(
                        out=psE[:, hf * 512:(hf + 1) * 512],
                        lhsT=y1bf_sb[:, p * 128:(p + 1) * 128],
                        rhs=woutb[:, p * DM + hf * 512:p * DM + (hf + 1) * 512],
                        start=(d == 0), stop=(d == NDB - 1)))
                last[f"wout_rd_{p}"] = pe_o
                last[f"y1bf_rd_{p}"] = pe_o
                last["pending_wout"] = None

            for db in range(NDB):
                par = db % 2
                em.wait("sync", PE, last.get(f"wout_rd_{par}", 0))
                lwo = dma(wout_s[:, db * DM:(db + 1) * DM],
                          woutb[:, par * DM:(par + 1) * DM])
                em.wait("sync", PE, last.get(f"wz_rd_{par}", 0))
                lwz = dma(wz_s[:, db * 1024:(db + 1) * 1024],
                          wzbuf[:, par * 1024:(par + 1) * 1024])
                em.wait("tensor", DVE, last["yacc_dve"])
                em.wait("tensor", DVE, last.get("yf_ps_rd", 0))
                pe_y = 0
                for t in range(TT):
                    pe_y = em.op("tensor", PE, 1, lambda E, d=db, t=t: E.matmul(
                        out=psA[:, 0:128],
                        lhsT=yacc_sb[:, t * DI + d * 128:t * DI + (d + 1) * 128],
                        rhs=idh_sb[:, t * 128:(t + 1) * 128],
                        start=(t == 0), stop=(t == TT - 1)))
                last["yacc_rd_pe"] = pe_y
                em.wait("vector", PE, pe_y)
                em.wait("vector", OUT, last.get("y1f_dump", 0))
                em.wait("vector", LD, 1)
                y1v = em.op("vector", DVE, 1, lambda E, d=db: E.scalar_tensor_tensor(
                    out=y1f_sb[:, 0:128], in0=u_sb[:, d * 128:(d + 1) * 128],
                    scalar=dvec_sb[:, d:d + 1], in1=psA[:, 0:128],
                    op0=ALU.mult, op1=ALU.add))
                last["yf_ps_rd"] = y1v
                em.wait("tensor", LD, lwz)
                if db == 0:
                    em.wait("tensor", ACT, last["xdbl_ps_rd"])
                em.wait("tensor", ACT, last.get("z_ps_rd", 0))
                pe_z = 0
                for kc in range(KC):
                    pe_z = em.op("tensor", PE, 1, lambda E, k=kc, p=par: E.matmul(
                        out=psC[:, 0:128],
                        lhsT=wzbuf[:, p * 1024 + k * 128:p * 1024 + (k + 1) * 128],
                        rhs=xn2my[:, k * 128:(k + 1) * 128],
                        start=(k == 0), stop=(k == KC - 1)))
                last[f"wz_rd_{par}"] = pe_z
                em.wait("scalar", PE, pe_z)
                em.wait("scalar", DVE, last.get("sz_rd", 0))
                szv = em.op("scalar", ACT, 1, lambda E: E.activation(
                    out=sz_sb[:], in_=psC[:, 0:128], func=AF.Silu))
                last["z_ps_rd"] = szv
                em.wait("vector", ACT, szv)
                em.wait("vector", PE, last.get(f"y1bf_rd_{par}", 0))
                y1b = em.op("vector", DVE, 1, lambda E, p=par: E.tensor_mul(
                    out=y1bf_sb[:, p * 128:(p + 1) * 128], in0=y1f_sb[:, 0:128],
                    in1=sz_sb[:]))
                last["sz_rd"] = y1b
                if last.get("pending_wout") is not None:
                    flush_wout()
                last["pending_wout"] = (db, par, y1b, lwo)
                if dbg and c == 0:
                    em.wait("sync", DVE, y1b)
                    last["y1f_dump"] = em.op("sync", OUT, 16, lambda E, d=db: E.dma_start(
                        out=dbg_outs["d_y1"][:, d * 128:(d + 1) * 128],
                        in_=y1f_sb[:, 0:128]))
            flush_wout()
            last["om_pe"] = em.cnt[PE]

        def emit_C2(c):
            sm = em.op("vector", DVE, 1, lambda E: E.tensor_scalar(
                out=scm_sb[:, 4:5], in0=sc_a[:, 9:10], scalar1=msel_sb[:, 1:2],
                scalar2=None, op0=ALU.mult))
            sm = em.op("vector", DVE, 1, lambda E: E.scalar_tensor_tensor(
                out=scm_sb[:, 0:1], in0=sc_a[:, 1:2], scalar=msel_sb[:, 0:1],
                in1=scm_sb[:, 4:5], op0=ALU.mult, op1=ALU.add))
            sm = em.op("vector", DVE, 1, lambda E: E.tensor_scalar(
                out=scm_sb[:, 5:6], in0=sc_d[:, 11:12], scalar1=msel_sb[:, 1:2],
                scalar2=None, op0=ALU.mult))
            sm = em.op("vector", DVE, 1, lambda E: E.scalar_tensor_tensor(
                out=scm_sb[:, 1:2], in0=sc_d[:, 3:4], scalar=msel_sb[:, 0:1],
                in1=scm_sb[:, 5:6], op0=ALU.mult, op1=ALU.add))

            em.wait("sync", DVE, last.get("xt_rd_dve", 0))
            em.wait("sync", ACT, last.get("xt_rd_act", 0))
            ldx = dma(xs_d[c * T:c * T + 128, :], xt_sb[:])
            em.wait("vector", LD, ldx)
            em.wait("vector", PE, last["om_pe"])
            xv = em.op("vector", DVE, 1, lambda E: E.tensor_add(
                out=big1[:], in0=xt_sb[:], in1=psE[:]))
            last["om_ps_rd"] = xv
            xv = em.op("vector", DVE, 1, lambda E: E.tensor_scalar(
                out=xt_sb[:], in0=xt_sb[:], scalar1=scm_sb[:, 0:1],
                scalar2=scm_sb[:, 1:2], op0=ALU.mult, op1=ALU.add))
            xv = em.op("vector", DVE, 1, lambda E: E.tensor_add(
                out=x2_sb[:], in0=big1[:], in1=xt_sb[:]))
            last["xt_rd_dve"] = xv
            em.op("vector", DVE, 1, lambda E: E.bn_stats(
                out=bn_sb[:, 0:6], in_=x2_sb[:, 0:512]))
            em.op("vector", DVE, 1, lambda E: E.bn_stats(
                out=bn_sb[:, 6:12], in_=x2_sb[:, 512:DM]))
            em.op("vector", DVE, 1, lambda E: E.engine_nop())
            bs = em.op("vector", DVE, 1, lambda E: E.bn_aggr(
                out=bn_sb[:, 12:14], in_=bn_sb[:, 0:12]))
            em.wait("scalar", DVE, bs)
            em.op("scalar", ACT, 1, lambda E: E.activation(
                out=sc_a[:, 13:14], in_=bn_sb[:, 13:14], func=AF.Ln,
                bias=ko_sb[:, 2:3]))
            ac = em.op("scalar", ACT, 1, lambda E: E.activation(
                out=sc_a[:, 14:15], in_=sc_a[:, 13:14], func=AF.Exp,
                scale=ko_sb[:, 5:6]))
            em.wait("vector", ACT, ac)
            q3v = em.op("vector", DVE, 1, lambda E: E.scalar_tensor_tensor(
                out=scm_sb[:, 3:4], in0=bn_sb[:, 12:13], scalar=-1.0,
                in1=sc_a[:, 14:15], op0=ALU.mult, op1=ALU.mult))
            xn3v = em.op("vector", DVE, 1, lambda E: E.tensor_scalar(
                out=xn3t_sb[:], in0=x2_sb[:],
                scalar1=sc_a[:, 14:15], scalar2=scm_sb[:, 3:4],
                op0=ALU.mult, op1=ALU.add))
            last["x2_rd_dve2"] = xn3v
            em.wait("tensor", DVE, xn3v)
            em.wait("tensor", ACT, last.get("psT_rd", 0))
            pe_t = 0
            for kc in range(KC):
                pe_t = em.op("tensor", PE, 1, lambda E, k=kc: E.matmul(
                    out=psT[:, k * 128:(k + 1) * 128],
                    lhsT=xn3t_sb[:, k * 128:(k + 1) * 128], rhs=idb_sb[:],
                    is_transpose=True, start=(k == 0), stop=(k == KC - 1)))
            em.wait("scalar", PE, pe_t)
            em.wait("scalar", PE, last.get("xn3f_rd", 0))
            ev = 0
            for kc in range(KC):
                ev = em.op("scalar", ACT, 1, lambda E, k=kc: E.activation(
                    out=xn3f_sb[:, k * 128:(k + 1) * 128],
                    in_=psT[:, k * 128:(k + 1) * 128], func=AF.Copy))
            last["psT_rd"] = ev
            last["xn3f_wr"] = ev


        def emit_C3(c, ga=iter(())):
            last["pending_down"] = None

            def flush_down():
                h, p, hhv = last["pending_down"]
                em.wait("tensor", DVE, hhv)
                if h == 0:
                    em.wait("tensor", DVE, last.get("x3_ffn_rd", 0))
                pe_dn = 0
                for half in range(2):
                    pe_dn = em.op("tensor", PE, 1, lambda E, h=h, hf=half,
                                  p=p: E.matmul(
                        out=psE[:, hf * 512:(hf + 1) * 512],
                        lhsT=hh_sb[:, p * 128:(p + 1) * 128],
                        rhs=wdb[:, p * DM + hf * 512:p * DM + (hf + 1) * 512],
                        start=(h == 0), stop=(h == NHC - 1)))
                last[f"hh_rd_{p}"] = pe_dn
                last[f"wffn_rd_{p}"] = pe_dn
                last["pending_down"] = None

            for hc in range(NHC):
                par = hc % 2
                next(ga, None)
                em.wait("sync", PE, last.get(f"wffn_rd_{par}", 0))
                dma(wg_s[:, hc * 1024:(hc + 1) * 1024], wgb[:, par * 1024:(par + 1) * 1024])
                dma(wu_s[:, hc * 1024:(hc + 1) * 1024], wub[:, par * 1024:(par + 1) * 1024])
                ldn = dma(wd_s[:, hc * DM:(hc + 1) * DM], wdb[:, par * DM:(par + 1) * DM])
                em.wait("tensor", LD, ldn)
                em.wait("tensor", ACT, last["xn3f_wr"])
                em.wait("tensor", DVE, last.get(f"g_dve_{par}", 0))
                em.wait("tensor", ACT, last.get(f"g_act_{par}", 0))
                gu_ps = psB if par == 0 else psF
                if hc == 0:
                    em.wait("tensor", ACT, last["dlt_ps_rd"])
                pe_g = 0
                for kc in range(KC):
                    pe_g = em.op("tensor", PE, 1, lambda E, k=kc, p=par, gp=gu_ps: E.matmul(
                        out=gp[:, 0:128],
                        lhsT=wgb[:, p * 1024 + k * 128:p * 1024 + (k + 1) * 128],
                        rhs=xn3f_sb[:, k * 128:(k + 1) * 128],
                        start=(k == 0), stop=(k == KC - 1)))
                em.wait("scalar", PE, pe_g)
                em.wait("scalar", DVE, last.get(f"sfn_rd_{par}", 0))
                sg = em.op("scalar", ACT, 1, lambda E, p=par, gp=gu_ps: E.activation(
                    out=sfn_sb[:, p * 128:(p + 1) * 128],
                    in_=gp[:, 0:128], func=AF.Silu,
                    scale=ko_sb[:, 6:7]))
                last[f"g_act_{par}"] = sg
                em.wait("tensor", ACT, sg)
                pe_u = 0
                for kc in range(KC):
                    pe_u = em.op("tensor", PE, 1, lambda E, k=kc, p=par, gp=gu_ps: E.matmul(
                        out=gp[:, 0:128],
                        lhsT=wub[:, p * 1024 + k * 128:p * 1024 + (k + 1) * 128],
                        rhs=xn3f_sb[:, k * 128:(k + 1) * 128],
                        start=(k == 0), stop=(k == KC - 1)))
                if last.get("pending_down") is not None:
                    flush_down()
                em.wait("vector", PE, pe_u)
                em.wait("vector", ACT, sg)
                em.wait("vector", PE, last.get(f"hh_rd_{par}", 0))
                hhv = em.op("vector", DVE, 1, lambda E, p=par, gp=gu_ps: E.scalar_tensor_tensor(
                    out=hh_sb[:, p * 128:(p + 1) * 128],
                    in0=gp[:, 0:128],
                    scalar=FS, in1=sfn_sb[:, p * 128:(p + 1) * 128],
                    op0=ALU.mult, op1=ALU.mult))
                last[f"sfn_rd_{par}"] = hhv
                last[f"g_dve_{par}"] = hhv
                last["pending_down"] = (hc, par, hhv)
            flush_down()
            ffn_pe = em.cnt[PE]
            last["xn3f_rd"] = ffn_pe
            em.wait("vector", PE, ffn_pe)
            em.wait("vector", OUT, last.get("x3_rd", 0))
            x3v = em.op("vector", DVE, 1, lambda E: E.scalar_tensor_tensor(
                out=x3_sb[:], in0=psE[:], scalar=FS,
                in1=x2_sb[:], op0=ALU.mult, op1=ALU.add))
            last["x3_ffn_rd"] = x3v
            last["ffn_ps_rd"] = x3v
            em.wait("sync", DVE, x3v)
            ov = em.op("sync", OUT, 16, lambda E, cc=c: E.dma_start(
                out=out_d[cc * 128:(cc + 1) * 128, :], in_=x3_sb[:]))
            last["x3_rd"] = ov


        for _ in emit_A(0):
            pass
        for c in range(NCH):
            emit_B(c)
            emit_C1(c)
            emit_C2(c)
            ga = emit_A(c + 1) if c + 1 < NCH else iter(())
            for _ in range(2):
                next(ga, None)
            emit_C3(c, ga)
            for _ in ga:
                pass

        em.wait("sync", OUT, em.cnt[OUT])

        @block.sync
        def _(E):
            em.replay("sync", E, sems)

        @block.vector
        def _(E):
            em.replay("vector", E, sems)

        @block.scalar
        def _(E):
            em.replay("scalar", E, sems)

        @block.tensor
        def _(E):
            em.replay("tensor", E, sems)

        @block.gpsimd
        def _(E):
            em.replay("gpsimd", E, sems)

    return nc


def _bf16():
    import ml_dtypes
    return ml_dtypes.bfloat16


def host_inputs(inputs, b, h=0):
    ident = np.eye(128, dtype=np.float32)
    zero = np.zeros((128, 128), dtype=np.float32)
    return {
        "idh": np.concatenate([ident if h == 0 else zero,
                               ident if h == 1 else zero], axis=1),
        "msel": np.tile(np.array([[1.0 - h, float(h)]], dtype=np.float32),
                        (128, 1)),
        "hoff": np.array([[h * 128]], dtype=np.int32),
        "xs": np.ascontiguousarray(np.concatenate(
            [inputs["L0_attn_x"][b][h * 128:],
             np.zeros((h * 128, DM), dtype=np.float32)])) if h else
        np.ascontiguousarray(inputs["L0_attn_x"][b]),
        "x": np.ascontiguousarray(inputs["L0_attn_x"][b]),
        "w_in": np.ascontiguousarray(inputs["L0_attn_W_in"]),
        "wxp": np.ascontiguousarray(inputs["L0_attn_W_xproj"]),
        "wdt": np.ascontiguousarray(inputs["L0_attn_W_dt"]),
        "wout": np.ascontiguousarray(inputs["L0_attn_W_out"]),
        "alog": np.ascontiguousarray(
            inputs["L0_attn_A_log"].reshape(NDB, 128, N).transpose(1, 0, 2).reshape(128, NDB * N)),
        "dvec": np.ascontiguousarray(inputs["L0_attn_D"].reshape(NDB, 128).T),
        "convw": np.ascontiguousarray(
            inputs["L0_attn_conv_w"].reshape(NDB, 128, 4).transpose(1, 0, 2).reshape(128, NDB * 4)),
        "convb": np.ascontiguousarray(inputs["L0_attn_conv_b"].reshape(NDB, 128).T),
        "bdt": np.ascontiguousarray(inputs["L0_attn_b_dt"].reshape(NDB, 128).T),
        "lnw": np.ascontiguousarray(inputs["L0_attn_ln_w"].reshape(1, DM)),
        "lnb": np.ascontiguousarray(inputs["L0_attn_ln_b"].reshape(1, DM)),
        "wgT": np.ascontiguousarray(inputs["L0_ffn_w_gate"].T),
        "wuT": np.ascontiguousarray(inputs["L0_ffn_w_up"].T),
        "wdT": np.ascontiguousarray(inputs["L0_ffn_w_down"].T),
        "idf": np.eye(128, dtype=np.float32),
        "idb": np.eye(128, dtype=np.float32).astype(_bf16()),
    }


def kernel(**inputs):
    nc = build_nc()
    in_maps = [host_inputs(inputs, core // 2, core % 2) for core in range(8)]
    res = run_bass_kernel_spmd(nc, in_maps, core_ids=list(range(8)))
    out = np.empty((4, L, DM), dtype=np.float32)
    nch = L // T
    for b in range(4):
        v = out[b].reshape(nch, 2, 128, DM)
        v[:, 0] = res.results[2 * b]["out"].reshape(nch, 128, DM)
        v[:, 1] = res.results[2 * b + 1]["out"].reshape(nch, 128, DM)
    return out


# version 12
# speedup vs baseline: 1.1058x; 1.0181x over previous
import contextlib
import numpy as np

import concourse.bass as bass
import concourse.mybir as mybir
from concourse.bass_utils import run_bass_kernel_spmd

ds = bass.ds

F32 = mybir.dt.float32
BF16 = mybir.dt.bfloat16
I32 = mybir.dt.int32
AF = mybir.ActivationFunctionType
ALU = mybir.AluOpType

L = 8192
DM = 1024
DI = 2048
NDB = 16
KC = 8
N = 64
G = 4
NGRP = N // G
T = 256
TT = T // 128
W = G * (T + 1)
HF = 3072
NHC = 24
EPS = 1e-5
FS = float(1.0 / 127.0)

SEMS = ("LD", "PE", "ACT", "DVE", "POOL", "OUT")
ENGS = ("sync", "vector", "scalar", "tensor", "gpsimd")


DMA_SEMS = ("LD", "OUT")


class Em:
    def __init__(self):
        self.q = {e: [] for e in ENGS}
        self.cnt = {s: 0 for s in SEMS}
        self.hw = {e: {s: 0 for s in SEMS} for e in ENGS}
        self.pend = {s: 0 for s in DMA_SEMS}

    def wait(self, eng, sem, val):
        if val is None or val <= 0:
            return
        if sem in DMA_SEMS:
            val = self.cnt[sem]
            self.pend[sem] = max(self.pend[sem], val)
        assert val <= self.cnt[sem], f"wait-on-future {sem} {val} > {self.cnt[sem]}"
        if self.hw[eng][sem] >= val:
            return
        self.hw[eng][sem] = val
        self.q[eng].append(("W", sem, val))

    ENG_SEM = {"vector": "DVE", "scalar": "ACT", "gpsimd": "POOL", "tensor": "PE"}

    def op(self, eng, sem, amt, fn):
        own = self.ENG_SEM.get(eng)
        if own is not None and eng != "tensor":
            v = self.cnt[own]
            if v > self.hw[eng][own]:
                self.hw[eng][own] = v
                self.q[eng].append(("W", own, v))
        if sem in DMA_SEMS and self.pend[sem] > self.hw[eng][sem]:
            v = self.pend[sem]
            self.hw[eng][sem] = v
            self.q[eng].append(("W", sem, v))
        self.cnt[sem] += amt
        self.q[eng].append(("I", sem, amt, fn))
        return self.cnt[sem]

    def nosem(self, eng, fn):
        self.q[eng].append(("N", fn))

    def barrier(self, *sems_now):
        snap = {s: self.cnt[s] for s in (sems_now or SEMS)}
        for e in ENGS:
            for s, v in snap.items():
                self.wait(e, s, v)

    def replay(self, eng, E, sems):
        for item in self.q[eng]:
            if item[0] == "W":
                E.wait_ge(sems[item[1]], item[2])
            elif item[0] == "I":
                item[3](E).then_inc(sems[item[1]], item[2])
            else:
                item[1](E)


def build_nc(nch=L // T, dbg=False):
    nc = bass.Bass("TRN2", target_bir_lowering=False)
    NCH = nch

    x_d = nc.dram_tensor("x", [L, DM], F32, kind="ExternalInput").ap()
    w_in_d = nc.dram_tensor("w_in", [DM, 2 * DI], F32, kind="ExternalInput").ap()
    wxp_d = nc.dram_tensor("wxp", [DI, 192], F32, kind="ExternalInput").ap()
    wdt_d = nc.dram_tensor("wdt", [64, DI], F32, kind="ExternalInput").ap()
    wout_d = nc.dram_tensor("wout", [DI, DM], F32, kind="ExternalInput").ap()
    alog_d = nc.dram_tensor("alog", [128, NDB * N], F32, kind="ExternalInput").ap()
    dvec_d = nc.dram_tensor("dvec", [128, NDB], F32, kind="ExternalInput").ap()
    convw_d = nc.dram_tensor("convw", [128, NDB * 4], F32, kind="ExternalInput").ap()
    convb_d = nc.dram_tensor("convb", [128, NDB], F32, kind="ExternalInput").ap()
    bdt_d = nc.dram_tensor("bdt", [128, NDB], F32, kind="ExternalInput").ap()
    lnw_d = nc.dram_tensor("lnw", [1, DM], F32, kind="ExternalInput").ap()
    lnb_d = nc.dram_tensor("lnb", [1, DM], F32, kind="ExternalInput").ap()
    wgT_d = nc.dram_tensor("wgT", [DM, HF], I32, kind="ExternalInput").ap()
    wuT_d = nc.dram_tensor("wuT", [DM, HF], I32, kind="ExternalInput").ap()
    wdT_d = nc.dram_tensor("wdT", [HF, DM], I32, kind="ExternalInput").ap()
    idf_d = nc.dram_tensor("idf", [128, 128], F32, kind="ExternalInput").ap()
    idb_d = nc.dram_tensor("idb", [128, 128], BF16, kind="ExternalInput").ap()
    idh_d = nc.dram_tensor("idh", [128, 256], F32, kind="ExternalInput").ap()
    msel_d = nc.dram_tensor("msel", [128, 2], F32, kind="ExternalInput").ap()
    hoff_d = nc.dram_tensor("hoff", [1, 1], I32, kind="ExternalInput").ap()
    xs_d = nc.dram_tensor("xs", [L, DM], F32, kind="ExternalInput").ap()
    out_d = nc.dram_tensor("out", [L // 2, DM], F32, kind="ExternalOutput").ap()

    wxi_s = nc.dram_tensor("wxi_s", [128, NDB * KC * 128], BF16, kind="Internal").ap()
    wz_s = nc.dram_tensor("wz_s", [128, NDB * KC * 128], BF16, kind="Internal").ap()
    wout_s = nc.dram_tensor("wout_s", [128, NDB * DM], BF16, kind="Internal").ap()
    wg_s = nc.dram_tensor("wg_s", [128, NHC * KC * 128], BF16, kind="Internal").ap()
    wu_s = nc.dram_tensor("wu_s", [128, NHC * KC * 128], BF16, kind="Internal").ap()
    wd_s = nc.dram_tensor("wd_s", [128, NHC * DM], BF16, kind="Internal").ap()
    bflat_d = nc.dram_tensor("bflat_d", [N, T], F32, kind="Internal").ap()
    u_d = nc.dram_tensor("u_d", [128, NDB * T], F32, kind="Internal").ap()
    xn2_d = nc.dram_tensor("xn2_d", [128, KC * T], BF16, kind="Internal").ap()

    dbg_outs = {}
    if dbg:
        for nm, shape in (("d_xn2", [128, KC * T]), ("d_u", [128, NDB * T]),
                          ("d_delta", [128, NDB * T]), ("d_B", [64, T]),
                          ("d_C", [64, T]), ("d_yacc", [128, TT * DI]),
                          ("d_y1", [128, NDB * 128])):
            dt = BF16 if nm == "d_xn2" else F32
            dbg_outs[nm] = nc.dram_tensor(nm, shape, dt, kind="ExternalOutput").ap()

    em = Em()
    SN = {}
    sreg = nc.sync.alloc_register("hreg")
    nc.sync.enable_hardware_checks = False
    es = contextlib.ExitStack()
    with es:
        sb = lambda name, shape, dt=F32: es.enter_context(nc.sbuf_tensor(name, shape, dt))
        idh_sb = sb("idh_sb", [128, 256])
        msel_sb = sb("msel_sb", [128, 2])
        hoff_sb = es.enter_context(nc.sbuf_tensor("hoff_sb", [1, 1], I32))
        xn2my = sb("xn2my", [128, KC * 128], BF16)
        scm_sb = sb("scm_sb", [128, 8])
        wxp_sb = sb("wxp_sb", [128, NDB * 256], BF16)
        wdt_sb = sb("wdt_sb", [128, DI], BF16)
        a_sb = sb("a_sb", [128, NDB * N])
        dvec_sb = sb("dvec_sb", [128, NDB])
        convw_sb = sb("convw_sb", [128, NDB * 4])
        convb_sb = sb("convb_sb", [128, NDB])
        bdt_sb = sb("bdt_sb", [128, NDB])
        lnwb_sb = sb("lnwb_sb", [128, 2 * DM])
        row_sb = sb("row_sb", [1, G * T])
        ones1_sb = sb("ones1_sb", [1, 128])
        idf_sb = sb("idf_sb", [128, 128])
        idb_sb = sb("idb_sb", [128, 128], BF16)
        wstream = sb("wstream", [128, 2048], BF16)
        wzbuf = sb("wzbuf", [128, 2 * 1024], BF16)
        woutb = sb("woutb", [128, 2 * 1024], BF16)
        wgb = sb("wgb", [128, 2 * 1024], BF16)
        wub = sb("wub", [128, 2 * 1024], BF16)
        wdb = sb("wdb", [128, 2 * 1024], BF16)
        xt_sb = sb("xt_sb", [128, DM])
        bn_sb = sb("bn_sb", [128, 16])
        ko_sb = sb("ko_sb", [128, 8])
        sc_d = sb("sc_d", [128, TT * 8])
        sc_a = sb("sc_a", [128, TT * 8])
        big1 = sb("big1", [128, DM])
        xn2tm = sb("xn2tm", [128, DM], BF16)
        xn2fm = sb("xn2fm", [128, KC * T], BF16)
        xib = sb("xib", [128, 3 + T])
        cvb = sb("cvb", [128, T])
        halo_sb = sb("halo_sb", [128, NDB * 3])
        u_sb = sb("u_sb", [128, NDB * T])
        ubf_sb = sb("ubf_sb", [128, 2 * T], BF16)
        delta_sb = sb("delta_sb", [128, NDB * T])
        du_sb = sb("du_sb", [128, NDB * T])
        dtrb_sb = sb("dtrb_sb", [128, T], BF16)
        Bsb = sb("Bsb", [128, T])
        Csb = sb("Csb", [128, T])
        ctm_sb = sb("ctm_sb", [128, TT * N])
        dec_sb = [sb(f"dec{i}", [128, W]) for i in range(2)]
        dbu_sb = [sb(f"dbu{i}", [128, W]) for i in range(2)]
        NH = 6
        h_sb = [sb(f"h{i}", [128, W]) for i in range(NH)]
        bbc_sb = sb("bbc_sb", [128, 2 * G * T])
        hstate_sb = sb("hstate_sb", [128, NDB * N])
        yacc_sb = sb("yacc_sb", [128, TT * DI])
        y1f_sb = sb("y1f_sb", [128, T])
        sz_sb = sb("sz_sb", [128, 128])
        y1bf_sb = sb("y1bf_sb", [128, 2 * 128], BF16)
        x2_sb = sb("x2_sb", [128, DM])
        xn3f_sb = sb("xn3f_sb", [128, KC * 128], BF16)
        hh_sb = sb("hh_sb", [128, 256], BF16)

        ssp_sb = cvb
        sfn_sb = big1
        xn3t_sb = xn2tm
        x3_sb = big1
        psA = es.enter_context(nc.psum_tensor("psA", [128, 512], F32))
        psB = es.enter_context(nc.psum_tensor("psB", [128, 512], F32))
        psC = es.enter_context(nc.psum_tensor("psC", [128, 512], F32))
        psE = es.enter_context(nc.psum_tensor("psE", [128, 1024], F32))
        psF = es.enter_context(nc.psum_tensor("psF", [128, 1024], F32))
        psT = es.enter_context(nc.psum_tensor("psT", [128, 1024], BF16))

        sems = {s: es.enter_context(nc.semaphore(s)) for s in SEMS}
        block = es.enter_context(nc.Block())

        LD, PE, ACT, DVE, POOL, OUT = SEMS

        def dma(src, dst):
            return em.op("sync", LD, 16, lambda E, s=src, d=dst: E.dma_start(out=d, in_=s))

        ldh = dma(hoff_d, hoff_sb[:])
        em.wait("sync", LD, ldh)
        em.nosem("sync", lambda E: E.reg_load(sreg, hoff_sb[0:1, 0:1]))
        em.nosem("sync", lambda E: SN.__setitem__("t", E.snap(sreg)))
        dma(idh_d, idh_sb[:])
        dma(msel_d, msel_sb[:])
        ld0 = dma(alog_d, a_sb[:])
        dma(dvec_d, dvec_sb[:])
        dma(convw_d, convw_sb[:])
        dma(convb_d, convb_sb[:])
        dma(bdt_d, bdt_sb[:])
        dma(idf_d, idf_sb[:])
        ld_idb = dma(idb_d, idb_sb[:])

        em.wait("scalar", LD, ld0)
        a_exp = em.op("scalar", ACT, 1, lambda E: E.activation(
            out=big1[:], in_=a_sb[:], func=AF.Exp))
        em.wait("vector", ACT, a_exp)
        a_done = em.op("vector", DVE, 1, lambda E: E.tensor_scalar_mul(
            out=a_sb[:], in0=big1[:], scalar1=-1.0))
        ones_done = em.op("vector", DVE, 1, lambda E: E.memset(ones1_sb[:], 1.0))
        for i, v in enumerate((1.0 + EPS, EPS * EPS, EPS, 1.0, -1.0, -0.5, FS)):
            kdone = em.op("vector", DVE, 1, lambda E, ii=i, vv=v: E.memset(
                ko_sb[:, ii:ii + 1], vv))
        em.wait("scalar", DVE, kdone)
        z_dve = em.op("vector", DVE, 1, lambda E: E.memset(halo_sb[:], 0.0))
        z_dve = em.op("vector", DVE, 1, lambda E: E.memset(hstate_sb[:], 0.0))
        for d in dec_sb:
            z_dve = em.op("vector", DVE, 1, lambda E, dd=d: E.memset(dd[:], 0.0))

        em.wait("tensor", DVE, ones_done)
        em.wait("tensor", LD, ld_idb)
        lnbc_done = 0
        for which, srcd in ((0, lnw_d), (1, lnb_d)):
            em.wait("sync", ACT, lnbc_done)
            lrow = dma(srcd, row_sb[:, 0:DM])
            em.wait("tensor", LD, lrow)
            pe_c = 0
            for half in range(2):
                pe_c = em.op("tensor", PE, 1, lambda E, hf=half: E.matmul(
                    out=psE[:, hf * 512:(hf + 1) * 512], lhsT=ones1_sb[:],
                    rhs=row_sb[:, hf * 512:hf * 512 + 512], start=True, stop=True))
            em.wait("scalar", PE, pe_c)
            for half in range(2):
                lnbc_done = em.op("scalar", ACT, 1, lambda E, w=which, hf=half:
                                  E.activation(
                    out=lnwb_sb[:, w * DM + hf * 512: w * DM + hf * 512 + 512],
                    in_=psE[:, hf * 512:(hf + 1) * 512], func=AF.Copy))

        wi32 = xt_sb[:].bitcast(I32)
        cvt = a_done
        for dst_s, c0 in ((wxi_s, 0), (wz_s, DI)):
            for kc in range(KC):
                for hh in range(2):
                    em.wait("sync", DVE, cvt)
                    lw = dma(w_in_d[kc * 128:(kc + 1) * 128,
                                    c0 + hh * DM:c0 + (hh + 1) * DM], big1[:])
                    em.wait("vector", LD, lw)
                    cvt = em.op("vector", DVE, 1, lambda E, h2=hh: E.tensor_copy(
                        out=wstream[:, h2 * DM:(h2 + 1) * DM], in_=big1[:]))
                em.wait("sync", DVE, cvt)
                dv = dst_s.rearrange("p (db k j) -> p db k j", db=NDB, k=KC)[:, :, kc, :]
                sv = wstream[:].rearrange("p (db j) -> p db j", db=NDB)
                wb = em.op("sync", LD, 16, lambda E, d=dv, s=sv: E.dma_start(out=d, in_=s))
                em.wait("vector", LD, wb)
        for kc in range(NDB):
            em.wait("sync", DVE, cvt)
            lw = dma(wout_d[kc * 128:(kc + 1) * 128, :], big1[:])
            em.wait("vector", LD, lw)
            cvt = em.op("vector", DVE, 1, lambda E: E.tensor_copy(
                out=wstream[:, 0:DM], in_=big1[:]))
            em.wait("sync", DVE, cvt)
            wb = em.op("sync", LD, 16, lambda E, k=kc: E.dma_start(
                out=wout_s[:, k * DM:(k + 1) * DM], in_=wstream[:, 0:DM]))
            em.wait("vector", LD, wb)
        zw = em.op("vector", DVE, 1, lambda E: E.memset(wxp_sb[:], 0.0))
        for kc in range(NDB):
            em.wait("sync", DVE, max(cvt, zw))
            lw = dma(wxp_d[kc * 128:(kc + 1) * 128, :], big1[:, 0:192])
            em.wait("vector", LD, lw)
            cvt = em.op("vector", DVE, 1, lambda E, k=kc: E.tensor_copy(
                out=wxp_sb[:, k * 256:k * 256 + 192], in_=big1[:, 0:192]))
        for hh in range(2):
            em.wait("sync", DVE, cvt)
            lw = em.op("sync", LD, 16, lambda E, h2=hh: E.dma_start(
                out=big1[0:64, :], in_=wdt_d[:, h2 * DM:(h2 + 1) * DM]))
            em.wait("vector", LD, lw)
            cvt = em.op("vector", DVE, 1, lambda E, h2=hh: E.tensor_copy(
                out=wdt_sb[0:64, h2 * DM:(h2 + 1) * DM], in_=big1[0:64, :]))
            em.wait("sync", DVE, cvt)
        for src_d, dst_s in ((wgT_d, wg_s), (wuT_d, wu_s)):
            for kc in range(KC):
                for hh in range(3):
                    em.wait("sync", DVE, cvt)
                    lw = dma(src_d[kc * 128:(kc + 1) * 128, hh * 1024:(hh + 1) * 1024], wi32)
                    em.wait("vector", LD, lw)
                    cvt = em.op("vector", DVE, 1, lambda E: E.tensor_copy(
                        out=wstream[:, 0:DM], in_=wi32))
                    em.wait("sync", DVE, cvt)
                    dv = dst_s.rearrange("p (hc k j) -> p hc k j", hc=NHC, k=KC)[
                        :, hh * 8:(hh + 1) * 8, kc, :]
                    sv = wstream[:, 0:DM].rearrange("p (hc j) -> p hc j", hc=8)
                    wb = em.op("sync", LD, 16, lambda E, d=dv, s=sv: E.dma_start(out=d, in_=s))
                    em.wait("vector", LD, wb)
        for hc in range(NHC):
            em.wait("sync", DVE, cvt)
            lw = dma(wdT_d[hc * 128:(hc + 1) * 128, :], wi32)
            em.wait("vector", LD, lw)
            cvt = em.op("vector", DVE, 1, lambda E: E.tensor_copy(
                out=wstream[:, 0:DM], in_=wi32))
            em.wait("sync", DVE, cvt)
            wb = em.op("sync", LD, 16, lambda E, h=hc: E.dma_start(
                out=wd_s[:, h * DM:(h + 1) * DM], in_=wstream[:, 0:DM]))
            em.wait("vector", LD, wb)

        em.barrier()

        last = {"dec_rd": [0, 0], "dbu_rd": [0, 0],
                "h_rd_pe": [0] * 6, "h_rd_act": [0] * 6}
        for i in range(6):
            last[f"h_wr_{i}"] = 0

        def emit_A(c):
            for tt in range(TT):
                em.wait("sync", DVE, last.get("xt_rd_dve", 0))
                ldx = dma(x_d[c * T + tt * 128: c * T + (tt + 1) * 128, :], xt_sb[:])
                em.wait("vector", LD, ldx)
                em.op("vector", DVE, 1, lambda E: E.bn_stats(
                    out=bn_sb[:, 0:6], in_=xt_sb[:, 0:512]))
                em.op("vector", DVE, 1, lambda E: E.bn_stats(
                    out=bn_sb[:, 6:12], in_=xt_sb[:, 512:1024]))
                em.op("vector", DVE, 1, lambda E: E.engine_nop())
                bs = em.op("vector", DVE, 1, lambda E: E.bn_aggr(
                    out=bn_sb[:, 12:14], in_=bn_sb[:, 0:12]))
                o = tt * 8
                bs = em.op("vector", DVE, 1, lambda E, o=o: E.tensor_copy(
                    out=sc_d[:, o:o + 2], in_=bn_sb[:, 12:14]))
                em.wait("scalar", DVE, bs)
                em.op("scalar", ACT, 1, lambda E, o=o: E.activation(
                    out=sc_a[:, o + 3:o + 4], in_=sc_d[:, o + 1:o + 2], func=AF.Ln,
                    scale=ko_sb[:, 0:1], bias=ko_sb[:, 1:2]))
                em.op("scalar", ACT, 1, lambda E, o=o: E.activation(
                    out=sc_a[:, o + 4:o + 5], in_=sc_d[:, o + 1:o + 2], func=AF.Ln,
                    bias=ko_sb[:, 2:3]))
                em.op("scalar", ACT, 1, lambda E, o=o: E.activation(
                    out=sc_a[:, o:o + 1], in_=sc_a[:, o + 3:o + 4], func=AF.Exp,
                    scale=ko_sb[:, 5:6]))
                ac = em.op("scalar", ACT, 1, lambda E, o=o: E.activation(
                    out=sc_a[:, o + 1:o + 2], in_=sc_a[:, o + 4:o + 5], func=AF.Exp,
                    scale=ko_sb[:, 5:6]))
                em.wait("vector", ACT, ac)
                em.op("vector", DVE, 1, lambda E, o=o: E.scalar_tensor_tensor(
                    out=sc_d[:, o + 2:o + 3], in0=sc_d[:, o:o + 1], scalar=-1.0,
                    in1=sc_a[:, o:o + 1], op0=ALU.mult, op1=ALU.mult))
                em.op("vector", DVE, 1, lambda E, o=o: E.scalar_tensor_tensor(
                    out=sc_d[:, o + 3:o + 4], in0=sc_d[:, o:o + 1], scalar=-1.0,
                    in1=sc_a[:, o + 1:o + 2], op0=ALU.mult, op1=ALU.mult))
                xrd = em.op("vector", DVE, 1, lambda E, o=o: E.scalar_tensor_tensor(
                    out=xt_sb[:], in0=xt_sb[:], scalar=sc_a[:, o:o + 1],
                    in1=lnwb_sb[:, 0:DM], op0=ALU.mult, op1=ALU.mult))
                em.op("vector", DVE, 1, lambda E, o=o: E.scalar_tensor_tensor(
                    out=xt_sb[:], in0=lnwb_sb[:, 0:DM], scalar=sc_d[:, o + 2:o + 3],
                    in1=xt_sb[:], op0=ALU.mult, op1=ALU.add))
                dv = em.op("vector", DVE, 1, lambda E: E.tensor_add(
                    out=xn2tm[:], in0=xt_sb[:], in1=lnwb_sb[:, DM:2 * DM]))
                last["xt_rd_dve"] = dv
                em.wait("tensor", DVE, dv)
                em.wait("tensor", ACT, last.get("psT_rd", 0))
                pe_t = 0
                for kc in range(KC):
                    pe_t = em.op("tensor", PE, 1, lambda E, k=kc: E.matmul(
                        out=psT[:, k * 128:(k + 1) * 128],
                        lhsT=xn2tm[:, k * 128:(k + 1) * 128], rhs=idb_sb[:],
                        is_transpose=True, start=(k == 0), stop=(k == KC - 1)))
                em.wait("scalar", PE, pe_t)
                ev = 0
                for kc in range(KC):
                    ev = em.op("scalar", ACT, 1, lambda E, k=kc, t=tt: E.activation(
                        out=xn2fm[:, k * T + t * 128:k * T + t * 128 + 128],
                        in_=psT[:, k * 128:(k + 1) * 128], func=AF.Copy))
                last["xn2fm_wr"] = ev
                last["psT_rd"] = ev
                yield

            last["pending_xdbl"] = None

            def flush_xdbl():
                d, p, ub = last["pending_xdbl"]
                em.wait("tensor", DVE, ub)
                if d == 0:
                    em.wait("tensor", ACT, last.get("xdbl_ps_rd", 0))
                    em.wait("tensor", ACT, last.get("z_ps_rd", 0))
                    em.wait("tensor", DVE, last.get("rt_rd_B", 0))
                em.op("tensor", PE, 1, lambda E, d=d, p=p: E.matmul(
                    out=psC[:, 0:T], lhsT=wxp_sb[:, d * 256:d * 256 + 128],
                    rhs=ubf_sb[:, p * T:(p + 1) * T], start=(d == 0), stop=False))
                pe_d = em.op("tensor", PE, 1, lambda E, d=d, p=p: E.matmul(
                    out=psC[:, T:2 * T], lhsT=wxp_sb[:, d * 256 + 128:d * 256 + 256],
                    rhs=ubf_sb[:, p * T:(p + 1) * T], start=False, stop=(d == NDB - 1)))
                last[f"ubf_rd_{p}"] = pe_d
                last["pending_xdbl"] = None

            for db in range(NDB):
                par = db % 2
                em.wait("sync", PE, last.get(f"wxi_rd_{par}", 0))
                lw = dma(wxi_s[:, db * 1024:(db + 1) * 1024],
                         wstream[:, par * 1024:(par + 1) * 1024])
                em.wait("tensor", LD, lw)
                em.wait("tensor", ACT, last["xn2fm_wr"])
                em.wait("tensor", ACT, last.get("xi_ps_rd", 0))
                em.wait("tensor", DVE, last.get("g_dve_1", 0))
                em.wait("tensor", DVE, last.get("yf_ps_rd", 0))
                if last.get("pending_xdbl") is not None:
                    flush_xdbl()
                pe_x = 0
                for kc in range(KC):
                    pe_x = em.op("tensor", PE, 1, lambda E, k=kc, p=par: E.matmul(
                        out=psA[:, 0:T],
                        lhsT=wstream[:, p * 1024 + k * 128:p * 1024 + (k + 1) * 128],
                        rhs=xn2fm[:, k * T:(k + 1) * T],
                        start=(k == 0), stop=(k == KC - 1)))
                last[f"wxi_rd_{par}"] = pe_x
                em.wait("scalar", PE, pe_x)
                em.wait("scalar", DVE, last.get("xib_rd", 0))
                em.op("scalar", ACT, 1, lambda E, d=db: E.activation(
                    out=xib[:, 0:3], in_=halo_sb[:, d * 3:(d + 1) * 3], func=AF.Copy))
                ev = em.op("scalar", ACT, 1, lambda E: E.activation(
                    out=xib[:, 3:3 + T], in_=psA[:, 0:T], func=AF.Copy))
                last["xi_ps_rd"] = ev
                em.wait("vector", ACT, ev)
                em.wait("vector", ACT, last.get("cvb_rd", 0))
                cv = em.op("vector", DVE, 1, lambda E, d=db: E.tensor_scalar(
                    out=cvb[:], in0=xib[:, 0:T], scalar1=convw_sb[:, d * 4:d * 4 + 1],
                    scalar2=None, op0=ALU.mult))
                for k in range(1, 4):
                    cv = em.op("vector", DVE, 1, lambda E, d=db, kk=k: E.scalar_tensor_tensor(
                        out=cvb[:], in0=xib[:, kk:kk + T],
                        scalar=convw_sb[:, d * 4 + kk:d * 4 + kk + 1],
                        in1=cvb[:], op0=ALU.mult, op1=ALU.add))
                last["xib_rd"] = cv
                em.wait("scalar", DVE, cv)
                uu = em.op("scalar", ACT, 1, lambda E, d=db: E.activation(
                    out=u_sb[:, d * T:(d + 1) * T], in_=cvb[:], func=AF.Silu,
                    bias=convb_sb[:, d:d + 1]))
                last["cvb_rd"] = uu
                em.op("scalar", ACT, 1, lambda E, d=db: E.activation(
                    out=halo_sb[:, d * 3:(d + 1) * 3], in_=xib[:, T:T + 3], func=AF.Copy))
                em.wait("vector", ACT, uu)
                em.wait("vector", PE, last.get(f"ubf_rd_{par}", 0))
                ub = em.op("vector", DVE, 1, lambda E, d=db, p=par: E.tensor_copy(
                    out=ubf_sb[:, p * T:(p + 1) * T], in_=u_sb[:, d * T:(d + 1) * T]))
                last["pending_xdbl"] = (db, par, ub)
                yield

            flush_xdbl()
            last["u_wr"] = uu
            xdbl_pe = em.cnt[PE]

            em.wait("scalar", PE, xdbl_pe)
            em.wait("scalar", LD, 1)
            ev_dtr = em.op("scalar", ACT, 1, lambda E: E.activation(
                out=dtrb_sb[0:64, :], in_=psC[0:64, 0:T], func=AF.Copy))
            last["ev_dtr"] = ev_dtr
            ev_B = em.op("scalar", ACT, 1, lambda E: E.activation(
                out=Bsb[64:128, :], in_=psC[64:128, 0:T], func=AF.Copy))
            ev_C = em.op("scalar", ACT, 1, lambda E: E.activation(
                out=Csb[0:64, :], in_=psC[0:64, T:2 * T], func=AF.Copy))
            last["xdbl_ps_rd"] = ev_C
            em.wait("sync", ACT, ev_B)
            bf1 = em.op("sync", LD, 16, lambda E: E.dma_start(
                out=bflat_d, in_=Bsb[64:128, :]))
            last["bf1"] = bf1
            em.wait("tensor", ACT, ev_C)
            em.wait("tensor", ACT, last["xi_ps_rd"])
            em.wait("tensor", DVE, last.get("g_dve_1", 0))
            em.wait("tensor", DVE, last.get("yf_ps_rd", 0))
            pe_c2 = 0
            for tt in range(TT):
                pe_c2 = em.op("tensor", PE, 1, lambda E, t=tt: E.matmul(
                    out=psA[:, 256 + t * 64:256 + (t + 1) * 64],
                    lhsT=Csb[0:64, t * 128:(t + 1) * 128], rhs=idf_sb[0:64, 0:64],
                    is_transpose=True, start=(t == 0), stop=(t == TT - 1)))
            em.wait("scalar", PE, pe_c2)
            ev_ctm = em.op("scalar", ACT, 1, lambda E: E.activation(
                out=ctm_sb[:], in_=psA[:, 256:256 + TT * N], func=AF.Copy))
            last["ev_ctm"] = ev_ctm

            em.wait("sync", ACT, uu)
            em.wait("sync", DVE, last.get("yf_ps_rd", 0))
            bo2 = em.op("sync", LD, 16, lambda E: E.dma_start(out=u_d, in_=u_sb[:]))
            em.wait("sync", ACT, last["xn2fm_wr"])
            bo1 = em.op("sync", LD, 16, lambda E: E.dma_start(out=xn2_d, in_=xn2fm[:]))
            em.wait("sync", LD, bo1)
            em.wait("sync", PE, max(last.get("wz_rd_0", 0), last.get("wz_rd_1", 0)))
            em.op("sync", LD, 16, lambda E: E.dma_start(
                out=xn2my[:].rearrange("p (k t) -> p k t", k=KC),
                in_=xn2_d.rearrange("p (k t) -> p k t", k=KC)[:, :, ds(SN["t"], 128)]))

            if dbg and c == 0:
                em.wait("sync", ACT, last["xn2fm_wr"])
                em.op("sync", OUT, 16, lambda E: E.dma_start(
                    out=dbg_outs["d_xn2"], in_=xn2fm[:]))
                em.wait("sync", ACT, last["u_wr"])
                em.op("sync", OUT, 16, lambda E: E.dma_start(out=dbg_outs["d_u"], in_=u_sb[:]))
                em.wait("sync", ACT, ev_B)
                em.op("sync", OUT, 16, lambda E: E.dma_start(out=dbg_outs["d_B"], in_=Bsb[64:128, :]))
                em.wait("sync", ACT, ev_C)
                em.op("sync", OUT, 16, lambda E: E.dma_start(out=dbg_outs["d_C"], in_=Csb[0:64, :]))


        def emit_B(c):
            for db in range(NDB):
                em.wait("tensor", ACT, last["ev_dtr"])
                em.wait("tensor", ACT, last.get("dlt_ps_rd", 0))
                if db == 0:
                    em.wait("tensor", ACT, last.get("bbc_ps_rd", 0))
                pe_dl = em.op("tensor", PE, 1, lambda E, d=db: E.matmul(
                    out=psF[:, 512:512 + T], lhsT=wdt_sb[0:64, d * 128:(d + 1) * 128],
                    rhs=dtrb_sb[0:64, :], start=True, stop=True))
                em.wait("scalar", PE, pe_dl)
                em.wait("scalar", DVE, last.get("ssp_rd", 0))
                em.op("scalar", ACT, 1, lambda E, d=db: E.activation(
                    out=ssp_sb[:], in_=psF[:, 512:512 + T], func=AF.Abs,
                    bias=bdt_sb[:, d:d + 1]))
                em.op("scalar", ACT, 1, lambda E: E.activation(
                    out=ssp_sb[:], in_=ssp_sb[:], func=AF.Exp, scale=ko_sb[:, 4:5]))
                em.op("scalar", ACT, 1, lambda E: E.activation(
                    out=ssp_sb[:], in_=ssp_sb[:], func=AF.Ln, bias=ko_sb[:, 3:4]))
                em.wait("scalar", DVE, last.get("sz_rd", 0))
                em.wait("scalar", OUT, 1)
                a4 = em.op("scalar", ACT, 1, lambda E, d=db: E.activation(
                    out=y1f_sb[:], in_=psF[:, 512:512 + T], func=AF.Relu,
                    bias=bdt_sb[:, d:d + 1]))
                last["dlt_ps_rd"] = a4
                em.wait("vector", ACT, a4)
                em.wait("vector", ACT, last.get(f"delta_rd_{db}", 0))
                ddl = em.op("vector", DVE, 1, lambda E, d=db: E.tensor_add(
                    out=delta_sb[:, d * T:(d + 1) * T], in0=y1f_sb[:], in1=ssp_sb[:]))
                last[f"delta_wr_{db}"] = ddl
                last["ssp_rd"] = ddl
                em.wait("vector", ACT, last["u_wr"])
                em.op("vector", DVE, 1, lambda E, d=db: E.tensor_mul(
                    out=du_sb[:, d * T:(d + 1) * T], in0=delta_sb[:, d * T:(d + 1) * T],
                    in1=u_sb[:, d * T:(d + 1) * T]))
                yield
            if dbg and c == 0:
                em.wait("sync", DVE, em.cnt[DVE])
                em.op("sync", OUT, 16, lambda E: E.dma_start(
                    out=dbg_outs["d_delta"], in_=delta_sb[:]))
                em.wait("sync", OUT, 1)
            em.wait("sync", DVE, em.cnt[DVE])
            em.wait("sync", LD, 1)
            em.op("sync", LD, 16, lambda E: E.dma_start(
                out=u_sb[:, 0:NDB * 128].rearrange("p (db t) -> p db t", db=NDB),
                in_=u_d.rearrange("p (db t) -> p db t", db=NDB)[:, :, ds(SN["t"], 128)]))
            yield
            em.wait("vector", PE, last.get("yacc_rd_pe", 0))
            em.op("vector", DVE, 1, lambda E: E.memset(yacc_sb[:], 0.0))
            em.wait("tensor", ACT, last["ev_ctm"])
            em.wait("tensor", ACT, last["xdbl_ps_rd"])
            em.wait("tensor", ACT, last["dlt_ps_rd"])
            for grp in range(NGRP):
                bpar = grp % 2
                em.wait("sync", LD, last["bf1"])
                em.wait("sync", PE, last.get("row_rd_pe", 0))
                bfg = em.op("sync", LD, 16, lambda E, g=grp: E.dma_start(
                    out=row_sb[0:1, :],
                    in_=bflat_d[g * G:(g + 1) * G, :].rearrange("n t -> (n t)")[None, :]))
                em.wait("tensor", LD, bfg)
                ev_b = 0
                for half in range(2):
                    em.wait("tensor", ACT, last.get("bbc_ps_rd", 0))
                    pe_b = em.op("tensor", PE, 1, lambda E, hf=half: E.matmul(
                        out=psF[:, 512:1024], lhsT=ones1_sb[:],
                        rhs=row_sb[:, hf * 512:(hf + 1) * 512],
                        start=True, stop=True))
                    last["row_rd_pe"] = pe_b
                    em.wait("scalar", PE, pe_b)
                    if half == 0:
                        em.wait("scalar", DVE, last.get(f"bbc_rd_pool_{bpar}", 0))
                    ev_b = em.op("scalar", ACT, 1, lambda E, hf=half, bp=bpar: E.activation(
                        out=bbc_sb[:, bp * 1024 + hf * 512:bp * 1024 + (hf + 1) * 512],
                        in_=psF[:, 512:1024], func=AF.Copy))
                    last["bbc_ps_rd"] = ev_b

                for db in range(NDB):
                    par = db % 2
                    hq = (grp * NDB + db) % NH
                    dec, dbuf = dec_sb[par], dbu_sb[par]
                    em.wait("scalar", DVE, max(last["dec_rd"][par],
                                               last.get(f"delta_wr_{db}", 0)))
                    ac_d = 0
                    for g in range(G):
                        nn = grp * G + g
                        ac_d = em.op("scalar", ACT, 1, lambda E, d=db, n=nn, g=g, dd=dec:
                                     E.activation(
                            out=dd[:, g * (T + 1) + 1:(g + 1) * (T + 1)],
                            in_=delta_sb[:, d * T:(d + 1) * T], func=AF.Exp,
                            scale=a_sb[:, d * N + n:d * N + n + 1]))
                    last[f"delta_rd_{db}"] = ac_d
                    if last.get("pending_gather") is not None:
                        pd, pg0, phq, psc = last["pending_gather"]
                        em.wait("scalar", DVE, psc)
                        gat = em.op("scalar", ACT, 1, lambda E, d=pd, g0=pg0,
                                    hh=h_sb[phq]: E.activation(
                            out=hstate_sb[:, d * N + g0 * G:d * N + (g0 + 1) * G],
                            in_=hh[:].rearrange("p (g t) -> p g t", g=G)[:, :, T],
                            func=AF.Copy))
                        last["h_rd_act"][phq] = gat
                        last["pending_gather"] = None
                    dbu_pay = dbuf[:].rearrange("p (g t) -> p g t", g=G)[:, :, 1:]
                    em.wait("vector", ACT, ev_b)
                    em.wait("vector", ACT, ac_d)
                    em.wait("vector", ACT, last["h_rd_act"][hq])
                    po = em.op("vector", DVE, 1, lambda E, d=db, dp=dbu_pay, bp=bpar:
                               E.tensor_tensor(
                        out=dp,
                        in0=du_sb[:, d * T:(d + 1) * T].unsqueeze(1).broadcast_to(
                            [128, G, T]),
                        in1=bbc_sb[:, bp * 1024:(bp + 1) * 1024].rearrange(
                            "p (g t) -> p g t", g=G),
                        op=ALU.mult))
                    last[f"bbc_rd_pool_{bpar}"] = po
                    bnd = em.op("vector", DVE, 1, lambda E, d=db, g0=grp, dd=dbuf:
                                E.tensor_copy(
                        out=dd[:].rearrange("p (g t) -> p g t", g=G)[:, :, 0],
                        in_=hstate_sb[:, d * N + g0 * G:d * N + (g0 + 1) * G]))
                    em.wait("vector", PE, last["h_rd_pe"][hq])
                    em.wait("vector", ACT, last["h_rd_act"][hq])
                    sc_v = em.op("vector", DVE, 1, lambda E, dd=dec, bb=dbuf, hh=h_sb[hq]:
                                 E.tensor_tensor_scan(
                        out=hh[:], data0=dd[:], data1=bb[:], initial=0.0,
                        op0=ALU.mult, op1=ALU.add))
                    last["dec_rd"][par] = sc_v
                    last["dbu_rd"][par] = sc_v
                    last[f"h_wr_{hq}"] = sc_v
                    last["pending_gather"] = (db, grp, hq, sc_v)
                    yield

                    if db % 4 == 3:
                        q0 = db - 3
                        hqs = [(grp * NDB + q0 + q) % NH for q in range(4)]
                        for g in range(G):
                            nn = grp * G + g
                            for tt in range(TT):
                                use_a = (g * TT + tt) % 2 == 0
                                rt = psA if use_a else psC
                                key = "rt_rd_A" if use_a else "rt_rd_B"
                                em.wait("tensor", DVE, last.get(key, 0))
                                pe_r = 0
                                for q in range(4):
                                    em.wait("tensor", DVE, last[f"h_wr_{hqs[q]}"])
                                    pe_r = em.op("tensor", PE, 1,
                                                 lambda E, qq=hqs[q], qi=q, gg=g, t=tt, r=rt: E.matmul(
                                        out=r[:, qi * 128:(qi + 1) * 128],
                                        lhsT=h_sb[qq][:, gg * (T + 1) + 1 + t * 128:
                                                      gg * (T + 1) + 1 + (t + 1) * 128],
                                        rhs=idf_sb[:], is_transpose=True,
                                        start=(qi == 0), stop=(qi == 3)))
                                for q in range(4):
                                    last["h_rd_pe"][hqs[q]] = pe_r
                                em.wait("vector", PE, pe_r)
                                st_v = em.op("vector", DVE, 1,
                                             lambda E, n=nn, t=tt, r=rt, qb=q0:
                                             E.scalar_tensor_tensor(
                                    out=yacc_sb[:, t * DI + qb * 128:t * DI + qb * 128 + 512],
                                    in0=r[:, 0:512],
                                    scalar=ctm_sb[:, t * N + n:t * N + n + 1],
                                    in1=yacc_sb[:, t * DI + qb * 128:t * DI + qb * 128 + 512],
                                    op0=ALU.mult, op1=ALU.add))
                                last[key] = st_v
            if last.get("pending_gather") is not None:
                pd, pg0, phq, psc = last["pending_gather"]
                em.wait("scalar", DVE, psc)
                gat = em.op("scalar", ACT, 1, lambda E, d=pd, g0=pg0,
                            hh=h_sb[phq]: E.activation(
                    out=hstate_sb[:, d * N + g0 * G:d * N + (g0 + 1) * G],
                    in_=hh[:].rearrange("p (g t) -> p g t", g=G)[:, :, T],
                    func=AF.Copy))
                last["h_rd_act"][phq] = gat
                last["pending_gather"] = None
            yacc_dve = em.cnt[DVE]
            last["yacc_dve"] = yacc_dve

            if dbg and c == 0:
                em.wait("sync", DVE, yacc_dve)
                em.op("sync", OUT, 16, lambda E: E.dma_start(
                    out=dbg_outs["d_yacc"], in_=yacc_sb[:]))


        def emit_C1(c):
            last["pending_wout"] = None

            def flush_wout():
                d, p, y1b, lwo = last["pending_wout"]
                em.wait("tensor", DVE, y1b)
                em.wait("tensor", LD, lwo)
                if d == 0:
                    em.wait("tensor", DVE, last.get("ffn_ps_rd", 0))
                    em.wait("tensor", ACT, last.get("bbc_ps_rd", 0))
                pe_o = 0
                for half in range(2):
                    pe_o = em.op("tensor", PE, 1, lambda E, d=d, hf=half,
                                 p=p: E.matmul(
                        out=psE[:, hf * 512:(hf + 1) * 512],
                        lhsT=y1bf_sb[:, p * 128:(p + 1) * 128],
                        rhs=woutb[:, p * DM + hf * 512:p * DM + (hf + 1) * 512],
                        start=(d == 0), stop=(d == NDB - 1)))
                last[f"wout_rd_{p}"] = pe_o
                last[f"y1bf_rd_{p}"] = pe_o
                last["pending_wout"] = None

            for db in range(NDB):
                par = db % 2
                em.wait("sync", PE, last.get(f"wout_rd_{par}", 0))
                lwo = dma(wout_s[:, db * DM:(db + 1) * DM],
                          woutb[:, par * DM:(par + 1) * DM])
                em.wait("sync", PE, last.get(f"wz_rd_{par}", 0))
                lwz = dma(wz_s[:, db * 1024:(db + 1) * 1024],
                          wzbuf[:, par * 1024:(par + 1) * 1024])
                em.wait("tensor", DVE, last["yacc_dve"])
                em.wait("tensor", DVE, last.get("yf_ps_rd", 0))
                pe_y = 0
                for t in range(TT):
                    pe_y = em.op("tensor", PE, 1, lambda E, d=db, t=t: E.matmul(
                        out=psA[:, 0:128],
                        lhsT=yacc_sb[:, t * DI + d * 128:t * DI + (d + 1) * 128],
                        rhs=idh_sb[:, t * 128:(t + 1) * 128],
                        start=(t == 0), stop=(t == TT - 1)))
                last["yacc_rd_pe"] = pe_y
                em.wait("vector", PE, pe_y)
                em.wait("vector", OUT, last.get("y1f_dump", 0))
                em.wait("vector", LD, 1)
                y1v = em.op("vector", DVE, 1, lambda E, d=db: E.scalar_tensor_tensor(
                    out=y1f_sb[:, 0:128], in0=u_sb[:, d * 128:(d + 1) * 128],
                    scalar=dvec_sb[:, d:d + 1], in1=psA[:, 0:128],
                    op0=ALU.mult, op1=ALU.add))
                last["yf_ps_rd"] = y1v
                em.wait("tensor", LD, lwz)
                if db == 0:
                    em.wait("tensor", ACT, last["xdbl_ps_rd"])
                    em.wait("tensor", DVE, last.get("rt_rd_B", 0))
                em.wait("tensor", ACT, last.get("z_ps_rd", 0))
                pe_z = 0
                for kc in range(KC):
                    pe_z = em.op("tensor", PE, 1, lambda E, k=kc, p=par: E.matmul(
                        out=psC[:, 0:128],
                        lhsT=wzbuf[:, p * 1024 + k * 128:p * 1024 + (k + 1) * 128],
                        rhs=xn2my[:, k * 128:(k + 1) * 128],
                        start=(k == 0), stop=(k == KC - 1)))
                last[f"wz_rd_{par}"] = pe_z
                em.wait("scalar", PE, pe_z)
                em.wait("scalar", DVE, last.get("sz_rd", 0))
                szv = em.op("scalar", ACT, 1, lambda E: E.activation(
                    out=sz_sb[:], in_=psC[:, 0:128], func=AF.Silu))
                last["z_ps_rd"] = szv
                em.wait("vector", ACT, szv)
                em.wait("vector", PE, last.get(f"y1bf_rd_{par}", 0))
                y1b = em.op("vector", DVE, 1, lambda E, p=par: E.tensor_mul(
                    out=y1bf_sb[:, p * 128:(p + 1) * 128], in0=y1f_sb[:, 0:128],
                    in1=sz_sb[:]))
                last["sz_rd"] = y1b
                if last.get("pending_wout") is not None:
                    flush_wout()
                last["pending_wout"] = (db, par, y1b, lwo)
                if dbg and c == 0:
                    em.wait("sync", DVE, y1b)
                    last["y1f_dump"] = em.op("sync", OUT, 16, lambda E, d=db: E.dma_start(
                        out=dbg_outs["d_y1"][:, d * 128:(d + 1) * 128],
                        in_=y1f_sb[:, 0:128]))
            flush_wout()
            last["om_pe"] = em.cnt[PE]

        def emit_C2(c):
            sm = em.op("vector", DVE, 1, lambda E: E.tensor_scalar(
                out=scm_sb[:, 4:5], in0=sc_a[:, 9:10], scalar1=msel_sb[:, 1:2],
                scalar2=None, op0=ALU.mult))
            sm = em.op("vector", DVE, 1, lambda E: E.scalar_tensor_tensor(
                out=scm_sb[:, 0:1], in0=sc_a[:, 1:2], scalar=msel_sb[:, 0:1],
                in1=scm_sb[:, 4:5], op0=ALU.mult, op1=ALU.add))
            sm = em.op("vector", DVE, 1, lambda E: E.tensor_scalar(
                out=scm_sb[:, 5:6], in0=sc_d[:, 11:12], scalar1=msel_sb[:, 1:2],
                scalar2=None, op0=ALU.mult))
            sm = em.op("vector", DVE, 1, lambda E: E.scalar_tensor_tensor(
                out=scm_sb[:, 1:2], in0=sc_d[:, 3:4], scalar=msel_sb[:, 0:1],
                in1=scm_sb[:, 5:6], op0=ALU.mult, op1=ALU.add))

            em.wait("sync", DVE, last.get("xt_rd_dve", 0))
            em.wait("sync", ACT, last.get("xt_rd_act", 0))
            ldx = dma(xs_d[c * T:c * T + 128, :], xt_sb[:])
            em.wait("vector", LD, ldx)
            em.wait("vector", PE, last["om_pe"])
            xv = em.op("vector", DVE, 1, lambda E: E.tensor_add(
                out=big1[:], in0=xt_sb[:], in1=psE[:]))
            last["om_ps_rd"] = xv
            xv = em.op("vector", DVE, 1, lambda E: E.tensor_scalar(
                out=xt_sb[:], in0=xt_sb[:], scalar1=scm_sb[:, 0:1],
                scalar2=scm_sb[:, 1:2], op0=ALU.mult, op1=ALU.add))
            xv = em.op("vector", DVE, 1, lambda E: E.tensor_add(
                out=x2_sb[:], in0=big1[:], in1=xt_sb[:]))
            last["xt_rd_dve"] = xv
            em.op("vector", DVE, 1, lambda E: E.bn_stats(
                out=bn_sb[:, 0:6], in_=x2_sb[:, 0:512]))
            em.op("vector", DVE, 1, lambda E: E.bn_stats(
                out=bn_sb[:, 6:12], in_=x2_sb[:, 512:DM]))
            em.op("vector", DVE, 1, lambda E: E.engine_nop())
            bs = em.op("vector", DVE, 1, lambda E: E.bn_aggr(
                out=bn_sb[:, 12:14], in_=bn_sb[:, 0:12]))
            em.wait("scalar", DVE, bs)
            em.op("scalar", ACT, 1, lambda E: E.activation(
                out=sc_a[:, 13:14], in_=bn_sb[:, 13:14], func=AF.Ln,
                bias=ko_sb[:, 2:3]))
            ac = em.op("scalar", ACT, 1, lambda E: E.activation(
                out=sc_a[:, 14:15], in_=sc_a[:, 13:14], func=AF.Exp,
                scale=ko_sb[:, 5:6]))
            em.wait("vector", ACT, ac)
            q3v = em.op("vector", DVE, 1, lambda E: E.scalar_tensor_tensor(
                out=scm_sb[:, 3:4], in0=bn_sb[:, 12:13], scalar=-1.0,
                in1=sc_a[:, 14:15], op0=ALU.mult, op1=ALU.mult))
            xn3v = em.op("vector", DVE, 1, lambda E: E.tensor_scalar(
                out=xn3t_sb[:], in0=x2_sb[:],
                scalar1=sc_a[:, 14:15], scalar2=scm_sb[:, 3:4],
                op0=ALU.mult, op1=ALU.add))
            last["x2_rd_dve2"] = xn3v
            em.wait("tensor", DVE, xn3v)
            em.wait("tensor", ACT, last.get("psT_rd", 0))
            pe_t = 0
            for kc in range(KC):
                pe_t = em.op("tensor", PE, 1, lambda E, k=kc: E.matmul(
                    out=psT[:, k * 128:(k + 1) * 128],
                    lhsT=xn3t_sb[:, k * 128:(k + 1) * 128], rhs=idb_sb[:],
                    is_transpose=True, start=(k == 0), stop=(k == KC - 1)))
            em.wait("scalar", PE, pe_t)
            em.wait("scalar", PE, last.get("xn3f_rd", 0))
            ev = 0
            for kc in range(KC):
                ev = em.op("scalar", ACT, 1, lambda E, k=kc: E.activation(
                    out=xn3f_sb[:, k * 128:(k + 1) * 128],
                    in_=psT[:, k * 128:(k + 1) * 128], func=AF.Copy))
            last["psT_rd"] = ev
            last["xn3f_wr"] = ev


        def emit_C3(c, ga=iter(())):
            last["pending_down"] = None

            def flush_down():
                h, p, hhv = last["pending_down"]
                em.wait("tensor", DVE, hhv)
                if h == 0:
                    em.wait("tensor", DVE, last.get("x3_ffn_rd", 0))
                pe_dn = 0
                for half in range(2):
                    pe_dn = em.op("tensor", PE, 1, lambda E, h=h, hf=half,
                                  p=p: E.matmul(
                        out=psE[:, hf * 512:(hf + 1) * 512],
                        lhsT=hh_sb[:, p * 128:(p + 1) * 128],
                        rhs=wdb[:, p * DM + hf * 512:p * DM + (hf + 1) * 512],
                        start=(h == 0), stop=(h == NHC - 1)))
                last[f"hh_rd_{p}"] = pe_dn
                last[f"wffn_rd_{p}"] = pe_dn
                last["pending_down"] = None

            for hc in range(NHC):
                par = hc % 2
                for _ in range(3):
                    next(ga, None)
                em.wait("sync", PE, last.get(f"wffn_rd_{par}", 0))
                dma(wg_s[:, hc * 1024:(hc + 1) * 1024], wgb[:, par * 1024:(par + 1) * 1024])
                dma(wu_s[:, hc * 1024:(hc + 1) * 1024], wub[:, par * 1024:(par + 1) * 1024])
                ldn = dma(wd_s[:, hc * DM:(hc + 1) * DM], wdb[:, par * DM:(par + 1) * DM])
                em.wait("tensor", LD, ldn)
                em.wait("tensor", ACT, last["xn3f_wr"])
                em.wait("tensor", DVE, last.get(f"g_dve_{par}", 0))
                em.wait("tensor", ACT, last.get(f"g_act_{par}", 0))
                gu_ps = psB if par == 0 else psF
                if hc == 0:
                    em.wait("tensor", ACT, last["dlt_ps_rd"])
                pe_g = 0
                for kc in range(KC):
                    pe_g = em.op("tensor", PE, 1, lambda E, k=kc, p=par, gp=gu_ps: E.matmul(
                        out=gp[:, 0:128],
                        lhsT=wgb[:, p * 1024 + k * 128:p * 1024 + (k + 1) * 128],
                        rhs=xn3f_sb[:, k * 128:(k + 1) * 128],
                        start=(k == 0), stop=(k == KC - 1)))
                em.wait("scalar", PE, pe_g)
                em.wait("scalar", DVE, last.get(f"sfn_rd_{par}", 0))
                sg = em.op("scalar", ACT, 1, lambda E, p=par, gp=gu_ps: E.activation(
                    out=sfn_sb[:, p * 128:(p + 1) * 128],
                    in_=gp[:, 0:128], func=AF.Silu,
                    scale=ko_sb[:, 6:7]))
                last[f"g_act_{par}"] = sg
                em.wait("tensor", ACT, sg)
                pe_u = 0
                for kc in range(KC):
                    pe_u = em.op("tensor", PE, 1, lambda E, k=kc, p=par, gp=gu_ps: E.matmul(
                        out=gp[:, 0:128],
                        lhsT=wub[:, p * 1024 + k * 128:p * 1024 + (k + 1) * 128],
                        rhs=xn3f_sb[:, k * 128:(k + 1) * 128],
                        start=(k == 0), stop=(k == KC - 1)))
                if last.get("pending_down") is not None:
                    flush_down()
                em.wait("vector", PE, pe_u)
                em.wait("vector", ACT, sg)
                em.wait("vector", PE, last.get(f"hh_rd_{par}", 0))
                hhv = em.op("vector", DVE, 1, lambda E, p=par, gp=gu_ps: E.scalar_tensor_tensor(
                    out=hh_sb[:, p * 128:(p + 1) * 128],
                    in0=gp[:, 0:128],
                    scalar=FS, in1=sfn_sb[:, p * 128:(p + 1) * 128],
                    op0=ALU.mult, op1=ALU.mult))
                last[f"sfn_rd_{par}"] = hhv
                last[f"g_dve_{par}"] = hhv
                last["pending_down"] = (hc, par, hhv)
            flush_down()
            ffn_pe = em.cnt[PE]
            last["xn3f_rd"] = ffn_pe
            em.wait("vector", PE, ffn_pe)
            em.wait("vector", OUT, last.get("x3_rd", 0))
            x3v = em.op("vector", DVE, 1, lambda E: E.scalar_tensor_tensor(
                out=x3_sb[:], in0=psE[:], scalar=FS,
                in1=x2_sb[:], op0=ALU.mult, op1=ALU.add))
            last["x3_ffn_rd"] = x3v
            last["ffn_ps_rd"] = x3v
            em.wait("sync", DVE, x3v)
            ov = em.op("sync", OUT, 16, lambda E, cc=c: E.dma_start(
                out=out_d[cc * 128:(cc + 1) * 128, :], in_=x3_sb[:]))
            last["x3_rd"] = ov


        import itertools
        for _ in emit_A(0):
            pass
        for _ in emit_B(0):
            pass
        for c in range(NCH):
            emit_C1(c)
            emit_C2(c)
            if c + 1 < NCH:
                gchain = itertools.chain(emit_A(c + 1), emit_B(c + 1))
            else:
                gchain = iter(())
            for _ in range(2):
                next(gchain, None)
            emit_C3(c, gchain)
            for _ in gchain:
                pass

        em.wait("sync", OUT, em.cnt[OUT])

        @block.sync
        def _(E):
            em.replay("sync", E, sems)

        @block.vector
        def _(E):
            em.replay("vector", E, sems)

        @block.scalar
        def _(E):
            em.replay("scalar", E, sems)

        @block.tensor
        def _(E):
            em.replay("tensor", E, sems)

        @block.gpsimd
        def _(E):
            em.replay("gpsimd", E, sems)

    return nc


def _bf16():
    import ml_dtypes
    return ml_dtypes.bfloat16


def host_inputs(inputs, b, h=0):
    ident = np.eye(128, dtype=np.float32)
    zero = np.zeros((128, 128), dtype=np.float32)
    return {
        "idh": np.concatenate([ident if h == 0 else zero,
                               ident if h == 1 else zero], axis=1),
        "msel": np.tile(np.array([[1.0 - h, float(h)]], dtype=np.float32),
                        (128, 1)),
        "hoff": np.array([[h * 128]], dtype=np.int32),
        "xs": np.ascontiguousarray(np.concatenate(
            [inputs["L0_attn_x"][b][h * 128:],
             np.zeros((h * 128, DM), dtype=np.float32)])) if h else
        np.ascontiguousarray(inputs["L0_attn_x"][b]),
        "x": np.ascontiguousarray(inputs["L0_attn_x"][b]),
        "w_in": np.ascontiguousarray(inputs["L0_attn_W_in"]),
        "wxp": np.ascontiguousarray(inputs["L0_attn_W_xproj"]),
        "wdt": np.ascontiguousarray(inputs["L0_attn_W_dt"]),
        "wout": np.ascontiguousarray(inputs["L0_attn_W_out"]),
        "alog": np.ascontiguousarray(
            inputs["L0_attn_A_log"].reshape(NDB, 128, N).transpose(1, 0, 2).reshape(128, NDB * N)),
        "dvec": np.ascontiguousarray(inputs["L0_attn_D"].reshape(NDB, 128).T),
        "convw": np.ascontiguousarray(
            inputs["L0_attn_conv_w"].reshape(NDB, 128, 4).transpose(1, 0, 2).reshape(128, NDB * 4)),
        "convb": np.ascontiguousarray(inputs["L0_attn_conv_b"].reshape(NDB, 128).T),
        "bdt": np.ascontiguousarray(inputs["L0_attn_b_dt"].reshape(NDB, 128).T),
        "lnw": np.ascontiguousarray(inputs["L0_attn_ln_w"].reshape(1, DM)),
        "lnb": np.ascontiguousarray(inputs["L0_attn_ln_b"].reshape(1, DM)),
        "wgT": np.ascontiguousarray(inputs["L0_ffn_w_gate"].T),
        "wuT": np.ascontiguousarray(inputs["L0_ffn_w_up"].T),
        "wdT": np.ascontiguousarray(inputs["L0_ffn_w_down"].T),
        "idf": np.eye(128, dtype=np.float32),
        "idb": np.eye(128, dtype=np.float32).astype(_bf16()),
    }


def kernel(**inputs):
    nc = build_nc()
    in_maps = [host_inputs(inputs, core // 2, core % 2) for core in range(8)]
    res = run_bass_kernel_spmd(nc, in_maps, core_ids=list(range(8)))
    out = np.empty((4, L, DM), dtype=np.float32)
    nch = L // T
    for b in range(4):
        v = out[b].reshape(nch, 2, 128, DM)
        v[:, 0] = res.results[2 * b]["out"].reshape(nch, 128, DM)
        v[:, 1] = res.results[2 * b + 1]["out"].reshape(nch, 128, DM)
    return out


# version 13
# speedup vs baseline: 1.1096x; 1.0035x over previous
import contextlib
import numpy as np

import concourse.bass as bass
import concourse.mybir as mybir
from concourse.bass_utils import run_bass_kernel_spmd

ds = bass.ds

F32 = mybir.dt.float32
BF16 = mybir.dt.bfloat16
I32 = mybir.dt.int32
AF = mybir.ActivationFunctionType
ALU = mybir.AluOpType

L = 8192
DM = 1024
DI = 2048
NDB = 16
KC = 8
N = 64
G = 4
NGRP = N // G
T = 256
TT = T // 128
W = G * (T + 1)
HF = 3072
NHC = 24
EPS = 1e-5
FS = float(1.0 / 127.0)

SEMS = ("LD", "PE", "ACT", "DVE", "POOL", "OUT")
ENGS = ("sync", "vector", "scalar", "tensor", "gpsimd")


DMA_SEMS = ("LD", "OUT")


class Em:
    def __init__(self):
        self.q = {e: [] for e in ENGS}
        self.cnt = {s: 0 for s in SEMS}
        self.hw = {e: {s: 0 for s in SEMS} for e in ENGS}
        self.pend = {s: 0 for s in DMA_SEMS}

    def wait(self, eng, sem, val):
        if val is None or val <= 0:
            return
        if sem in DMA_SEMS:
            val = self.cnt[sem]
            self.pend[sem] = max(self.pend[sem], val)
        assert val <= self.cnt[sem], f"wait-on-future {sem} {val} > {self.cnt[sem]}"
        if self.hw[eng][sem] >= val:
            return
        self.hw[eng][sem] = val
        self.q[eng].append(("W", sem, val))

    ENG_SEM = {"vector": "DVE", "scalar": "ACT", "gpsimd": "POOL", "tensor": "PE"}

    def op(self, eng, sem, amt, fn):
        own = self.ENG_SEM.get(eng)
        if own is not None and eng != "tensor":
            v = self.cnt[own]
            if v > self.hw[eng][own]:
                self.hw[eng][own] = v
                self.q[eng].append(("W", own, v))
        if sem in DMA_SEMS and self.pend[sem] > self.hw[eng][sem]:
            v = self.pend[sem]
            self.hw[eng][sem] = v
            self.q[eng].append(("W", sem, v))
        self.cnt[sem] += amt
        self.q[eng].append(("I", sem, amt, fn))
        return self.cnt[sem]

    def nosem(self, eng, fn):
        self.q[eng].append(("N", fn))

    def barrier(self, *sems_now):
        snap = {s: self.cnt[s] for s in (sems_now or SEMS)}
        for e in ENGS:
            for s, v in snap.items():
                self.wait(e, s, v)

    def replay(self, eng, E, sems):
        for item in self.q[eng]:
            if item[0] == "W":
                E.wait_ge(sems[item[1]], item[2])
            elif item[0] == "I":
                item[3](E).then_inc(sems[item[1]], item[2])
            else:
                item[1](E)


def build_nc(nch=L // T, dbg=False):
    nc = bass.Bass("TRN2", target_bir_lowering=False)
    NCH = nch

    x_d = nc.dram_tensor("x", [L, DM], F32, kind="ExternalInput").ap()
    w_in_d = nc.dram_tensor("w_in", [DM, 2 * DI], F32, kind="ExternalInput").ap()
    wxp_d = nc.dram_tensor("wxp", [DI, 192], F32, kind="ExternalInput").ap()
    wdt_d = nc.dram_tensor("wdt", [64, DI], F32, kind="ExternalInput").ap()
    wout_d = nc.dram_tensor("wout", [DI, DM], F32, kind="ExternalInput").ap()
    alog_d = nc.dram_tensor("alog", [128, NDB * N], F32, kind="ExternalInput").ap()
    dvec_d = nc.dram_tensor("dvec", [128, NDB], F32, kind="ExternalInput").ap()
    convw_d = nc.dram_tensor("convw", [128, NDB * 4], F32, kind="ExternalInput").ap()
    convb_d = nc.dram_tensor("convb", [128, NDB], F32, kind="ExternalInput").ap()
    bdt_d = nc.dram_tensor("bdt", [128, NDB], F32, kind="ExternalInput").ap()
    lnw_d = nc.dram_tensor("lnw", [1, DM], F32, kind="ExternalInput").ap()
    lnb_d = nc.dram_tensor("lnb", [1, DM], F32, kind="ExternalInput").ap()
    wgT_d = nc.dram_tensor("wgT", [DM, HF], I32, kind="ExternalInput").ap()
    wuT_d = nc.dram_tensor("wuT", [DM, HF], I32, kind="ExternalInput").ap()
    wdT_d = nc.dram_tensor("wdT", [HF, DM], I32, kind="ExternalInput").ap()
    idf_d = nc.dram_tensor("idf", [128, 128], F32, kind="ExternalInput").ap()
    idb_d = nc.dram_tensor("idb", [128, 128], BF16, kind="ExternalInput").ap()
    idh_d = nc.dram_tensor("idh", [128, 256], F32, kind="ExternalInput").ap()
    msel_d = nc.dram_tensor("msel", [128, 2], F32, kind="ExternalInput").ap()
    hoff_d = nc.dram_tensor("hoff", [1, 1], I32, kind="ExternalInput").ap()
    xs_d = nc.dram_tensor("xs", [L, DM], F32, kind="ExternalInput").ap()
    out_d = nc.dram_tensor("out", [L // 2, DM], F32, kind="ExternalOutput").ap()

    wxi_s = nc.dram_tensor("wxi_s", [128, NDB * KC * 128], BF16, kind="Internal").ap()
    wz_s = nc.dram_tensor("wz_s", [128, NDB * KC * 128], BF16, kind="Internal").ap()
    wout_s = nc.dram_tensor("wout_s", [128, NDB * DM], BF16, kind="Internal").ap()
    wg_s = nc.dram_tensor("wg_s", [128, NHC * KC * 128], BF16, kind="Internal").ap()
    wu_s = nc.dram_tensor("wu_s", [128, NHC * KC * 128], BF16, kind="Internal").ap()
    wd_s = nc.dram_tensor("wd_s", [128, NHC * DM], BF16, kind="Internal").ap()
    bflat_d = nc.dram_tensor("bflat_d", [N, T], F32, kind="Internal").ap()
    u_d = nc.dram_tensor("u_d", [128, NDB * T], F32, kind="Internal").ap()
    xn2_d = nc.dram_tensor("xn2_d", [128, KC * T], BF16, kind="Internal").ap()

    dbg_outs = {}
    if dbg:
        for nm, shape in (("d_xn2", [128, KC * T]), ("d_u", [128, NDB * T]),
                          ("d_delta", [128, NDB * T]), ("d_B", [64, T]),
                          ("d_C", [64, T]), ("d_yacc", [128, TT * DI]),
                          ("d_y1", [128, NDB * 128])):
            dt = BF16 if nm == "d_xn2" else F32
            dbg_outs[nm] = nc.dram_tensor(nm, shape, dt, kind="ExternalOutput").ap()

    em = Em()
    SN = {}
    sreg = nc.sync.alloc_register("hreg")
    nc.sync.enable_hardware_checks = False
    es = contextlib.ExitStack()
    with es:
        sb = lambda name, shape, dt=F32: es.enter_context(nc.sbuf_tensor(name, shape, dt))
        idh_sb = sb("idh_sb", [128, 256])
        msel_sb = sb("msel_sb", [128, 2])
        hoff_sb = es.enter_context(nc.sbuf_tensor("hoff_sb", [1, 1], I32))
        xn2my = sb("xn2my", [128, KC * 128], BF16)
        scm_sb = sb("scm_sb", [128, 8])
        wxp_sb = sb("wxp_sb", [128, NDB * 256], BF16)
        wdt_sb = sb("wdt_sb", [128, DI], BF16)
        a_sb = sb("a_sb", [128, NDB * N])
        dvec_sb = sb("dvec_sb", [128, NDB])
        convw_sb = sb("convw_sb", [128, NDB * 4])
        convb_sb = sb("convb_sb", [128, NDB])
        bdt_sb = sb("bdt_sb", [128, NDB])
        lnwb_sb = sb("lnwb_sb", [128, 2 * DM])
        row_sb = sb("row_sb", [1, G * T])
        ones1_sb = sb("ones1_sb", [1, 128])
        idf_sb = sb("idf_sb", [128, 128])
        idb_sb = sb("idb_sb", [128, 128], BF16)
        wstream = sb("wstream", [128, 2048], BF16)
        wzbuf = sb("wzbuf", [128, 2 * 1024], BF16)
        woutb = sb("woutb", [128, 2 * 1024], BF16)
        wgb = sb("wgb", [128, 2 * 1024], BF16)
        wub = sb("wub", [128, 2 * 1024], BF16)
        wdb = sb("wdb", [128, 2 * 1024], BF16)
        xt_sb = sb("xt_sb", [128, DM])
        bn_sb = sb("bn_sb", [128, 16])
        ko_sb = sb("ko_sb", [128, 8])
        sc_d = sb("sc_d", [128, TT * 8])
        sc_a = sb("sc_a", [128, TT * 8])
        big1 = sb("big1", [128, DM])
        xn2tm = sb("xn2tm", [128, DM], BF16)
        xn2fm = sb("xn2fm", [128, KC * T], BF16)
        xib = sb("xib", [128, 3 + T])
        cvb = sb("cvb", [128, T])
        halo_sb = sb("halo_sb", [128, NDB * 3])
        u_sb = sb("u_sb", [128, NDB * T])
        ubf_sb = sb("ubf_sb", [128, 2 * T], BF16)
        delta_sb = sb("delta_sb", [128, NDB * T])
        du_sb = sb("du_sb", [128, NDB * T])
        dtrb_sb = sb("dtrb_sb", [128, T], BF16)
        Bsb = sb("Bsb", [128, T])
        Csb = sb("Csb", [128, T])
        ctm_sb = sb("ctm_sb", [128, TT * N])
        dec_sb = [sb(f"dec{i}", [128, W]) for i in range(2)]
        dbu_sb = [sb(f"dbu{i}", [128, W]) for i in range(2)]
        NH = 6
        h_sb = [sb(f"h{i}", [128, W]) for i in range(NH)]
        bbc_sb = sb("bbc_sb", [128, 2 * G * T])
        hstate_sb = sb("hstate_sb", [128, NDB * N])
        yacc_sb = sb("yacc_sb", [128, TT * DI])
        y1f_sb = sb("y1f_sb", [128, T])
        sz_sb = sb("sz_sb", [128, 128])
        y1bf_sb = sb("y1bf_sb", [128, 2 * 128], BF16)
        x2_sb = sb("x2_sb", [128, DM])
        xn3f_sb = sb("xn3f_sb", [128, KC * 128], BF16)
        hh_sb = sb("hh_sb", [128, 256], BF16)

        ssp_sb = cvb
        sfn_sb = big1
        xn3t_sb = xn2tm
        x3_sb = big1
        psA = es.enter_context(nc.psum_tensor("psA", [128, 512], F32))
        psB = es.enter_context(nc.psum_tensor("psB", [128, 512], F32))
        psC = es.enter_context(nc.psum_tensor("psC", [128, 512], F32))
        psE = es.enter_context(nc.psum_tensor("psE", [128, 1024], F32))
        psF = es.enter_context(nc.psum_tensor("psF", [128, 1024], F32))
        psT = es.enter_context(nc.psum_tensor("psT", [128, 1024], BF16))

        sems = {s: es.enter_context(nc.semaphore(s)) for s in SEMS}
        block = es.enter_context(nc.Block())

        LD, PE, ACT, DVE, POOL, OUT = SEMS

        def dma(src, dst):
            return em.op("sync", LD, 16, lambda E, s=src, d=dst: E.dma_start(out=d, in_=s))

        ldh = dma(hoff_d, hoff_sb[:])
        em.wait("sync", LD, ldh)
        em.nosem("sync", lambda E: E.reg_load(sreg, hoff_sb[0:1, 0:1]))
        em.nosem("sync", lambda E: SN.__setitem__("t", E.snap(sreg)))
        dma(idh_d, idh_sb[:])
        dma(msel_d, msel_sb[:])
        ld0 = dma(alog_d, a_sb[:])
        dma(dvec_d, dvec_sb[:])
        dma(convw_d, convw_sb[:])
        dma(convb_d, convb_sb[:])
        dma(bdt_d, bdt_sb[:])
        dma(idf_d, idf_sb[:])
        ld_idb = dma(idb_d, idb_sb[:])

        em.wait("scalar", LD, ld0)
        a_exp = em.op("scalar", ACT, 1, lambda E: E.activation(
            out=big1[:], in_=a_sb[:], func=AF.Exp))
        em.wait("vector", ACT, a_exp)
        a_done = em.op("vector", DVE, 1, lambda E: E.tensor_scalar_mul(
            out=a_sb[:], in0=big1[:], scalar1=-1.0))
        ones_done = em.op("vector", DVE, 1, lambda E: E.memset(ones1_sb[:], 1.0))
        for i, v in enumerate((1.0 + EPS, EPS * EPS, EPS, 1.0, -1.0, -0.5, FS)):
            kdone = em.op("vector", DVE, 1, lambda E, ii=i, vv=v: E.memset(
                ko_sb[:, ii:ii + 1], vv))
        em.wait("scalar", DVE, kdone)
        z_dve = em.op("vector", DVE, 1, lambda E: E.memset(halo_sb[:], 0.0))
        z_dve = em.op("vector", DVE, 1, lambda E: E.memset(hstate_sb[:], 0.0))
        for d in dec_sb:
            z_dve = em.op("vector", DVE, 1, lambda E, dd=d: E.memset(dd[:], 0.0))

        em.wait("tensor", DVE, ones_done)
        em.wait("tensor", LD, ld_idb)
        lnbc_done = 0
        for which, srcd in ((0, lnw_d), (1, lnb_d)):
            em.wait("sync", ACT, lnbc_done)
            lrow = dma(srcd, row_sb[:, 0:DM])
            em.wait("tensor", LD, lrow)
            pe_c = 0
            for half in range(2):
                pe_c = em.op("tensor", PE, 1, lambda E, hf=half: E.matmul(
                    out=psE[:, hf * 512:(hf + 1) * 512], lhsT=ones1_sb[:],
                    rhs=row_sb[:, hf * 512:hf * 512 + 512], start=True, stop=True))
            em.wait("scalar", PE, pe_c)
            for half in range(2):
                lnbc_done = em.op("scalar", ACT, 1, lambda E, w=which, hf=half:
                                  E.activation(
                    out=lnwb_sb[:, w * DM + hf * 512: w * DM + hf * 512 + 512],
                    in_=psE[:, hf * 512:(hf + 1) * 512], func=AF.Copy))

        wi32 = xt_sb[:].bitcast(I32)
        cvt = a_done
        for dst_s, c0 in ((wxi_s, 0), (wz_s, DI)):
            for kc in range(KC):
                for hh in range(2):
                    em.wait("sync", DVE, cvt)
                    lw = dma(w_in_d[kc * 128:(kc + 1) * 128,
                                    c0 + hh * DM:c0 + (hh + 1) * DM], big1[:])
                    em.wait("vector", LD, lw)
                    cvt = em.op("vector", DVE, 1, lambda E, h2=hh: E.tensor_copy(
                        out=wstream[:, h2 * DM:(h2 + 1) * DM], in_=big1[:]))
                em.wait("sync", DVE, cvt)
                dv = dst_s.rearrange("p (db k j) -> p db k j", db=NDB, k=KC)[:, :, kc, :]
                sv = wstream[:].rearrange("p (db j) -> p db j", db=NDB)
                wb = em.op("sync", LD, 16, lambda E, d=dv, s=sv: E.dma_start(out=d, in_=s))
                em.wait("vector", LD, wb)
        for kc in range(NDB):
            em.wait("sync", DVE, cvt)
            lw = dma(wout_d[kc * 128:(kc + 1) * 128, :], big1[:])
            em.wait("vector", LD, lw)
            cvt = em.op("vector", DVE, 1, lambda E: E.tensor_copy(
                out=wstream[:, 0:DM], in_=big1[:]))
            em.wait("sync", DVE, cvt)
            wb = em.op("sync", LD, 16, lambda E, k=kc: E.dma_start(
                out=wout_s[:, k * DM:(k + 1) * DM], in_=wstream[:, 0:DM]))
            em.wait("vector", LD, wb)
        zw = em.op("vector", DVE, 1, lambda E: E.memset(wxp_sb[:], 0.0))
        for kc in range(NDB):
            em.wait("sync", DVE, max(cvt, zw))
            lw = dma(wxp_d[kc * 128:(kc + 1) * 128, :], big1[:, 0:192])
            em.wait("vector", LD, lw)
            cvt = em.op("vector", DVE, 1, lambda E, k=kc: E.tensor_copy(
                out=wxp_sb[:, k * 256:k * 256 + 192], in_=big1[:, 0:192]))
        for hh in range(2):
            em.wait("sync", DVE, cvt)
            lw = em.op("sync", LD, 16, lambda E, h2=hh: E.dma_start(
                out=big1[0:64, :], in_=wdt_d[:, h2 * DM:(h2 + 1) * DM]))
            em.wait("vector", LD, lw)
            cvt = em.op("vector", DVE, 1, lambda E, h2=hh: E.tensor_copy(
                out=wdt_sb[0:64, h2 * DM:(h2 + 1) * DM], in_=big1[0:64, :]))
            em.wait("sync", DVE, cvt)
        for src_d, dst_s in ((wgT_d, wg_s), (wuT_d, wu_s)):
            for kc in range(KC):
                for hh in range(3):
                    em.wait("sync", DVE, cvt)
                    lw = dma(src_d[kc * 128:(kc + 1) * 128, hh * 1024:(hh + 1) * 1024], wi32)
                    em.wait("vector", LD, lw)
                    cvt = em.op("vector", DVE, 1, lambda E: E.tensor_copy(
                        out=wstream[:, 0:DM], in_=wi32))
                    em.wait("sync", DVE, cvt)
                    dv = dst_s.rearrange("p (hc k j) -> p hc k j", hc=NHC, k=KC)[
                        :, hh * 8:(hh + 1) * 8, kc, :]
                    sv = wstream[:, 0:DM].rearrange("p (hc j) -> p hc j", hc=8)
                    wb = em.op("sync", LD, 16, lambda E, d=dv, s=sv: E.dma_start(out=d, in_=s))
                    em.wait("vector", LD, wb)
        for hc in range(NHC):
            em.wait("sync", DVE, cvt)
            lw = dma(wdT_d[hc * 128:(hc + 1) * 128, :], wi32)
            em.wait("vector", LD, lw)
            cvt = em.op("vector", DVE, 1, lambda E: E.tensor_copy(
                out=wstream[:, 0:DM], in_=wi32))
            em.wait("sync", DVE, cvt)
            wb = em.op("sync", LD, 16, lambda E, h=hc: E.dma_start(
                out=wd_s[:, h * DM:(h + 1) * DM], in_=wstream[:, 0:DM]))
            em.wait("vector", LD, wb)

        em.barrier()

        last = {"dec_rd": [0, 0], "dbu_rd": [0, 0],
                "h_rd_pe": [0] * 6, "h_rd_act": [0] * 6}
        for i in range(6):
            last[f"h_wr_{i}"] = 0

        def emit_A(c):
            for tt in range(TT):
                em.wait("sync", DVE, last.get("xt_rd_dve", 0))
                ldx = dma(x_d[c * T + tt * 128: c * T + (tt + 1) * 128, :], xt_sb[:])
                em.wait("vector", LD, ldx)
                em.op("vector", DVE, 1, lambda E: E.bn_stats(
                    out=bn_sb[:, 0:6], in_=xt_sb[:, 0:512]))
                em.op("vector", DVE, 1, lambda E: E.bn_stats(
                    out=bn_sb[:, 6:12], in_=xt_sb[:, 512:1024]))
                em.op("vector", DVE, 1, lambda E: E.engine_nop())
                bs = em.op("vector", DVE, 1, lambda E: E.bn_aggr(
                    out=bn_sb[:, 12:14], in_=bn_sb[:, 0:12]))
                o = tt * 8
                bs = em.op("vector", DVE, 1, lambda E, o=o: E.tensor_copy(
                    out=sc_d[:, o:o + 2], in_=bn_sb[:, 12:14]))
                em.wait("scalar", DVE, bs)
                em.op("scalar", ACT, 1, lambda E, o=o: E.activation(
                    out=sc_a[:, o + 3:o + 4], in_=sc_d[:, o + 1:o + 2], func=AF.Ln,
                    scale=ko_sb[:, 0:1], bias=ko_sb[:, 1:2]))
                em.op("scalar", ACT, 1, lambda E, o=o: E.activation(
                    out=sc_a[:, o + 4:o + 5], in_=sc_d[:, o + 1:o + 2], func=AF.Ln,
                    bias=ko_sb[:, 2:3]))
                em.op("scalar", ACT, 1, lambda E, o=o: E.activation(
                    out=sc_a[:, o:o + 1], in_=sc_a[:, o + 3:o + 4], func=AF.Exp,
                    scale=ko_sb[:, 5:6]))
                ac = em.op("scalar", ACT, 1, lambda E, o=o: E.activation(
                    out=sc_a[:, o + 1:o + 2], in_=sc_a[:, o + 4:o + 5], func=AF.Exp,
                    scale=ko_sb[:, 5:6]))
                em.wait("vector", ACT, ac)
                em.op("vector", DVE, 1, lambda E, o=o: E.scalar_tensor_tensor(
                    out=sc_d[:, o + 2:o + 3], in0=sc_d[:, o:o + 1], scalar=-1.0,
                    in1=sc_a[:, o:o + 1], op0=ALU.mult, op1=ALU.mult))
                em.op("vector", DVE, 1, lambda E, o=o: E.scalar_tensor_tensor(
                    out=sc_d[:, o + 3:o + 4], in0=sc_d[:, o:o + 1], scalar=-1.0,
                    in1=sc_a[:, o + 1:o + 2], op0=ALU.mult, op1=ALU.mult))
                xrd = em.op("vector", DVE, 1, lambda E, o=o: E.scalar_tensor_tensor(
                    out=xt_sb[:], in0=xt_sb[:], scalar=sc_a[:, o:o + 1],
                    in1=lnwb_sb[:, 0:DM], op0=ALU.mult, op1=ALU.mult))
                em.op("vector", DVE, 1, lambda E, o=o: E.scalar_tensor_tensor(
                    out=xt_sb[:], in0=lnwb_sb[:, 0:DM], scalar=sc_d[:, o + 2:o + 3],
                    in1=xt_sb[:], op0=ALU.mult, op1=ALU.add))
                dv = em.op("vector", DVE, 1, lambda E: E.tensor_add(
                    out=xn2tm[:], in0=xt_sb[:], in1=lnwb_sb[:, DM:2 * DM]))
                last["xt_rd_dve"] = dv
                em.wait("tensor", DVE, dv)
                em.wait("tensor", ACT, last.get("psT_rd", 0))
                pe_t = 0
                for kc in range(KC):
                    pe_t = em.op("tensor", PE, 1, lambda E, k=kc: E.matmul(
                        out=psT[:, k * 128:(k + 1) * 128],
                        lhsT=xn2tm[:, k * 128:(k + 1) * 128], rhs=idb_sb[:],
                        is_transpose=True, start=(k == 0), stop=(k == KC - 1)))
                em.wait("scalar", PE, pe_t)
                ev = 0
                for kc in range(KC):
                    ev = em.op("scalar", ACT, 1, lambda E, k=kc, t=tt: E.activation(
                        out=xn2fm[:, k * T + t * 128:k * T + t * 128 + 128],
                        in_=psT[:, k * 128:(k + 1) * 128], func=AF.Copy))
                last["xn2fm_wr"] = ev
                last["psT_rd"] = ev
                yield

            last["pending_xdbl"] = None

            def flush_xdbl():
                d, p, ub = last["pending_xdbl"]
                em.wait("tensor", DVE, ub)
                if d == 0:
                    em.wait("tensor", ACT, last.get("xdbl_ps_rd", 0))
                    em.wait("tensor", ACT, last.get("z_ps_rd", 0))
                    em.wait("tensor", DVE, last.get("rt_rd_B", 0))
                em.op("tensor", PE, 1, lambda E, d=d, p=p: E.matmul(
                    out=psC[:, 0:T], lhsT=wxp_sb[:, d * 256:d * 256 + 128],
                    rhs=ubf_sb[:, p * T:(p + 1) * T], start=(d == 0), stop=False))
                pe_d = em.op("tensor", PE, 1, lambda E, d=d, p=p: E.matmul(
                    out=psC[:, T:2 * T], lhsT=wxp_sb[:, d * 256 + 128:d * 256 + 256],
                    rhs=ubf_sb[:, p * T:(p + 1) * T], start=False, stop=(d == NDB - 1)))
                last[f"ubf_rd_{p}"] = pe_d
                last["pending_xdbl"] = None

            for db in range(NDB):
                par = db % 2
                em.wait("sync", PE, last.get(f"wxi_rd_{par}", 0))
                lw = dma(wxi_s[:, db * 1024:(db + 1) * 1024],
                         wstream[:, par * 1024:(par + 1) * 1024])
                em.wait("tensor", LD, lw)
                em.wait("tensor", ACT, last["xn2fm_wr"])
                em.wait("tensor", ACT, last.get("xi_ps_rd", 0))
                em.wait("tensor", DVE, last.get("g_dve_1", 0))
                em.wait("tensor", DVE, last.get("yf_ps_rd", 0))
                if last.get("pending_xdbl") is not None:
                    flush_xdbl()
                pe_x = 0
                for kc in range(KC):
                    pe_x = em.op("tensor", PE, 1, lambda E, k=kc, p=par: E.matmul(
                        out=psA[:, 0:T],
                        lhsT=wstream[:, p * 1024 + k * 128:p * 1024 + (k + 1) * 128],
                        rhs=xn2fm[:, k * T:(k + 1) * T],
                        start=(k == 0), stop=(k == KC - 1)))
                last[f"wxi_rd_{par}"] = pe_x
                em.wait("scalar", PE, pe_x)
                em.wait("scalar", DVE, last.get("xib_rd", 0))
                em.op("scalar", ACT, 1, lambda E, d=db: E.activation(
                    out=xib[:, 0:3], in_=halo_sb[:, d * 3:(d + 1) * 3], func=AF.Copy))
                ev = em.op("scalar", ACT, 1, lambda E: E.activation(
                    out=xib[:, 3:3 + T], in_=psA[:, 0:T], func=AF.Copy))
                last["xi_ps_rd"] = ev
                em.wait("vector", ACT, ev)
                em.wait("vector", ACT, last.get("cvb_rd", 0))
                cv = em.op("vector", DVE, 1, lambda E, d=db: E.tensor_scalar(
                    out=cvb[:], in0=xib[:, 0:T], scalar1=convw_sb[:, d * 4:d * 4 + 1],
                    scalar2=None, op0=ALU.mult))
                for k in range(1, 4):
                    cv = em.op("vector", DVE, 1, lambda E, d=db, kk=k: E.scalar_tensor_tensor(
                        out=cvb[:], in0=xib[:, kk:kk + T],
                        scalar=convw_sb[:, d * 4 + kk:d * 4 + kk + 1],
                        in1=cvb[:], op0=ALU.mult, op1=ALU.add))
                last["xib_rd"] = cv
                em.wait("scalar", DVE, cv)
                uu = em.op("scalar", ACT, 1, lambda E, d=db: E.activation(
                    out=u_sb[:, d * T:(d + 1) * T], in_=cvb[:], func=AF.Silu,
                    bias=convb_sb[:, d:d + 1]))
                last["cvb_rd"] = uu
                em.op("scalar", ACT, 1, lambda E, d=db: E.activation(
                    out=halo_sb[:, d * 3:(d + 1) * 3], in_=xib[:, T:T + 3], func=AF.Copy))
                em.wait("vector", ACT, uu)
                em.wait("vector", PE, last.get(f"ubf_rd_{par}", 0))
                ub = em.op("vector", DVE, 1, lambda E, d=db, p=par: E.tensor_copy(
                    out=ubf_sb[:, p * T:(p + 1) * T], in_=u_sb[:, d * T:(d + 1) * T]))
                last["pending_xdbl"] = (db, par, ub)
                yield

            flush_xdbl()
            last["u_wr"] = uu
            xdbl_pe = em.cnt[PE]

            em.wait("scalar", PE, xdbl_pe)
            em.wait("scalar", LD, 1)
            ev_dtr = em.op("scalar", ACT, 1, lambda E: E.activation(
                out=dtrb_sb[0:64, :], in_=psC[0:64, 0:T], func=AF.Copy))
            last["ev_dtr"] = ev_dtr
            ev_B = em.op("scalar", ACT, 1, lambda E: E.activation(
                out=Bsb[64:128, :], in_=psC[64:128, 0:T], func=AF.Copy))
            ev_C = em.op("scalar", ACT, 1, lambda E: E.activation(
                out=Csb[0:64, :], in_=psC[0:64, T:2 * T], func=AF.Copy))
            last["xdbl_ps_rd"] = ev_C
            em.wait("sync", ACT, ev_B)
            bf1 = em.op("sync", LD, 16, lambda E: E.dma_start(
                out=bflat_d, in_=Bsb[64:128, :]))
            last["bf1"] = bf1
            em.wait("tensor", ACT, ev_C)
            em.wait("tensor", ACT, last["xi_ps_rd"])
            em.wait("tensor", DVE, last.get("g_dve_1", 0))
            em.wait("tensor", DVE, last.get("yf_ps_rd", 0))
            pe_c2 = 0
            for tt in range(TT):
                pe_c2 = em.op("tensor", PE, 1, lambda E, t=tt: E.matmul(
                    out=psA[:, 256 + t * 64:256 + (t + 1) * 64],
                    lhsT=Csb[0:64, t * 128:(t + 1) * 128], rhs=idf_sb[0:64, 0:64],
                    is_transpose=True, start=(t == 0), stop=(t == TT - 1)))
            em.wait("scalar", PE, pe_c2)
            ev_ctm = em.op("scalar", ACT, 1, lambda E: E.activation(
                out=ctm_sb[:], in_=psA[:, 256:256 + TT * N], func=AF.Copy))
            last["ev_ctm"] = ev_ctm

            em.wait("sync", ACT, uu)
            em.wait("sync", DVE, last.get("yf_ps_rd", 0))
            bo2 = em.op("sync", LD, 16, lambda E: E.dma_start(out=u_d, in_=u_sb[:]))
            em.wait("sync", ACT, last["xn2fm_wr"])
            bo1 = em.op("sync", LD, 16, lambda E: E.dma_start(out=xn2_d, in_=xn2fm[:]))
            em.wait("sync", LD, bo1)
            em.wait("sync", PE, max(last.get("wz_rd_0", 0), last.get("wz_rd_1", 0)))
            em.op("sync", LD, 16, lambda E: E.dma_start(
                out=xn2my[:].rearrange("p (k t) -> p k t", k=KC),
                in_=xn2_d.rearrange("p (k t) -> p k t", k=KC)[:, :, ds(SN["t"], 128)]))

            if dbg and c == 0:
                em.wait("sync", ACT, last["xn2fm_wr"])
                em.op("sync", OUT, 16, lambda E: E.dma_start(
                    out=dbg_outs["d_xn2"], in_=xn2fm[:]))
                em.wait("sync", ACT, last["u_wr"])
                em.op("sync", OUT, 16, lambda E: E.dma_start(out=dbg_outs["d_u"], in_=u_sb[:]))
                em.wait("sync", ACT, ev_B)
                em.op("sync", OUT, 16, lambda E: E.dma_start(out=dbg_outs["d_B"], in_=Bsb[64:128, :]))
                em.wait("sync", ACT, ev_C)
                em.op("sync", OUT, 16, lambda E: E.dma_start(out=dbg_outs["d_C"], in_=Csb[0:64, :]))


        def emit_B(c):
            for db in range(NDB):
                em.wait("tensor", ACT, last["ev_dtr"])
                em.wait("tensor", ACT, last.get("dlt_ps_rd", 0))
                if db == 0:
                    em.wait("tensor", ACT, last.get("bbc_ps_rd", 0))
                pe_dl = em.op("tensor", PE, 1, lambda E, d=db: E.matmul(
                    out=psF[:, 512:512 + T], lhsT=wdt_sb[0:64, d * 128:(d + 1) * 128],
                    rhs=dtrb_sb[0:64, :], start=True, stop=True))
                em.wait("scalar", PE, pe_dl)
                em.wait("scalar", DVE, last.get("ssp_rd", 0))
                em.op("scalar", ACT, 1, lambda E, d=db: E.activation(
                    out=ssp_sb[:], in_=psF[:, 512:512 + T], func=AF.Abs,
                    bias=bdt_sb[:, d:d + 1]))
                em.op("scalar", ACT, 1, lambda E: E.activation(
                    out=ssp_sb[:], in_=ssp_sb[:], func=AF.Exp, scale=ko_sb[:, 4:5]))
                em.op("scalar", ACT, 1, lambda E: E.activation(
                    out=ssp_sb[:], in_=ssp_sb[:], func=AF.Ln, bias=ko_sb[:, 3:4]))
                em.wait("scalar", DVE, last.get("sz_rd", 0))
                em.wait("scalar", OUT, 1)
                a4 = em.op("scalar", ACT, 1, lambda E, d=db: E.activation(
                    out=y1f_sb[:], in_=psF[:, 512:512 + T], func=AF.Relu,
                    bias=bdt_sb[:, d:d + 1]))
                last["dlt_ps_rd"] = a4
                em.wait("vector", ACT, a4)
                em.wait("vector", ACT, last.get(f"delta_rd_{db}", 0))
                ddl = em.op("vector", DVE, 1, lambda E, d=db: E.tensor_add(
                    out=delta_sb[:, d * T:(d + 1) * T], in0=y1f_sb[:], in1=ssp_sb[:]))
                last[f"delta_wr_{db}"] = ddl
                last["ssp_rd"] = ddl
                em.wait("vector", ACT, last["u_wr"])
                em.op("vector", DVE, 1, lambda E, d=db: E.tensor_mul(
                    out=du_sb[:, d * T:(d + 1) * T], in0=delta_sb[:, d * T:(d + 1) * T],
                    in1=u_sb[:, d * T:(d + 1) * T]))
                yield
            if dbg and c == 0:
                em.wait("sync", DVE, em.cnt[DVE])
                em.op("sync", OUT, 16, lambda E: E.dma_start(
                    out=dbg_outs["d_delta"], in_=delta_sb[:]))
                em.wait("sync", OUT, 1)
            em.wait("sync", DVE, em.cnt[DVE])
            em.wait("sync", LD, 1)
            em.op("sync", LD, 16, lambda E: E.dma_start(
                out=u_sb[:, 0:NDB * 128].rearrange("p (db t) -> p db t", db=NDB),
                in_=u_d.rearrange("p (db t) -> p db t", db=NDB)[:, :, ds(SN["t"], 128)]))
            yield
            em.wait("vector", PE, last.get("yacc_rd_pe", 0))
            em.op("vector", DVE, 1, lambda E: E.memset(yacc_sb[:], 0.0))
            em.wait("tensor", ACT, last["ev_ctm"])
            em.wait("tensor", ACT, last["xdbl_ps_rd"])
            em.wait("tensor", ACT, last["dlt_ps_rd"])
            for grp in range(NGRP):
                bpar = grp % 2
                em.wait("sync", LD, last["bf1"])
                em.wait("sync", PE, last.get("row_rd_pe", 0))
                bfg = em.op("sync", LD, 16, lambda E, g=grp: E.dma_start(
                    out=row_sb[0:1, :],
                    in_=bflat_d[g * G:(g + 1) * G, :].rearrange("n t -> (n t)")[None, :]))
                em.wait("tensor", LD, bfg)
                ev_b = 0
                for half in range(2):
                    em.wait("tensor", ACT, last.get("bbc_ps_rd", 0))
                    pe_b = em.op("tensor", PE, 1, lambda E, hf=half: E.matmul(
                        out=psF[:, 512:1024], lhsT=ones1_sb[:],
                        rhs=row_sb[:, hf * 512:(hf + 1) * 512],
                        start=True, stop=True))
                    last["row_rd_pe"] = pe_b
                    em.wait("scalar", PE, pe_b)
                    if half == 0:
                        em.wait("scalar", DVE, last.get(f"bbc_rd_pool_{bpar}", 0))
                    ev_b = em.op("scalar", ACT, 1, lambda E, hf=half, bp=bpar: E.activation(
                        out=bbc_sb[:, bp * 1024 + hf * 512:bp * 1024 + (hf + 1) * 512],
                        in_=psF[:, 512:1024], func=AF.Copy))
                    last["bbc_ps_rd"] = ev_b

                for db in range(NDB):
                    par = db % 2
                    hq = (grp * NDB + db) % NH
                    dec, dbuf = dec_sb[par], dbu_sb[par]
                    em.wait("scalar", DVE, max(last["dec_rd"][par],
                                               last.get(f"delta_wr_{db}", 0)))
                    ac_d = 0
                    for g in range(G):
                        nn = grp * G + g
                        ac_d = em.op("scalar", ACT, 1, lambda E, d=db, n=nn, g=g, dd=dec:
                                     E.activation(
                            out=dd[:, g * (T + 1) + 1:(g + 1) * (T + 1)],
                            in_=delta_sb[:, d * T:(d + 1) * T], func=AF.Exp,
                            scale=a_sb[:, d * N + n:d * N + n + 1]))
                    last[f"delta_rd_{db}"] = ac_d
                    if last.get("pending_gather") is not None:
                        pd, pg0, phq, psc = last["pending_gather"]
                        em.wait("scalar", DVE, psc)
                        gat = em.op("scalar", ACT, 1, lambda E, d=pd, g0=pg0,
                                    hh=h_sb[phq]: E.activation(
                            out=hstate_sb[:, d * N + g0 * G:d * N + (g0 + 1) * G],
                            in_=hh[:].rearrange("p (g t) -> p g t", g=G)[:, :, T],
                            func=AF.Copy))
                        last["h_rd_act"][phq] = gat
                        last["pending_gather"] = None
                    dbu_pay = dbuf[:].rearrange("p (g t) -> p g t", g=G)[:, :, 1:]
                    em.wait("vector", ACT, ev_b)
                    em.wait("vector", ACT, ac_d)
                    em.wait("vector", ACT, last["h_rd_act"][hq])
                    po = em.op("vector", DVE, 1, lambda E, d=db, dp=dbu_pay, bp=bpar:
                               E.tensor_tensor(
                        out=dp,
                        in0=du_sb[:, d * T:(d + 1) * T].unsqueeze(1).broadcast_to(
                            [128, G, T]),
                        in1=bbc_sb[:, bp * 1024:(bp + 1) * 1024].rearrange(
                            "p (g t) -> p g t", g=G),
                        op=ALU.mult))
                    last[f"bbc_rd_pool_{bpar}"] = po
                    bnd = em.op("vector", DVE, 1, lambda E, d=db, g0=grp, dd=dbuf:
                                E.tensor_copy(
                        out=dd[:].rearrange("p (g t) -> p g t", g=G)[:, :, 0],
                        in_=hstate_sb[:, d * N + g0 * G:d * N + (g0 + 1) * G]))
                    em.wait("vector", PE, last["h_rd_pe"][hq])
                    em.wait("vector", ACT, last["h_rd_act"][hq])
                    sc_v = em.op("vector", DVE, 1, lambda E, dd=dec, bb=dbuf, hh=h_sb[hq]:
                                 E.tensor_tensor_scan(
                        out=hh[:], data0=dd[:], data1=bb[:], initial=0.0,
                        op0=ALU.mult, op1=ALU.add))
                    last["dec_rd"][par] = sc_v
                    last["dbu_rd"][par] = sc_v
                    last[f"h_wr_{hq}"] = sc_v
                    last["pending_gather"] = (db, grp, hq, sc_v)
                    yield

                    if db % 4 == 3:
                        q0 = db - 3
                        hqs = [(grp * NDB + q0 + q) % NH for q in range(4)]
                        for g in range(G):
                            nn = grp * G + g
                            for tt in range(TT):
                                use_a = (g * TT + tt) % 2 == 0
                                rt = psA if use_a else psC
                                key = "rt_rd_A" if use_a else "rt_rd_B"
                                em.wait("tensor", DVE, last.get(key, 0))
                                pe_r = 0
                                for q in range(4):
                                    em.wait("tensor", DVE, last[f"h_wr_{hqs[q]}"])
                                    pe_r = em.op("tensor", PE, 1,
                                                 lambda E, qq=hqs[q], qi=q, gg=g, t=tt, r=rt: E.matmul(
                                        out=r[:, qi * 128:(qi + 1) * 128],
                                        lhsT=h_sb[qq][:, gg * (T + 1) + 1 + t * 128:
                                                      gg * (T + 1) + 1 + (t + 1) * 128],
                                        rhs=idf_sb[:], is_transpose=True,
                                        start=(qi == 0), stop=(qi == 3)))
                                for q in range(4):
                                    last["h_rd_pe"][hqs[q]] = pe_r
                                em.wait("vector", PE, pe_r)
                                st_v = em.op("vector", DVE, 1,
                                             lambda E, n=nn, t=tt, r=rt, qb=q0:
                                             E.scalar_tensor_tensor(
                                    out=yacc_sb[:, t * DI + qb * 128:t * DI + qb * 128 + 512],
                                    in0=r[:, 0:512],
                                    scalar=ctm_sb[:, t * N + n:t * N + n + 1],
                                    in1=yacc_sb[:, t * DI + qb * 128:t * DI + qb * 128 + 512],
                                    op0=ALU.mult, op1=ALU.add))
                                last[key] = st_v
            if last.get("pending_gather") is not None:
                pd, pg0, phq, psc = last["pending_gather"]
                em.wait("scalar", DVE, psc)
                gat = em.op("scalar", ACT, 1, lambda E, d=pd, g0=pg0,
                            hh=h_sb[phq]: E.activation(
                    out=hstate_sb[:, d * N + g0 * G:d * N + (g0 + 1) * G],
                    in_=hh[:].rearrange("p (g t) -> p g t", g=G)[:, :, T],
                    func=AF.Copy))
                last["h_rd_act"][phq] = gat
                last["pending_gather"] = None
            yacc_dve = em.cnt[DVE]
            last["yacc_dve"] = yacc_dve

            if dbg and c == 0:
                em.wait("sync", DVE, yacc_dve)
                em.op("sync", OUT, 16, lambda E: E.dma_start(
                    out=dbg_outs["d_yacc"], in_=yacc_sb[:]))


        def emit_C1(c):
            last["pending_wout"] = None

            def flush_wout():
                d, p, y1b, lwo = last["pending_wout"]
                em.wait("tensor", DVE, y1b)
                em.wait("tensor", LD, lwo)
                if d == 0:
                    em.wait("tensor", DVE, last.get("ffn_ps_rd", 0))
                    em.wait("tensor", ACT, last.get("bbc_ps_rd", 0))
                pe_o = 0
                for half in range(2):
                    pe_o = em.op("tensor", PE, 1, lambda E, d=d, hf=half,
                                 p=p: E.matmul(
                        out=psE[:, hf * 512:(hf + 1) * 512],
                        lhsT=y1bf_sb[:, p * 128:(p + 1) * 128],
                        rhs=woutb[:, p * DM + hf * 512:p * DM + (hf + 1) * 512],
                        start=(d == 0), stop=(d == NDB - 1)))
                last[f"wout_rd_{p}"] = pe_o
                last[f"y1bf_rd_{p}"] = pe_o
                last["pending_wout"] = None

            for db in range(NDB):
                par = db % 2
                em.wait("sync", PE, last.get(f"wout_rd_{par}", 0))
                lwo = dma(wout_s[:, db * DM:(db + 1) * DM],
                          woutb[:, par * DM:(par + 1) * DM])
                em.wait("sync", PE, last.get(f"wz_rd_{par}", 0))
                lwz = dma(wz_s[:, db * 1024:(db + 1) * 1024],
                          wzbuf[:, par * 1024:(par + 1) * 1024])
                em.wait("tensor", DVE, last["yacc_dve"])
                em.wait("tensor", DVE, last.get("yf_ps_rd", 0))
                pe_y = 0
                for t in range(TT):
                    pe_y = em.op("tensor", PE, 1, lambda E, d=db, t=t: E.matmul(
                        out=psA[:, 0:128],
                        lhsT=yacc_sb[:, t * DI + d * 128:t * DI + (d + 1) * 128],
                        rhs=idh_sb[:, t * 128:(t + 1) * 128],
                        start=(t == 0), stop=(t == TT - 1)))
                last["yacc_rd_pe"] = pe_y
                em.wait("vector", PE, pe_y)
                em.wait("vector", OUT, last.get("y1f_dump", 0))
                em.wait("vector", LD, 1)
                y1v = em.op("vector", DVE, 1, lambda E, d=db: E.scalar_tensor_tensor(
                    out=y1f_sb[:, 0:128], in0=u_sb[:, d * 128:(d + 1) * 128],
                    scalar=dvec_sb[:, d:d + 1], in1=psA[:, 0:128],
                    op0=ALU.mult, op1=ALU.add))
                last["yf_ps_rd"] = y1v
                em.wait("tensor", LD, lwz)
                if db == 0:
                    em.wait("tensor", ACT, last["xdbl_ps_rd"])
                    em.wait("tensor", DVE, last.get("rt_rd_B", 0))
                em.wait("tensor", ACT, last.get("z_ps_rd", 0))
                pe_z = 0
                for kc in range(KC):
                    pe_z = em.op("tensor", PE, 1, lambda E, k=kc, p=par: E.matmul(
                        out=psC[:, 0:128],
                        lhsT=wzbuf[:, p * 1024 + k * 128:p * 1024 + (k + 1) * 128],
                        rhs=xn2my[:, k * 128:(k + 1) * 128],
                        start=(k == 0), stop=(k == KC - 1)))
                last[f"wz_rd_{par}"] = pe_z
                em.wait("scalar", PE, pe_z)
                em.wait("scalar", DVE, last.get("sz_rd", 0))
                szv = em.op("scalar", ACT, 1, lambda E: E.activation(
                    out=sz_sb[:], in_=psC[:, 0:128], func=AF.Silu))
                last["z_ps_rd"] = szv
                em.wait("vector", ACT, szv)
                em.wait("vector", PE, last.get(f"y1bf_rd_{par}", 0))
                y1b = em.op("vector", DVE, 1, lambda E, p=par: E.tensor_mul(
                    out=y1bf_sb[:, p * 128:(p + 1) * 128], in0=y1f_sb[:, 0:128],
                    in1=sz_sb[:]))
                last["sz_rd"] = y1b
                if last.get("pending_wout") is not None:
                    flush_wout()
                last["pending_wout"] = (db, par, y1b, lwo)
                if dbg and c == 0:
                    em.wait("sync", DVE, y1b)
                    last["y1f_dump"] = em.op("sync", OUT, 16, lambda E, d=db: E.dma_start(
                        out=dbg_outs["d_y1"][:, d * 128:(d + 1) * 128],
                        in_=y1f_sb[:, 0:128]))
            flush_wout()
            last["om_pe"] = em.cnt[PE]

        def emit_C2(c):
            sm = em.op("vector", DVE, 1, lambda E: E.tensor_scalar(
                out=scm_sb[:, 4:5], in0=sc_a[:, 9:10], scalar1=msel_sb[:, 1:2],
                scalar2=None, op0=ALU.mult))
            sm = em.op("vector", DVE, 1, lambda E: E.scalar_tensor_tensor(
                out=scm_sb[:, 0:1], in0=sc_a[:, 1:2], scalar=msel_sb[:, 0:1],
                in1=scm_sb[:, 4:5], op0=ALU.mult, op1=ALU.add))
            sm = em.op("vector", DVE, 1, lambda E: E.tensor_scalar(
                out=scm_sb[:, 5:6], in0=sc_d[:, 11:12], scalar1=msel_sb[:, 1:2],
                scalar2=None, op0=ALU.mult))
            sm = em.op("vector", DVE, 1, lambda E: E.scalar_tensor_tensor(
                out=scm_sb[:, 1:2], in0=sc_d[:, 3:4], scalar=msel_sb[:, 0:1],
                in1=scm_sb[:, 5:6], op0=ALU.mult, op1=ALU.add))

            em.wait("sync", DVE, last.get("xt_rd_dve", 0))
            em.wait("sync", ACT, last.get("xt_rd_act", 0))
            ldx = dma(xs_d[c * T:c * T + 128, :], xt_sb[:])
            em.wait("vector", LD, ldx)
            em.wait("vector", PE, last["om_pe"])
            xv = em.op("vector", DVE, 1, lambda E: E.tensor_add(
                out=big1[:], in0=xt_sb[:], in1=psE[:]))
            last["om_ps_rd"] = xv
            xv = em.op("vector", DVE, 1, lambda E: E.tensor_scalar(
                out=xt_sb[:], in0=xt_sb[:], scalar1=scm_sb[:, 0:1],
                scalar2=scm_sb[:, 1:2], op0=ALU.mult, op1=ALU.add))
            xv = em.op("vector", DVE, 1, lambda E: E.tensor_add(
                out=x2_sb[:], in0=big1[:], in1=xt_sb[:]))
            last["xt_rd_dve"] = xv
            em.op("vector", DVE, 1, lambda E: E.bn_stats(
                out=bn_sb[:, 0:6], in_=x2_sb[:, 0:512]))
            em.op("vector", DVE, 1, lambda E: E.bn_stats(
                out=bn_sb[:, 6:12], in_=x2_sb[:, 512:DM]))
            em.op("vector", DVE, 1, lambda E: E.engine_nop())
            bs = em.op("vector", DVE, 1, lambda E: E.bn_aggr(
                out=bn_sb[:, 12:14], in_=bn_sb[:, 0:12]))
            em.wait("scalar", DVE, bs)
            em.op("scalar", ACT, 1, lambda E: E.activation(
                out=sc_a[:, 13:14], in_=bn_sb[:, 13:14], func=AF.Ln,
                bias=ko_sb[:, 2:3]))
            ac = em.op("scalar", ACT, 1, lambda E: E.activation(
                out=sc_a[:, 14:15], in_=sc_a[:, 13:14], func=AF.Exp,
                scale=ko_sb[:, 5:6]))
            em.wait("vector", ACT, ac)
            q3v = em.op("vector", DVE, 1, lambda E: E.scalar_tensor_tensor(
                out=scm_sb[:, 3:4], in0=bn_sb[:, 12:13], scalar=-1.0,
                in1=sc_a[:, 14:15], op0=ALU.mult, op1=ALU.mult))
            xn3v = em.op("vector", DVE, 1, lambda E: E.tensor_scalar(
                out=xn3t_sb[:], in0=x2_sb[:],
                scalar1=sc_a[:, 14:15], scalar2=scm_sb[:, 3:4],
                op0=ALU.mult, op1=ALU.add))
            last["x2_rd_dve2"] = xn3v
            em.wait("tensor", DVE, xn3v)
            em.wait("tensor", ACT, last.get("psT_rd", 0))
            pe_t = 0
            for kc in range(KC):
                pe_t = em.op("tensor", PE, 1, lambda E, k=kc: E.matmul(
                    out=psT[:, k * 128:(k + 1) * 128],
                    lhsT=xn3t_sb[:, k * 128:(k + 1) * 128], rhs=idb_sb[:],
                    is_transpose=True, start=(k == 0), stop=(k == KC - 1)))
            em.wait("scalar", PE, pe_t)
            em.wait("scalar", PE, last.get("xn3f_rd", 0))
            ev = 0
            for kc in range(KC):
                ev = em.op("scalar", ACT, 1, lambda E, k=kc: E.activation(
                    out=xn3f_sb[:, k * 128:(k + 1) * 128],
                    in_=psT[:, k * 128:(k + 1) * 128], func=AF.Copy))
            last["psT_rd"] = ev
            last["xn3f_wr"] = ev


        def emit_C3(c, ga=iter(())):
            last["pending_down"] = None

            def flush_down():
                h, p, hhv = last["pending_down"]
                em.wait("tensor", DVE, hhv)
                if h == 0:
                    em.wait("tensor", DVE, last.get("x3_ffn_rd", 0))
                pe_dn = 0
                for half in range(2):
                    pe_dn = em.op("tensor", PE, 1, lambda E, h=h, hf=half,
                                  p=p: E.matmul(
                        out=psE[:, hf * 512:(hf + 1) * 512],
                        lhsT=hh_sb[:, p * 128:(p + 1) * 128],
                        rhs=wdb[:, p * DM + hf * 512:p * DM + (hf + 1) * 512],
                        start=(h == 0), stop=(h == NHC - 1)))
                last[f"hh_rd_{p}"] = pe_dn
                last[f"wffn_rd_{p}"] = pe_dn
                last["pending_down"] = None

            for hc in range(NHC):
                par = hc % 2
                for _ in range(6):
                    next(ga, None)
                em.wait("sync", PE, last.get(f"wffn_rd_{par}", 0))
                dma(wg_s[:, hc * 1024:(hc + 1) * 1024], wgb[:, par * 1024:(par + 1) * 1024])
                dma(wu_s[:, hc * 1024:(hc + 1) * 1024], wub[:, par * 1024:(par + 1) * 1024])
                ldn = dma(wd_s[:, hc * DM:(hc + 1) * DM], wdb[:, par * DM:(par + 1) * DM])
                em.wait("tensor", LD, ldn)
                em.wait("tensor", ACT, last["xn3f_wr"])
                em.wait("tensor", DVE, last.get(f"g_dve_{par}", 0))
                em.wait("tensor", ACT, last.get(f"g_act_{par}", 0))
                gu_ps = psB if par == 0 else psF
                if hc == 0:
                    em.wait("tensor", ACT, last["dlt_ps_rd"])
                pe_g = 0
                for kc in range(KC):
                    pe_g = em.op("tensor", PE, 1, lambda E, k=kc, p=par, gp=gu_ps: E.matmul(
                        out=gp[:, 0:128],
                        lhsT=wgb[:, p * 1024 + k * 128:p * 1024 + (k + 1) * 128],
                        rhs=xn3f_sb[:, k * 128:(k + 1) * 128],
                        start=(k == 0), stop=(k == KC - 1)))
                em.wait("scalar", PE, pe_g)
                em.wait("scalar", DVE, last.get(f"sfn_rd_{par}", 0))
                sg = em.op("scalar", ACT, 1, lambda E, p=par, gp=gu_ps: E.activation(
                    out=sfn_sb[:, p * 128:(p + 1) * 128],
                    in_=gp[:, 0:128], func=AF.Silu,
                    scale=ko_sb[:, 6:7]))
                last[f"g_act_{par}"] = sg
                em.wait("tensor", ACT, sg)
                pe_u = 0
                for kc in range(KC):
                    pe_u = em.op("tensor", PE, 1, lambda E, k=kc, p=par, gp=gu_ps: E.matmul(
                        out=gp[:, 0:128],
                        lhsT=wub[:, p * 1024 + k * 128:p * 1024 + (k + 1) * 128],
                        rhs=xn3f_sb[:, k * 128:(k + 1) * 128],
                        start=(k == 0), stop=(k == KC - 1)))
                if last.get("pending_down") is not None:
                    flush_down()
                em.wait("vector", PE, pe_u)
                em.wait("vector", ACT, sg)
                em.wait("vector", PE, last.get(f"hh_rd_{par}", 0))
                hhv = em.op("vector", DVE, 1, lambda E, p=par, gp=gu_ps: E.scalar_tensor_tensor(
                    out=hh_sb[:, p * 128:(p + 1) * 128],
                    in0=gp[:, 0:128],
                    scalar=FS, in1=sfn_sb[:, p * 128:(p + 1) * 128],
                    op0=ALU.mult, op1=ALU.mult))
                last[f"sfn_rd_{par}"] = hhv
                last[f"g_dve_{par}"] = hhv
                last["pending_down"] = (hc, par, hhv)
            flush_down()
            ffn_pe = em.cnt[PE]
            last["xn3f_rd"] = ffn_pe
            em.wait("vector", PE, ffn_pe)
            em.wait("vector", OUT, last.get("x3_rd", 0))
            x3v = em.op("vector", DVE, 1, lambda E: E.scalar_tensor_tensor(
                out=x3_sb[:], in0=psE[:], scalar=FS,
                in1=x2_sb[:], op0=ALU.mult, op1=ALU.add))
            last["x3_ffn_rd"] = x3v
            last["ffn_ps_rd"] = x3v
            em.wait("sync", DVE, x3v)
            ov = em.op("sync", OUT, 16, lambda E, cc=c: E.dma_start(
                out=out_d[cc * 128:(cc + 1) * 128, :], in_=x3_sb[:]))
            last["x3_rd"] = ov


        import itertools
        for _ in emit_A(0):
            pass
        for _ in emit_B(0):
            pass
        for c in range(NCH):
            emit_C1(c)
            emit_C2(c)
            if c + 1 < NCH:
                gchain = itertools.chain(emit_A(c + 1), emit_B(c + 1))
            else:
                gchain = iter(())
            for _ in range(2):
                next(gchain, None)
            emit_C3(c, gchain)
            for _ in gchain:
                pass

        em.wait("sync", OUT, em.cnt[OUT])

        @block.sync
        def _(E):
            em.replay("sync", E, sems)

        @block.vector
        def _(E):
            em.replay("vector", E, sems)

        @block.scalar
        def _(E):
            em.replay("scalar", E, sems)

        @block.tensor
        def _(E):
            em.replay("tensor", E, sems)

        @block.gpsimd
        def _(E):
            em.replay("gpsimd", E, sems)

    return nc


def _bf16():
    import ml_dtypes
    return ml_dtypes.bfloat16


def host_inputs(inputs, b, h=0):
    ident = np.eye(128, dtype=np.float32)
    zero = np.zeros((128, 128), dtype=np.float32)
    return {
        "idh": np.concatenate([ident if h == 0 else zero,
                               ident if h == 1 else zero], axis=1),
        "msel": np.tile(np.array([[1.0 - h, float(h)]], dtype=np.float32),
                        (128, 1)),
        "hoff": np.array([[h * 128]], dtype=np.int32),
        "xs": np.ascontiguousarray(np.concatenate(
            [inputs["L0_attn_x"][b][h * 128:],
             np.zeros((h * 128, DM), dtype=np.float32)])) if h else
        np.ascontiguousarray(inputs["L0_attn_x"][b]),
        "x": np.ascontiguousarray(inputs["L0_attn_x"][b]),
        "w_in": np.ascontiguousarray(inputs["L0_attn_W_in"]),
        "wxp": np.ascontiguousarray(inputs["L0_attn_W_xproj"]),
        "wdt": np.ascontiguousarray(inputs["L0_attn_W_dt"]),
        "wout": np.ascontiguousarray(inputs["L0_attn_W_out"]),
        "alog": np.ascontiguousarray(
            inputs["L0_attn_A_log"].reshape(NDB, 128, N).transpose(1, 0, 2).reshape(128, NDB * N)),
        "dvec": np.ascontiguousarray(inputs["L0_attn_D"].reshape(NDB, 128).T),
        "convw": np.ascontiguousarray(
            inputs["L0_attn_conv_w"].reshape(NDB, 128, 4).transpose(1, 0, 2).reshape(128, NDB * 4)),
        "convb": np.ascontiguousarray(inputs["L0_attn_conv_b"].reshape(NDB, 128).T),
        "bdt": np.ascontiguousarray(inputs["L0_attn_b_dt"].reshape(NDB, 128).T),
        "lnw": np.ascontiguousarray(inputs["L0_attn_ln_w"].reshape(1, DM)),
        "lnb": np.ascontiguousarray(inputs["L0_attn_ln_b"].reshape(1, DM)),
        "wgT": np.ascontiguousarray(inputs["L0_ffn_w_gate"].T),
        "wuT": np.ascontiguousarray(inputs["L0_ffn_w_up"].T),
        "wdT": np.ascontiguousarray(inputs["L0_ffn_w_down"].T),
        "idf": np.eye(128, dtype=np.float32),
        "idb": np.eye(128, dtype=np.float32).astype(_bf16()),
    }


def kernel(**inputs):
    nc = build_nc()
    in_maps = [host_inputs(inputs, core // 2, core % 2) for core in range(8)]
    res = run_bass_kernel_spmd(nc, in_maps, core_ids=list(range(8)))
    out = np.empty((4, L, DM), dtype=np.float32)
    nch = L // T
    for b in range(4):
        v = out[b].reshape(nch, 2, 128, DM)
        v[:, 0] = res.results[2 * b]["out"].reshape(nch, 128, DM)
        v[:, 1] = res.results[2 * b + 1]["out"].reshape(nch, 128, DM)
    return out


# version 14
# speedup vs baseline: 1.1157x; 1.0055x over previous
import contextlib
import numpy as np

import concourse.bass as bass
import concourse.mybir as mybir
from concourse.bass_utils import run_bass_kernel_spmd

ds = bass.ds

F32 = mybir.dt.float32
BF16 = mybir.dt.bfloat16
I32 = mybir.dt.int32
AF = mybir.ActivationFunctionType
ALU = mybir.AluOpType

L = 8192
DM = 1024
DI = 2048
NDB = 16
KC = 8
N = 64
G = 4
NGRP = N // G
T = 256
TT = T // 128
W = G * (T + 1)
HF = 3072
NHC = 24
EPS = 1e-5
FS = float(1.0 / 127.0)

SEMS = ("LD", "PE", "ACT", "DVE", "POOL", "OUT")
ENGS = ("sync", "vector", "scalar", "tensor", "gpsimd")


DMA_SEMS = ("LD", "OUT")


class Em:
    def __init__(self):
        self.q = {e: [] for e in ENGS}
        self.cnt = {s: 0 for s in SEMS}
        self.hw = {e: {s: 0 for s in SEMS} for e in ENGS}
        self.pend = {s: 0 for s in DMA_SEMS}

    def wait(self, eng, sem, val):
        if val is None or val <= 0:
            return
        if sem in DMA_SEMS:
            val = self.cnt[sem]
            self.pend[sem] = max(self.pend[sem], val)
        assert val <= self.cnt[sem], f"wait-on-future {sem} {val} > {self.cnt[sem]}"
        if self.hw[eng][sem] >= val:
            return
        self.hw[eng][sem] = val
        self.q[eng].append(("W", sem, val))

    ENG_SEM = {"vector": "DVE", "scalar": "ACT", "gpsimd": "POOL", "tensor": "PE"}

    def op(self, eng, sem, amt, fn):
        own = self.ENG_SEM.get(eng)
        if own is not None and eng != "tensor":
            v = self.cnt[own]
            if v > self.hw[eng][own]:
                self.hw[eng][own] = v
                self.q[eng].append(("W", own, v))
        if sem in DMA_SEMS and self.pend[sem] > self.hw[eng][sem]:
            v = self.pend[sem]
            self.hw[eng][sem] = v
            self.q[eng].append(("W", sem, v))
        self.cnt[sem] += amt
        self.q[eng].append(("I", sem, amt, fn))
        return self.cnt[sem]

    def nosem(self, eng, fn):
        self.q[eng].append(("N", fn))

    def barrier(self, *sems_now):
        snap = {s: self.cnt[s] for s in (sems_now or SEMS)}
        for e in ENGS:
            for s, v in snap.items():
                self.wait(e, s, v)

    def replay(self, eng, E, sems):
        for item in self.q[eng]:
            if item[0] == "W":
                E.wait_ge(sems[item[1]], item[2])
            elif item[0] == "I":
                item[3](E).then_inc(sems[item[1]], item[2])
            else:
                item[1](E)


def build_nc(nch=L // T, dbg=False):
    nc = bass.Bass("TRN2", target_bir_lowering=False)
    NCH = nch

    x_d = nc.dram_tensor("x", [L, DM], F32, kind="ExternalInput").ap()
    w_in_d = nc.dram_tensor("w_in", [DM, 2 * DI], F32, kind="ExternalInput").ap()
    wxp_d = nc.dram_tensor("wxp", [DI, 192], F32, kind="ExternalInput").ap()
    wdt_d = nc.dram_tensor("wdt", [64, DI], F32, kind="ExternalInput").ap()
    wout_d = nc.dram_tensor("wout", [DI, DM], F32, kind="ExternalInput").ap()
    alog_d = nc.dram_tensor("alog", [128, NDB * N], F32, kind="ExternalInput").ap()
    dvec_d = nc.dram_tensor("dvec", [128, NDB], F32, kind="ExternalInput").ap()
    convw_d = nc.dram_tensor("convw", [128, NDB * 4], F32, kind="ExternalInput").ap()
    convb_d = nc.dram_tensor("convb", [128, NDB], F32, kind="ExternalInput").ap()
    bdt_d = nc.dram_tensor("bdt", [128, NDB], F32, kind="ExternalInput").ap()
    lnw_d = nc.dram_tensor("lnw", [1, DM], F32, kind="ExternalInput").ap()
    lnb_d = nc.dram_tensor("lnb", [1, DM], F32, kind="ExternalInput").ap()
    wgT_d = nc.dram_tensor("wgT", [DM, HF], I32, kind="ExternalInput").ap()
    wuT_d = nc.dram_tensor("wuT", [DM, HF], I32, kind="ExternalInput").ap()
    wdT_d = nc.dram_tensor("wdT", [HF, DM], I32, kind="ExternalInput").ap()
    idf_d = nc.dram_tensor("idf", [128, 128], F32, kind="ExternalInput").ap()
    idb_d = nc.dram_tensor("idb", [128, 128], BF16, kind="ExternalInput").ap()
    idh_d = nc.dram_tensor("idh", [128, 256], F32, kind="ExternalInput").ap()
    msel_d = nc.dram_tensor("msel", [128, 2], F32, kind="ExternalInput").ap()
    hoff_d = nc.dram_tensor("hoff", [1, 1], I32, kind="ExternalInput").ap()
    xs_d = nc.dram_tensor("xs", [L, DM], F32, kind="ExternalInput").ap()
    out_d = nc.dram_tensor("out", [L // 2, DM], F32, kind="ExternalOutput").ap()

    wxi_s = nc.dram_tensor("wxi_s", [128, NDB * KC * 128], BF16, kind="Internal").ap()
    wz_s = nc.dram_tensor("wz_s", [128, NDB * KC * 128], BF16, kind="Internal").ap()
    wout_s = nc.dram_tensor("wout_s", [128, NDB * DM], BF16, kind="Internal").ap()
    wg_s = nc.dram_tensor("wg_s", [128, NHC * KC * 128], BF16, kind="Internal").ap()
    wu_s = nc.dram_tensor("wu_s", [128, NHC * KC * 128], BF16, kind="Internal").ap()
    wd_s = nc.dram_tensor("wd_s", [128, NHC * DM], BF16, kind="Internal").ap()
    bflat_d = nc.dram_tensor("bflat_d", [N, T], F32, kind="Internal").ap()
    u_d = nc.dram_tensor("u_d", [128, NDB * T], F32, kind="Internal").ap()
    xn2_d = nc.dram_tensor("xn2_d", [128, KC * T], BF16, kind="Internal").ap()

    dbg_outs = {}
    if dbg:
        for nm, shape in (("d_xn2", [128, KC * T]), ("d_u", [128, NDB * T]),
                          ("d_delta", [128, NDB * T]), ("d_B", [64, T]),
                          ("d_C", [64, T]), ("d_yacc", [128, TT * DI]),
                          ("d_y1", [128, NDB * 128])):
            dt = BF16 if nm == "d_xn2" else F32
            dbg_outs[nm] = nc.dram_tensor(nm, shape, dt, kind="ExternalOutput").ap()

    em = Em()
    SN = {}
    sreg = nc.sync.alloc_register("hreg")
    nc.sync.enable_hardware_checks = False
    es = contextlib.ExitStack()
    with es:
        sb = lambda name, shape, dt=F32: es.enter_context(nc.sbuf_tensor(name, shape, dt))
        idh_sb = sb("idh_sb", [128, 256])
        msel_sb = sb("msel_sb", [128, 2])
        hoff_sb = es.enter_context(nc.sbuf_tensor("hoff_sb", [1, 1], I32))
        xn2my = sb("xn2my", [128, KC * 128], BF16)
        scm_sb = sb("scm_sb", [128, 8])
        wxp_sb = sb("wxp_sb", [128, NDB * 256], BF16)
        wdt_sb = sb("wdt_sb", [128, DI], BF16)
        a_sb = sb("a_sb", [128, NDB * N])
        dvec_sb = sb("dvec_sb", [128, NDB])
        convw_sb = sb("convw_sb", [128, NDB * 4])
        convb_sb = sb("convb_sb", [128, NDB])
        bdt_sb = sb("bdt_sb", [128, NDB])
        lnwb_sb = sb("lnwb_sb", [128, 2 * DM])
        row_sb = sb("row_sb", [1, G * T])
        ones1_sb = sb("ones1_sb", [1, 128])
        idf_sb = sb("idf_sb", [128, 128])
        idb_sb = sb("idb_sb", [128, 128], BF16)
        wstream = sb("wstream", [128, 2048], BF16)
        wzbuf = sb("wzbuf", [128, 2 * 1024], BF16)
        woutb = sb("woutb", [128, 2 * 1024], BF16)
        wgb = sb("wgb", [128, 2 * 1024], BF16)
        wub = sb("wub", [128, 2 * 1024], BF16)
        wdb = sb("wdb", [128, 2 * 1024], BF16)
        xt_sb = sb("xt_sb", [128, DM])
        bn_sb = sb("bn_sb", [128, 16])
        ko_sb = sb("ko_sb", [128, 8])
        sc_d = sb("sc_d", [128, TT * 8])
        sc_a = sb("sc_a", [128, TT * 8])
        big1 = sb("big1", [128, DM])
        xn2tm = sb("xn2tm", [128, DM], BF16)
        xn2fm = sb("xn2fm", [128, KC * T], BF16)
        xib = sb("xib", [128, 3 + T])
        cvb = sb("cvb", [128, T])
        halo_sb = sb("halo_sb", [128, NDB * 3])
        u_sb = sb("u_sb", [128, NDB * T])
        ubf_sb = sb("ubf_sb", [128, 2 * T], BF16)
        delta_sb = sb("delta_sb", [128, NDB * T])
        du_sb = sb("du_sb", [128, NDB * T])
        dtrb_sb = sb("dtrb_sb", [128, T], BF16)
        Bsb = sb("Bsb", [128, T])
        Csb = sb("Csb", [128, T])
        ctm_sb = sb("ctm_sb", [128, TT * N])
        dec_sb = [sb(f"dec{i}", [128, W]) for i in range(2)]
        dbu_sb = [sb(f"dbu{i}", [128, W]) for i in range(2)]
        NH = 6
        h_sb = [sb(f"h{i}", [128, W]) for i in range(NH)]
        bbc_sb = sb("bbc_sb", [128, 2 * G * T])
        hstate_sb = sb("hstate_sb", [128, NDB * N])
        yacc_sb = sb("yacc_sb", [128, TT * DI])
        y1f_sb = sb("y1f_sb", [128, T])
        sz_sb = sb("sz_sb", [128, 128])
        y1bf_sb = sb("y1bf_sb", [128, 2 * 128], BF16)
        x2_sb = sb("x2_sb", [128, DM])
        xn3f_sb = sb("xn3f_sb", [128, KC * 128], BF16)
        hh_sb = sb("hh_sb", [128, 256], BF16)

        ssp_sb = cvb
        sfn_sb = big1
        xn3t_sb = xn2tm
        x3_sb = big1
        psA = es.enter_context(nc.psum_tensor("psA", [128, 512], F32))
        psB = es.enter_context(nc.psum_tensor("psB", [128, 512], F32))
        psC = es.enter_context(nc.psum_tensor("psC", [128, 512], F32))
        psE = es.enter_context(nc.psum_tensor("psE", [128, 1024], F32))
        psF = es.enter_context(nc.psum_tensor("psF", [128, 1024], F32))
        psT = es.enter_context(nc.psum_tensor("psT", [128, 1024], BF16))

        sems = {s: es.enter_context(nc.semaphore(s)) for s in SEMS}
        block = es.enter_context(nc.Block())

        LD, PE, ACT, DVE, POOL, OUT = SEMS

        def dma(src, dst):
            return em.op("sync", LD, 16, lambda E, s=src, d=dst: E.dma_start(out=d, in_=s))

        ldh = dma(hoff_d, hoff_sb[:])
        em.wait("sync", LD, ldh)
        em.nosem("sync", lambda E: E.reg_load(sreg, hoff_sb[0:1, 0:1]))
        em.nosem("sync", lambda E: SN.__setitem__("t", E.snap(sreg)))
        dma(idh_d, idh_sb[:])
        dma(msel_d, msel_sb[:])
        ld0 = dma(alog_d, a_sb[:])
        dma(dvec_d, dvec_sb[:])
        dma(convw_d, convw_sb[:])
        dma(convb_d, convb_sb[:])
        dma(bdt_d, bdt_sb[:])
        dma(idf_d, idf_sb[:])
        ld_idb = dma(idb_d, idb_sb[:])

        em.wait("scalar", LD, ld0)
        a_exp = em.op("scalar", ACT, 1, lambda E: E.activation(
            out=big1[:], in_=a_sb[:], func=AF.Exp))
        em.wait("vector", ACT, a_exp)
        a_done = em.op("vector", DVE, 1, lambda E: E.tensor_scalar_mul(
            out=a_sb[:], in0=big1[:], scalar1=-1.0))
        ones_done = em.op("vector", DVE, 1, lambda E: E.memset(ones1_sb[:], 1.0))
        for i, v in enumerate((1.0 + EPS, EPS * EPS, EPS, 1.0, -1.0, -0.5, FS)):
            kdone = em.op("vector", DVE, 1, lambda E, ii=i, vv=v: E.memset(
                ko_sb[:, ii:ii + 1], vv))
        em.wait("scalar", DVE, kdone)
        z_dve = em.op("vector", DVE, 1, lambda E: E.memset(halo_sb[:], 0.0))
        z_dve = em.op("vector", DVE, 1, lambda E: E.memset(hstate_sb[:], 0.0))
        for d in dec_sb:
            z_dve = em.op("vector", DVE, 1, lambda E, dd=d: E.memset(dd[:], 0.0))

        em.wait("tensor", DVE, ones_done)
        em.wait("tensor", LD, ld_idb)
        lnbc_done = 0
        for which, srcd in ((0, lnw_d), (1, lnb_d)):
            em.wait("sync", ACT, lnbc_done)
            lrow = dma(srcd, row_sb[:, 0:DM])
            em.wait("tensor", LD, lrow)
            pe_c = 0
            for half in range(2):
                pe_c = em.op("tensor", PE, 1, lambda E, hf=half: E.matmul(
                    out=psE[:, hf * 512:(hf + 1) * 512], lhsT=ones1_sb[:],
                    rhs=row_sb[:, hf * 512:hf * 512 + 512], start=True, stop=True))
            em.wait("scalar", PE, pe_c)
            for half in range(2):
                lnbc_done = em.op("scalar", ACT, 1, lambda E, w=which, hf=half:
                                  E.activation(
                    out=lnwb_sb[:, w * DM + hf * 512: w * DM + hf * 512 + 512],
                    in_=psE[:, hf * 512:(hf + 1) * 512], func=AF.Copy))

        wi32 = xt_sb[:].bitcast(I32)
        cvt = a_done
        for dst_s, c0 in ((wxi_s, 0), (wz_s, DI)):
            for kc in range(KC):
                for hh in range(2):
                    em.wait("sync", DVE, cvt)
                    lw = dma(w_in_d[kc * 128:(kc + 1) * 128,
                                    c0 + hh * DM:c0 + (hh + 1) * DM], big1[:])
                    em.wait("vector", LD, lw)
                    cvt = em.op("vector", DVE, 1, lambda E, h2=hh: E.tensor_copy(
                        out=wstream[:, h2 * DM:(h2 + 1) * DM], in_=big1[:]))
                em.wait("sync", DVE, cvt)
                dv = dst_s.rearrange("p (db k j) -> p db k j", db=NDB, k=KC)[:, :, kc, :]
                sv = wstream[:].rearrange("p (db j) -> p db j", db=NDB)
                wb = em.op("sync", LD, 16, lambda E, d=dv, s=sv: E.dma_start(out=d, in_=s))
                em.wait("vector", LD, wb)
        for kc in range(NDB):
            em.wait("sync", DVE, cvt)
            lw = dma(wout_d[kc * 128:(kc + 1) * 128, :], big1[:])
            em.wait("vector", LD, lw)
            cvt = em.op("vector", DVE, 1, lambda E: E.tensor_copy(
                out=wstream[:, 0:DM], in_=big1[:]))
            em.wait("sync", DVE, cvt)
            wb = em.op("sync", LD, 16, lambda E, k=kc: E.dma_start(
                out=wout_s[:, k * DM:(k + 1) * DM], in_=wstream[:, 0:DM]))
            em.wait("vector", LD, wb)
        zw = em.op("vector", DVE, 1, lambda E: E.memset(wxp_sb[:], 0.0))
        for kc in range(NDB):
            em.wait("sync", DVE, max(cvt, zw))
            lw = dma(wxp_d[kc * 128:(kc + 1) * 128, :], big1[:, 0:192])
            em.wait("vector", LD, lw)
            cvt = em.op("vector", DVE, 1, lambda E, k=kc: E.tensor_copy(
                out=wxp_sb[:, k * 256:k * 256 + 192], in_=big1[:, 0:192]))
        for hh in range(2):
            em.wait("sync", DVE, cvt)
            lw = em.op("sync", LD, 16, lambda E, h2=hh: E.dma_start(
                out=big1[0:64, :], in_=wdt_d[:, h2 * DM:(h2 + 1) * DM]))
            em.wait("vector", LD, lw)
            cvt = em.op("vector", DVE, 1, lambda E, h2=hh: E.tensor_copy(
                out=wdt_sb[0:64, h2 * DM:(h2 + 1) * DM], in_=big1[0:64, :]))
            em.wait("sync", DVE, cvt)
        for src_d, dst_s in ((wgT_d, wg_s), (wuT_d, wu_s)):
            for kc in range(KC):
                for hh in range(3):
                    em.wait("sync", DVE, cvt)
                    lw = dma(src_d[kc * 128:(kc + 1) * 128, hh * 1024:(hh + 1) * 1024], wi32)
                    em.wait("vector", LD, lw)
                    cvt = em.op("vector", DVE, 1, lambda E: E.tensor_copy(
                        out=wstream[:, 0:DM], in_=wi32))
                    em.wait("sync", DVE, cvt)
                    dv = dst_s.rearrange("p (hc k j) -> p hc k j", hc=NHC, k=KC)[
                        :, hh * 8:(hh + 1) * 8, kc, :]
                    sv = wstream[:, 0:DM].rearrange("p (hc j) -> p hc j", hc=8)
                    wb = em.op("sync", LD, 16, lambda E, d=dv, s=sv: E.dma_start(out=d, in_=s))
                    em.wait("vector", LD, wb)
        for hc in range(NHC):
            em.wait("sync", DVE, cvt)
            lw = dma(wdT_d[hc * 128:(hc + 1) * 128, :], wi32)
            em.wait("vector", LD, lw)
            cvt = em.op("vector", DVE, 1, lambda E: E.tensor_copy(
                out=wstream[:, 0:DM], in_=wi32))
            em.wait("sync", DVE, cvt)
            wb = em.op("sync", LD, 16, lambda E, h=hc: E.dma_start(
                out=wd_s[:, h * DM:(h + 1) * DM], in_=wstream[:, 0:DM]))
            em.wait("vector", LD, wb)

        em.barrier()

        last = {"dec_rd": [0, 0], "dbu_rd": [0, 0],
                "h_rd_pe": [0] * 6, "h_rd_act": [0] * 6}
        for i in range(6):
            last[f"h_wr_{i}"] = 0

        def emit_A(c):
            for tt in range(TT):
                em.wait("sync", DVE, last.get("xt_rd_dve", 0))
                ldx = dma(x_d[c * T + tt * 128: c * T + (tt + 1) * 128, :], xt_sb[:])
                em.wait("vector", LD, ldx)
                em.op("vector", DVE, 1, lambda E: E.bn_stats(
                    out=bn_sb[:, 0:6], in_=xt_sb[:, 0:512]))
                em.op("vector", DVE, 1, lambda E: E.bn_stats(
                    out=bn_sb[:, 6:12], in_=xt_sb[:, 512:1024]))
                em.op("vector", DVE, 1, lambda E: E.engine_nop())
                bs = em.op("vector", DVE, 1, lambda E: E.bn_aggr(
                    out=bn_sb[:, 12:14], in_=bn_sb[:, 0:12]))
                o = tt * 8
                bs = em.op("vector", DVE, 1, lambda E, o=o: E.tensor_copy(
                    out=sc_d[:, o:o + 2], in_=bn_sb[:, 12:14]))
                em.wait("scalar", DVE, bs)
                em.op("scalar", ACT, 1, lambda E, o=o: E.activation(
                    out=sc_a[:, o + 3:o + 4], in_=sc_d[:, o + 1:o + 2], func=AF.Ln,
                    scale=ko_sb[:, 0:1], bias=ko_sb[:, 1:2]))
                em.op("scalar", ACT, 1, lambda E, o=o: E.activation(
                    out=sc_a[:, o + 4:o + 5], in_=sc_d[:, o + 1:o + 2], func=AF.Ln,
                    bias=ko_sb[:, 2:3]))
                em.op("scalar", ACT, 1, lambda E, o=o: E.activation(
                    out=sc_a[:, o:o + 1], in_=sc_a[:, o + 3:o + 4], func=AF.Exp,
                    scale=ko_sb[:, 5:6]))
                ac = em.op("scalar", ACT, 1, lambda E, o=o: E.activation(
                    out=sc_a[:, o + 1:o + 2], in_=sc_a[:, o + 4:o + 5], func=AF.Exp,
                    scale=ko_sb[:, 5:6]))
                em.wait("vector", ACT, ac)
                em.op("vector", DVE, 1, lambda E, o=o: E.scalar_tensor_tensor(
                    out=sc_d[:, o + 2:o + 3], in0=sc_d[:, o:o + 1], scalar=-1.0,
                    in1=sc_a[:, o:o + 1], op0=ALU.mult, op1=ALU.mult))
                em.op("vector", DVE, 1, lambda E, o=o: E.scalar_tensor_tensor(
                    out=sc_d[:, o + 3:o + 4], in0=sc_d[:, o:o + 1], scalar=-1.0,
                    in1=sc_a[:, o + 1:o + 2], op0=ALU.mult, op1=ALU.mult))
                xrd = em.op("vector", DVE, 1, lambda E, o=o: E.scalar_tensor_tensor(
                    out=xt_sb[:], in0=xt_sb[:], scalar=sc_a[:, o:o + 1],
                    in1=lnwb_sb[:, 0:DM], op0=ALU.mult, op1=ALU.mult))
                em.op("vector", DVE, 1, lambda E, o=o: E.scalar_tensor_tensor(
                    out=xt_sb[:], in0=lnwb_sb[:, 0:DM], scalar=sc_d[:, o + 2:o + 3],
                    in1=xt_sb[:], op0=ALU.mult, op1=ALU.add))
                dv = em.op("vector", DVE, 1, lambda E: E.tensor_add(
                    out=xn2tm[:], in0=xt_sb[:], in1=lnwb_sb[:, DM:2 * DM]))
                last["xt_rd_dve"] = dv
                em.wait("tensor", DVE, dv)
                em.wait("tensor", ACT, last.get("psT_rd", 0))
                pe_t = 0
                for kc in range(KC):
                    pe_t = em.op("tensor", PE, 1, lambda E, k=kc: E.matmul(
                        out=psT[:, k * 128:(k + 1) * 128],
                        lhsT=xn2tm[:, k * 128:(k + 1) * 128], rhs=idb_sb[:],
                        is_transpose=True, start=(k == 0), stop=(k == KC - 1)))
                em.wait("scalar", PE, pe_t)
                ev = 0
                for kc in range(KC):
                    ev = em.op("scalar", ACT, 1, lambda E, k=kc, t=tt: E.activation(
                        out=xn2fm[:, k * T + t * 128:k * T + t * 128 + 128],
                        in_=psT[:, k * 128:(k + 1) * 128], func=AF.Copy))
                last["xn2fm_wr"] = ev
                last["psT_rd"] = ev
                yield

            last["pending_xdbl"] = None

            def flush_xdbl():
                d, p, ub = last["pending_xdbl"]
                em.wait("tensor", DVE, ub)
                if d == 0:
                    em.wait("tensor", ACT, last.get("xdbl_ps_rd", 0))
                    em.wait("tensor", ACT, last.get("z_ps_rd", 0))
                    em.wait("tensor", DVE, last.get("rt_rd_B", 0))
                em.op("tensor", PE, 1, lambda E, d=d, p=p: E.matmul(
                    out=psC[:, 0:T], lhsT=wxp_sb[:, d * 256:d * 256 + 128],
                    rhs=ubf_sb[:, p * T:(p + 1) * T], start=(d == 0), stop=False))
                pe_d = em.op("tensor", PE, 1, lambda E, d=d, p=p: E.matmul(
                    out=psC[:, T:2 * T], lhsT=wxp_sb[:, d * 256 + 128:d * 256 + 256],
                    rhs=ubf_sb[:, p * T:(p + 1) * T], start=False, stop=(d == NDB - 1)))
                last[f"ubf_rd_{p}"] = pe_d
                last["pending_xdbl"] = None

            for db in range(NDB):
                par = db % 2
                em.wait("sync", PE, last.get(f"wxi_rd_{par}", 0))
                lw = dma(wxi_s[:, db * 1024:(db + 1) * 1024],
                         wstream[:, par * 1024:(par + 1) * 1024])
                em.wait("tensor", LD, lw)
                em.wait("tensor", ACT, last["xn2fm_wr"])
                em.wait("tensor", ACT, last.get("xi_ps_rd", 0))
                em.wait("tensor", DVE, last.get("g_dve_1", 0))
                em.wait("tensor", DVE, last.get("yf_ps_rd", 0))
                if last.get("pending_xdbl") is not None:
                    flush_xdbl()
                pe_x = 0
                for kc in range(KC):
                    pe_x = em.op("tensor", PE, 1, lambda E, k=kc, p=par: E.matmul(
                        out=psA[:, 0:T],
                        lhsT=wstream[:, p * 1024 + k * 128:p * 1024 + (k + 1) * 128],
                        rhs=xn2fm[:, k * T:(k + 1) * T],
                        start=(k == 0), stop=(k == KC - 1)))
                last[f"wxi_rd_{par}"] = pe_x
                em.wait("scalar", PE, pe_x)
                em.wait("scalar", DVE, last.get("xib_rd", 0))
                em.op("scalar", ACT, 1, lambda E, d=db: E.activation(
                    out=xib[:, 0:3], in_=halo_sb[:, d * 3:(d + 1) * 3], func=AF.Copy))
                ev = em.op("scalar", ACT, 1, lambda E: E.activation(
                    out=xib[:, 3:3 + T], in_=psA[:, 0:T], func=AF.Copy))
                last["xi_ps_rd"] = ev
                em.wait("vector", ACT, ev)
                em.wait("vector", ACT, last.get("cvb_rd", 0))
                cv = em.op("vector", DVE, 1, lambda E, d=db: E.tensor_scalar(
                    out=cvb[:], in0=xib[:, 0:T], scalar1=convw_sb[:, d * 4:d * 4 + 1],
                    scalar2=None, op0=ALU.mult))
                for k in range(1, 4):
                    cv = em.op("vector", DVE, 1, lambda E, d=db, kk=k: E.scalar_tensor_tensor(
                        out=cvb[:], in0=xib[:, kk:kk + T],
                        scalar=convw_sb[:, d * 4 + kk:d * 4 + kk + 1],
                        in1=cvb[:], op0=ALU.mult, op1=ALU.add))
                last["xib_rd"] = cv
                em.wait("scalar", DVE, cv)
                uu = em.op("scalar", ACT, 1, lambda E, d=db: E.activation(
                    out=u_sb[:, d * T:(d + 1) * T], in_=cvb[:], func=AF.Silu,
                    bias=convb_sb[:, d:d + 1]))
                last["cvb_rd"] = uu
                em.op("scalar", ACT, 1, lambda E, d=db: E.activation(
                    out=halo_sb[:, d * 3:(d + 1) * 3], in_=xib[:, T:T + 3], func=AF.Copy))
                em.wait("vector", ACT, uu)
                em.wait("vector", PE, last.get(f"ubf_rd_{par}", 0))
                ub = em.op("vector", DVE, 1, lambda E, d=db, p=par: E.tensor_copy(
                    out=ubf_sb[:, p * T:(p + 1) * T], in_=u_sb[:, d * T:(d + 1) * T]))
                last["pending_xdbl"] = (db, par, ub)
                yield

            flush_xdbl()
            last["u_wr"] = uu
            xdbl_pe = em.cnt[PE]

            em.wait("scalar", PE, xdbl_pe)
            em.wait("scalar", LD, 1)
            ev_dtr = em.op("scalar", ACT, 1, lambda E: E.activation(
                out=dtrb_sb[0:64, :], in_=psC[0:64, 0:T], func=AF.Copy))
            last["ev_dtr"] = ev_dtr
            ev_B = em.op("scalar", ACT, 1, lambda E: E.activation(
                out=Bsb[64:128, :], in_=psC[64:128, 0:T], func=AF.Copy))
            ev_C = em.op("scalar", ACT, 1, lambda E: E.activation(
                out=Csb[0:64, :], in_=psC[0:64, T:2 * T], func=AF.Copy))
            last["xdbl_ps_rd"] = ev_C
            em.wait("sync", ACT, ev_B)
            bf1 = em.op("sync", LD, 16, lambda E: E.dma_start(
                out=bflat_d, in_=Bsb[64:128, :]))
            last["bf1"] = bf1
            em.wait("tensor", ACT, ev_C)
            em.wait("tensor", ACT, last["xi_ps_rd"])
            em.wait("tensor", DVE, last.get("g_dve_1", 0))
            em.wait("tensor", DVE, last.get("yf_ps_rd", 0))
            pe_c2 = 0
            for tt in range(TT):
                pe_c2 = em.op("tensor", PE, 1, lambda E, t=tt: E.matmul(
                    out=psA[:, 256 + t * 64:256 + (t + 1) * 64],
                    lhsT=Csb[0:64, t * 128:(t + 1) * 128], rhs=idf_sb[0:64, 0:64],
                    is_transpose=True, start=(t == 0), stop=(t == TT - 1)))
            em.wait("scalar", PE, pe_c2)
            ev_ctm = em.op("scalar", ACT, 1, lambda E: E.activation(
                out=ctm_sb[:], in_=psA[:, 256:256 + TT * N], func=AF.Copy))
            last["ev_ctm"] = ev_ctm

            em.wait("sync", ACT, uu)
            em.wait("sync", DVE, last.get("yf_ps_rd", 0))
            bo2 = em.op("sync", LD, 16, lambda E: E.dma_start(out=u_d, in_=u_sb[:]))
            em.wait("sync", ACT, last["xn2fm_wr"])
            bo1 = em.op("sync", LD, 16, lambda E: E.dma_start(out=xn2_d, in_=xn2fm[:]))
            em.wait("sync", LD, bo1)
            em.wait("sync", PE, max(last.get("wz_rd_0", 0), last.get("wz_rd_1", 0)))
            em.op("sync", LD, 16, lambda E: E.dma_start(
                out=xn2my[:].rearrange("p (k t) -> p k t", k=KC),
                in_=xn2_d.rearrange("p (k t) -> p k t", k=KC)[:, :, ds(SN["t"], 128)]))

            if dbg and c == 0:
                em.wait("sync", ACT, last["xn2fm_wr"])
                em.op("sync", OUT, 16, lambda E: E.dma_start(
                    out=dbg_outs["d_xn2"], in_=xn2fm[:]))
                em.wait("sync", ACT, last["u_wr"])
                em.op("sync", OUT, 16, lambda E: E.dma_start(out=dbg_outs["d_u"], in_=u_sb[:]))
                em.wait("sync", ACT, ev_B)
                em.op("sync", OUT, 16, lambda E: E.dma_start(out=dbg_outs["d_B"], in_=Bsb[64:128, :]))
                em.wait("sync", ACT, ev_C)
                em.op("sync", OUT, 16, lambda E: E.dma_start(out=dbg_outs["d_C"], in_=Csb[0:64, :]))


        def emit_B(c):
            for db in range(NDB):
                em.wait("tensor", ACT, last["ev_dtr"])
                em.wait("tensor", ACT, last.get("dlt_ps_rd", 0))
                if db == 0:
                    em.wait("tensor", ACT, last.get("bbc_ps_rd", 0))
                pe_dl = em.op("tensor", PE, 1, lambda E, d=db: E.matmul(
                    out=psF[:, 512:512 + T], lhsT=wdt_sb[0:64, d * 128:(d + 1) * 128],
                    rhs=dtrb_sb[0:64, :], start=True, stop=True))
                em.wait("scalar", PE, pe_dl)
                em.wait("scalar", DVE, last.get("ssp_rd", 0))
                em.op("scalar", ACT, 1, lambda E, d=db: E.activation(
                    out=ssp_sb[:], in_=psF[:, 512:512 + T], func=AF.Abs,
                    bias=bdt_sb[:, d:d + 1]))
                em.op("scalar", ACT, 1, lambda E: E.activation(
                    out=ssp_sb[:], in_=ssp_sb[:], func=AF.Exp, scale=ko_sb[:, 4:5]))
                em.op("scalar", ACT, 1, lambda E: E.activation(
                    out=ssp_sb[:], in_=ssp_sb[:], func=AF.Ln, bias=ko_sb[:, 3:4]))
                em.wait("scalar", DVE, last.get("sz_rd", 0))
                em.wait("scalar", OUT, 1)
                a4 = em.op("scalar", ACT, 1, lambda E, d=db: E.activation(
                    out=y1f_sb[:], in_=psF[:, 512:512 + T], func=AF.Relu,
                    bias=bdt_sb[:, d:d + 1]))
                last["dlt_ps_rd"] = a4
                em.wait("vector", ACT, a4)
                em.wait("vector", ACT, last.get(f"delta_rd_{db}", 0))
                ddl = em.op("vector", DVE, 1, lambda E, d=db: E.tensor_add(
                    out=delta_sb[:, d * T:(d + 1) * T], in0=y1f_sb[:], in1=ssp_sb[:]))
                last[f"delta_wr_{db}"] = ddl
                last["ssp_rd"] = ddl
                em.wait("vector", ACT, last["u_wr"])
                em.op("vector", DVE, 1, lambda E, d=db: E.tensor_mul(
                    out=du_sb[:, d * T:(d + 1) * T], in0=delta_sb[:, d * T:(d + 1) * T],
                    in1=u_sb[:, d * T:(d + 1) * T]))
                yield
            if dbg and c == 0:
                em.wait("sync", DVE, em.cnt[DVE])
                em.op("sync", OUT, 16, lambda E: E.dma_start(
                    out=dbg_outs["d_delta"], in_=delta_sb[:]))
                em.wait("sync", OUT, 1)
            em.wait("sync", DVE, em.cnt[DVE])
            em.wait("sync", LD, 1)
            em.op("sync", LD, 16, lambda E: E.dma_start(
                out=u_sb[:, 0:NDB * 128].rearrange("p (db t) -> p db t", db=NDB),
                in_=u_d.rearrange("p (db t) -> p db t", db=NDB)[:, :, ds(SN["t"], 128)]))
            yield
            em.wait("vector", PE, last.get("yacc_rd_pe", 0))
            em.op("vector", DVE, 1, lambda E: E.memset(yacc_sb[:], 0.0))
            em.wait("tensor", ACT, last["ev_ctm"])
            em.wait("tensor", ACT, last["xdbl_ps_rd"])
            em.wait("tensor", ACT, last["dlt_ps_rd"])
            for grp in range(NGRP):
                bpar = grp % 2
                em.wait("sync", LD, last["bf1"])
                em.wait("sync", PE, last.get("row_rd_pe", 0))
                bfg = em.op("sync", LD, 16, lambda E, g=grp: E.dma_start(
                    out=row_sb[0:1, :],
                    in_=bflat_d[g * G:(g + 1) * G, :].rearrange("n t -> (n t)")[None, :]))
                em.wait("tensor", LD, bfg)
                ev_b = 0
                for half in range(2):
                    em.wait("tensor", ACT, last.get("bbc_ps_rd", 0))
                    pe_b = em.op("tensor", PE, 1, lambda E, hf=half: E.matmul(
                        out=psF[:, 512:1024], lhsT=ones1_sb[:],
                        rhs=row_sb[:, hf * 512:(hf + 1) * 512],
                        start=True, stop=True))
                    last["row_rd_pe"] = pe_b
                    em.wait("scalar", PE, pe_b)
                    if half == 0:
                        em.wait("scalar", DVE, last.get(f"bbc_rd_pool_{bpar}", 0))
                    ev_b = em.op("scalar", ACT, 1, lambda E, hf=half, bp=bpar: E.activation(
                        out=bbc_sb[:, bp * 1024 + hf * 512:bp * 1024 + (hf + 1) * 512],
                        in_=psF[:, 512:1024], func=AF.Copy))
                    last["bbc_ps_rd"] = ev_b

                for db in range(NDB):
                    par = db % 2
                    hq = (grp * NDB + db) % NH
                    dec, dbuf = dec_sb[par], dbu_sb[par]
                    em.wait("scalar", DVE, max(last["dec_rd"][par],
                                               last.get(f"delta_wr_{db}", 0)))
                    ac_d = 0
                    for g in range(G):
                        nn = grp * G + g
                        ac_d = em.op("scalar", ACT, 1, lambda E, d=db, n=nn, g=g, dd=dec:
                                     E.activation(
                            out=dd[:, g * (T + 1) + 1:(g + 1) * (T + 1)],
                            in_=delta_sb[:, d * T:(d + 1) * T], func=AF.Exp,
                            scale=a_sb[:, d * N + n:d * N + n + 1]))
                    last[f"delta_rd_{db}"] = ac_d
                    if last.get("pending_gather") is not None:
                        pd, pg0, phq, psc = last["pending_gather"]
                        em.wait("scalar", DVE, psc)
                        gat = em.op("scalar", ACT, 1, lambda E, d=pd, g0=pg0,
                                    hh=h_sb[phq]: E.activation(
                            out=hstate_sb[:, d * N + g0 * G:d * N + (g0 + 1) * G],
                            in_=hh[:].rearrange("p (g t) -> p g t", g=G)[:, :, T],
                            func=AF.Copy))
                        last["h_rd_act"][phq] = gat
                        last["pending_gather"] = None
                    dbu_pay = dbuf[:].rearrange("p (g t) -> p g t", g=G)[:, :, 1:]
                    em.wait("vector", ACT, ev_b)
                    em.wait("vector", ACT, ac_d)
                    em.wait("vector", ACT, last["h_rd_act"][hq])
                    po = em.op("vector", DVE, 1, lambda E, d=db, dp=dbu_pay, bp=bpar:
                               E.tensor_tensor(
                        out=dp,
                        in0=du_sb[:, d * T:(d + 1) * T].unsqueeze(1).broadcast_to(
                            [128, G, T]),
                        in1=bbc_sb[:, bp * 1024:(bp + 1) * 1024].rearrange(
                            "p (g t) -> p g t", g=G),
                        op=ALU.mult))
                    last[f"bbc_rd_pool_{bpar}"] = po
                    bnd = em.op("vector", DVE, 1, lambda E, d=db, g0=grp, dd=dbuf:
                                E.tensor_copy(
                        out=dd[:].rearrange("p (g t) -> p g t", g=G)[:, :, 0],
                        in_=hstate_sb[:, d * N + g0 * G:d * N + (g0 + 1) * G]))
                    em.wait("vector", PE, last["h_rd_pe"][hq])
                    em.wait("vector", ACT, last["h_rd_act"][hq])
                    sc_v = em.op("vector", DVE, 1, lambda E, dd=dec, bb=dbuf, hh=h_sb[hq]:
                                 E.tensor_tensor_scan(
                        out=hh[:], data0=dd[:], data1=bb[:], initial=0.0,
                        op0=ALU.mult, op1=ALU.add))
                    last["dec_rd"][par] = sc_v
                    last["dbu_rd"][par] = sc_v
                    last[f"h_wr_{hq}"] = sc_v
                    last["pending_gather"] = (db, grp, hq, sc_v)
                    yield

                    if db % 4 == 3:
                        q0 = db - 3
                        hqs = [(grp * NDB + q0 + q) % NH for q in range(4)]
                        for g in range(G):
                            nn = grp * G + g
                            for tt in range(TT):
                                use_a = (g * TT + tt) % 2 == 0
                                rt = psA if use_a else psC
                                key = "rt_rd_A" if use_a else "rt_rd_B"
                                em.wait("tensor", DVE, last.get(key, 0))
                                pe_r = 0
                                for q in range(4):
                                    em.wait("tensor", DVE, last[f"h_wr_{hqs[q]}"])
                                    pe_r = em.op("tensor", PE, 1,
                                                 lambda E, qq=hqs[q], qi=q, gg=g, t=tt, r=rt: E.matmul(
                                        out=r[:, qi * 128:(qi + 1) * 128],
                                        lhsT=h_sb[qq][:, gg * (T + 1) + 1 + t * 128:
                                                      gg * (T + 1) + 1 + (t + 1) * 128],
                                        rhs=idf_sb[:], is_transpose=True,
                                        start=(qi == 0), stop=(qi == 3)))
                                for q in range(4):
                                    last["h_rd_pe"][hqs[q]] = pe_r
                                em.wait("vector", PE, pe_r)
                                st_v = em.op("vector", DVE, 1,
                                             lambda E, n=nn, t=tt, r=rt, qb=q0:
                                             E.scalar_tensor_tensor(
                                    out=yacc_sb[:, t * DI + qb * 128:t * DI + qb * 128 + 512],
                                    in0=r[:, 0:512],
                                    scalar=ctm_sb[:, t * N + n:t * N + n + 1],
                                    in1=yacc_sb[:, t * DI + qb * 128:t * DI + qb * 128 + 512],
                                    op0=ALU.mult, op1=ALU.add))
                                last[key] = st_v
            if last.get("pending_gather") is not None:
                pd, pg0, phq, psc = last["pending_gather"]
                em.wait("scalar", DVE, psc)
                gat = em.op("scalar", ACT, 1, lambda E, d=pd, g0=pg0,
                            hh=h_sb[phq]: E.activation(
                    out=hstate_sb[:, d * N + g0 * G:d * N + (g0 + 1) * G],
                    in_=hh[:].rearrange("p (g t) -> p g t", g=G)[:, :, T],
                    func=AF.Copy))
                last["h_rd_act"][phq] = gat
                last["pending_gather"] = None
            yacc_dve = em.cnt[DVE]
            last["yacc_dve"] = yacc_dve

            if dbg and c == 0:
                em.wait("sync", DVE, yacc_dve)
                em.op("sync", OUT, 16, lambda E: E.dma_start(
                    out=dbg_outs["d_yacc"], in_=yacc_sb[:]))


        def emit_C1(c):
            last["pending_wout"] = None

            def flush_wout():
                d, p, y1b, lwo = last["pending_wout"]
                em.wait("tensor", DVE, y1b)
                em.wait("tensor", LD, lwo)
                if d == 0:
                    em.wait("tensor", DVE, last.get("ffn_ps_rd", 0))
                    em.wait("tensor", ACT, last.get("bbc_ps_rd", 0))
                pe_o = 0
                for half in range(2):
                    pe_o = em.op("tensor", PE, 1, lambda E, d=d, hf=half,
                                 p=p: E.matmul(
                        out=psE[:, hf * 512:(hf + 1) * 512],
                        lhsT=y1bf_sb[:, p * 128:(p + 1) * 128],
                        rhs=woutb[:, p * DM + hf * 512:p * DM + (hf + 1) * 512],
                        start=(d == 0), stop=(d == NDB - 1)))
                last[f"wout_rd_{p}"] = pe_o
                last[f"y1bf_rd_{p}"] = pe_o
                last["pending_wout"] = None

            for db in range(NDB):
                par = db % 2
                em.wait("sync", PE, last.get(f"wout_rd_{par}", 0))
                lwo = dma(wout_s[:, db * DM:(db + 1) * DM],
                          woutb[:, par * DM:(par + 1) * DM])
                em.wait("sync", PE, last.get(f"wz_rd_{par}", 0))
                lwz = dma(wz_s[:, db * 1024:(db + 1) * 1024],
                          wzbuf[:, par * 1024:(par + 1) * 1024])
                em.wait("tensor", DVE, last["yacc_dve"])
                em.wait("tensor", DVE, last.get("yf_ps_rd", 0))
                pe_y = 0
                for t in range(TT):
                    pe_y = em.op("tensor", PE, 1, lambda E, d=db, t=t: E.matmul(
                        out=psA[:, 0:128],
                        lhsT=yacc_sb[:, t * DI + d * 128:t * DI + (d + 1) * 128],
                        rhs=idh_sb[:, t * 128:(t + 1) * 128],
                        start=(t == 0), stop=(t == TT - 1)))
                last["yacc_rd_pe"] = pe_y
                em.wait("vector", PE, pe_y)
                em.wait("vector", OUT, last.get("y1f_dump", 0))
                em.wait("vector", LD, 1)
                y1v = em.op("vector", DVE, 1, lambda E, d=db: E.scalar_tensor_tensor(
                    out=y1f_sb[:, 0:128], in0=u_sb[:, d * 128:(d + 1) * 128],
                    scalar=dvec_sb[:, d:d + 1], in1=psA[:, 0:128],
                    op0=ALU.mult, op1=ALU.add))
                last["yf_ps_rd"] = y1v
                em.wait("tensor", LD, lwz)
                if db == 0:
                    em.wait("tensor", ACT, last["xdbl_ps_rd"])
                    em.wait("tensor", DVE, last.get("rt_rd_B", 0))
                em.wait("tensor", ACT, last.get("z_ps_rd", 0))
                pe_z = 0
                for kc in range(KC):
                    pe_z = em.op("tensor", PE, 1, lambda E, k=kc, p=par: E.matmul(
                        out=psC[:, 0:128],
                        lhsT=wzbuf[:, p * 1024 + k * 128:p * 1024 + (k + 1) * 128],
                        rhs=xn2my[:, k * 128:(k + 1) * 128],
                        start=(k == 0), stop=(k == KC - 1)))
                last[f"wz_rd_{par}"] = pe_z
                em.wait("scalar", PE, pe_z)
                em.wait("scalar", DVE, last.get("sz_rd", 0))
                szv = em.op("scalar", ACT, 1, lambda E: E.activation(
                    out=sz_sb[:], in_=psC[:, 0:128], func=AF.Silu))
                last["z_ps_rd"] = szv
                em.wait("vector", ACT, szv)
                em.wait("vector", PE, last.get(f"y1bf_rd_{par}", 0))
                y1b = em.op("vector", DVE, 1, lambda E, p=par: E.tensor_mul(
                    out=y1bf_sb[:, p * 128:(p + 1) * 128], in0=y1f_sb[:, 0:128],
                    in1=sz_sb[:]))
                last["sz_rd"] = y1b
                if last.get("pending_wout") is not None:
                    flush_wout()
                last["pending_wout"] = (db, par, y1b, lwo)
                if dbg and c == 0:
                    em.wait("sync", DVE, y1b)
                    last["y1f_dump"] = em.op("sync", OUT, 16, lambda E, d=db: E.dma_start(
                        out=dbg_outs["d_y1"][:, d * 128:(d + 1) * 128],
                        in_=y1f_sb[:, 0:128]))
            flush_wout()
            last["om_pe"] = em.cnt[PE]

        def emit_C2(c):
            sm = em.op("vector", DVE, 1, lambda E: E.tensor_scalar(
                out=scm_sb[:, 4:5], in0=sc_a[:, 9:10], scalar1=msel_sb[:, 1:2],
                scalar2=None, op0=ALU.mult))
            sm = em.op("vector", DVE, 1, lambda E: E.scalar_tensor_tensor(
                out=scm_sb[:, 0:1], in0=sc_a[:, 1:2], scalar=msel_sb[:, 0:1],
                in1=scm_sb[:, 4:5], op0=ALU.mult, op1=ALU.add))
            sm = em.op("vector", DVE, 1, lambda E: E.tensor_scalar(
                out=scm_sb[:, 5:6], in0=sc_d[:, 11:12], scalar1=msel_sb[:, 1:2],
                scalar2=None, op0=ALU.mult))
            sm = em.op("vector", DVE, 1, lambda E: E.scalar_tensor_tensor(
                out=scm_sb[:, 1:2], in0=sc_d[:, 3:4], scalar=msel_sb[:, 0:1],
                in1=scm_sb[:, 5:6], op0=ALU.mult, op1=ALU.add))

            em.wait("sync", DVE, last.get("xt_rd_dve", 0))
            em.wait("sync", ACT, last.get("xt_rd_act", 0))
            ldx = dma(xs_d[c * T:c * T + 128, :], xt_sb[:])
            em.wait("vector", LD, ldx)
            em.wait("vector", PE, last["om_pe"])
            xv = em.op("vector", DVE, 1, lambda E: E.tensor_add(
                out=big1[:], in0=xt_sb[:], in1=psE[:]))
            last["om_ps_rd"] = xv
            xv = em.op("vector", DVE, 1, lambda E: E.tensor_scalar(
                out=xt_sb[:], in0=xt_sb[:], scalar1=scm_sb[:, 0:1],
                scalar2=scm_sb[:, 1:2], op0=ALU.mult, op1=ALU.add))
            xv = em.op("vector", DVE, 1, lambda E: E.tensor_add(
                out=x2_sb[:], in0=big1[:], in1=xt_sb[:]))
            last["xt_rd_dve"] = xv
            em.op("vector", DVE, 1, lambda E: E.bn_stats(
                out=bn_sb[:, 0:6], in_=x2_sb[:, 0:512]))
            em.op("vector", DVE, 1, lambda E: E.bn_stats(
                out=bn_sb[:, 6:12], in_=x2_sb[:, 512:DM]))
            em.op("vector", DVE, 1, lambda E: E.engine_nop())
            bs = em.op("vector", DVE, 1, lambda E: E.bn_aggr(
                out=bn_sb[:, 12:14], in_=bn_sb[:, 0:12]))
            em.wait("scalar", DVE, bs)
            em.op("scalar", ACT, 1, lambda E: E.activation(
                out=sc_a[:, 13:14], in_=bn_sb[:, 13:14], func=AF.Ln,
                bias=ko_sb[:, 2:3]))
            ac = em.op("scalar", ACT, 1, lambda E: E.activation(
                out=sc_a[:, 14:15], in_=sc_a[:, 13:14], func=AF.Exp,
                scale=ko_sb[:, 5:6]))
            em.wait("vector", ACT, ac)
            q3v = em.op("vector", DVE, 1, lambda E: E.scalar_tensor_tensor(
                out=scm_sb[:, 3:4], in0=bn_sb[:, 12:13], scalar=-1.0,
                in1=sc_a[:, 14:15], op0=ALU.mult, op1=ALU.mult))
            xn3v = em.op("vector", DVE, 1, lambda E: E.tensor_scalar(
                out=xn3t_sb[:], in0=x2_sb[:],
                scalar1=sc_a[:, 14:15], scalar2=scm_sb[:, 3:4],
                op0=ALU.mult, op1=ALU.add))
            last["x2_rd_dve2"] = xn3v
            em.wait("tensor", DVE, xn3v)
            em.wait("tensor", ACT, last.get("psT_rd", 0))
            pe_t = 0
            for kc in range(KC):
                pe_t = em.op("tensor", PE, 1, lambda E, k=kc: E.matmul(
                    out=psT[:, k * 128:(k + 1) * 128],
                    lhsT=xn3t_sb[:, k * 128:(k + 1) * 128], rhs=idb_sb[:],
                    is_transpose=True, start=(k == 0), stop=(k == KC - 1)))
            em.wait("scalar", PE, pe_t)
            em.wait("scalar", PE, last.get("xn3f_rd", 0))
            ev = 0
            for kc in range(KC):
                ev = em.op("scalar", ACT, 1, lambda E, k=kc: E.activation(
                    out=xn3f_sb[:, k * 128:(k + 1) * 128],
                    in_=psT[:, k * 128:(k + 1) * 128], func=AF.Copy))
            last["psT_rd"] = ev
            last["xn3f_wr"] = ev


        def emit_C3(c, ga=iter(())):
            last["pending_down"] = None

            def flush_down():
                h, p, hhv = last["pending_down"]
                em.wait("tensor", DVE, hhv)
                if h == 0:
                    em.wait("tensor", DVE, last.get("x3_ffn_rd", 0))
                pe_dn = 0
                for half in range(2):
                    pe_dn = em.op("tensor", PE, 1, lambda E, h=h, hf=half,
                                  p=p: E.matmul(
                        out=psE[:, hf * 512:(hf + 1) * 512],
                        lhsT=hh_sb[:, p * 128:(p + 1) * 128],
                        rhs=wdb[:, p * DM + hf * 512:p * DM + (hf + 1) * 512],
                        start=(h == 0), stop=(h == NHC - 1)))
                last[f"hh_rd_{p}"] = pe_dn
                last[f"wffn_rd_{p}"] = pe_dn
                last["pending_down"] = None

            for hc in range(NHC):
                par = hc % 2
                for _ in range(12):
                    next(ga, None)
                em.wait("sync", PE, last.get(f"wffn_rd_{par}", 0))
                dma(wg_s[:, hc * 1024:(hc + 1) * 1024], wgb[:, par * 1024:(par + 1) * 1024])
                dma(wu_s[:, hc * 1024:(hc + 1) * 1024], wub[:, par * 1024:(par + 1) * 1024])
                ldn = dma(wd_s[:, hc * DM:(hc + 1) * DM], wdb[:, par * DM:(par + 1) * DM])
                em.wait("tensor", LD, ldn)
                em.wait("tensor", ACT, last["xn3f_wr"])
                em.wait("tensor", DVE, last.get(f"g_dve_{par}", 0))
                em.wait("tensor", ACT, last.get(f"g_act_{par}", 0))
                gu_ps = psB if par == 0 else psF
                if hc == 0:
                    em.wait("tensor", ACT, last["dlt_ps_rd"])
                pe_g = 0
                for kc in range(KC):
                    pe_g = em.op("tensor", PE, 1, lambda E, k=kc, p=par, gp=gu_ps: E.matmul(
                        out=gp[:, 0:128],
                        lhsT=wgb[:, p * 1024 + k * 128:p * 1024 + (k + 1) * 128],
                        rhs=xn3f_sb[:, k * 128:(k + 1) * 128],
                        start=(k == 0), stop=(k == KC - 1)))
                em.wait("scalar", PE, pe_g)
                em.wait("scalar", DVE, last.get(f"sfn_rd_{par}", 0))
                sg = em.op("scalar", ACT, 1, lambda E, p=par, gp=gu_ps: E.activation(
                    out=sfn_sb[:, p * 128:(p + 1) * 128],
                    in_=gp[:, 0:128], func=AF.Silu,
                    scale=ko_sb[:, 6:7]))
                last[f"g_act_{par}"] = sg
                em.wait("tensor", ACT, sg)
                pe_u = 0
                for kc in range(KC):
                    pe_u = em.op("tensor", PE, 1, lambda E, k=kc, p=par, gp=gu_ps: E.matmul(
                        out=gp[:, 0:128],
                        lhsT=wub[:, p * 1024 + k * 128:p * 1024 + (k + 1) * 128],
                        rhs=xn3f_sb[:, k * 128:(k + 1) * 128],
                        start=(k == 0), stop=(k == KC - 1)))
                if last.get("pending_down") is not None:
                    flush_down()
                em.wait("vector", PE, pe_u)
                em.wait("vector", ACT, sg)
                em.wait("vector", PE, last.get(f"hh_rd_{par}", 0))
                hhv = em.op("vector", DVE, 1, lambda E, p=par, gp=gu_ps: E.scalar_tensor_tensor(
                    out=hh_sb[:, p * 128:(p + 1) * 128],
                    in0=gp[:, 0:128],
                    scalar=FS, in1=sfn_sb[:, p * 128:(p + 1) * 128],
                    op0=ALU.mult, op1=ALU.mult))
                last[f"sfn_rd_{par}"] = hhv
                last[f"g_dve_{par}"] = hhv
                last["pending_down"] = (hc, par, hhv)
            flush_down()
            ffn_pe = em.cnt[PE]
            last["xn3f_rd"] = ffn_pe
            em.wait("vector", PE, ffn_pe)
            em.wait("vector", OUT, last.get("x3_rd", 0))
            x3v = em.op("vector", DVE, 1, lambda E: E.scalar_tensor_tensor(
                out=x3_sb[:], in0=psE[:], scalar=FS,
                in1=x2_sb[:], op0=ALU.mult, op1=ALU.add))
            last["x3_ffn_rd"] = x3v
            last["ffn_ps_rd"] = x3v
            em.wait("sync", DVE, x3v)
            ov = em.op("sync", OUT, 16, lambda E, cc=c: E.dma_start(
                out=out_d[cc * 128:(cc + 1) * 128, :], in_=x3_sb[:]))
            last["x3_rd"] = ov


        import itertools
        for _ in emit_A(0):
            pass
        for _ in emit_B(0):
            pass
        for c in range(NCH):
            emit_C1(c)
            emit_C2(c)
            if c + 1 < NCH:
                gchain = itertools.chain(emit_A(c + 1), emit_B(c + 1))
            else:
                gchain = iter(())
            for _ in range(2):
                next(gchain, None)
            emit_C3(c, gchain)
            for _ in gchain:
                pass

        em.wait("sync", OUT, em.cnt[OUT])

        @block.sync
        def _(E):
            em.replay("sync", E, sems)

        @block.vector
        def _(E):
            em.replay("vector", E, sems)

        @block.scalar
        def _(E):
            em.replay("scalar", E, sems)

        @block.tensor
        def _(E):
            em.replay("tensor", E, sems)

        @block.gpsimd
        def _(E):
            em.replay("gpsimd", E, sems)

    return nc


def _bf16():
    import ml_dtypes
    return ml_dtypes.bfloat16


def host_inputs(inputs, b, h=0):
    ident = np.eye(128, dtype=np.float32)
    zero = np.zeros((128, 128), dtype=np.float32)
    return {
        "idh": np.concatenate([ident if h == 0 else zero,
                               ident if h == 1 else zero], axis=1),
        "msel": np.tile(np.array([[1.0 - h, float(h)]], dtype=np.float32),
                        (128, 1)),
        "hoff": np.array([[h * 128]], dtype=np.int32),
        "xs": np.ascontiguousarray(np.concatenate(
            [inputs["L0_attn_x"][b][h * 128:],
             np.zeros((h * 128, DM), dtype=np.float32)])) if h else
        np.ascontiguousarray(inputs["L0_attn_x"][b]),
        "x": np.ascontiguousarray(inputs["L0_attn_x"][b]),
        "w_in": np.ascontiguousarray(inputs["L0_attn_W_in"]),
        "wxp": np.ascontiguousarray(inputs["L0_attn_W_xproj"]),
        "wdt": np.ascontiguousarray(inputs["L0_attn_W_dt"]),
        "wout": np.ascontiguousarray(inputs["L0_attn_W_out"]),
        "alog": np.ascontiguousarray(
            inputs["L0_attn_A_log"].reshape(NDB, 128, N).transpose(1, 0, 2).reshape(128, NDB * N)),
        "dvec": np.ascontiguousarray(inputs["L0_attn_D"].reshape(NDB, 128).T),
        "convw": np.ascontiguousarray(
            inputs["L0_attn_conv_w"].reshape(NDB, 128, 4).transpose(1, 0, 2).reshape(128, NDB * 4)),
        "convb": np.ascontiguousarray(inputs["L0_attn_conv_b"].reshape(NDB, 128).T),
        "bdt": np.ascontiguousarray(inputs["L0_attn_b_dt"].reshape(NDB, 128).T),
        "lnw": np.ascontiguousarray(inputs["L0_attn_ln_w"].reshape(1, DM)),
        "lnb": np.ascontiguousarray(inputs["L0_attn_ln_b"].reshape(1, DM)),
        "wgT": np.ascontiguousarray(inputs["L0_ffn_w_gate"].T),
        "wuT": np.ascontiguousarray(inputs["L0_ffn_w_up"].T),
        "wdT": np.ascontiguousarray(inputs["L0_ffn_w_down"].T),
        "idf": np.eye(128, dtype=np.float32),
        "idb": np.eye(128, dtype=np.float32).astype(_bf16()),
    }


def kernel(**inputs):
    nc = build_nc()
    in_maps = [host_inputs(inputs, core // 2, core % 2) for core in range(8)]
    res = run_bass_kernel_spmd(nc, in_maps, core_ids=list(range(8)))
    out = np.empty((4, L, DM), dtype=np.float32)
    nch = L // T
    for b in range(4):
        v = out[b].reshape(nch, 2, 128, DM)
        v[:, 0] = res.results[2 * b]["out"].reshape(nch, 128, DM)
        v[:, 1] = res.results[2 * b + 1]["out"].reshape(nch, 128, DM)
    return out
